# version 04
# baseline (speedup 1.0000x reference)
import numpy as np
import ml_dtypes
from contextlib import ExitStack

import concourse.bass as bass
import concourse.mybir as mybir
import concourse.library_config as libcfg
from concourse.library_overlay import lower_extended_insts
from concourse.bass_utils import run_bass_kernel_spmd

F32 = mybir.dt.float32
BF16 = mybir.dt.bfloat16
I32 = mybir.dt.int32
I16 = mybir.dt.int16
U32 = mybir.dt.uint32
AX = mybir.AxisListType
OP = mybir.AluOpType
AF = mybir.ActivationFunctionType
BF = ml_dtypes.bfloat16

NC = 8
S = 2048
D = 1024
TLOC = 4096
ENGS = ("sync", "scalar", "tensor", "vector", "gpsimd")


class Prog:

    def __init__(self, nc):
        self.nc = nc
        self.steps = {e: [] for e in ENGS}
        self.count = {e: 0 for e in ENGS}
        self.sems = {}

    def alloc_sems(self, es):
        for e in ENGS:
            self.sems[e] = es.enter_context(self.nc.semaphore(f"sem_{e}"))
            self.sems["d_" + e] = es.enter_context(self.nc.semaphore(f"dsem_{e}"))
            self.count["d_" + e] = 0

    def step(self, eng, fn, after=(), dma=0, tag=""):
        waits = [(self.sems[s], v) for (s, v) in after]
        key = ("d_" + eng) if dma else eng
        amount = 16 * dma if dma else 1
        prev_total = self.count[key]
        self.count[key] += amount
        tok = (key, self.count[key])
        self.steps[eng].append((fn, waits, self.sems[key], amount, dma, tag,
                                prev_total))
        return tok

    def barrier(self):
        toks = [(k, self.count[k]) for k in self.count]
        outs = []
        for e in ENGS:
            outs.append(self.step(e, lambda eng: None,
                                  after=[t for t in toks if t[0] != e],
                                  tag="barrier"))
        return outs

    def emit(self, block):
        def mk(eng_name):
            def run(eng):
                for (fn, waits, sem, amount, dma, tag, prev_total) in self.steps[eng_name]:
                    for (s, v) in waits:
                        eng.wait_ge(s, v)
                    if dma:
                        if prev_total:
                            eng.wait_ge(sem, prev_total)
                        state = {"left": amount}

                        def inc(instr, n=16):
                            instr.then_inc(sem, n)
                            state["left"] -= n
                            return instr
                        fn(eng, inc)
                        assert state["left"] == 0, (eng_name, tag, state["left"])
                    else:
                        fn(eng)
                        if eng_name == "sync":
                            eng.nop().then_inc(sem, amount)
                        else:
                            eng.drain().then_inc(sem, amount)
            return run
        block.sync(mk("sync"))
        block.scalar(mk("scalar"))
        block.tensor(mk("tensor"))
        block.vector(mk("vector"))
        block.gpsimd(mk("gpsimd"))

    def final_wait(self, eng):
        toks = [(k, self.count[k]) for k in self.count]
        self.step(eng, lambda E: None,
                  after=[t for t in toks if t[0] != eng], tag="final")


def host_constants(n_cores=NC):
    import jax
    cpu = jax.devices("cpu")[0]
    with jax.default_device(cpu):
        rot = np.asarray(jax.random.normal(jax.random.key(42),
                                           (16, 64, 4, 32), dtype=np.float32))
    c = {}
    c["rot_all"] = np.ascontiguousarray(rot.reshape(16, 64, 128))
    c["ident"] = np.eye(128, dtype=np.float32).astype(BF)
    c["identf"] = np.eye(128, dtype=np.float32)
    c["tri_incl"] = (np.arange(128)[:, None] <= np.arange(128)[None, :]).astype(BF)
    c["tri_excl64"] = (np.arange(64)[:, None] < np.arange(64)[None, :]).astype(np.float32)
    c["ones_1x128f"] = np.ones((1, 128), np.float32)
    c["ones_col_bf"] = np.ones((128, 1), np.float32).astype(BF)
    c["iota32_row"] = np.tile(np.arange(32, dtype=np.float32), (128, 1))
    c["iota32x4_row"] = np.tile(np.arange(32, dtype=np.float32), (128, 4))
    c["big128"] = np.full((128, 128), 1e9, np.float32)
    c["iota256_row"] = np.tile(np.arange(256, dtype=np.float32), (128, 1))
    c["big"] = np.full((128, 32), 1e9, np.float32)
    c["iota_tok"] = np.ascontiguousarray(
        (np.arange(128, dtype=np.float32)[:, None]
         + 128.0 * np.arange(256, dtype=np.float32)[None, :]))
    t = c["iota_tok"].astype(np.int64)
    cc = t >> 12
    ee = (t >> 11) & 1
    c["iota_byex"] = np.ascontiguousarray(
        (t - 2048 * cc + (n_cores * 2048 - 2048) * ee).astype(np.float32))
    no = np.zeros((1, 256), np.float32)
    for n in range(4):
        no[0, n * 64:(n + 1) * 64] = n * 2048.0 - 1.0
    c["nround_off"] = no
    return c


def prep_inputs(inputs, n_cores=NC):
    x = np.asarray(inputs["L0_attn_x"])
    consts = host_constants(n_cores)
    w = {k: np.asarray(v) for k, v in inputs.items() if k != "L0_attn_x"}
    per_core = []
    for c in range(NC):
        xb = x[2 * c:2 * c + 2].reshape(TLOC, D)
        m = dict(
            x_tm=np.ascontiguousarray(xb),
            x_fm=np.ascontiguousarray(xb.T),
            w_qk_t=np.ascontiguousarray(w["L0_attn_w_qk"].T),
            w_v_t=np.ascontiguousarray(w["L0_attn_w_v"].T),
            w_out_t=np.ascontiguousarray(w["L0_attn_w_out"].T),
            qw=w["L0_ffn_qw"], kw=w["L0_ffn_kw"], vw=w["L0_ffn_vw"],
            ow=w["L0_ffn_ow"],
            qb=w["L0_ffn_qb"].reshape(1, -1), kb=w["L0_ffn_kb"].reshape(1, -1),
            vb=w["L0_ffn_vb"].reshape(1, -1), ob=w["L0_ffn_ob"].reshape(1, -1),
            gate_w=w["L0_ffn_gate_w"],
            w1e=np.ascontiguousarray(w["L0_ffn_w1"][c]),
            w2e=np.ascontiguousarray(w["L0_ffn_w2"][c]),
            norm1_w=w["L0_ffn_norm1_w"].reshape(1, -1),
            norm2_w=w["L0_ffn_norm2_w"].reshape(1, -1),
            gtok_base=np.full((1, 1), 4096.0 * c, np.float32),
            expid=np.full((1, 1), float(c), np.float32),
            qb_fm=np.ascontiguousarray(
                w["L0_ffn_qb"].reshape(8, 128).T),
            kb_fm=np.ascontiguousarray(
                w["L0_ffn_kb"].reshape(2, 128).T),
            gate_w_fm=np.ascontiguousarray(
                w["L0_ffn_gate_w"].reshape(8, 128, 8).transpose(1, 0, 2)),
        )
        m.update(consts)
        per_core.append(m)
    return per_core


DT_MAP = {np.dtype(np.float32): F32, np.dtype(BF): BF16,
          np.dtype(np.int32): I32, np.dtype(np.int16): I16}


def build(sample_map, tap_names=(), phase=3, n_cores=NC):
    nc = bass.Bass("TRN2", target_bir_lowering=False, num_devices=NC)
    ins = {k: nc.dram_tensor(k, list(v.shape), DT_MAP[v.dtype],
                             kind="ExternalInput").ap()
           for k, v in sample_map.items()}
    out = nc.dram_tensor("out", [TLOC, D], F32, kind="ExternalOutput").ap()

    es = ExitStack()
    with es:
        P = Prog(nc)
        P.alloc_sems(es)

        def sbf(name, shape, dt=F32):
            return es.enter_context(nc.sbuf_tensor(name, shape, dt))

        def psf(name, shape, dt=F32):
            return es.enter_context(nc.psum_tensor(name, shape, dt))

        tap_outs = []

        def tap(name, ap, tok, shape, dt):
            if name not in tap_names:
                return tok
            t = nc.dram_tensor(f"tap_{name}", list(shape), dt,
                               kind="ExternalOutput").ap()
            tap_outs.append(name)

            def f(eng, inc):
                inc(eng.dma_start(out=t, in_=ap))
            return P.step("sync", f, after=[tok], dma=1, tag=f"tap_{name}")

        build_body(nc, es, P, ins, out, sbf, psf, tap, phase,
                   n_cores=n_cores)
        block = es.enter_context(nc.Block())
        P.emit(block)
    return nc, tap_outs


def build_body(nc, es, P, ins, out, sbf, psf, tap, phase, n_cores=NC):
    x1_tm = sbf("g_x1_tm", [128, 32, D], BF16)
    qb_s = sbf("g_qb", [128, 8])
    kb_s = sbf("g_kb", [128, 2])
    PS = [psf(f"PSb{i}", [128, 512]) for i in range(8)]
    t0 = P.step("gpsimd", lambda e: None, tag="t0")

    t_lsh = emit_lsh(nc, es, P, ins, sbf, PS, tap, x1_tm, t0)

    if phase == 1:
        ob = emit_lsh.outstage[:, 0:2, :]
        tok = t_lsh
        stok = t_lsh
        for i in range(16):
            def cvt(eng, i=i):
                eng.tensor_copy(ob[:], x1_tm[:, 2 * i:2 * i + 2, :])
            tok = P.step("vector", cvt, after=[tok, stok], tag=f"cvt{i}")

            def st(eng, inc, i=i):
                inc(eng.dma_start(
                    out=out.rearrange("(b p) f -> p b f", p=128)[:, 2 * i:2 * i + 2],
                    in_=ob[:]))
            stok = P.step("sync", st, after=[tok], dma=1, tag=f"st{i}")
        P.final_wait("sync")
        return

    L = dict(emit_lsh.ctx)
    L["qb_s"] = qb_s
    L["kb_s"] = kb_s
    if phase >= 3:
        pre = emit_moe_pre(nc, P, L, t_lsh, n_cores=n_cores)
    t_gqa = emit_gqa(nc, es, P, ins, L, tap, x1_tm, t_lsh,
                              n_cores=n_cores)
    H = emit_gqa.handles
    if phase >= 3:
        t_moe = emit_moe(nc, es, P, ins, L, H, pre, tap, x1_tm,
                                  out, t_gqa, n_cores=n_cores)
        P.final_wait("sync")
        return

    if phase == 2:
        R1 = L["R1"]
        xb = R1[:, 4096:4608].bitcast(BF16)
        xf = R1[:, 0:1024]
        tok = t_gqa
        stok = t_gqa
        ltok = t_gqa
        for i in range(32):
            def ld(eng, inc, i=i):
                inc(eng.dma_start(
                    out=xb[:],
                    in_=H["x2loc"].ap()[i // 16].rearrange(
                        "(b p) f -> p b f", p=128)[:, i % 16, :]))
            ltok = P.step("sync", ld, after=[tok, t_gqa], dma=1, tag=f"x2l{i}")

            def cvt(eng, i=i):
                eng.tensor_copy(xf[:], xb[:])
            tok = P.step("vector", cvt, after=[ltok, stok], tag=f"x2c{i}")

            def st(eng, inc, i=i):
                inc(eng.dma_start(
                    out=out.rearrange("(b p) f -> p b f", p=128)[:, i, :],
                    in_=xf[:]))
            stok = P.step("sync", st, after=[tok], dma=1, tag=f"x2s{i}")

        def wcoll(eng):
            eng.wait_ge(H["sem_coll"], H["n_coll"])
        P.step("gpsimd", wcoll, after=[stok], tag="wcoll")
        P.final_wait("sync")
        return


def kernel(**inputs):
    per_core = prep_inputs(inputs)
    nc, _ = build(per_core[0])
    lower_extended_insts(nc)
    res = run_bass_kernel_spmd(nc, per_core, core_ids=list(range(NC)))
    outs = [np.asarray(res.results[c]["out"]) for c in range(NC)]
    return np.concatenate(outs, axis=0).reshape(16, 2048, 1024)


def kernel_taps(inputs, tap_names=(), on_hw=True, n_cores=NC):
    per_core = prep_inputs(inputs, n_cores=n_cores)
    nc, tnames = build(per_core[0], tap_names=tap_names, n_cores=n_cores)
    lower_extended_insts(nc)
    if on_hw:
        res = run_bass_kernel_spmd(nc, per_core[:n_cores],
                                   core_ids=list(range(n_cores)))
        results = [{k: np.asarray(v) for k, v in r.items()} for r in res.results]
    else:
        from simrun import simrun
        results = simrun(nc, per_core[:n_cores],
                         ["out"] + [f"tap_{n}" for n in tnames], num_cores=n_cores)
    return results, tnames




F32 = mybir.dt.float32
BF16 = mybir.dt.bfloat16
I32 = mybir.dt.int32
AX = mybir.AxisListType
OP = mybir.AluOpType
AF = mybir.ActivationFunctionType

S = 2048
D = 1024
NH = 16
DH = 64
NR = 4
NT = 16
NSlot = NR * S
QKV_W = 256
OUT_W = 80


def bfv(region, off_f32, n_bf16):
    return region[:, off_f32:off_f32 + n_bf16 // 2].bitcast(BF16)


def emit_lsh(nc, es, P, ins, sb, ps, tap, x1_tm, t_start):
    rot = sb("lsh_rot", [64, NH, 128], BF16)
    tri = sb("c_tri", [128, 128], BF16)
    tri64 = sb("c_tri64", [64, 64], F32)
    ones1 = sb("c_ones1", [1, 128], F32)
    iota256 = sb("c_iota256", [128, 256], F32)
    iota32x4 = sb("c_iota32x4", [128, 128], F32)
    big128 = sb("c_big128", [128, 128], F32)
    nround_off = sb("c_nroff", [1, 256], F32)
    ident = sb("c_ident", [128, 128], BF16)
    identf = sb("c_identf", [128, 128], F32)
    ones_col = sb("c_onescol", [128, 1], BF16)

    def ld_consts(eng, inc):
        inc(eng.dma_start(out=rot[:], in_=ins["rot_all"].rearrange("h d r -> d h r")))
        inc(eng.dma_start(out=tri[:], in_=ins["tri_incl"]))
        inc(eng.dma_start(out=tri64[:], in_=ins["tri_excl64"]))
        inc(eng.dma_start(out=ones1[:], in_=ins["ones_1x128f"]))
        inc(eng.dma_start(out=iota256[:], in_=ins["iota256_row"]))
        inc(eng.dma_start(out=iota32x4[:], in_=ins["iota32x4_row"]))
        inc(eng.dma_start(out=big128[:], in_=ins["big128"]))
        inc(eng.dma_start(out=nround_off[:], in_=ins["nround_off"]))
        inc(eng.dma_start(out=ident[:], in_=ins["ident"]))
        inc(eng.dma_start(out=identf[:], in_=ins["identf"]))
        inc(eng.dma_start(out=ones_col[:], in_=ins["ones_col_bf"]))
    t_lc = P.step("gpsimd", ld_consts, dma=11, tag="lsh_consts")

    lshsort = nc.dram_tensor("lshsort", [NSlot, QKV_W], BF16).ap()
    lshout = nc.dram_tensor("lshout", [NSlot, OUT_W], BF16).ap()
    rowb = nc.dram_tensor("rowbounce", [512], F32).ap()
    qk_hbm = nc.dram_tensor("qk_hbm", [S, D], BF16).ap()
    v_hbm = nc.dram_tensor("v_hbm", [S, D], BF16).ap()

    R1 = sb("lsh_R1", [128, 8192], F32)
    R2 = sb("lsh_R2", [128, 4096], F32)
    R3 = sb("lsh_R3", [128, 4224], F32)
    R4 = sb("lsh_R4", [128, 2560], F32)
    R5 = sb("lsh_R5", [128, 2560], F32)

    xfm = bfv(R1, 0, 16384).rearrange("p (c t) -> p c t", c=8)
    qFS = bfv(R1, 0, NSlot)[0:64]
    kFS = bfv(R1, 4096, NSlot)[0:64]
    xres = bfv(R1, 0, 4096).rearrange("p (a f) -> p a f", f=D)
    outstage = R1[:].rearrange("p (a f) -> p a f", f=D)

    wqkt = bfv(R2, 0, 8192).rearrange("p (c j) -> p c j", c=8)
    qkS = bfv(R2, 0, 8192).rearrange("p (b w) -> p b w", w=128)
    uns = bfv(R2, 0, 64 * OUT_W).rearrange("p (b w) -> p b w", w=OUT_W)

    wvt = bfv(R3, 0, 8192).rearrange("p (c j) -> p c j", c=8)
    vvb = bfv(R3, 0, 64 * 65).rearrange("p (b c) -> p b c", c=65)
    vvb2 = bfv(R3, 2080, 64 * 65).rearrange("p (b c) -> p b c", c=65)
    prefbuf = R3[:, 0:4096].rearrange("p (t k) -> p t k", k=256)
    scrA = R3[:, 0:1152].rearrange("p (a k) -> p a k", a=4)
    woutt = bfv(R3, 0, 8192).rearrange("p (c j) -> p c j", c=8)

    sortout = bfv(R4, 0, 64 * OUT_W).rearrange("p (b w) -> p b w", w=OUT_W)
    scrAi = R3[:, 0:1152].bitcast(I32).rearrange("p (a k) -> p a k", a=4)
    trash = R4[:, 0:1024]
    O4 = bfv(R4, 0, NT * 256).rearrange("p (t k) -> p t k", k=256)
    exclrep = R4[:, 2048:2304]

    stg = bfv(R5, 0, NT * QKV_W).rearrange("p (t w) -> p t w", w=QKV_W)
    qkFMh = bfv(R5, 0, NT * 128).rearrange("p (t w) -> p t w", w=128)[0:64]
    rsum = R5[:, 0:NT * OUT_W].rearrange("p (t w) -> p t w", w=OUT_W)
    attnh = R5[:, NT * OUT_W:NT * OUT_W + NT * 64].rearrange("p (t w) -> p t w", w=64)
    rec = R5[:, NT * OUT_W + NT * 64:NT * OUT_W + NT * 64 + NT].rearrange(
        "p (t o) -> p t o", o=1)

    attnFM = sb("lsh_attnFM", [128, 8, S], BF16)
    sqt = sb("lsh_sqt", [128, D], F32)
    S2 = sb("lsh_S2", [128, NT, NH], F32)
    rsb = sb("lsh_rs", [128, NT, NH], F32)
    negC = sb("lsh_negC", [128, NH], F32)
    scr16 = sb("lsh_scr16", [16, 128], F32)
    scr1 = sb("lsh_scr1", [1, 256], F32)
    lnm = sb("lsh_lnm", [128, NT, 8], F32)
    bounce = sb("lsh_bounce", [128, 512], BF16)
    bounce2 = sb("lsh_bounce2", [128, 512], BF16)
    qkh = sb("lsh_qkh", [128, NT, 64], BF16)
    kh = sb("lsh_kh", [128, NT, 64], BF16)
    vh = sb("lsh_vh", [128, NT, 64], BF16)
    buckets = sb("lsh_bkt", [128, NT, NR], F32)
    amaxf = sb("lsh_amaxf", [128, 32], F32)
    amaxi = sb("lsh_amaxi", [128, 16], I32)
    runtot = sb("lsh_runtot", [1, 256], F32)
    cnt64 = sb("lsh_cnt64", [64, 4], F32)
    pos4 = sb("lsh_pos4", [128, NT, NR], F32)
    pos4i = sb("lsh_pos4i", [128, NT * NR], I32)
    probs = bfv(R2, 3072, 512)

    PS = ps
    p_dots = PS[0][:]
    p_cnt = PS[5][0:1, 0:256]
    p_pv = PS[4][:, 0:160]
    p_misc = PS[1][:]
    p_b = [PS[2][:], PS[3][:]]

    p_misc_bf = p_misc.bitcast(BF16)
    T = dict(p_misc_bf=p_misc_bf, rot=rot, tri=tri, tri64=tri64, ones1=ones1, iota256=iota256,
             iota32x4=iota32x4, big128=big128, nround_off=nround_off, ident=ident,
             negC=negC, lshsort=lshsort, lshout=lshout, rowb=rowb, identf=identf,
             attnFM=attnFM, qkh=qkh, kh=kh, vh=vh, qkFMh=qkFMh,
             buckets=buckets, scrA=scrA, scrAi=scrAi, amaxf=amaxf, amaxi=amaxi,
             O4=O4, prefbuf=prefbuf,
             runtot=runtot, exclrep=exclrep, cnt64=cnt64, pos4=pos4,
             pos4i=pos4i, stg=stg, qkS=qkS, qFS=qFS, kFS=kFS, vvb=vvb, vvb2=vvb2, probs=probs,
             sortout=sortout, uns=uns, rsum=rsum, rec=rec, attnh=attnh,
             p_dots=p_dots, p_pv=p_pv, p_misc=p_misc, p_cnt=p_cnt,
             ones_col=ones_col, rsb=rsb,
             qk_hbm=qk_hbm, v_hbm=v_hbm)

    lsem = dict(d=es.enter_context(nc.semaphore("lsh_fd")),
                e=es.enter_context(nc.semaphore("lsh_fe")),
                p=es.enter_context(nc.semaphore("lsh_fp")),
                v=es.enter_context(nc.semaphore("lsh_fv")),
                cd=[0], ce=[0], cp=[0], cv=[0])
    T["lsem"] = lsem
    T["PS"] = PS

    def initR5(eng):
        eng.memset(R5[:], 0.0)
    t_i5 = P.step("vector", initR5, tag="lshi5")
    prev = P.step("gpsimd", lambda e: None, after=[t_lc, t_start, t_i5], tag="lsh0")
    emit_lsh.outstage = outstage
    emit_lsh.ctx = dict(R1=R1, R2=R2, R3=R3, R4=R4, R5=R5, PS=PS,
                        ones1=ones1, ident=ident, identf=identf,
                        ones_col=ones_col, attnFM=attnFM, tri=tri, rowb=rowb,
                        iota256=iota256, big128=big128)
    NHEADS = 16
    NBATCH = 2
    for b in range(NBATCH):
        prev = emit_batch_prep(nc, P, ins, b, xfm, wqkt, wvt, qk_hbm, v_hbm,
                               bounce, bounce2, trash, sqt, S2, rsb, negC,
                               scr16, scr1, ones1, identf, p_b, prev, tap)
        for h in range(NHEADS):
            prev = emit_head(nc, P, b, h, prev, T, tap)
        if True:
            prev = emit_batch_tail(nc, P, b, ins, x1_tm, attnFM, woutt, xres,
                                   sqt, trash, lnm, p_b, prev, tap)
    return prev


def emit_batch_prep(nc, P, ins, b, xfm, wqkt, wvt, qk_hbm, v_hbm, bounce,
                    bounce2, trash, sqt, S2, rsb, negC, scr16, scr1, ones1,
                    identf, p_b, prev, tap):
    S_ = S

    def ld(eng, inc):
        inc(eng.dma_start(
            out=xfm[:],
            in_=ins["x_fm"].rearrange("(c p) t -> p c t", p=128)[:, :, b * S_:(b + 1) * S_]))
        inc(eng.dma_start(out=wqkt[:],
                          in_=ins["w_qk_t"].rearrange("(c p) j -> p c j", p=128)))
        inc(eng.dma_start(out=wvt[:],
                          in_=ins["w_v_t"].rearrange("(c p) j -> p c j", p=128)))
    t_ld = P.step("gpsimd", ld, after=[prev], dma=3, tag=f"b{b}ldx")

    toks = []
    sq_toks = []
    st_toks = []
    for wi, (wt, hbm) in enumerate(((wqkt, qk_hbm), (wvt, v_hbm))):
        for ti in range(NT):
            for jh in range(2):
                pp = p_b[(ti * 2 + jh) % 2]
                bb = bounce if jh == 0 else bounce2

                def mm(eng, ti=ti, jh=jh, wt=wt, pp=pp):
                    for c in range(8):
                        eng.matmul(pp[:], xfm[:, c, ti * 128:(ti + 1) * 128],
                                   wt[:, c, jh * 512:(jh + 1) * 512],
                                   start=(c == 0), stop=(c == 7))
                t_mm = P.step("tensor", mm,
                              after=[t_ld] + (toks[-2:-1] if len(toks) >= 2 else []),
                              tag=f"b{b}pj{wi}.{ti}.{jh}")

                def ev(eng, ti=ti, jh=jh, pp=pp, bb=bb):
                    eng.copy(out=bb[:], in_=pp[:])
                aft = [t_mm] + (st_toks[-2:-1] if len(st_toks) >= 2 else [])
                t_ev = P.step("scalar", ev, after=aft, tag=f"b{b}pe{wi}.{ti}.{jh}")
                toks.append(t_ev)

                if wi == 0:
                    def sq(eng, ti=ti, jh=jh, bb=bb):
                        eng.activation(out=trash[:, 0:512], in_=bb[:], func=AF.Square)
                    t_sq = P.step("scalar", sq,
                                  after=[t_ev] + (sq_toks[-1:] if sq_toks else []),
                                  tag=f"b{b}sq{ti}.{jh}")

                    def red(eng, ti=ti, jh=jh):
                        eng.tensor_reduce(
                            S2[:, ti, jh * 8:(jh + 1) * 8],
                            trash[:, 0:512].rearrange("p (h d) -> p h d", d=DH),
                            axis=AX.X, op=OP.add)
                    sq_toks.append(P.step("vector", red, after=[t_sq],
                                          tag=f"b{b}rd{ti}.{jh}"))

                def st(eng, inc, ti=ti, jh=jh, hbm=hbm, bb=bb):
                    inc(eng.dma_start(
                        out=hbm.rearrange("(bb p) j -> p bb j", p=128)
                        [:, ti, jh * 512:(jh + 1) * 512],
                        in_=bb[:]))
                extra = [sq_toks[-1]] if wi == 0 else []
                st_toks.append(P.step("sync", st, after=[t_ev] + extra, dma=1,
                                      tag=f"b{b}ps{wi}.{ti}.{jh}"))
    t_proj = P.step("sync", lambda e: None, after=st_toks[-2:] + sq_toks[-1:],
                    tag=f"b{b}pjd")

    def rs_fn(eng):
        eng.tensor_scalar(rsb[:], S2[:], 1.0 / DH, 1e-6, op0=OP.mult, op1=OP.add)
        eng.drain()
        eng.reciprocal(rsb[:], rsb[:])
    t_rs0 = P.step("vector", rs_fn, after=[t_proj], tag=f"b{b}rs0")

    def rs_sqrt(eng):
        eng.activation(out=rsb[:], in_=rsb[:], func=AF.Sqrt, scale=1.0 / DH)
    t_rs = P.step("scalar", rs_sqrt, after=[t_rs0], tag=f"b{b}rs1")

    def cmax(eng):
        eng.tensor_reduce(negC[:, 0:NH], S2[:].rearrange("p t h -> p h t"),
                          axis=AX.X, op=OP.max)
    t_cm = P.step("vector", cmax, after=[t_proj], tag=f"b{b}cm")

    def ctp(eng):
        eng.transpose(p_b[0][0:NH, 0:128], negC[:, 0:NH], identf[:])
    t_ct = P.step("tensor", ctp, after=[t_cm], tag=f"b{b}ct")

    def cred(eng):
        eng.tensor_reduce(scr16[0:NH, 0:1], p_b[0][0:NH, 0:128], axis=AX.X, op=OP.max)
    t_cr = P.step("vector", cred, after=[t_ct], tag=f"b{b}cr")

    def csqrt(eng):
        eng.activation(out=scr16[0:NH, 1:2], in_=scr16[0:NH, 0:1], func=AF.Sqrt)
    t_cs = P.step("scalar", csqrt, after=[t_cr], tag=f"b{b}cs")

    def ctp2(eng):
        eng.transpose(p_b[0][0:1, 0:NH], scr16[0:NH, 1:2], identf[0:NH, 0:NH])
    t_ct2 = P.step("tensor", ctp2, after=[t_cs, t_cr], tag=f"b{b}ct2")

    def cneg(eng):
        eng.tensor_scalar(scr1[0:1, 0:NH], p_b[0][0:1, 0:NH], -1.0, None, op0=OP.mult)
    t_cn = P.step("vector", cneg, after=[t_ct2], tag=f"b{b}cn")

    def crep(eng):
        eng.matmul(p_b[1][:, 0:NH], ones1[:], scr1[0:1, 0:NH], start=True, stop=True)
    t_crp = P.step("tensor", crep, after=[t_cn], tag=f"b{b}crp")

    def cev(eng):
        eng.tensor_copy(negC[:], p_b[1][:, 0:NH])
    t_c = P.step("vector", cev, after=[t_crp], tag=f"b{b}cev")

    return P.step("vector", lambda e: None, after=[t_rs, t_c], tag=f"b{b}prep")


def emit_head(nc, P, b, h, t_chain, T, tap):
    HSTG = "Z"
    def ldh(eng, inc):
        inc(eng.dma_start(
            out=T["qkh"][:],
            in_=T["qk_hbm"].rearrange("(bb p) j -> p bb j", p=128)
            [:, :, h * 64:(h + 1) * 64]))
        inc(eng.dma_start(
            out=T["vh"][:],
            in_=T["v_hbm"].rearrange("(bb p) j -> p bb j", p=128)
            [:, :, h * 64:(h + 1) * 64]))
    tok = P.step("gpsimd", ldh, after=[t_chain], dma=2, tag=f"H{b}.{h}ld")

    def khf(eng):
        eng.tensor_tensor(
            T["kh"][:], T["qkh"][:],
            T["rsb"][:, :, h:h + 1].to_broadcast([128, NT, DH]), op=OP.mult)
    t_kh = P.step("vector", khf, after=[tok], tag=f"H{h}kh")
    if HSTG == "A":
        return t_kh

    for g in range(4):
        def tp(eng, g=g):
            for q in range(4):
                ti = g * 4 + q
                eng.transpose(T["p_misc_bf"][0:64, 128 * q:128 * (q + 1)],
                              T["qkh"][:, ti, :], T["ident"][:])
        t_tp = P.step("tensor", tp, after=[tok], tag=f"H{h}tp{g}")

        def ev(eng, g=g):
            eng.copy(
                out=T["qkFMh"][0:64, g * 4:(g + 1) * 4, :].rearrange("p a t -> p (a t)"),
                in_=T["p_misc_bf"][0:64, 0:512])
        tok = P.step("scalar", ev, after=[t_tp], tag=f"H{h}te{g}")

    for g in range(4):
        def rv(eng, g=g):
            for q in range(4):
                ti = g * 4 + q
                eng.matmul(T["p_misc"][:, 128 * q:128 * (q + 1)],
                           T["qkFMh"][0:64, ti, :], T["rot"][:, h, :],
                           start=True, stop=True)
        t_rv = P.step("tensor", rv, after=[tok], tag=f"H{h}rv{g}")

        def amax(eng, g=g):
            v = T["p_misc"][:].rearrange("p (a n r) -> p a n r", n=NR, r=32)
            A = T["scrA"]
            Ai = T["scrAi"]
            mp = A[:, :, 0:4]
            mn = A[:, :, 4:8]
            idxp = T["amaxf"][:, 0:16].rearrange("p (a k) -> p a k", a=4)
            idxn = T["amaxf"][:, 16:32].rearrange("p (a k) -> p a k", a=4)
            usep = T["amaxi"][:].rearrange("p (a k) -> p a k", a=4)
            cand4 = A[:, :, 32:160].rearrange("p a (n r) -> p a n r", r=32)
            cand = A[:, :, 32:160]
            maski4 = Ai[:, :, 160:288].rearrange("p a (n r) -> p a n r", r=32)
            maski = Ai[:, :, 160:288]
            eng.tensor_reduce(mp, v, axis=AX.X, op=OP.max)
            eng.tensor_reduce(mn, v, axis=AX.X, op=OP.min)
            eng.drain()
            bc = [128, 4, NR, 32]
            bcf = [128, 4, 128]
            iota = T["iota32x4"][:].rearrange("p (a r) -> p a r", a=1).to_broadcast(bcf)
            bigc = T["big128"][:].rearrange("p (a r) -> p a r", a=1).to_broadcast(bcf)
            eng.tensor_tensor(maski4, v, mp.rearrange("p a (n o) -> p a n o", o=1)
                              .to_broadcast(bc), op=OP.is_ge)
            eng.drain()
            eng.select(cand, maski, iota, bigc, add_drain=True)
            eng.drain()
            eng.tensor_reduce(idxp, cand4, axis=AX.X, op=OP.min)
            eng.drain()
            eng.tensor_tensor(maski4, v, mn.rearrange("p a (n o) -> p a n o", o=1)
                              .to_broadcast(bc), op=OP.is_le)
            eng.drain()
            eng.select(cand, maski, iota, bigc, add_drain=True)
            eng.drain()
            eng.tensor_reduce(idxn, cand4, axis=AX.X, op=OP.min)
            eng.tensor_tensor(mp, mp, mn, op=OP.add)
            eng.drain()
            eng.tensor_tensor(usep, mp, mn, op=OP.is_ge) if False else None
            eng.tensor_scalar(usep, mp, 0.0, None, op0=OP.is_ge)
            eng.tensor_scalar(idxn, idxn, 32.0, None, op0=OP.add)
            eng.drain()
            eng.select(T["buckets"][:, g * 4:(g + 1) * 4, :], usep, idxp, idxn, add_drain=True)
        tok = P.step("vector", amax, after=[t_rv], tag=f"H{h}am{g}")

    if HSTG == "B":
        return tok

    def boff(eng):
        eng.tensor_scalar(T["scrA"][:, 0:1, 0:NR], T["iota256"][0:128, 0:NR],
                          64.0, None, op0=OP.mult)
        eng.drain()
        eng.tensor_tensor(T["buckets"][:], T["buckets"][:],
                          T["scrA"][:, 0:1, 0:NR].to_broadcast([128, NT, NR]),
                          op=OP.add)
    tok = P.step("vector", boff, after=[tok], tag=f"H{h}bo")

    def onehot(eng):
        eng.tensor_tensor(
            T["O4"][:].rearrange("p t (n k) -> p t n k", k=64),
            T["iota256"][:].rearrange("p (t nk) -> p t nk", t=1)
            .rearrange("p t (n k) -> p t n k", k=64)
            .to_broadcast([128, NT, NR, 64]),
            T["buckets"][:].rearrange("p t (n o) -> p t n o", o=1)
            .to_broadcast([128, NT, NR, 64]),
            op=OP.is_equal)
    tok = P.step("vector", onehot, after=[tok], tag=f"H{h}oh")
    tok = tap(f"buckets_{b}_{h}", T["buckets"][:], tok, [128, NT, NR], F32)

    def rt0(eng):
        eng.memset(T["runtot"][:], 0.0)
    tok = P.step("vector", rt0, after=[tok], tag=f"H{h}rt0")

    t_ev_prev = tok
    for ti in range(NT):
        def mmp(eng, ti=ti):
            eng.matmul(T["p_dots"][:, 0:256], T["tri"][:], T["O4"][:, ti, :],
                       start=True, stop=False)
            eng.matmul(T["p_dots"][:, 0:256], T["ones1"][:], T["runtot"][:],
                       start=False, stop=True)
            eng.matmul(T["p_cnt"][:], T["ones_col"][:], T["O4"][:, ti, :],
                       start=True, stop=True)
        t_mm = P.step("tensor", mmp, after=[t_ev_prev], tag=f"H{h}pf{ti}")

        def evp(eng, ti=ti):
            eng.tensor_copy(T["prefbuf"][:, ti, :], T["p_dots"][:, 0:256])
            eng.drain()
            eng.tensor_tensor(T["runtot"][:], T["runtot"][:], T["p_cnt"][:],
                              op=OP.add)
        t_ev_prev = P.step("vector", evp, after=[t_mm], tag=f"H{h}pe{ti}")
    tok = t_ev_prev

    if HSTG == "C":
        return tok

    def rl1(eng, inc):
        inc(eng.dma_start(out=T["rowb"][0:256], in_=T["runtot"][0:1, :]))
    tok = P.step("gpsimd", rl1, after=[tok], dma=1, tag=f"H{h}r1")

    def rl2(eng, inc):
        inc(eng.dma_start(out=T["cnt64"][:],
                          in_=T["rowb"][0:256].rearrange("(n k) -> k n", k=64),
                      allow_slow_non_contiguous=True))
    tok = P.step("gpsimd", rl2, after=[tok], dma=1, tag=f"H{h}r2")

    def mme(eng):
        eng.matmul(T["p_misc"][0:64, 0:4], T["tri64"][:], T["cnt64"][:],
                   start=True, stop=True)
    tok = P.step("tensor", mme, after=[tok], tag=f"H{h}me")

    def eve(eng):
        eng.tensor_copy(T["cnt64"][:], T["p_misc"][0:64, 0:4])
    tok = P.step("vector", eve, after=[tok], tag=f"H{h}ee")

    def rl3(eng, inc):
        inc(eng.dma_start(out=T["rowb"][0:256].rearrange("(n k) -> k n", k=64),
                          in_=T["cnt64"][:], allow_slow_non_contiguous=True))
    tok = P.step("gpsimd", rl3, after=[tok], dma=1, tag=f"H{h}r3")

    def rl4(eng, inc):
        inc(eng.dma_start(out=T["runtot"][0:1, :], in_=T["rowb"][0:256]))
    tok = P.step("gpsimd", rl4, after=[tok], dma=1, tag=f"H{h}r4")

    def addoff(eng):
        eng.tensor_tensor(T["runtot"][:], T["runtot"][:], T["nround_off"][:],
                          op=OP.add)
    tok = P.step("vector", addoff, after=[tok], tag=f"H{h}ao")

    def repl(eng):
        eng.matmul(T["p_misc"][:, 0:256], T["ones1"][:], T["runtot"][:],
                   start=True, stop=True)
    tok = P.step("tensor", repl, after=[tok], tag=f"H{h}rp")

    def evr(eng):
        eng.tensor_copy(T["exclrep"][:], T["p_misc"][:, 0:256])
    tok = P.step("vector", evr, after=[tok], tag=f"H{h}er")

    def posf(eng):
        eng.tensor_tensor(T["prefbuf"][:], T["prefbuf"][:],
                          T["exclrep"][:].rearrange("p (t k) -> p t k", t=1)
                          .to_broadcast([128, NT, 256]), op=OP.add)
        eng.drain()
        eng.tensor_tensor(T["prefbuf"][:], T["prefbuf"][:], T["O4"][:], op=OP.mult)
        eng.drain()
        eng.tensor_reduce(T["pos4"][:],
                          T["prefbuf"][:].rearrange("p t (n k) -> p t n k", k=64),
                          axis=AX.X, op=OP.add)
        eng.drain()
        eng.tensor_copy(T["pos4i"][:], T["pos4"][:].rearrange("p t n -> p (t n)"))
    t_pos = P.step("vector", posf, after=[tok], tag=f"H{h}po")
    t_pos = tap(f"pos4_{b}_{h}", T["pos4"][:], t_pos, [128, NT, NR], F32)

    if HSTG == "D":
        return t_pos

    def stage(eng):
        st = T["stg"][:].rearrange("p t (q d) -> p t q d", d=64)
        eng.tensor_copy(st[:, :, 0, :], T["qkh"][:])
        eng.tensor_copy(st[:, :, 1, :], T["kh"][:])
        eng.tensor_copy(st[:, :, 2, :], T["vh"][:])
        eng.memset(st[:, :, 3, :], 0.0)
    t_stage = P.step("vector", stage, after=[t_pos, t_kh], tag=f"H{h}sg")

    def scat(eng, inc):
        for ti in range(NT):
            for n in range(NR):
                inc(eng.indirect_dma_start(
                    T["lshsort"],
                    bass.IndirectOffsetOnAxis(
                        T["pos4i"][:, ti * NR + n:ti * NR + n + 1], 0),
                    T["stg"][:, ti, :], None))
    t_scat = P.step("gpsimd", scat, after=[t_stage], dma=64, tag=f"H{h}sc")

    if HSTG == "E":
        return t_scat

    def rb(eng, inc):
        src = T["lshsort"].rearrange("(bb p) w -> p bb w", p=128)
        inc(eng.dma_start(out=T["qkS"][:, 0:32, :], in_=src[:, 0:32, 0:128]))
        inc(eng.dma_start(out=T["qkS"][:, 32:64, :], in_=src[:, 32:64, 0:128]))
        inc(eng.dma_start(out=T["vvb"][:, :, 0:64], in_=src[:, :, 128:192]))
        inc(eng.dma_start(out=T["vvb2"][64:128, :, 0:64], in_=src[0:64, :, 128:192]))
        inc(eng.dma_start(out=T["vvb2"][0:64, 1:64, 0:64],
                          in_=src[64:128, 0:63, 128:192]))
        inc(eng.dma_start(out=T["vvb2"][0:64, 0:1, 0:64],
                          in_=src[64:128, 63:64, 128:192]))
    t_rb = P.step("sync", rb, after=[t_scat], dma=6, tag=f"H{h}rb")

    def vones(eng):
        eng.memset(T["vvb"][:, :, 64:65], 1.0)
        eng.memset(T["vvb2"][:, :, 64:65], 1.0)
    t_rb = P.step("vector", vones, after=[t_rb], tag=f"H{h}vo")

    tok = t_rb
    for g in range(16):
        def tps(eng, g=g):
            for q in range(4):
                ti = g * 4 + q
                eng.transpose(T["p_misc_bf"][:, 128 * q:128 * (q + 1)],
                              T["qkS"][:, ti, :], T["ident"][:])
        t_t = P.step("tensor", tps, after=[tok], tag=f"H{h}st{g}")

        def tse(eng, g=g):
            eng.copy(out=T["qFS"][:, g * 512:(g + 1) * 512],
                     in_=T["p_misc_bf"][0:64, 0:512])
            eng.copy(out=T["kFS"][:, g * 512:(g + 1) * 512],
                     in_=T["p_misc_bf"][64:128, 0:512])
        tok = P.step("scalar", tse, after=[t_t], tag=f"H{h}se{g}")
    t_sorted = tok

    if HSTG == "F":
        return t_sorted
    L = T["lsem"]
    sld, sle, slp, slv = L["d"], L["e"], L["p"], L["v"]
    base_d, base_e = L["cd"][0], L["ce"][0]
    base_p, base_v = L["cp"][0], L["cv"][0]
    PSp = T["PS"]

    def pvslot(s):
        if s < 4:
            return PSp[4][:, 65 * s:65 * s + 65]
        return PSp[5][:, 65 * (s - 4):65 * (s - 4) + 65]

    def g_pe(eng, h=h, base_d=base_d, base_e=base_e, base_p=base_p,
             base_v=base_v):
        def emit_pv(g):
            last = None
            for q in range(8):
                c = g * 8 + q
                po = 64 * (c % 2)
                s = (c // 2) % 8
                if c % 2 == 1:
                    vsrc = T["vvb"][:, (c - 1) // 2, 0:65]
                else:
                    vsrc = T["vvb2"][:, c // 2, 0:65]
                sl = pvslot(s)
                last = eng.matmul(sl[po:po + 64, :],
                                  T["probs"][:, 64 * q:64 * q + 64],
                                  vsrc, start=True, stop=True,
                                  tile_position=(0, po))
            last.then_inc(slp, 1)
        for g in range(16):
            if base_e + g > 0:
                eng.wait_ge(sle, base_e + g)
            last = None
            for q in range(8):
                c = g * 8 + q
                rhs = T["qFS"][:, 64 * c:64 * c + 64]
                if c % 2 == 0:
                    lo = (64 * c - 64) % NSlot
                    eng.matmul(T["p_dots"][0:64, 64 * q:64 * q + 64],
                               T["kFS"][:, lo:lo + 64], rhs,
                               start=True, stop=True, tile_position=(0, 0))
                    last = eng.matmul(
                        T["p_dots"][64:128, 64 * q:64 * q + 64],
                        T["kFS"][:, 64 * c:64 * c + 64], rhs,
                        start=True, stop=True, tile_position=(0, 64))
                else:
                    last = eng.matmul(
                        T["p_dots"][:, 64 * q:64 * q + 64],
                        T["kFS"][:, 64 * c - 64:64 * c + 64],
                        rhs, start=True, stop=True)
            last.then_inc(sld, 1)
            if g >= 1:
                k = g - 1
                if base_v + k - 1 > 0:
                    eng.wait_ge(slv, base_v + k - 1)
                emit_pv(k)
        eng.wait_ge(sle, base_e + 16)
        if base_v + 14 > 0:
            eng.wait_ge(slv, base_v + 14)
        emit_pv(15)
    t_gpe = P.step("tensor", g_pe, after=[t_sorted], tag=f"H{h}GPE")
    L["cd"][0] += 16
    L["cp"][0] += 16

    def g_act(eng, h=h, base_d=base_d, base_p=base_p):
        for g in range(16):
            eng.wait_ge(sld, base_d + g + 1)
            if g >= 1:
                eng.wait_ge(slp, base_p + g)
            eng.activation(out=T["probs"][:], in_=T["p_dots"][:],
                           func=AF.Exp, bias=T["negC"][:, h:h + 1],
                           scale=1.0).then_inc(sle, 1)
    t_gact = P.step("scalar", g_act, after=[t_sorted], tag=f"H{h}GAX")
    L["ce"][0] += 16

    def g_dve(eng, base_p=base_p):
        for g in range(16):
            eng.wait_ge(slp, base_p + g + 1)
            last = None
            for i in range(4):
                blk = 4 * g + i
                s = blk % 8
                last = eng.tensor_copy(T["sortout"][:, blk, 0:65],
                                       pvslot(s))
            last.then_inc(slv, 1)
    t_pv_done = P.step("vector", g_dve, after=[t_sorted], tag=f"H{h}GDV")
    L["cv"][0] += 16
    t_gfin = (t_gpe, t_gact)

    def wout(eng, inc):
        dst = T["lshout"].rearrange("(bb p) w -> p bb w", p=128)
        inc(eng.dma_start(out=dst[:, 0:32, :], in_=T["sortout"][:, 0:32, :]))
        inc(eng.dma_start(out=dst[:, 32:64, :], in_=T["sortout"][:, 32:64, :]))
    t_wo = P.step("sync", wout, after=[t_pv_done], dma=2, tag=f"H{h}wo")

    def unsort(eng, inc):
        for ti in range(NT):
            for n in range(NR):
                inc(eng.indirect_dma_start(
                    T["uns"][:, n * NT + ti, :], None,
                    T["lshout"],
                    bass.IndirectOffsetOnAxis(
                        T["pos4i"][:, ti * NR + n:ti * NR + n + 1], 0)))
    t_un = P.step("gpsimd", unsort, after=[t_wo], dma=64, tag=f"H{h}un")

    if HSTG == "H":
        return t_un

    def comb(eng):
        u = T["uns"]
        eng.tensor_tensor(T["rsum"][:], u[:, 0:16, :], u[:, 16:32, :], op=OP.add)
        eng.drain()
        eng.tensor_tensor(T["rsum"][:], T["rsum"][:], u[:, 32:48, :], op=OP.add)
        eng.drain()
        eng.tensor_tensor(T["rsum"][:], T["rsum"][:], u[:, 48:64, :], op=OP.add)
        eng.drain()
        eng.reciprocal(T["rec"][:], T["rsum"][:, :, 64:65])
        eng.drain()
        eng.tensor_tensor(T["attnh"][:], T["rsum"][:, :, 0:64],
                          T["rec"][:].to_broadcast([128, NT, 64]), op=OP.mult)
    t_cb = P.step("vector", comb, after=[t_un], tag=f"H{h}cb")
    t_cb = tap(f"attnh_{b}_{h}", T["attnh"][:], t_cb, [128, NT, 64], F32)

    tok = t_cb
    for g in range(4):
        def tpa(eng, g=g):
            for q in range(4):
                ti = g * 4 + q
                eng.transpose(T["p_misc"][0:64, 128 * q:128 * (q + 1)],
                              T["attnh"][:, ti, :], T["identf"][:])
        t_t = P.step("tensor", tpa, after=[tok], tag=f"H{h}at{g}")

        def tae(eng, g=g):
            eng.copy(
                out=T["attnFM"][(h % 2) * 64:(h % 2) * 64 + 64, h // 2,
                                g * 512:(g + 1) * 512],
                in_=T["p_misc"][0:64, 0:512])
        tok = P.step("scalar", tae, after=[t_t], tag=f"H{h}ae{g}")
    return tok


def emit_batch_tail(nc, P, b, ins, x1_tm, attnFM, woutt, xres, sqt, trash,
                    lnm, p_b, prev, tap):
    def ldw(eng, inc):
        inc(eng.dma_start(out=woutt[:],
                          in_=ins["w_out_t"].rearrange("(c p) j -> p c j", p=128)))
    t_w = P.step("gpsimd", ldw, after=[prev], dma=1, tag=f"b{b}tw")

    toks = []
    ln_toks = []
    t_x = None
    for ti in range(NT):
        if ti % 4 == 0:
            def ldr(eng, inc, ti=ti):
                inc(eng.dma_start(
                    out=xres[:],
                    in_=ins["x_tm"].rearrange("(bb p) f -> p bb f", p=128)
                    [:, b * NT + ti:b * NT + ti + 4, :]))
            t_x = P.step("gpsimd", ldr,
                         after=[prev] + (ln_toks[-1:] if ln_toks else []),
                         dma=1, tag=f"b{b}tx{ti}")
        for jh in range(2):
            pp = p_b[(ti * 2 + jh) % 2]

            def mm(eng, ti=ti, jh=jh, pp=pp):
                for c in range(8):
                    eng.matmul(pp[:], attnFM[:, c, ti * 128:(ti + 1) * 128],
                               woutt[:, c, jh * 512:(jh + 1) * 512],
                               start=(c == 0), stop=(c == 7))
            t_mm = P.step("tensor", mm,
                          after=[prev, t_w] + (toks[-2:-1] if len(toks) >= 2 else []),
                          tag=f"b{b}tm{ti}.{jh}")

            def ev(eng, ti=ti, jh=jh, pp=pp):
                eng.tensor_tensor(
                    sqt[:, jh * 512:(jh + 1) * 512], pp[:],
                    xres[:, ti % 4, jh * 512:(jh + 1) * 512], op=OP.add)
            toks.append(P.step("vector", ev,
                               after=[t_mm, t_x] + (ln_toks[-1:] if ln_toks else []),
                               tag=f"b{b}tv{ti}.{jh}"))

        def lnmean(eng, ti=ti):
            eng.tensor_reduce(lnm[:, ti, 0:1], sqt[:], axis=AX.X, op=OP.add)
        t_m = P.step("vector", lnmean, after=toks[-2:], tag=f"b{b}lm{ti}")

        def lnsq(eng, ti=ti):
            eng.activation(out=trash[:, 0:1024], in_=sqt[:], func=AF.Square,
                           accum_out=lnm[:, ti, 1:2])
        t_s = P.step("scalar", lnsq, after=toks[-2:], tag=f"b{b}ls{ti}")

        def lnst(eng, ti=ti):
            L = lnm[:, ti, :]
            eng.tensor_scalar(L[:, 2:3], L[:, 0:1], 1.0 / D, None, op0=OP.mult)
            eng.drain()
            eng.tensor_tensor(L[:, 3:4], L[:, 2:3], L[:, 2:3], op=OP.mult)
            eng.tensor_scalar(L[:, 5:6], L[:, 1:2], 1.0 / D, None, op0=OP.mult)
            eng.drain()
            eng.tensor_tensor(L[:, 3:4], L[:, 5:6], L[:, 3:4], op=OP.subtract)
            eng.drain()
            eng.tensor_scalar(L[:, 3:4], L[:, 3:4], 1e-5, None, op0=OP.add)
            eng.drain()
            eng.reciprocal(L[:, 3:4], L[:, 3:4])
        t_st = P.step("vector", lnst, after=[t_m, t_s], tag=f"b{b}lt{ti}")

        def lnrs(eng, ti=ti):
            eng.activation(out=lnm[:, ti, 3:4], in_=lnm[:, ti, 3:4], func=AF.Sqrt)
        t_rs = P.step("scalar", lnrs, after=[t_st], tag=f"b{b}lr{ti}")

        def lnap(eng, ti=ti):
            eng.tensor_scalar(
                x1_tm[:, b * NT + ti, :], sqt[:],
                lnm[:, ti, 2:3], lnm[:, ti, 3:4],
                op0=OP.subtract, op1=OP.mult)
        ln_toks.append(P.step("vector", lnap, after=[t_rs], tag=f"b{b}la{ti}"))
    return P.step("vector", lambda e: None, after=ln_toks[-1:], tag=f"b{b}tl")



F32 = mybir.dt.float32
BF16 = mybir.dt.bfloat16
I32 = mybir.dt.int32
AX = mybir.AxisListType
OP = mybir.AluOpType
AF = mybir.ActivationFunctionType

S = 2048
D = 1024
NH = 16
DH = 64
NKV = 4


def bfv(region, off_f32, n_bf16):
    return region[:, off_f32:off_f32 + n_bf16 // 2].bitcast(BF16)


def emit_gqa(nc, es, P, ins, L, tap, x1_tm, prev, n_cores=8):
    R1, R2, R3, R4, R5 = L["R1"], L["R2"], L["R3"], L["R4"], L["R5"]
    PS = L["PS"]
    ones1, ident, identf = L["ones1"], L["ident"], L["identf"]
    ones_col = L["ones_col"]
    iota256, big128 = L["iota256"], L["big128"]

    qfm = nc.dram_tensor("g_qfm", [1024, S], BF16).ap()
    kfm = nc.dram_tensor("g_kfm", [256, S], BF16).ap()
    vtm = nc.dram_tensor("g_vtm", [S, 256], BF16).ap()
    ofm = nc.dram_tensor("g_ofm", [1024, S], BF16).ap()
    x2loc = nc.dram_tensor("g_x2loc", [2, S, D], BF16)
    rt_in = nc.dram_tensor("g_rtin", [2 * S, 4], F32)
    byex = nc.dram_tensor("g_byex", [2, n_cores * S, D], BF16,
                          addr_space="Shared")
    rt_all = nc.dram_tensor("g_rtall", [n_cores * 2 * S, 4], F32,
                            addr_space="Shared")
    groups = [list(range(n_cores))]
    sem_coll = es.enter_context(nc.semaphore("sem_coll"))
    coll_n = [0]
    fs = dict(st=es.enter_context(nc.semaphore("gs_fst")),
              ex=es.enter_context(nc.semaphore("gs_fex")),
              pv=es.enter_context(nc.semaphore("gs_fpv")),
              qn=es.enter_context(nc.semaphore("gs_fqn")),
              cst=[0], cex=[0], cpv=[0], cqn=[0])

    x1FMw = bfv(R2, 0, 4096).rearrange("p (c t) -> p c t", c=8)
    kwS = bfv(R2, 2048, 2048).rearrange("p (c j) -> p c j", c=8)
    vwS = bfv(R2, 3072, 2048).rearrange("p (c j) -> p c j", c=8)
    qwS = bfv(R3, 0, 8192).rearrange("p (c j) -> p c j", c=8)
    owS = bfv(R3, 0, 8192).rearrange("p (c j) -> p c j", c=8)
    qv = [bfv(R1, 0, 2048)[0:65], bfv(R1, 1024, 2048)[0:65]]
    kv = [bfv(R1, 2048, 2048)[0:65], bfv(R1, 3072, 2048)[0:65]]
    vT = [bfv(R1, 4096, 1040).rearrange("p (b c) -> p b c", c=65),
          bfv(R1, 4616, 1040).rearrange("p (b c) -> p b c", c=65)]
    OFMp = [bfv(R1, 5136, 2048), bfv(R1, 6160, 2048)]
    sqb = L["attnFM"][:].rearrange("p a t -> p (a t)")[0:64, 0:2048]
    scr_rc = R2[0:1, 0:512]
    scrq2 = R2[0:1, 512:1024]
    scrq_a = R2[0:1, 1024:1536]
    maxs = R2[0:1, 1536:1544]
    rbc = R1[0:64, 7184:7696]
    OallS = bfv(R1, 0, 16384).rearrange("p (c t) -> p c t", c=8)

    a_f32 = R4[:, 0:1024]
    x2fin = R4[:, 1024:2048]
    xfmT = [R4[:, 2048:2176], R4[:, 2176:2304]]
    s2a = R4[:, 2304:2305]
    s2b = R4[:, 2305:2306]
    s2 = R4[:, 2306:2307]
    rstd = R4[:, 2307:2308]
    epscol = R4[:, 2308:2309]
    obS = R1[0:1, 0:1024]
    n1wS = R1[0:1, 1024:2048]
    vbS = R1[0:1, 2048:2304]

    obrep = bfv(R5, 0, 1024)
    n1wrep = bfv(R5, 512, 1024)
    vbrep = bfv(R5, 1024, 256)
    gwS = R5[:, 1152:1216].rearrange("p (c e) -> p c e", e=8)
    zTM = R5[:, 1216:1344].rearrange("p (b e) -> p b e", e=8)
    bounce = [bfv(R5, 1344, 512), bfv(R5, 1600, 512)]
    m1all = R5[:, 1856:1872]
    m2all = R5[:, 1872:1888]
    e1all = R5[:, 1888:1904]
    e2all = R5[:, 1904:1920]
    w1all = R5[:, 1920:1936]
    w2all = R5[:, 1936:1952]
    wsum = R5[:, 1952:1968]
    neg30 = R5[:, 1968:1976]
    msk8i = R5[:, 1976:1984].bitcast(I32)
    cnd8 = R5[:, 1984:1992]
    rtblk = R5[:, 1992:2056].rearrange("p (b c) -> p b c", c=4)
    probsT = [bfv(R5, 1344, 512), bfv(R5, 1600, 512)]

    qb_s = L["qb_s"]; kb_s = L["kb_s"]

    p_n = PS[5][0:1, 0:512]
    tpbf = PS[1][:].bitcast(BF16)

    def ld_small(eng, inc):
        inc(eng.dma_start(out=qb_s[:], in_=ins["qb_fm"]))
        inc(eng.dma_start(out=kb_s[:], in_=ins["kb_fm"]))
        inc(eng.dma_start(out=obS, in_=ins["ob"]))
        inc(eng.dma_start(out=n1wS, in_=ins["norm1_w"]))
        inc(eng.dma_start(out=vbS, in_=ins["vb"]))
        inc(eng.dma_start(out=gwS[:], in_=ins["gate_w_fm"]))
    t_ls = P.step("sync", ld_small, after=[prev], dma=6, tag="gq_ls")

    def reps_mm(eng):
        eng.matmul(PS[0][:, 0:512], ones1[0:1, 0:128], obS[0:1, 0:512],
                   start=True, stop=True)
        eng.matmul(PS[2][:, 0:512], ones1[0:1, 0:128], obS[0:1, 512:1024],
                   start=True, stop=True)
        eng.matmul(PS[3][:, 0:512], ones1[0:1, 0:128], n1wS[0:1, 0:512],
                   start=True, stop=True)
        eng.matmul(PS[4][:, 0:512], ones1[0:1, 0:128], n1wS[0:1, 512:1024],
                   start=True, stop=True)
        eng.matmul(PS[6][:, 0:256], ones1[0:1, 0:128], vbS[0:1, 0:256],
                   start=True, stop=True)
    t_rm = P.step("tensor", reps_mm, after=[t_ls, prev], tag="gq_rm")

    def reps_ev(eng):
        eng.tensor_copy(obrep[:, 0:512], PS[0][:, 0:512])
        eng.tensor_copy(obrep[:, 512:1024], PS[2][:, 0:512])
        eng.tensor_copy(n1wrep[:, 0:512], PS[3][:, 0:512])
        eng.tensor_copy(n1wrep[:, 512:1024], PS[4][:, 0:512])
        eng.tensor_copy(vbrep[:], PS[6][:, 0:256])
        eng.memset(epscol, 1e-6)
        eng.memset(neg30, -1e30)
    t_rep = P.step("vector", reps_ev, after=[t_rm], tag="gq_rep")

    prev = P.step("tensor", lambda e: None, after=[t_rep, prev], tag="gq_st")

    xdma_toks = []
    rt_toks = []
    for ex in range(2):
        prev = emit_gp(nc, P, ins, ex, x1_tm, x1FMw, qwS, kwS, vwS, qb_s,
                       kb_s, vbrep, bounce, qfm, kfm, vtm, PS, tpbf, ident,
                       prev)
        prev = emit_gs(nc, P, ex, qv, kv, vT, sqb, OFMp, probsT, scr_rc,
                       scrq2, scrq_a, maxs, rbc, qfm, kfm, vtm, ofm, PS, p_n,
                       ones1, ones_col, fs, prev)
        prev, t_xd = emit_go(nc, P, ins, ex, x1_tm, OallS, owS, ofm, obrep,
                             n1wrep, gwS, a_f32, x2fin, xfmT, s2a, s2b, s2,
                             rstd, epscol, zTM, x2loc, PS, identf, prev)
        xdma_toks.append(t_xd)

        def ag(eng, ex=ex):
            eng.collective_compute(
                "AllGather", OP.bypass, replica_groups=groups,
                ins=[x2loc.ap()[ex]], outs=[byex.ap()[ex]],
            ).then_inc(sem_coll, 1)
        coll_n[0] += 1
        P.step("gpsimd", ag, after=[t_xd], tag=f"gq_ag{ex}")

        prev, t_rt = emit_gating(nc, P, ex, zTM, m1all, m2all, e1all, e2all,
                                 w1all, w2all, wsum, neg30, msk8i, cnd8,
                                 rtblk, iota256, big128, rt_in, tap, prev)
        rt_toks.append(t_rt)

    def agr(eng):
        eng.collective_compute(
            "AllGather", OP.bypass, replica_groups=groups,
            ins=[rt_in.ap()], outs=[rt_all.ap()]).then_inc(sem_coll, 1)
    coll_n[0] += 1
    P.step("gpsimd", agr, after=rt_toks, tag="gq_agr")

    emit_gqa.handles = dict(byex=byex, rt_all=rt_all, x2loc=x2loc,
                            sem_coll=sem_coll, n_coll=coll_n[0])
    return prev


def emit_gp(nc, P, ins, ex, x1_tm, x1FMw, qwS, kwS, vwS, qb_s, kb_s, vbrep,
            bounce, qfm, kfm, vtm, PS, tpbf, ident, prev):
    B0 = 16 * ex

    def ldw(eng, inc):
        inc(eng.dma_start(out=qwS[:],
                          in_=ins["qw"].rearrange("(c p) j -> p c j", p=128)))
        inc(eng.dma_start(out=kwS[:],
                          in_=ins["kw"].rearrange("(c p) j -> p c j", p=128)))
        inc(eng.dma_start(out=vwS[:],
                          in_=ins["vw"].rearrange("(c p) j -> p c j", p=128)))
    t_w = P.step("gpsimd", ldw, after=[prev], dma=3, tag=f"gp{ex}w")

    qdst = qfm.rearrange("(pr p) t -> p pr t", p=128)
    kdst = kfm.rearrange("(pr p) t -> p pr t", p=128)
    vdst = vtm.rearrange("(blk p) v -> p blk v", p=128)
    t_ev_prev = [t_w, t_w]
    t_dma_prev = t_w
    for w in range(4):
        t_tp_ev = []
        for b4 in range(4):
            def tpp(eng, w=w, b4=b4):
                for c in range(8):
                    eng.transpose(tpbf[:, 128 * c:128 * (c + 1)],
                                  x1_tm[:, B0 + w * 4 + b4,
                                        128 * c:128 * (c + 1)], ident[:])
            t_t = P.step("tensor", tpp,
                         after=[prev] + t_tp_ev[-1:], tag=f"gp{ex}t{w}.{b4}")

            def tpe(eng, w=w, b4=b4):
                for c in range(8):
                    eng.tensor_copy(x1FMw[:, c, 128 * b4:128 * (b4 + 1)],
                                    tpbf[:, 128 * c:128 * (c + 1)])
            t_tp_ev.append(P.step("vector", tpe, after=[t_t],
                                  tag=f"gp{ex}e{w}.{b4}"))
        t_x1w = t_tp_ev[-1]

        jobs = []
        for pr in range(8):
            jobs.append(("q", pr))
        jobs += [("k", 0), ("k", 1)]
        jobs += [("v", 0), ("v", 1), ("v", 2), ("v", 3)]
        for ji, (kind, idx) in enumerate(jobs):
            bank = PS[[0, 2][ji % 2]]
            bb = bounce[ji % 2]

            def mm(eng, kind=kind, idx=idx, bank=bank, w=w):
                for c in range(8):
                    if kind == "q":
                        eng.matmul(bank[:, 0:512],
                                   qwS[:, c, 128 * idx:128 * (idx + 1)],
                                   x1FMw[:, c, :], start=(c == 0),
                                   stop=(c == 7))
                    elif kind == "k":
                        eng.matmul(bank[:, 0:512],
                                   kwS[:, c, 128 * idx:128 * (idx + 1)],
                                   x1FMw[:, c, :], start=(c == 0),
                                   stop=(c == 7))
                    else:
                        eng.matmul(bank[:, 0:256],
                                   x1FMw[:, c, 128 * idx:128 * (idx + 1)],
                                   vwS[:, c, :], start=(c == 0),
                                   stop=(c == 7))
            t_mm = P.step("tensor", mm,
                          after=[t_x1w, t_w] + t_ev_prev[-2:-1],
                          tag=f"gp{ex}m{w}.{ji}")

            if kind == "v":
                def ev(eng, idx=idx, bank=bank, bb=bb):
                    eng.tensor_tensor(bb[:, 0:256], bank[:, 0:256],
                                      vbrep[:], op=OP.add)
            elif kind == "q":
                def ev(eng, idx=idx, bank=bank, bb=bb):
                    eng.tensor_scalar(bb[:, 0:512], bank[:, 0:512],
                                      qb_s[:, idx:idx + 1], None, op0=OP.add)
            else:
                def ev(eng, idx=idx, bank=bank, bb=bb):
                    eng.tensor_scalar(bb[:, 0:512], bank[:, 0:512],
                                      kb_s[:, idx:idx + 1], None, op0=OP.add)
            eng_name = "vector"
            t_ev = P.step(eng_name, ev, after=[t_mm, t_dma_prev],
                          tag=f"gp{ex}v{w}.{ji}")
            t_ev_prev.append(t_ev)

            def st(eng, inc, kind=kind, idx=idx, bb=bb, w=w):
                if kind == "q":
                    inc(eng.dma_start(out=qdst[:, idx, 512 * w:512 * (w + 1)],
                                      in_=bb[:, 0:512]))
                elif kind == "k":
                    inc(eng.dma_start(out=kdst[:, idx, 512 * w:512 * (w + 1)],
                                      in_=bb[:, 0:512]))
                else:
                    inc(eng.dma_start(out=vdst[:, w * 4 + idx, :],
                                      in_=bb[:, 0:256]))
            t_dma_prev = P.step("sync", st, after=[t_ev], dma=1,
                                tag=f"gp{ex}s{w}.{ji}")
        prev = t_dma_prev
    return P.step("tensor", lambda e: None, after=[prev], tag=f"gp{ex}fin")


def emit_gs(nc, P, ex, qv, kv, vT, sqb, OFMp, probsT, scr_rc, scrq2, scrq_a,
            maxs, rbc, qfm, kfm, vtm, ofm, PS, p_n, ones1, ones_col, fs,
            prev):
    odst = ofm.rearrange("(pr p) t -> p pr t", p=128)
    vsrc = vtm.rearrange("(blk p) v -> p blk v", p=128)
    sst, sex, spv = fs["st"], fs["ex"], fs["pv"]
    cst, cex, cpv = fs["cst"], fs["cex"], fs["cpv"]
    last_use_q = {0: prev, 1: prev}
    last_use_k = {0: prev, 1: prev}
    t_onorm = [prev, prev]
    t_ofm_dma = prev
    t_qsq_free = prev
    t_pe_last = prev
    pending = None

    for g in range(4):
        gs = g % 2

        def ldk(eng, inc, g=g, gs=gs):
            inc(eng.dma_start(out=kv[gs][0:64, :],
                              in_=kfm[64 * g:64 * (g + 1), :]))
            inc(eng.dma_start(
                out=vT[gs][:, :, 0:64],
                in_=vsrc[:, :, 64 * g:64 * (g + 1)]))
        t_lk = P.step("sync", ldk, after=[last_use_k[gs], prev], dma=2,
                      tag=f"gs{ex}lk{g}")

        def kones(eng, gs=gs):
            eng.memset(kv[gs][64:65, :], 1.0)
            eng.memset(vT[gs][:, :, 64:65], 1.0)
        t_ko = P.step("vector", kones, after=[t_lk], tag=f"gs{ex}ko{g}")

        def ksq(eng, gs=gs):
            eng.activation(out=sqb[:], in_=kv[gs][0:64, :], func=AF.Square)
        t_kq = P.step("scalar", ksq, after=[t_ko, t_qsq_free],
                      tag=f"gs{ex}kq{g}")

        st0, pv0 = cst[0], cpv[0]

        def nmm(eng, st0=st0, pv0=pv0):
            for w4 in range(4):
                if w4 >= 1:
                    eng.wait_ge(spv, pv0 + w4)
                eng.matmul(p_n, ones_col[0:64, :],
                           sqb[:, 512 * w4:512 * (w4 + 1)],
                           start=True, stop=True).then_inc(sst, 1)
        t_nm = P.step("tensor", nmm, after=[t_kq, t_pe_last],
                      tag=f"gs{ex}km{g}")
        cst[0] += 4
        t_pe_last = t_nm

        def nmx(eng, st0=st0):
            for w4 in range(4):
                eng.wait_ge(sst, st0 + w4 + 1)
                eng.tensor_reduce(maxs[0:1, w4:w4 + 1], p_n, axis=AX.X,
                                  op=OP.max).then_inc(spv, 1)
        t_mx = P.step("vector", nmx, after=[t_kq], tag=f"gs{ex}kx{g}")
        cpv[0] += 4

        def kmax(eng):
            eng.tensor_reduce(maxs[0:1, 4:5], maxs[0:1, 0:4], axis=AX.X,
                              op=OP.max)
        t_km = P.step("vector", kmax, after=[t_mx], tag=f"gs{ex}kf{g}")

        def ksqrt(eng):
            eng.activation(out=maxs[0:1, 5:6], in_=maxs[0:1, 4:5],
                           func=AF.Sqrt)
        t_maxk = P.step("scalar", ksqrt, after=[t_km], tag=f"gs{ex}ks{g}")

        for hh in range(4):
            h = 4 * g + hh
            hs = h % 2

            def ldq(eng, inc, h=h, hs=hs):
                inc(eng.dma_start(out=qv[hs][0:64, :],
                                  in_=qfm[64 * h:64 * (h + 1), :]))
            t_lq = P.step("sync", ldq, after=[last_use_q[hs], prev], dma=1,
                          tag=f"gs{ex}lq{h}")

            def qsq(eng, hs=hs):
                eng.activation(out=sqb[:], in_=qv[hs][0:64, :],
                               func=AF.Square)
            t_qq = P.step("scalar", qsq, after=[t_lq, t_maxk, t_qsq_free],
                          tag=f"gs{ex}qq{h}")

            st0, ex0, pv0 = cst[0], cex[0], cpv[0]

            def qmm(eng, st0=st0, ex0=ex0):
                for w4 in range(4):
                    if w4 >= 1:
                        eng.wait_ge(sex, ex0 + w4)
                    eng.matmul(p_n, ones_col[0:64, :],
                               sqb[:, 512 * w4:512 * (w4 + 1)],
                               start=True, stop=True).then_inc(sst, 1)
            t_qm = P.step("tensor", qmm, after=[t_qq, t_pe_last],
                          tag=f"gs{ex}qm{h}")
            cst[0] += 4
            t_pe_last = t_qm
            t_qsq_free = t_qm

            def qsr(eng, st0=st0, pv0=pv0):
                for w4 in range(4):
                    eng.wait_ge(sst, st0 + w4 + 1)
                    if w4 >= 1:
                        eng.wait_ge(spv, pv0 + w4)
                    eng.activation(out=scrq_a, in_=p_n,
                                   func=AF.Sqrt).then_inc(sex, 1)
            t_qs = P.step("scalar", qsr, after=[t_qq], tag=f"gs{ex}qs{h}")
            cex[0] += 4

            def qev(eng, ex0=ex0, hs=hs):
                for w4 in range(4):
                    eng.wait_ge(sex, ex0 + w4 + 1)
                    eng.tensor_scalar(
                        qv[hs][64:65, 512 * w4:512 * (w4 + 1)], scrq_a,
                        maxs[0:1, 5:6], -1.0, op0=OP.mult,
                        op1=OP.mult).then_inc(spv, 1)
            t_qn = P.step("vector", qev, after=[t_qq], tag=f"gs{ex}qe{h}")
            cpv[0] += 4

            for w in range(4):
                pvb = PS[2 + w % 2]
                st0, ex0, pv0 = cst[0], cex[0], cpv[0]

                def pemm(eng, w=w, hs=hs, gs=gs, ex0=ex0, pvb=pvb):
                    for k2 in range(8):
                        if ex0 + 2 * k2 > 0:
                            eng.wait_ge(sex, ex0 + 2 * k2)
                        for j in range(2):
                            blk = 2 * k2 + j
                            eng.matmul(PS[j][:, 0:512],
                                       kv[gs][:, 128 * blk:128 * (blk + 1)],
                                       qv[hs][:, 512 * w:512 * (w + 1)],
                                       start=True, stop=True
                                       ).then_inc(sst, 1)
                        if k2 >= 1:
                            for j in range(2):
                                blk = 2 * (k2 - 1) + j
                                eng.matmul(pvb[0:65, 0:512],
                                           vT[gs][:, blk, :], probsT[j][:],
                                           start=(blk == 0), stop=False
                                           ).then_inc(spv, 1)
                    eng.wait_ge(sex, ex0 + 16)
                    eng.matmul(pvb[0:65, 0:512], vT[gs][:, 14, :],
                               probsT[0][:], start=False, stop=False
                               ).then_inc(spv, 1)
                    eng.matmul(pvb[0:65, 0:512], vT[gs][:, 15, :],
                               probsT[1][:], start=False, stop=True
                               ).then_inc(spv, 1)
                t_pe = P.step("tensor", pemm,
                              after=[t_qn, t_ko, t_onorm[w % 2], t_pe_last],
                              tag=f"gs{ex}PE{h}.{w}")
                cst[0] += 16
                cpv[0] += 16
                t_pe_last = t_pe

                def actx(eng, st0=st0, pv0=pv0):
                    for blk in range(16):
                        eng.wait_ge(sst, st0 + blk + 1)
                        eng.wait_ge(spv, pv0 + blk - 1 if blk >= 2 else pv0)
                        eng.activation(out=probsT[blk % 2][:],
                                       in_=PS[blk % 2][:], func=AF.Exp,
                                       scale=0.125).then_inc(sex, 1)
                t_ax = P.step("scalar", actx, after=[t_qs],
                              tag=f"gs{ex}AX{h}.{w}")
                cex[0] += 16

                def emit_epi(pw, t_pe_pw):
                    nonlocal t_pe_last
                    pvbp = PS[2 + pw % 2]

                    def rcp(eng, pvbp=pvbp):
                        eng.reciprocal(scr_rc, pvbp[64:65, 0:512])
                    t_rc = P.step("vector", rcp, after=[t_pe_pw],
                                  tag=f"gs{ex}r{h}.{pw}")

                    def bcast(eng):
                        eng.matmul(PS[6][0:64, 0:512], ones1[0:1, 0:64],
                                   scr_rc, start=True, stop=True)
                    t_bc = P.step("tensor", bcast,
                                  after=[t_rc, t_onorm[(pw + 1) % 2],
                                         t_pe_last],
                                  tag=f"gs{ex}b{h}.{pw}")
                    t_pe_last = t_bc

                    def onorm(eng, h=h, pw=pw, pvbp=pvbp):
                        eng.tensor_copy(rbc[:], PS[6][0:64, 0:512])
                        eng.drain()
                        eng.tensor_tensor(
                            OFMp[(h // 2) % 2][64 * (h % 2):
                                               64 * (h % 2) + 64,
                                               512 * pw:512 * (pw + 1)],
                            pvbp[0:64, 0:512], rbc[:], op=OP.mult)
                    t_onorm[pw % 2] = P.step("vector", onorm,
                                             after=[t_bc, t_ofm_dma],
                                             tag=f"gs{ex}o{h}.{pw}")

                if pending is not None:
                    emit_epi(pending[0], pending[1])
                pending = (w, t_pe)
            emit_epi(pending[0], pending[1])
            pending = None
            last_use_q[hs] = t_pe
            last_use_k[gs] = t_pe
            if h % 2 == 1:
                def ofdma(eng, inc, h=h):
                    inc(eng.dma_start(out=odst[:, h // 2, :],
                                      in_=OFMp[(h // 2) % 2][:]))
                t_ofm_dma = P.step("sync", ofdma,
                                   after=[t_onorm[0], t_onorm[1]], dma=1,
                                   tag=f"gs{ex}od{h}")
    return P.step("tensor", lambda e: None,
                  after=[t_ofm_dma, t_onorm[0], t_onorm[1]],
                  tag=f"gs{ex}fin")


def emit_go(nc, P, ins, ex, x1_tm, OallS, owS, ofm, obrep, n1wrep, gwS,
            a_f32, x2fin, xfmT, s2a, s2b, s2, rstd, epscol, zTM, x2loc, PS,
            identf, prev):
    B0 = 16 * ex

    def ldo(eng, inc):
        inc(eng.dma_start(out=OallS[:],
                          in_=ofm.rearrange("(c p) t -> p c t", p=128)))
        inc(eng.dma_start(out=owS[:],
                          in_=ins["ow"].rearrange("(c p) j -> p c j",
                                                  p=128)))
    t_lo = P.step("gpsimd", ldo, after=[prev], dma=2, tag=f"go{ex}w")

    t_ev = {0: t_lo, 1: t_lo}
    t_x2b = t_lo
    t_xd = t_lo
    t_pz = t_lo
    for blk in range(16):
        for half in range(2):
            def mm(eng, blk=blk, half=half):
                for c in range(8):
                    eng.matmul(PS[half][:, 0:512],
                               OallS[:, c, 128 * blk:128 * (blk + 1)],
                               owS[:, c, 512 * half:512 * (half + 1)],
                               start=(c == 0), stop=(c == 7))
            t_mm = P.step("tensor", mm, after=[t_lo, t_ev[half]],
                          tag=f"go{ex}m{blk}.{half}")

            def ev(eng, blk=blk, half=half):
                eng.tensor_tensor(a_f32[:, 512 * half:512 * (half + 1)],
                                  PS[half][:, 0:512],
                                  obrep[:, 512 * half:512 * (half + 1)],
                                  op=OP.add)
                eng.drain()
                eng.tensor_tensor(x2fin[:, 512 * half:512 * (half + 1)],
                                  a_f32[:, 512 * half:512 * (half + 1)],
                                  x1_tm[:, B0 + blk,
                                        512 * half:512 * (half + 1)],
                                  op=OP.add)
            t_ev[half] = P.step("vector", ev, after=[t_mm, t_x2b],
                                tag=f"go{ex}e{blk}.{half}")

        def sq(eng):
            eng.activation(out=a_f32[:, 0:512], in_=x2fin[:, 0:512],
                           func=AF.Square, accum_out=s2a)
            eng.drain()
            eng.activation(out=a_f32[:, 512:1024], in_=x2fin[:, 512:1024],
                           func=AF.Square, accum_out=s2b)
        t_sq = P.step("scalar", sq, after=[t_ev[0], t_ev[1]],
                      tag=f"go{ex}q{blk}")

        def st(eng):
            eng.tensor_tensor(s2, s2a, s2b, op=OP.add)
        t_s = P.step("vector", st, after=[t_sq], tag=f"go{ex}s{blk}")

        def rs1(eng):
            eng.tensor_scalar(rstd, s2, 1.0 / D, 1e-6, op0=OP.mult,
                              op1=OP.add)
            eng.drain()
            eng.reciprocal(rstd, rstd)
        t_r0 = P.step("vector", rs1, after=[t_s], tag=f"go{ex}r{blk}a")

        def rs2(eng):
            eng.activation(out=rstd, in_=rstd, func=AF.Sqrt)
        t_r = P.step("scalar", rs2, after=[t_r0], tag=f"go{ex}r{blk}")

        def x2b(eng):
            eng.tensor_scalar(a_f32[:], x2fin[:], rstd, None, op0=OP.mult)
            eng.drain()
            eng.tensor_tensor(x2fin[:], a_f32[:], n1wrep[:], op=OP.mult)
        t_x2b = P.step("vector", x2b, after=[t_r, t_xd], tag=f"go{ex}x{blk}")

        def xd(eng, inc, blk=blk, ex=ex):
            inc(eng.dma_start(
                out=x2loc.ap()[ex].rearrange("(b p) f -> p b f", p=128)
                [:, blk, :], in_=x2fin[:]))
        t_xd = P.step("gpsimd", xd, after=[t_x2b], dma=1, tag=f"go{ex}d{blk}")

        t_tp = t_x2b
        for cb in range(4):
            def tpp(eng, cb=cb):
                for j in range(2):
                    c = 2 * cb + j
                    eng.transpose(PS[4][:, 128 * j:128 * (j + 1)],
                                  x2fin[:, 128 * c:128 * (c + 1)], identf[:])
            t_t = P.step("tensor", tpp, after=[t_x2b, t_tp],
                         tag=f"go{ex}t{blk}.{cb}")

            def tpe(eng, cb=cb):
                for j in range(2):
                    eng.tensor_copy(xfmT[j][:], PS[4][:, 128 * j:128 * (j + 1)])
            t_e = P.step("vector", tpe, after=[t_t], tag=f"go{ex}f{blk}.{cb}")

            def gz(eng, cb=cb, blk=blk):
                for j in range(2):
                    c = 2 * cb + j
                    eng.matmul(PS[5][:, 0:8], xfmT[j][:], gwS[:, c, :],
                               start=(c == 0), stop=(c == 7))
            t_tp = P.step("tensor", gz, after=[t_e, t_pz],
                          tag=f"go{ex}g{blk}.{cb}")

        def pzev(eng, blk=blk):
            eng.tensor_copy(zTM[:, blk, :], PS[5][:, 0:8])
        t_pz = P.step("vector", pzev, after=[t_tp], tag=f"go{ex}z{blk}")
    t_fin = P.step("vector", lambda e: None, after=[t_pz, t_xd, t_x2b],
                   tag=f"go{ex}fin")
    return t_fin, t_xd


def emit_gating(nc, P, ex, zTM, m1all, m2all, e1all, e2all, w1all, w2all,
                wsum, neg30, msk8i, cnd8, rtblk, iota256, big128, rt_in,
                tap, prev):
    iota8 = iota256[:, 0:8]
    big8 = big128[:, 0:8]

    def m1r(eng):
        eng.tensor_reduce(m1all[:], zTM[:], axis=AX.X, op=OP.max)
    t = P.step("vector", m1r, after=[prev], tag=f"gt{ex}m1")

    msk8 = msk8i.bitcast(F32)
    for blk in range(16):
        def sel(eng, blk=blk):
            z = zTM[:, blk, :]
            eng.tensor_tensor(msk8[:], z, m1all[:, blk:blk + 1]
                              .to_broadcast([128, 8]), op=OP.is_ge)
            eng.drain()
            eng.tensor_scalar(msk8[:], msk8[:], -1e9, 1e9, op0=OP.mult,
                              op1=OP.add)
            eng.drain()
            eng.tensor_tensor(cnd8[:], msk8[:], iota8, op=OP.add)
            eng.drain()
            eng.tensor_reduce(e1all[:, blk:blk + 1], cnd8[:],
                              axis=AX.X, op=OP.min)
            eng.drain()
            eng.tensor_tensor(msk8[:], iota8, e1all[:, blk:blk + 1]
                              .to_broadcast([128, 8]), op=OP.is_equal)
            eng.drain()
            eng.tensor_scalar(msk8[:], msk8[:], -1e30, None, op0=OP.mult)
            eng.drain()
            eng.tensor_tensor(cnd8[:], z, msk8[:], op=OP.add)
            eng.drain()
            eng.tensor_reduce(m2all[:, blk:blk + 1], cnd8[:],
                              axis=AX.X, op=OP.max)
            eng.drain()
            eng.tensor_tensor(msk8[:], cnd8[:], m2all[:, blk:blk + 1]
                              .to_broadcast([128, 8]), op=OP.is_ge)
            eng.drain()
            eng.tensor_scalar(msk8[:], msk8[:], -1e9, 1e9, op0=OP.mult,
                              op1=OP.add)
            eng.drain()
            eng.tensor_tensor(cnd8[:], msk8[:], iota8, op=OP.add)
            eng.drain()
            eng.tensor_reduce(e2all[:, blk:blk + 1], cnd8[:],
                              axis=AX.X, op=OP.min)
            eng.drain()
        t = P.step("vector", sel, after=[t], tag=f"gt{ex}s{blk}")

    def sig(eng):
        eng.activation(out=m1all[:], in_=m1all[:], func=AF.Sigmoid)
        eng.drain()
        eng.activation(out=m2all[:], in_=m2all[:], func=AF.Sigmoid)
    t = P.step("scalar", sig, after=[t], tag=f"gt{ex}sg")

    def wts(eng):
        eng.tensor_tensor(wsum[:], m1all[:], m2all[:], op=OP.add)
        eng.drain()
        eng.tensor_scalar(wsum[:], wsum[:], 1e-6, None, op0=OP.add)
        eng.drain()
        eng.reciprocal(wsum[:], wsum[:])
        eng.drain()
        eng.tensor_tensor(w1all[:], m1all[:], wsum[:], op=OP.mult)
        eng.tensor_tensor(w2all[:], m2all[:], wsum[:], op=OP.mult)
    t = P.step("vector", wts, after=[t], tag=f"gt{ex}w")

    def asm(eng):
        eng.tensor_copy(rtblk[:, :, 0], e1all[:])
        eng.tensor_copy(rtblk[:, :, 1], e2all[:])
        eng.tensor_copy(rtblk[:, :, 2], w1all[:])
        eng.tensor_copy(rtblk[:, :, 3], w2all[:])
    t = P.step("vector", asm, after=[t], tag=f"gt{ex}a")
    t = tap(f"rt_{ex}", rtblk[:], t, [128, 16, 4], F32)

    def rtd(eng, inc, ex=ex):
        inc(eng.dma_start(
            out=rt_in.ap().rearrange("(b p) c -> p b c", p=128)
            [:, 16 * ex:16 * (ex + 1), :],
            in_=rtblk[:]))
    t_rt = P.step("sync", rtd, after=[t], dma=1, tag=f"gt{ex}d")
    return t, t_rt



F32 = mybir.dt.float32
BF16 = mybir.dt.bfloat16
I32 = mybir.dt.int32
AX = mybir.AxisListType
OP = mybir.AluOpType
AF = mybir.ActivationFunctionType

D = 1024
DFF = 2048
CAP = 10240


def bfv(region, off_f32, n_bf16):
    return region[:, off_f32:off_f32 + n_bf16 // 2].bitcast(BF16)


def emit_moe_pre(nc, P, L, t_start, n_cores=8):
    R5 = L["R5"]
    NTG = n_cores * 4096
    Z = nc.dram_tensor("m_Z", [NTG + 128, D], BF16)
    slotmap = nc.dram_tensor("m_slot", [CAP + 128, 4], F32)
    zeros = bfv(R5, 2056, 512)
    smfill = R5[:, 2320:2324]

    def zfill(eng):
        eng.memset(zeros[:], 0.0)
        eng.memset(smfill[:], 0.0)
        eng.drain()
        eng.memset(smfill[:, 1:2], float(NTG))
    t_zf = P.step("vector", zfill, after=[t_start], tag="mz0")

    zview = Z.ap().rearrange("(n p) d -> p n d", p=128)
    t_zz = t_zf
    for i in range(NTG // 128 + 1):
        def zz(eng, inc, i=i):
            inc(eng.dma_start(out=zview[:, i, 0:512], in_=zeros[:]))
            inc(eng.dma_start(out=zview[:, i, 512:1024], in_=zeros[:]))
        t_zz = P.step("scalar", zz, after=[t_zf, t_zz], dma=2, tag=f"mzz{i}")
    smview = slotmap.ap().rearrange("(n p) c -> p n c", p=128)
    t_sf = t_zf
    for i in range(CAP // 128 + 1):
        def sf(eng, inc, i=i):
            inc(eng.dma_start(out=smview[:, i, :], in_=smfill[:]))
        t_sf = P.step("scalar", sf, after=[t_zf, t_sf], dma=1, tag=f"msf{i}")
    return dict(Z=Z, slotmap=slotmap, t_zz=t_zz, t_sf=t_sf)


def emit_moe(nc, es, P, ins, L, H, pre, tap, x1_tm, out, prev, n_cores=8):
    R1, R2, R3, R4, R5 = L["R1"], L["R2"], L["R3"], L["R4"], L["R5"]
    PS = L["PS"]
    ones1, ident, identf = L["ones1"], L["ident"], L["identf"]
    ones_col, tri = L["ones_col"], L["tri"]
    rowb = L["rowb"]
    sem_coll = H["sem_coll"]
    byex, rt_all, x2loc = H["byex"], H["rt_all"], H["x2loc"]
    Z, slotmap = pre["Z"], pre["slotmap"]
    NTG = n_cores * 4096
    NBG = NTG // 128
    NC2 = n_cores * 2048
    nch = (NBG + 127) // 128
    groups = [list(range(n_cores))]

    moe_loc = nc.dram_tensor("m_out", [4096, D], BF16)
    x1_hbm = nc.dram_tensor("m_x1", [4096, D], BF16).ap()

    h1FM = bfv(R1, 0, 8192).rearrange("p (f t) -> p f t", f=16)
    xgFM = bfv(R1, 4096, 4096).rearrange("p (c t) -> p c t", c=8)
    xgTM = bfv(R1, 6144, 4096).rearrange("p (b f) -> p b f", b=4)
    rtS = R2[:, 0:1024].rearrange("p (b c) -> p b c", c=4)
    expid_rep = R2[:, 1024:1025]
    expid_st = R2[0:1, 1025:1026]
    colsS = R2[0:1, 1282:1538]
    chunkS = R2[0:1, 1538:1546]
    n2wS = R2[0:1, 2048:3072]
    coloffs = R2[0:1, 3072:3328]
    zstage = bfv(R3, 0, 4096).rearrange("p (b f) -> p b f", b=4)
    keepf = R3[:, 0:256]
    nkeep = R3[:, 256:512]
    sc1 = R3[:, 512:768]
    sc2 = R3[:, 768:1024]
    outstage = R3[:, 2048:3072]
    i1 = R4[:, 0:256]
    i2 = R4[:, 256:512]
    ind = R4[:, 512:768]
    indb = bfv(R4, 768, 256)
    colsb = bfv(R4, 896, 2)
    wsel = R4[:, 900:1156]
    rank = R4[:, 1156:1412]
    pay = R4[:, 1412:2436].rearrange("p (g c) -> p g c", c=4)
    dsti = R3[:, 1280:1536].bitcast(I32)
    iotok = R3[:, 1536:1792]
    colchf = R4[:, 2436:2444]
    excl2 = R4[:, 2444:2446]
    exadd = R4[:, 2446:2447]
    tA = R4[:, 0:1024]
    tB = R4[:, 1024:2048]
    slotS = R5[:, 0:16].rearrange("p (b c) -> p b c", c=4)
    slotI = R5[:, 16:20].bitcast(I32)
    scrA = R5[:, 20:24]
    scrB = R5[:, 24:28]
    scrC = R5[:, 28:32]
    gidxI = R5[:, 32:36].bitcast(I32)
    byexS = R3[:, 1024:1280]
    n2wrep = bfv(R5, 64, 1024)
    lnm = R5[:, 580:596].rearrange("p (b c) -> p b c", c=8)
    xflat = x1_tm[:].rearrange("p b f -> p (b f)")
    w1S = xflat[:, 0:16384].rearrange("p (c j) -> p c j", c=8)
    w2S = xflat[:, 16384:32768].rearrange("p (f j) -> p f j", f=16)
    xbm = bfv(R1, 0, 1024)
    xbx = bfv(R1, 512, 1024)
    xb1 = bfv(R1, 1024, 1024)

    def x1spill(eng, inc):
        inc(eng.dma_start(out=x1_hbm.rearrange("(b p) f -> p b f", p=128),
                          in_=x1_tm[:]))
    t_sp = P.step("sync", x1spill, after=[prev], dma=1, tag="mx1s")

    def wload(eng, inc):
        inc(eng.dma_start(out=w1S[:],
                          in_=ins["w1e"].rearrange("(c p) j -> p c j", p=128)))
        inc(eng.dma_start(out=w2S[:],
                          in_=ins["w2e"].rearrange("(f p) j -> p f j", p=128)))
    t_wl = P.step("gpsimd", wload, after=[t_sp], dma=2, tag="mwl")

    def rtload(eng, inc):
        eng.wait_ge(sem_coll, 3)
        inc(eng.dma_start(out=rtS[:, 0:NBG, :],
                          in_=rt_all.ap().rearrange("(b p) c -> p b c",
                                                    p=128)))
        inc(eng.dma_start(out=n2wS, in_=ins["norm2_w"]))
        inc(eng.dma_start(out=expid_st, in_=ins["expid"]))
        inc(eng.dma_start(out=iotok[:, 0:NBG],
                          in_=ins["iota_tok"][:, 0:NBG]))
        inc(eng.dma_start(out=byexS[:, 0:NBG],
                          in_=ins["iota_byex"][:, 0:NBG]))
    t_rl = P.step("sync", rtload, after=[prev], dma=5, tag="mrl")

    def repid(eng):
        eng.matmul(PS[0][:, 0:1], ones1[0:1, 0:128], expid_st,
                   start=True, stop=True)
    t_ri = P.step("tensor", repid, after=[t_rl, prev], tag="mri")

    def repev(eng):
        eng.tensor_copy(expid_rep[:], PS[0][:, 0:1])
    t_re = P.step("vector", repev, after=[t_ri], tag="mre")

    def indf(eng):
        eng.tensor_scalar(i1[:, 0:NBG], rtS[:, 0:NBG, 0], expid_rep[:], None,
                          op0=OP.is_equal)
        eng.tensor_scalar(i2[:, 0:NBG], rtS[:, 0:NBG, 1], expid_rep[:], None,
                          op0=OP.is_equal)
        eng.drain()
        eng.tensor_tensor(ind[:, 0:NBG], i1[:, 0:NBG], i2[:, 0:NBG],
                          op=OP.max)
        eng.drain()
        eng.tensor_copy(indb[:, 0:NBG], ind[:, 0:NBG])
        eng.tensor_tensor(i1[:, 0:NBG], i1[:, 0:NBG], rtS[:, 0:NBG, 2],
                          op=OP.mult)
        eng.tensor_tensor(i2[:, 0:NBG], i2[:, 0:NBG], rtS[:, 0:NBG, 3],
                          op=OP.mult)
        eng.drain()
        eng.tensor_tensor(wsel[:, 0:NBG], i1[:, 0:NBG], i2[:, 0:NBG],
                          op=OP.add)
    t_if = P.step("vector", indf, after=[t_re], tag="mif")

    def pref(eng):
        eng.matmul(PS[0][:, 0:NBG], tri[:], indb[:, 0:NBG],
                   start=True, stop=True)
        eng.matmul(PS[1][0:1, 0:NBG], ones_col[:], indb[:, 0:NBG],
                   start=True, stop=True)
    t_pf = P.step("tensor", pref, after=[t_if], tag="mpf")

    def colev(eng):
        eng.memset(colsS[0:1, 0:256], 0.0)
        eng.drain()
        eng.tensor_copy(colsS[0:1, 0:NBG], PS[1][0:1, 0:NBG])
    t_ce = P.step("vector", colev, after=[t_pf], tag="mce")

    def cb1(eng, inc):
        inc(eng.dma_start(out=rowb[0:256], in_=colsS[0:1, 0:256]))
    t_b1 = P.step("sync", cb1, after=[t_ce], dma=1, tag="mb1")

    def cb2(eng, inc):
        inc(eng.dma_start(
            out=colchf[:, 0:2],
            in_=rowb[0:256].rearrange("(a p) -> p a", p=128),
            allow_slow_non_contiguous=True))
    t_b2 = P.step("gpsimd", cb2, after=[t_b1], dma=1, tag="mb2")

    def cbc(eng):
        eng.tensor_copy(colsb[:, 0:2], colchf[:, 0:2])
    t_bc = P.step("vector", cbc, after=[t_b2], tag="mbc")

    def excl(eng):
        eng.matmul(PS[2][:, 0:2], tri[:], colsb[:, 0:2],
                   start=True, stop=True)
        eng.matmul(PS[3][0:1, 0:2], ones_col[:], colsb[:, 0:2],
                   start=True, stop=True)
    t_ex = P.step("tensor", excl, after=[t_bc], tag="mex")

    def exev(eng):
        eng.tensor_tensor(excl2[:, 0:2], PS[2][:, 0:2],
                          colchf[:, 0:2], op=OP.subtract)
    t_exv = P.step("vector", exev, after=[t_ex], tag="mexv")

    def totev(eng):
        eng.tensor_copy(chunkS[0:1, 0:2], PS[3][0:1, 0:2])
    t_tv = P.step("vector", totev, after=[t_ex], tag="mtv")

    def totmm(eng):
        eng.matmul(PS[4][:, 0:1], ones1[0:1, 0:128], chunkS[0:1, 0:1],
                   start=True, stop=True)
    t_tm = P.step("tensor", totmm, after=[t_tv], tag="mtm")

    def totadd(eng):
        eng.tensor_copy(exadd[:], PS[4][:, 0:1])
        eng.drain()
        eng.tensor_tensor(excl2[:, 1:2], excl2[:, 1:2], exadd[:],
                          op=OP.add)
    t_exv = P.step("vector", totadd, after=[t_tm, t_exv], tag="mta")

    def cb3(eng, inc):
        inc(eng.dma_start(out=rowb[0:256].rearrange("(a p) -> p a", p=128),
                          in_=excl2[:, 0:2],
                          allow_slow_non_contiguous=True))
    t_b3 = P.step("sync", cb3, after=[t_exv], dma=1, tag="mb3")

    def cb4(eng, inc):
        inc(eng.dma_start(out=coloffs[0:1, 0:256], in_=rowb[0:256]))
    t_b4 = P.step("gpsimd", cb4, after=[t_b3], dma=1, tag="mb4")

    def offmm(eng):
        eng.matmul(PS[1][:, 0:NBG], ones1[0:1, 0:128], coloffs[0:1, 0:NBG],
                   start=True, stop=True)
    t_om = P.step("tensor", offmm, after=[t_b4, t_ce], tag="mom")

    def rankf(eng):
        eng.tensor_tensor(rank[:, 0:NBG], PS[0][:, 0:NBG], ind[:, 0:NBG],
                          op=OP.subtract)
        eng.drain()
        eng.tensor_tensor(rank[:, 0:NBG], rank[:, 0:NBG], PS[1][:, 0:NBG],
                          op=OP.add)
    t_rk = P.step("vector", rankf, after=[t_om], tag="mrk")

    iota_p = None

    def payf(eng):
        N = NBG
        eng.tensor_scalar(keepf[:, 0:N], rank[:, 0:N], float(CAP) - 0.5,
                          None, op0=OP.is_le)
        eng.drain()
        eng.tensor_tensor(keepf[:, 0:N], keepf[:, 0:N], ind[:, 0:N],
                          op=OP.mult)
        eng.drain()
        eng.tensor_scalar(nkeep[:, 0:N], keepf[:, 0:N], -1.0, 1.0,
                          op0=OP.mult, op1=OP.add)
        eng.drain()
        eng.tensor_tensor(sc1[:, 0:N], rank[:, 0:N], keepf[:, 0:N],
                          op=OP.mult)
        eng.tensor_scalar(sc2[:, 0:N], nkeep[:, 0:N], iotok[:, 0:1],
                          None, op0=OP.mult)
        eng.drain()
        eng.tensor_tensor(sc1[:, 0:N], sc1[:, 0:N], sc2[:, 0:N], op=OP.add)
        eng.tensor_scalar(sc2[:, 0:N], nkeep[:, 0:N], float(CAP), None,
                          op0=OP.mult)
        eng.drain()
        eng.tensor_tensor(sc1[:, 0:N], sc1[:, 0:N], sc2[:, 0:N], op=OP.add)
        eng.drain()
        eng.tensor_copy(dsti[:, 0:N], sc1[:, 0:N])
        eng.tensor_copy(pay[:, 0:N, 0], byexS[:, 0:N])
        eng.tensor_copy(pay[:, 0:N, 1], iotok[:, 0:N])
        eng.tensor_copy(pay[:, 0:N, 2], wsel[:, 0:N])
        eng.memset(pay[:, 0:N, 3], 0.0)
    t_py = P.step("vector", payf, after=[t_rk], tag="mpy")

    def smscat(eng, inc):
        for g in range(NBG):
            inc(eng.indirect_dma_start(
                slotmap.ap(),
                bass.IndirectOffsetOnAxis(dsti[:, g:g + 1], 0),
                pay[:, g, :], None))
    t_sc = P.step("gpsimd", smscat, after=[t_py, pre["t_sf"]], dma=NBG,
                  tag="msc")

    def n2mm(eng):
        eng.matmul(PS[5][:, 0:512], ones1[0:1, 0:128], n2wS[0:1, 0:512],
                   start=True, stop=True)
        eng.matmul(PS[6][:, 0:512], ones1[0:1, 0:128], n2wS[0:1, 512:1024],
                   start=True, stop=True)
    t_n2 = P.step("tensor", n2mm, after=[t_rl, t_om], tag="mn2")

    def n2ev(eng):
        eng.tensor_copy(n2wrep[:, 0:512], PS[5][:, 0:512])
        eng.tensor_copy(n2wrep[:, 512:1024], PS[6][:, 0:512])
    t_n2e = P.step("vector", n2ev, after=[t_n2], tag="mn2e")

    tpbf = PS[0][:].bitcast(BF16)
    byex2 = byex.ap().rearrange("e r d -> (e r) d")
    NWND = CAP // 512
    t_wend = P.step("tensor", lambda e: None, after=[t_n2e, t_sc], tag="mws")
    t_zs = pre["t_zz"]
    t_gel = {0: t_wend, 1: t_wend}

    for wnd in range(NWND):
        def sl(eng, inc, wnd=wnd):
            if wnd == 0:
                eng.wait_ge(sem_coll, 2)
            inc(eng.dma_start(
                out=slotS[:],
                in_=slotmap.ap().rearrange("(b p) c -> p b c", p=128)
                [:, 4 * wnd:4 * wnd + 4, :]))
        t_sl = P.step("sync", sl, after=[t_sc, t_wend], dma=1,
                      tag=f"mw{wnd}s")

        def idx(eng, wnd=wnd):
            tok = slotS[:, :, 1]
            eng.tensor_copy(gidxI[:], slotS[:, :, 0])
            eng.tensor_scalar(scrB[:], tok, float(NTG) - 0.5, None,
                              op0=OP.is_ge)
            eng.drain()
            eng.tensor_scalar(scrB[:], scrB[:], iotok[:, 0:1], None,
                              op0=OP.mult)
            eng.drain()
            eng.tensor_tensor(scrA[:], tok, scrB[:], op=OP.add)
            eng.drain()
            eng.tensor_copy(slotI[:], scrA[:])
        t_idx = P.step("vector", idx, after=[t_sl, t_zs], tag=f"mw{wnd}i")

        def gath(eng, inc, wnd=wnd):
            for b in range(4):
                inc(eng.indirect_dma_start(
                    xgTM[:, b, :], None, byex2,
                    bass.IndirectOffsetOnAxis(gidxI[:, b:b + 1], 0)))
        t_g = P.step("gpsimd", gath, after=[t_idx, t_wend], dma=4,
                     tag=f"mw{wnd}g")

        t_tp_prev = t_g
        for b in range(4):
            def tpp(eng, b=b):
                for c in range(8):
                    eng.transpose(tpbf[:, 128 * c:128 * (c + 1)],
                                  xgTM[:, b, 128 * c:128 * (c + 1)],
                                  ident[:])
            t_tp = P.step("tensor", tpp, after=[t_g, t_wend, t_tp_prev],
                          tag=f"mw{wnd}t{b}")

            def tpe(eng, b=b):
                for c in range(8):
                    eng.tensor_copy(xgFM[:, c, 128 * b:128 * (b + 1)],
                                    tpbf[:, 128 * c:128 * (c + 1)])
            t_tp_prev = P.step("vector", tpe, after=[t_tp],
                               tag=f"mw{wnd}e{b}")
        t_tpe = t_tp_prev

        for f in range(16):
            bank = PS[f % 2]

            def h1m(eng, f=f, bank=bank):
                for c in range(8):
                    eng.matmul(bank[:, 0:512],
                               w1S[:, c, 128 * f:128 * (f + 1)],
                               xgFM[:, c, :], start=(c == 0), stop=(c == 7))
            t_m = P.step("tensor", h1m,
                         after=[t_tpe, t_wl, t_gel[f % 2], t_wend],
                         tag=f"mw{wnd}h1{f}")

            def gel(eng, f=f, bank=bank):
                eng.activation(out=h1FM[:, f, :], in_=bank[:, 0:512],
                               func=AF.Gelu, scale=1.0)
            t_gel[f % 2] = P.step("scalar", gel, after=[t_m],
                                  tag=f"mw{wnd}g{f}")
        t_h1 = P.step("tensor", lambda e: None,
                      after=[t_gel[0], t_gel[1]], tag=f"mw{wnd}h1f")

        t_ev_all = []
        for ts4 in range(4):
            for half in range(2):
                bi = ts4 * 2 + half
                bank = PS[bi]

                def h2m(eng, ts4=ts4, half=half, bank=bank):
                    for f in range(16):
                        eng.matmul(
                            bank[:, 0:512],
                            h1FM[:, f, 128 * ts4:128 * (ts4 + 1)],
                            w2S[:, f, 512 * half:512 * (half + 1)],
                            start=(f == 0), stop=(f == 15))
                t_m2 = P.step("tensor", h2m, after=[t_h1],
                              tag=f"mw{wnd}h2{bi}")

                def h2e(eng, ts4=ts4, half=half, bank=bank):
                    eng.tensor_scalar(
                        zstage[:, ts4, 512 * half:512 * (half + 1)],
                        bank[:, 0:512], slotS[:, ts4, 2:3], None,
                        op0=OP.mult)
                tok_ev = P.step("vector", h2e, after=[t_m2, t_zs],
                                tag=f"mw{wnd}z{bi}")
                t_ev_all.append(tok_ev)

        def zscat(eng, inc, wnd=wnd):
            for b in range(4):
                inc(eng.indirect_dma_start(
                    Z.ap(), bass.IndirectOffsetOnAxis(slotI[:, b:b + 1], 0),
                    zstage[:, b, :], None))
        t_zs = P.step("gpsimd", zscat, after=t_ev_all + [pre["t_zz"]],
                      dma=4, tag=f"mw{wnd}zs")
        t_wend = P.step("tensor", lambda e: None, after=t_ev_all,
                        tag=f"mw{wnd}end")

    def rsc(eng):
        eng.collective_compute(
            "ReduceScatter", OP.add, replica_groups=groups,
            ins=[Z.ap()[0:NTG]], outs=[moe_loc.ap()]).then_inc(sem_coll, 1)
    t_rs = P.step("gpsimd", rsc, after=[t_zs], tag="mrs")
    n_coll = H["n_coll"] + 1

    t_ld = prev
    t_st = t_wend
    t_ev_prev = t_wend
    for blk in range(32):
        e = blk // 16
        i = blk % 16

        def ld(eng, inc, blk=blk, e=e, i=i):
            if blk == 0:
                eng.wait_ge(sem_coll, n_coll)
            inc(eng.dma_start(
                out=xbm[:], in_=moe_loc.ap()
                .rearrange("(b p) f -> p b f", p=128)[:, blk, :]))
            inc(eng.dma_start(
                out=xbx[:], in_=x2loc.ap()[e]
                .rearrange("(b p) f -> p b f", p=128)[:, i, :]))
            inc(eng.dma_start(
                out=xb1[:], in_=x1_hbm
                .rearrange("(b p) f -> p b f", p=128)[:, blk, :]))
        t_ld = P.step("sync", ld, after=[t_rs, t_sp, t_ev_prev, t_ld],
                      dma=3, tag=f"mt{blk}l")

        def a2f(eng, blk=blk):
            eng.tensor_tensor(tA[:], xbm[:], xbx[:], op=OP.add)
        t_a = P.step("vector", a2f, after=[t_ld, t_st], tag=f"mt{blk}a")

        def sq(eng):
            eng.activation(out=tB[:, 0:512], in_=tA[:, 0:512],
                           func=AF.Square, accum_out=lnm[:, 0, 0:1])
            eng.drain()
            eng.activation(out=tB[:, 512:1024], in_=tA[:, 512:1024],
                           func=AF.Square, accum_out=lnm[:, 0, 1:2])
        t_q = P.step("scalar", sq, after=[t_a], tag=f"mt{blk}q")

        def rst(eng):
            eng.tensor_tensor(lnm[:, 0, 2:3], lnm[:, 0, 0:1],
                              lnm[:, 0, 1:2], op=OP.add)
            eng.drain()
            eng.tensor_scalar(lnm[:, 0, 2:3], lnm[:, 0, 2:3], 1.0 / D, 1e-6,
                              op0=OP.mult, op1=OP.add)
            eng.drain()
            eng.reciprocal(lnm[:, 0, 2:3], lnm[:, 0, 2:3])
        t_r1 = P.step("vector", rst, after=[t_q], tag=f"mt{blk}r")

        def rs2f(eng):
            eng.activation(out=lnm[:, 0, 3:4], in_=lnm[:, 0, 2:3],
                           func=AF.Sqrt)
        t_r2 = P.step("scalar", rs2f, after=[t_r1], tag=f"mt{blk}r2")

        def ff(eng):
            eng.tensor_scalar(tB[:], tA[:], lnm[:, 0, 3:4], None,
                              op0=OP.mult)
            eng.drain()
            eng.tensor_tensor(tA[:], tB[:], n2wrep[:], op=OP.mult)
            eng.drain()
            eng.tensor_tensor(tB[:], tA[:], xb1[:], op=OP.add)
        t_f = P.step("vector", ff, after=[t_r2], tag=f"mt{blk}f")

        def lmean(eng):
            eng.tensor_reduce(lnm[:, 1, 0:1], tB[:], axis=AX.X, op=OP.add)
        t_m1 = P.step("vector", lmean, after=[t_f], tag=f"mt{blk}m")

        def lsq(eng):
            eng.activation(out=tA[:, 0:512], in_=tB[:, 0:512],
                           func=AF.Square, accum_out=lnm[:, 1, 1:2])
            eng.drain()
            eng.activation(out=tA[:, 512:1024], in_=tB[:, 512:1024],
                           func=AF.Square, accum_out=lnm[:, 1, 2:3])
        t_q2 = P.step("scalar", lsq, after=[t_f], tag=f"mt{blk}q2")

        def lst(eng):
            Lr = lnm[:, 1, :]
            eng.tensor_tensor(Lr[:, 1:2], Lr[:, 1:2], Lr[:, 2:3], op=OP.add)
            eng.tensor_scalar(Lr[:, 3:4], Lr[:, 0:1], 1.0 / D, None,
                              op0=OP.mult)
            eng.drain()
            eng.tensor_tensor(Lr[:, 4:5], Lr[:, 3:4], Lr[:, 3:4], op=OP.mult)
            eng.tensor_scalar(Lr[:, 5:6], Lr[:, 1:2], 1.0 / D, None,
                              op0=OP.mult)
            eng.drain()
            eng.tensor_tensor(Lr[:, 4:5], Lr[:, 5:6], Lr[:, 4:5],
                              op=OP.subtract)
            eng.drain()
            eng.tensor_scalar(Lr[:, 4:5], Lr[:, 4:5], 1e-5, None, op0=OP.add)
            eng.drain()
            eng.reciprocal(Lr[:, 4:5], Lr[:, 4:5])
        t_s2 = P.step("vector", lst, after=[t_m1, t_q2], tag=f"mt{blk}s")

        def lsr(eng):
            eng.activation(out=lnm[:, 1, 4:5], in_=lnm[:, 1, 4:5],
                           func=AF.Sqrt)
        t_s3 = P.step("scalar", lsr, after=[t_s2], tag=f"mt{blk}s3")

        def lap(eng):
            eng.tensor_scalar(outstage[:], tB[:], lnm[:, 1, 3:4],
                              lnm[:, 1, 4:5], op0=OP.subtract, op1=OP.mult)
        t_ev_prev = P.step("vector", lap, after=[t_s3], tag=f"mt{blk}o")

        def sto(eng, inc, blk=blk):
            inc(eng.dma_start(
                out=out.rearrange("(b p) f -> p b f", p=128)[:, blk, :],
                in_=outstage[:]))
        t_st = P.step("sync", sto, after=[t_ev_prev], dma=1,
                      tag=f"mt{blk}st")
    return P.step("sync", lambda e: None, after=[t_st], tag="mfin")



# version 06
# speedup vs baseline: 1.0057x; 1.0057x over previous
import numpy as np
import ml_dtypes
from contextlib import ExitStack

import concourse.bass as bass
import concourse.mybir as mybir
import concourse.library_config as libcfg
from concourse.library_overlay import lower_extended_insts
from concourse.bass_utils import run_bass_kernel_spmd

F32 = mybir.dt.float32
BF16 = mybir.dt.bfloat16
I32 = mybir.dt.int32
I16 = mybir.dt.int16
U32 = mybir.dt.uint32
AX = mybir.AxisListType
OP = mybir.AluOpType
AF = mybir.ActivationFunctionType
BF = ml_dtypes.bfloat16

NC = 8
S = 2048
D = 1024
TLOC = 4096
ENGS = ("sync", "scalar", "tensor", "vector", "gpsimd")


class Prog:

    def __init__(self, nc):
        self.nc = nc
        self.steps = {e: [] for e in ENGS}
        self.count = {e: 0 for e in ENGS}
        self.sems = {}

    def alloc_sems(self, es):
        for e in ENGS:
            self.sems[e] = es.enter_context(self.nc.semaphore(f"sem_{e}"))
            self.sems["d_" + e] = es.enter_context(self.nc.semaphore(f"dsem_{e}"))
            self.count["d_" + e] = 0

    def step(self, eng, fn, after=(), dma=0, tag=""):
        waits = [(self.sems[s], v) for (s, v) in after]
        key = ("d_" + eng) if dma else eng
        amount = 16 * dma if dma else 1
        prev_total = self.count[key]
        self.count[key] += amount
        tok = (key, self.count[key])
        self.steps[eng].append((fn, waits, self.sems[key], amount, dma, tag,
                                prev_total))
        return tok

    def barrier(self):
        toks = [(k, self.count[k]) for k in self.count]
        outs = []
        for e in ENGS:
            outs.append(self.step(e, lambda eng: None,
                                  after=[t for t in toks if t[0] != e],
                                  tag="barrier"))
        return outs

    def emit(self, block):
        def mk(eng_name):
            def run(eng):
                for (fn, waits, sem, amount, dma, tag, prev_total) in self.steps[eng_name]:
                    for (s, v) in waits:
                        eng.wait_ge(s, v)
                    if dma:
                        if prev_total:
                            eng.wait_ge(sem, prev_total)
                        state = {"left": amount}

                        def inc(instr, n=16):
                            instr.then_inc(sem, n)
                            state["left"] -= n
                            return instr
                        fn(eng, inc)
                        assert state["left"] == 0, (eng_name, tag, state["left"])
                    else:
                        fn(eng)
                        if eng_name == "sync":
                            eng.nop().then_inc(sem, amount)
                        else:
                            eng.drain().then_inc(sem, amount)
            return run
        block.sync(mk("sync"))
        block.scalar(mk("scalar"))
        block.tensor(mk("tensor"))
        block.vector(mk("vector"))
        block.gpsimd(mk("gpsimd"))

    def final_wait(self, eng):
        toks = [(k, self.count[k]) for k in self.count]
        self.step(eng, lambda E: None,
                  after=[t for t in toks if t[0] != eng], tag="final")


def host_constants(n_cores=NC):
    import jax
    cpu = jax.devices("cpu")[0]
    with jax.default_device(cpu):
        rot = np.asarray(jax.random.normal(jax.random.key(42),
                                           (16, 64, 4, 32), dtype=np.float32))
    c = {}
    c["rot_all"] = np.ascontiguousarray(rot.reshape(16, 64, 128))
    c["ident"] = np.eye(128, dtype=np.float32).astype(BF)
    c["identf"] = np.eye(128, dtype=np.float32)
    c["tri_incl"] = (np.arange(128)[:, None] <= np.arange(128)[None, :]).astype(BF)
    c["tri_excl64"] = (np.arange(64)[:, None] < np.arange(64)[None, :]).astype(np.float32)
    c["ones_1x128f"] = np.ones((1, 128), np.float32)
    c["ones_col_bf"] = np.ones((128, 1), np.float32).astype(BF)
    c["iota32_row"] = np.tile(np.arange(32, dtype=np.float32), (128, 1))
    c["iota32x4_row"] = np.tile(np.arange(32, dtype=np.float32), (128, 4))
    c["big128"] = np.full((128, 128), 1e9, np.float32)
    c["iota256_row"] = np.tile(np.arange(256, dtype=np.float32), (128, 1))
    c["big"] = np.full((128, 32), 1e9, np.float32)
    c["iota_tok"] = np.ascontiguousarray(
        (np.arange(128, dtype=np.float32)[:, None]
         + 128.0 * np.arange(256, dtype=np.float32)[None, :]))
    t = c["iota_tok"].astype(np.int64)
    cc = t >> 12
    ee = (t >> 11) & 1
    c["iota_byex"] = np.ascontiguousarray(
        (t - 2048 * cc + (n_cores * 2048 - 2048) * ee).astype(np.float32))
    no = np.zeros((1, 256), np.float32)
    for n in range(4):
        no[0, n * 64:(n + 1) * 64] = n * 2048.0 - 1.0
    c["nround_off"] = no
    return c


def prep_inputs(inputs, n_cores=NC):
    x = np.asarray(inputs["L0_attn_x"])
    consts = host_constants(n_cores)
    w = {k: np.asarray(v) for k, v in inputs.items() if k != "L0_attn_x"}
    per_core = []
    for c in range(NC):
        xb = x[2 * c:2 * c + 2].reshape(TLOC, D)
        m = dict(
            x_tm=np.ascontiguousarray(xb),
            x_fm=np.ascontiguousarray(xb.T),
            w_qk_t=np.ascontiguousarray(w["L0_attn_w_qk"].T),
            w_v_t=np.ascontiguousarray(w["L0_attn_w_v"].T),
            w_out_t=np.ascontiguousarray(w["L0_attn_w_out"].T),
            qw=w["L0_ffn_qw"], kw=w["L0_ffn_kw"], vw=w["L0_ffn_vw"],
            ow=w["L0_ffn_ow"],
            qb=w["L0_ffn_qb"].reshape(1, -1), kb=w["L0_ffn_kb"].reshape(1, -1),
            vb=w["L0_ffn_vb"].reshape(1, -1), ob=w["L0_ffn_ob"].reshape(1, -1),
            gate_w=w["L0_ffn_gate_w"],
            w1e=np.ascontiguousarray(w["L0_ffn_w1"][c]),
            w2e=np.ascontiguousarray(w["L0_ffn_w2"][c]),
            norm1_w=w["L0_ffn_norm1_w"].reshape(1, -1),
            norm2_w=w["L0_ffn_norm2_w"].reshape(1, -1),
            gtok_base=np.full((1, 1), 4096.0 * c, np.float32),
            expid=np.full((1, 1), float(c), np.float32),
            qb_fm=np.ascontiguousarray(
                w["L0_ffn_qb"].reshape(8, 128).T),
            kb_fm=np.ascontiguousarray(
                w["L0_ffn_kb"].reshape(2, 128).T),
            gate_w_fm=np.ascontiguousarray(
                w["L0_ffn_gate_w"].reshape(8, 128, 8).transpose(1, 0, 2)),
        )
        m.update(consts)
        per_core.append(m)
    return per_core


DT_MAP = {np.dtype(np.float32): F32, np.dtype(BF): BF16,
          np.dtype(np.int32): I32, np.dtype(np.int16): I16}


def build(sample_map, tap_names=(), phase=3, n_cores=NC):
    nc = bass.Bass("TRN2", target_bir_lowering=False, num_devices=NC)
    ins = {k: nc.dram_tensor(k, list(v.shape), DT_MAP[v.dtype],
                             kind="ExternalInput").ap()
           for k, v in sample_map.items()}
    out = nc.dram_tensor("out", [TLOC, D], F32, kind="ExternalOutput").ap()

    es = ExitStack()
    with es:
        P = Prog(nc)
        P.alloc_sems(es)

        def sbf(name, shape, dt=F32):
            return es.enter_context(nc.sbuf_tensor(name, shape, dt))

        def psf(name, shape, dt=F32):
            return es.enter_context(nc.psum_tensor(name, shape, dt))

        tap_outs = []

        def tap(name, ap, tok, shape, dt):
            if name not in tap_names:
                return tok
            t = nc.dram_tensor(f"tap_{name}", list(shape), dt,
                               kind="ExternalOutput").ap()
            tap_outs.append(name)

            def f(eng, inc):
                inc(eng.dma_start(out=t, in_=ap))
            return P.step("sync", f, after=[tok], dma=1, tag=f"tap_{name}")

        build_body(nc, es, P, ins, out, sbf, psf, tap, phase,
                   n_cores=n_cores)
        block = es.enter_context(nc.Block())
        P.emit(block)
    return nc, tap_outs


def build_body(nc, es, P, ins, out, sbf, psf, tap, phase, n_cores=NC):
    x1_tm = sbf("g_x1_tm", [128, 32, D], BF16)
    qb_s = sbf("g_qb", [128, 8])
    kb_s = sbf("g_kb", [128, 2])
    PS = [psf(f"PSb{i}", [128, 512]) for i in range(8)]
    t0 = P.step("gpsimd", lambda e: None, tag="t0")

    t_lsh = emit_lsh(nc, es, P, ins, sbf, PS, tap, x1_tm, t0)

    if phase == 1:
        ob = emit_lsh.outstage[:, 0:2, :]
        tok = t_lsh
        stok = t_lsh
        for i in range(16):
            def cvt(eng, i=i):
                eng.tensor_copy(ob[:], x1_tm[:, 2 * i:2 * i + 2, :])
            tok = P.step("vector", cvt, after=[tok, stok], tag=f"cvt{i}")

            def st(eng, inc, i=i):
                inc(eng.dma_start(
                    out=out.rearrange("(b p) f -> p b f", p=128)[:, 2 * i:2 * i + 2],
                    in_=ob[:]))
            stok = P.step("sync", st, after=[tok], dma=1, tag=f"st{i}")
        P.final_wait("sync")
        return

    L = dict(emit_lsh.ctx)
    L["qb_s"] = qb_s
    L["kb_s"] = kb_s
    if phase >= 3:
        pre = emit_moe_pre(nc, P, L, t_lsh, n_cores=n_cores)
    t_gqa = emit_gqa(nc, es, P, ins, L, tap, x1_tm, t_lsh,
                              n_cores=n_cores)
    H = emit_gqa.handles
    if phase >= 3:
        t_moe = emit_moe(nc, es, P, ins, L, H, pre, tap, x1_tm,
                                  out, t_gqa, n_cores=n_cores)
        P.final_wait("sync")
        return

    if phase == 2:
        R1 = L["R1"]
        xb = R1[:, 4096:4608].bitcast(BF16)
        xf = R1[:, 0:1024]
        tok = t_gqa
        stok = t_gqa
        ltok = t_gqa
        for i in range(32):
            def ld(eng, inc, i=i):
                inc(eng.dma_start(
                    out=xb[:],
                    in_=H["x2loc"].ap()[i // 16].rearrange(
                        "(b p) f -> p b f", p=128)[:, i % 16, :]))
            ltok = P.step("sync", ld, after=[tok, t_gqa], dma=1, tag=f"x2l{i}")

            def cvt(eng, i=i):
                eng.tensor_copy(xf[:], xb[:])
            tok = P.step("vector", cvt, after=[ltok, stok], tag=f"x2c{i}")

            def st(eng, inc, i=i):
                inc(eng.dma_start(
                    out=out.rearrange("(b p) f -> p b f", p=128)[:, i, :],
                    in_=xf[:]))
            stok = P.step("sync", st, after=[tok], dma=1, tag=f"x2s{i}")

        def wcoll(eng):
            eng.wait_ge(H["sem_coll"], H["n_coll"])
        P.step("gpsimd", wcoll, after=[stok], tag="wcoll")
        P.final_wait("sync")
        return


def kernel(**inputs):
    per_core = prep_inputs(inputs)
    nc, _ = build(per_core[0])
    lower_extended_insts(nc)
    res = run_bass_kernel_spmd(nc, per_core, core_ids=list(range(NC)))
    outs = [np.asarray(res.results[c]["out"]) for c in range(NC)]
    return np.concatenate(outs, axis=0).reshape(16, 2048, 1024)


def kernel_taps(inputs, tap_names=(), on_hw=True, n_cores=NC):
    per_core = prep_inputs(inputs, n_cores=n_cores)
    nc, tnames = build(per_core[0], tap_names=tap_names, n_cores=n_cores)
    lower_extended_insts(nc)
    if on_hw:
        res = run_bass_kernel_spmd(nc, per_core[:n_cores],
                                   core_ids=list(range(n_cores)))
        results = [{k: np.asarray(v) for k, v in r.items()} for r in res.results]
    else:
        from simrun import simrun
        results = simrun(nc, per_core[:n_cores],
                         ["out"] + [f"tap_{n}" for n in tnames], num_cores=n_cores)
    return results, tnames




F32 = mybir.dt.float32
BF16 = mybir.dt.bfloat16
I32 = mybir.dt.int32
AX = mybir.AxisListType
OP = mybir.AluOpType
AF = mybir.ActivationFunctionType

S = 2048
D = 1024
NH = 16
DH = 64
NR = 4
NT = 16
NSlot = NR * S
QKV_W = 256
OUT_W = 80


def bfv(region, off_f32, n_bf16):
    return region[:, off_f32:off_f32 + n_bf16 // 2].bitcast(BF16)


def emit_lsh(nc, es, P, ins, sb, ps, tap, x1_tm, t_start):
    rot = sb("lsh_rot", [64, NH, 128], BF16)
    tri = sb("c_tri", [128, 128], BF16)
    tri64 = sb("c_tri64", [64, 64], F32)
    ones1 = sb("c_ones1", [1, 128], F32)
    iota256 = sb("c_iota256", [128, 256], F32)
    iota32x4 = sb("c_iota32x4", [128, 128], F32)
    big128 = sb("c_big128", [128, 128], F32)
    nround_off = sb("c_nroff", [1, 256], F32)
    ident = sb("c_ident", [128, 128], BF16)
    identf = sb("c_identf", [128, 128], F32)
    ones_col = sb("c_onescol", [128, 1], BF16)

    def ld_consts(eng, inc):
        inc(eng.dma_start(out=rot[:], in_=ins["rot_all"].rearrange("h d r -> d h r")))
        inc(eng.dma_start(out=tri[:], in_=ins["tri_incl"]))
        inc(eng.dma_start(out=tri64[:], in_=ins["tri_excl64"]))
        inc(eng.dma_start(out=ones1[:], in_=ins["ones_1x128f"]))
        inc(eng.dma_start(out=iota256[:], in_=ins["iota256_row"]))
        inc(eng.dma_start(out=iota32x4[:], in_=ins["iota32x4_row"]))
        inc(eng.dma_start(out=big128[:], in_=ins["big128"]))
        inc(eng.dma_start(out=nround_off[:], in_=ins["nround_off"]))
        inc(eng.dma_start(out=ident[:], in_=ins["ident"]))
        inc(eng.dma_start(out=identf[:], in_=ins["identf"]))
        inc(eng.dma_start(out=ones_col[:], in_=ins["ones_col_bf"]))
    t_lc = P.step("gpsimd", ld_consts, dma=11, tag="lsh_consts")

    lshsort = nc.dram_tensor("lshsort", [NSlot, QKV_W], BF16).ap()
    lshout = nc.dram_tensor("lshout", [NSlot, OUT_W], BF16).ap()
    rowb = nc.dram_tensor("rowbounce", [512], F32).ap()
    qk_hbm = nc.dram_tensor("qk_hbm", [S, D], BF16).ap()
    v_hbm = nc.dram_tensor("v_hbm", [S, D], BF16).ap()

    R1 = sb("lsh_R1", [128, 8192], F32)
    R2 = sb("lsh_R2", [128, 4096], F32)
    R3 = sb("lsh_R3", [128, 4224], F32)
    R4 = sb("lsh_R4", [128, 2560], F32)
    R5 = sb("lsh_R5", [128, 2560], F32)

    xfm = bfv(R1, 0, 16384).rearrange("p (c t) -> p c t", c=8)
    qFS = bfv(R1, 0, NSlot)[0:64]
    kFS = bfv(R1, 4096, NSlot)[0:64]
    xres = bfv(R1, 0, 4096).rearrange("p (a f) -> p a f", f=D)
    outstage = R1[:].rearrange("p (a f) -> p a f", f=D)

    wqkt = bfv(R2, 0, 8192).rearrange("p (c j) -> p c j", c=8)
    qkS = bfv(R2, 0, 8192).rearrange("p (b w) -> p b w", w=128)
    uns = bfv(R2, 0, 64 * OUT_W).rearrange("p (b w) -> p b w", w=OUT_W)

    wvt = bfv(R3, 0, 8192).rearrange("p (c j) -> p c j", c=8)
    vvb = bfv(R3, 0, 64 * 65).rearrange("p (b c) -> p b c", c=65)
    vvb2 = bfv(R3, 2080, 64 * 65).rearrange("p (b c) -> p b c", c=65)
    prefbuf = R3[:, 0:4096].rearrange("p (t k) -> p t k", k=256)
    scrA = R3[:, 0:1152].rearrange("p (a k) -> p a k", a=4)
    woutt = bfv(R3, 0, 8192).rearrange("p (c j) -> p c j", c=8)

    sortout = bfv(R4, 0, 64 * OUT_W).rearrange("p (b w) -> p b w", w=OUT_W)
    scrAi = R3[:, 0:1152].bitcast(I32).rearrange("p (a k) -> p a k", a=4)
    trash = R4[:, 0:1024]
    O4 = bfv(R4, 0, NT * 256).rearrange("p (t k) -> p t k", k=256)
    exclrep = R4[:, 2048:2304]

    stg = bfv(R5, 0, NT * QKV_W).rearrange("p (t w) -> p t w", w=QKV_W)
    qkFMh = bfv(R5, 0, NT * 128).rearrange("p (t w) -> p t w", w=128)[0:64]
    rsum = R5[:, 0:NT * OUT_W].rearrange("p (t w) -> p t w", w=OUT_W)
    attnh = R5[:, NT * OUT_W:NT * OUT_W + NT * 64].rearrange("p (t w) -> p t w", w=64)
    rec = R5[:, NT * OUT_W + NT * 64:NT * OUT_W + NT * 64 + NT].rearrange(
        "p (t o) -> p t o", o=1)

    attnFM = sb("lsh_attnFM", [128, 8, S], BF16)
    sqt = sb("lsh_sqt", [128, D], F32)
    S2 = sb("lsh_S2", [128, NT, NH], F32)
    rsb = sb("lsh_rs", [128, NT, NH], F32)
    negC = sb("lsh_negC", [128, NH], F32)
    scr16 = sb("lsh_scr16", [16, 128], F32)
    scr1 = sb("lsh_scr1", [1, 256], F32)
    lnm = sb("lsh_lnm", [128, NT, 8], F32)
    bounce = sb("lsh_bounce", [128, 512], BF16)
    bounce2 = sb("lsh_bounce2", [128, 512], BF16)
    qkh = sb("lsh_qkh", [128, NT, 64], BF16)
    kh = sb("lsh_kh", [128, NT, 64], BF16)
    vh = sb("lsh_vh", [128, NT, 64], BF16)
    buckets = sb("lsh_bkt", [128, NT, NR], F32)
    amaxf = sb("lsh_amaxf", [128, 32], F32)
    amaxi = sb("lsh_amaxi", [128, 16], I32)
    runtot = sb("lsh_runtot", [1, 256], F32)
    cnt64 = sb("lsh_cnt64", [64, 4], F32)
    pos4 = sb("lsh_pos4", [128, NT, NR], F32)
    pos4i = sb("lsh_pos4i", [128, NT * NR], I32)
    probs = bfv(R2, 3072, 512)

    PS = ps
    p_dots = PS[0][:]
    p_cnt = PS[5][0:1, 0:256]
    p_pv = PS[4][:, 0:160]
    p_misc = PS[1][:]
    p_b = [PS[2][:], PS[3][:]]

    p_misc_bf = p_misc.bitcast(BF16)
    T = dict(p_misc_bf=p_misc_bf, rot=rot, tri=tri, tri64=tri64, ones1=ones1, iota256=iota256,
             iota32x4=iota32x4, big128=big128, nround_off=nround_off, ident=ident,
             negC=negC, lshsort=lshsort, lshout=lshout, rowb=rowb, identf=identf,
             attnFM=attnFM, qkh=qkh, kh=kh, vh=vh, qkFMh=qkFMh,
             buckets=buckets, scrA=scrA, scrAi=scrAi, amaxf=amaxf, amaxi=amaxi,
             O4=O4, prefbuf=prefbuf,
             runtot=runtot, exclrep=exclrep, cnt64=cnt64, pos4=pos4,
             pos4i=pos4i, stg=stg, qkS=qkS, qFS=qFS, kFS=kFS, vvb=vvb, vvb2=vvb2, probs=probs,
             sortout=sortout, uns=uns, rsum=rsum, rec=rec, attnh=attnh,
             p_dots=p_dots, p_pv=p_pv, p_misc=p_misc, p_cnt=p_cnt,
             ones_col=ones_col, rsb=rsb,
             qk_hbm=qk_hbm, v_hbm=v_hbm)

    lsem = dict(d=es.enter_context(nc.semaphore("lsh_fd")),
                e=es.enter_context(nc.semaphore("lsh_fe")),
                p=es.enter_context(nc.semaphore("lsh_fp")),
                v=es.enter_context(nc.semaphore("lsh_fv")),
                cd=[0], ce=[0], cp=[0], cv=[0])
    T["lsem"] = lsem
    T["PS"] = PS

    def initR5(eng):
        eng.memset(R5[:], 0.0)
    t_i5 = P.step("vector", initR5, tag="lshi5")
    prev = P.step("gpsimd", lambda e: None, after=[t_lc, t_start, t_i5], tag="lsh0")
    emit_lsh.outstage = outstage
    emit_lsh.ctx = dict(R1=R1, R2=R2, R3=R3, R4=R4, R5=R5, PS=PS,
                        ones1=ones1, ident=ident, identf=identf,
                        ones_col=ones_col, attnFM=attnFM, tri=tri, rowb=rowb,
                        iota256=iota256, big128=big128)
    NHEADS = 16
    NBATCH = 2
    for b in range(NBATCH):
        prev = emit_batch_prep(nc, P, ins, b, xfm, wqkt, wvt, qk_hbm, v_hbm,
                               bounce, bounce2, trash, sqt, S2, rsb, negC,
                               scr16, scr1, ones1, identf, p_b, prev, tap)
        for h in range(NHEADS):
            prev = emit_head(nc, P, b, h, prev, T, tap)
        if True:
            prev = emit_batch_tail(nc, P, b, ins, x1_tm, attnFM, woutt, xres,
                                   sqt, trash, lnm, p_b, prev, tap)
    return prev


def emit_batch_prep(nc, P, ins, b, xfm, wqkt, wvt, qk_hbm, v_hbm, bounce,
                    bounce2, trash, sqt, S2, rsb, negC, scr16, scr1, ones1,
                    identf, p_b, prev, tap):
    S_ = S

    def ld(eng, inc):
        inc(eng.dma_start(
            out=xfm[:],
            in_=ins["x_fm"].rearrange("(c p) t -> p c t", p=128)[:, :, b * S_:(b + 1) * S_]))
        inc(eng.dma_start(out=wqkt[:],
                          in_=ins["w_qk_t"].rearrange("(c p) j -> p c j", p=128)))
        inc(eng.dma_start(out=wvt[:],
                          in_=ins["w_v_t"].rearrange("(c p) j -> p c j", p=128)))
    t_ld1 = P.step("gpsimd", ld, after=[prev], dma=3, tag=f"b{b}ldx")
    t_ld2 = t_ld1

    toks = []
    sq_toks = []
    st_toks = []
    for wi, (wt, hbm) in enumerate(((wqkt, qk_hbm), (wvt, v_hbm))):
        for ti in range(NT):
            for jh in range(2):
                pp = p_b[(ti * 2 + jh) % 2]
                bb = bounce if jh == 0 else bounce2

                def mm(eng, ti=ti, jh=jh, wt=wt, pp=pp):
                    for c in range(8):
                        eng.matmul(pp[:], xfm[:, c, ti * 128:(ti + 1) * 128],
                                   wt[:, c, jh * 512:(jh + 1) * 512],
                                   start=(c == 0), stop=(c == 7))
                t_mm = P.step("tensor", mm,
                              after=[t_ld1, t_ld2] + (toks[-2:-1] if len(toks) >= 2 else []),
                              tag=f"b{b}pj{wi}.{ti}.{jh}")

                def ev(eng, ti=ti, jh=jh, pp=pp, bb=bb):
                    eng.copy(out=bb[:], in_=pp[:])
                aft = [t_mm] + (st_toks[-2:-1] if len(st_toks) >= 2 else [])
                t_ev = P.step("scalar", ev, after=aft, tag=f"b{b}pe{wi}.{ti}.{jh}")
                toks.append(t_ev)

                if wi == 0:
                    def sq(eng, ti=ti, jh=jh, bb=bb):
                        eng.activation(out=trash[:, 0:512], in_=bb[:], func=AF.Square)
                    t_sq = P.step("scalar", sq,
                                  after=[t_ev] + (sq_toks[-1:] if sq_toks else []),
                                  tag=f"b{b}sq{ti}.{jh}")

                    def red(eng, ti=ti, jh=jh):
                        eng.tensor_reduce(
                            S2[:, ti, jh * 8:(jh + 1) * 8],
                            trash[:, 0:512].rearrange("p (h d) -> p h d", d=DH),
                            axis=AX.X, op=OP.add)
                    sq_toks.append(P.step("vector", red, after=[t_sq],
                                          tag=f"b{b}rd{ti}.{jh}"))

                def st(eng, inc, ti=ti, jh=jh, hbm=hbm, bb=bb):
                    inc(eng.dma_start(
                        out=hbm.rearrange("(bb p) j -> p bb j", p=128)
                        [:, ti, jh * 512:(jh + 1) * 512],
                        in_=bb[:]))
                extra = [sq_toks[-1]] if wi == 0 else []
                st_toks.append(P.step("sync", st, after=[t_ev] + extra, dma=1,
                                      tag=f"b{b}ps{wi}.{ti}.{jh}"))
    t_proj = P.step("sync", lambda e: None, after=st_toks[-2:] + sq_toks[-1:],
                    tag=f"b{b}pjd")

    def rs_fn(eng):
        eng.tensor_scalar(rsb[:], S2[:], 1.0 / DH, 1e-6, op0=OP.mult, op1=OP.add)
        eng.drain()
        eng.reciprocal(rsb[:], rsb[:])
    t_rs0 = P.step("vector", rs_fn, after=[t_proj], tag=f"b{b}rs0")

    def rs_sqrt(eng):
        eng.activation(out=rsb[:], in_=rsb[:], func=AF.Sqrt, scale=1.0 / DH)
    t_rs = P.step("scalar", rs_sqrt, after=[t_rs0], tag=f"b{b}rs1")

    def cmax(eng):
        eng.tensor_reduce(negC[:, 0:NH], S2[:].rearrange("p t h -> p h t"),
                          axis=AX.X, op=OP.max)
    t_cm = P.step("vector", cmax, after=[t_proj], tag=f"b{b}cm")

    def ctp(eng):
        eng.transpose(p_b[0][0:NH, 0:128], negC[:, 0:NH], identf[:])
    t_ct = P.step("tensor", ctp, after=[t_cm], tag=f"b{b}ct")

    def cred(eng):
        eng.tensor_reduce(scr16[0:NH, 0:1], p_b[0][0:NH, 0:128], axis=AX.X, op=OP.max)
    t_cr = P.step("vector", cred, after=[t_ct], tag=f"b{b}cr")

    def csqrt(eng):
        eng.activation(out=scr16[0:NH, 1:2], in_=scr16[0:NH, 0:1], func=AF.Sqrt)
    t_cs = P.step("scalar", csqrt, after=[t_cr], tag=f"b{b}cs")

    def ctp2(eng):
        eng.transpose(p_b[0][0:1, 0:NH], scr16[0:NH, 1:2], identf[0:NH, 0:NH])
    t_ct2 = P.step("tensor", ctp2, after=[t_cs, t_cr], tag=f"b{b}ct2")

    def cneg(eng):
        eng.tensor_scalar(scr1[0:1, 0:NH], p_b[0][0:1, 0:NH], -1.0, None, op0=OP.mult)
    t_cn = P.step("vector", cneg, after=[t_ct2], tag=f"b{b}cn")

    def crep(eng):
        eng.matmul(p_b[1][:, 0:NH], ones1[:], scr1[0:1, 0:NH], start=True, stop=True)
    t_crp = P.step("tensor", crep, after=[t_cn], tag=f"b{b}crp")

    def cev(eng):
        eng.tensor_copy(negC[:], p_b[1][:, 0:NH])
    t_c = P.step("vector", cev, after=[t_crp], tag=f"b{b}cev")

    return P.step("vector", lambda e: None, after=[t_rs, t_c], tag=f"b{b}prep")


def emit_head(nc, P, b, h, t_chain, T, tap):
    HSTG = "Z"
    def ldh(eng, inc):
        inc(eng.dma_start(
            out=T["qkh"][:],
            in_=T["qk_hbm"].rearrange("(bb p) j -> p bb j", p=128)
            [:, :, h * 64:(h + 1) * 64]))
        inc(eng.dma_start(
            out=T["vh"][:],
            in_=T["v_hbm"].rearrange("(bb p) j -> p bb j", p=128)
            [:, :, h * 64:(h + 1) * 64]))
    tok = P.step("sync", ldh, after=[t_chain], dma=2, tag=f"H{b}.{h}ld")

    def khf(eng):
        eng.tensor_tensor(
            T["kh"][:], T["qkh"][:],
            T["rsb"][:, :, h:h + 1].to_broadcast([128, NT, DH]), op=OP.mult)
    t_kh = P.step("vector", khf, after=[tok], tag=f"H{h}kh")
    if HSTG == "A":
        return t_kh

    for g in range(4):
        def tp(eng, g=g):
            for q in range(4):
                ti = g * 4 + q
                eng.transpose(T["p_misc_bf"][0:64, 128 * q:128 * (q + 1)],
                              T["qkh"][:, ti, :], T["ident"][:])
        t_tp = P.step("tensor", tp, after=[tok], tag=f"H{h}tp{g}")

        def ev(eng, g=g):
            eng.copy(
                out=T["qkFMh"][0:64, g * 4:(g + 1) * 4, :].rearrange("p a t -> p (a t)"),
                in_=T["p_misc_bf"][0:64, 0:512])
        tok = P.step("scalar", ev, after=[t_tp], tag=f"H{h}te{g}")

    for g in range(4):
        def rv(eng, g=g):
            for q in range(4):
                ti = g * 4 + q
                eng.matmul(T["p_misc"][:, 128 * q:128 * (q + 1)],
                           T["qkFMh"][0:64, ti, :], T["rot"][:, h, :],
                           start=True, stop=True)
        t_rv = P.step("tensor", rv, after=[tok], tag=f"H{h}rv{g}")

        def amax(eng, g=g):
            v = T["p_misc"][:].rearrange("p (a n r) -> p a n r", n=NR, r=32)
            A = T["scrA"]
            Ai = T["scrAi"]
            mp = A[:, :, 0:4]
            mn = A[:, :, 4:8]
            idxp = T["amaxf"][:, 0:16].rearrange("p (a k) -> p a k", a=4)
            idxn = T["amaxf"][:, 16:32].rearrange("p (a k) -> p a k", a=4)
            usep = T["amaxi"][:].rearrange("p (a k) -> p a k", a=4)
            cand4 = A[:, :, 32:160].rearrange("p a (n r) -> p a n r", r=32)
            cand = A[:, :, 32:160]
            maski4 = Ai[:, :, 160:288].rearrange("p a (n r) -> p a n r", r=32)
            maski = Ai[:, :, 160:288]
            eng.tensor_reduce(mp, v, axis=AX.X, op=OP.max)
            eng.tensor_reduce(mn, v, axis=AX.X, op=OP.min)
            eng.drain()
            bc = [128, 4, NR, 32]
            bcf = [128, 4, 128]
            iota = T["iota32x4"][:].rearrange("p (a r) -> p a r", a=1).to_broadcast(bcf)
            bigc = T["big128"][:].rearrange("p (a r) -> p a r", a=1).to_broadcast(bcf)
            eng.tensor_tensor(maski4, v, mp.rearrange("p a (n o) -> p a n o", o=1)
                              .to_broadcast(bc), op=OP.is_ge)
            eng.drain()
            eng.select(cand, maski, iota, bigc, add_drain=True)
            eng.drain()
            eng.tensor_reduce(idxp, cand4, axis=AX.X, op=OP.min)
            eng.drain()
            eng.tensor_tensor(maski4, v, mn.rearrange("p a (n o) -> p a n o", o=1)
                              .to_broadcast(bc), op=OP.is_le)
            eng.drain()
            eng.select(cand, maski, iota, bigc, add_drain=True)
            eng.drain()
            eng.tensor_reduce(idxn, cand4, axis=AX.X, op=OP.min)
            eng.tensor_tensor(mp, mp, mn, op=OP.add)
            eng.drain()
            eng.tensor_tensor(usep, mp, mn, op=OP.is_ge) if False else None
            eng.tensor_scalar(usep, mp, 0.0, None, op0=OP.is_ge)
            eng.tensor_scalar(idxn, idxn, 32.0, None, op0=OP.add)
            eng.drain()
            eng.select(T["buckets"][:, g * 4:(g + 1) * 4, :], usep, idxp, idxn, add_drain=True)
        tok = P.step("vector", amax, after=[t_rv], tag=f"H{h}am{g}")

    if HSTG == "B":
        return tok

    def boff(eng):
        eng.tensor_scalar(T["scrA"][:, 0:1, 0:NR], T["iota256"][0:128, 0:NR],
                          64.0, None, op0=OP.mult)
        eng.drain()
        eng.tensor_tensor(T["buckets"][:], T["buckets"][:],
                          T["scrA"][:, 0:1, 0:NR].to_broadcast([128, NT, NR]),
                          op=OP.add)
    tok = P.step("vector", boff, after=[tok], tag=f"H{h}bo")

    def onehot(eng):
        eng.tensor_tensor(
            T["O4"][:].rearrange("p t (n k) -> p t n k", k=64),
            T["iota256"][:].rearrange("p (t nk) -> p t nk", t=1)
            .rearrange("p t (n k) -> p t n k", k=64)
            .to_broadcast([128, NT, NR, 64]),
            T["buckets"][:].rearrange("p t (n o) -> p t n o", o=1)
            .to_broadcast([128, NT, NR, 64]),
            op=OP.is_equal)
    tok = P.step("vector", onehot, after=[tok], tag=f"H{h}oh")
    tok = tap(f"buckets_{b}_{h}", T["buckets"][:], tok, [128, NT, NR], F32)

    def rt0(eng):
        eng.memset(T["runtot"][:], 0.0)
    tok = P.step("vector", rt0, after=[tok], tag=f"H{h}rt0")

    t_ev_prev = tok
    for ti in range(NT):
        def mmp(eng, ti=ti):
            eng.matmul(T["p_dots"][:, 0:256], T["tri"][:], T["O4"][:, ti, :],
                       start=True, stop=False)
            eng.matmul(T["p_dots"][:, 0:256], T["ones1"][:], T["runtot"][:],
                       start=False, stop=True)
            eng.matmul(T["p_cnt"][:], T["ones_col"][:], T["O4"][:, ti, :],
                       start=True, stop=True)
        t_mm = P.step("tensor", mmp, after=[t_ev_prev], tag=f"H{h}pf{ti}")

        def evp(eng, ti=ti):
            eng.tensor_copy(T["prefbuf"][:, ti, :], T["p_dots"][:, 0:256])
            eng.drain()
            eng.tensor_tensor(T["runtot"][:], T["runtot"][:], T["p_cnt"][:],
                              op=OP.add)
        t_ev_prev = P.step("vector", evp, after=[t_mm], tag=f"H{h}pe{ti}")
    tok = t_ev_prev

    if HSTG == "C":
        return tok

    def rl1(eng, inc):
        inc(eng.dma_start(out=T["rowb"][0:256], in_=T["runtot"][0:1, :]))
    tok = P.step("gpsimd", rl1, after=[tok], dma=1, tag=f"H{h}r1")

    def rl2(eng, inc):
        inc(eng.dma_start(out=T["cnt64"][:],
                          in_=T["rowb"][0:256].rearrange("(n k) -> k n", k=64),
                      allow_slow_non_contiguous=True))
    tok = P.step("gpsimd", rl2, after=[tok], dma=1, tag=f"H{h}r2")

    def mme(eng):
        eng.matmul(T["p_misc"][0:64, 0:4], T["tri64"][:], T["cnt64"][:],
                   start=True, stop=True)
    tok = P.step("tensor", mme, after=[tok], tag=f"H{h}me")

    def eve(eng):
        eng.tensor_copy(T["cnt64"][:], T["p_misc"][0:64, 0:4])
    tok = P.step("vector", eve, after=[tok], tag=f"H{h}ee")

    def rl3(eng, inc):
        inc(eng.dma_start(out=T["rowb"][0:256].rearrange("(n k) -> k n", k=64),
                          in_=T["cnt64"][:], allow_slow_non_contiguous=True))
    tok = P.step("gpsimd", rl3, after=[tok], dma=1, tag=f"H{h}r3")

    def rl4(eng, inc):
        inc(eng.dma_start(out=T["runtot"][0:1, :], in_=T["rowb"][0:256]))
    tok = P.step("gpsimd", rl4, after=[tok], dma=1, tag=f"H{h}r4")

    def addoff(eng):
        eng.tensor_tensor(T["runtot"][:], T["runtot"][:], T["nround_off"][:],
                          op=OP.add)
    tok = P.step("vector", addoff, after=[tok], tag=f"H{h}ao")

    def repl(eng):
        eng.matmul(T["p_misc"][:, 0:256], T["ones1"][:], T["runtot"][:],
                   start=True, stop=True)
    tok = P.step("tensor", repl, after=[tok], tag=f"H{h}rp")

    def evr(eng):
        eng.tensor_copy(T["exclrep"][:], T["p_misc"][:, 0:256])
    tok = P.step("vector", evr, after=[tok], tag=f"H{h}er")

    def posf(eng):
        eng.tensor_tensor(T["prefbuf"][:], T["prefbuf"][:],
                          T["exclrep"][:].rearrange("p (t k) -> p t k", t=1)
                          .to_broadcast([128, NT, 256]), op=OP.add)
        eng.drain()
        eng.tensor_tensor(T["prefbuf"][:], T["prefbuf"][:], T["O4"][:], op=OP.mult)
        eng.drain()
        eng.tensor_reduce(T["pos4"][:],
                          T["prefbuf"][:].rearrange("p t (n k) -> p t n k", k=64),
                          axis=AX.X, op=OP.add)
        eng.drain()
        eng.tensor_copy(T["pos4i"][:], T["pos4"][:].rearrange("p t n -> p (t n)"))
    t_pos = P.step("vector", posf, after=[tok], tag=f"H{h}po")
    t_pos = tap(f"pos4_{b}_{h}", T["pos4"][:], t_pos, [128, NT, NR], F32)

    if HSTG == "D":
        return t_pos

    def stage(eng):
        st = T["stg"][:].rearrange("p t (q d) -> p t q d", d=64)
        eng.tensor_copy(st[:, :, 0, :], T["qkh"][:])
        eng.tensor_copy(st[:, :, 1, :], T["kh"][:])
        eng.tensor_copy(st[:, :, 2, :], T["vh"][:])
        eng.memset(st[:, :, 3, :], 0.0)
    t_stage = P.step("vector", stage, after=[t_pos, t_kh], tag=f"H{h}sg")

    def scat(eng, inc):
        for ti in range(NT):
            for n in range(NR):
                inc(eng.indirect_dma_start(
                    T["lshsort"],
                    bass.IndirectOffsetOnAxis(
                        T["pos4i"][:, ti * NR + n:ti * NR + n + 1], 0),
                    T["stg"][:, ti, :], None))
    t_scat = P.step("gpsimd", scat, after=[t_stage], dma=64, tag=f"H{h}sc")

    if HSTG == "E":
        return t_scat

    def rb(eng, inc):
        src = T["lshsort"].rearrange("(bb p) w -> p bb w", p=128)
        inc(eng.dma_start(out=T["qkS"][:, 0:32, :], in_=src[:, 0:32, 0:128]))
        inc(eng.dma_start(out=T["qkS"][:, 32:64, :], in_=src[:, 32:64, 0:128]))
        inc(eng.dma_start(out=T["vvb"][:, :, 0:64], in_=src[:, :, 128:192]))
        inc(eng.dma_start(out=T["vvb2"][64:128, :, 0:64], in_=src[0:64, :, 128:192]))
        inc(eng.dma_start(out=T["vvb2"][0:64, 1:64, 0:64],
                          in_=src[64:128, 0:63, 128:192]))
        inc(eng.dma_start(out=T["vvb2"][0:64, 0:1, 0:64],
                          in_=src[64:128, 63:64, 128:192]))
    t_rb = P.step("sync", rb, after=[t_scat], dma=6, tag=f"H{h}rb")

    def vones(eng):
        eng.memset(T["vvb"][:, :, 64:65], 1.0)
        eng.memset(T["vvb2"][:, :, 64:65], 1.0)
    t_rb = P.step("vector", vones, after=[t_rb], tag=f"H{h}vo")

    tok = t_rb
    for g in range(16):
        def tps(eng, g=g):
            for q in range(4):
                ti = g * 4 + q
                eng.transpose(T["p_misc_bf"][:, 128 * q:128 * (q + 1)],
                              T["qkS"][:, ti, :], T["ident"][:])
        t_t = P.step("tensor", tps, after=[tok], tag=f"H{h}st{g}")

        def tse(eng, g=g):
            eng.copy(out=T["qFS"][:, g * 512:(g + 1) * 512],
                     in_=T["p_misc_bf"][0:64, 0:512])
            eng.copy(out=T["kFS"][:, g * 512:(g + 1) * 512],
                     in_=T["p_misc_bf"][64:128, 0:512])
        tok = P.step("scalar", tse, after=[t_t], tag=f"H{h}se{g}")
    t_sorted = tok

    if HSTG == "F":
        return t_sorted
    L = T["lsem"]
    sld, sle, slp, slv = L["d"], L["e"], L["p"], L["v"]
    base_d, base_e = L["cd"][0], L["ce"][0]
    base_p, base_v = L["cp"][0], L["cv"][0]
    PSp = T["PS"]

    def pvslot(s):
        if s < 4:
            return PSp[4][:, 65 * s:65 * s + 65]
        return PSp[5][:, 65 * (s - 4):65 * (s - 4) + 65]

    def g_pe(eng, h=h, base_d=base_d, base_e=base_e, base_p=base_p,
             base_v=base_v):
        def emit_pv(g):
            last = None
            for q in range(8):
                c = g * 8 + q
                po = 64 * (c % 2)
                s = (c // 2) % 8
                if c % 2 == 1:
                    vsrc = T["vvb"][:, (c - 1) // 2, 0:65]
                else:
                    vsrc = T["vvb2"][:, c // 2, 0:65]
                sl = pvslot(s)
                last = eng.matmul(sl[po:po + 64, :],
                                  T["probs"][:, 64 * q:64 * q + 64],
                                  vsrc, start=True, stop=True,
                                  tile_position=(0, po))
            last.then_inc(slp, 1)
        for g in range(16):
            if base_e + g > 0:
                eng.wait_ge(sle, base_e + g)
            last = None
            for q in range(8):
                c = g * 8 + q
                rhs = T["qFS"][:, 64 * c:64 * c + 64]
                if c % 2 == 0:
                    lo = (64 * c - 64) % NSlot
                    eng.matmul(T["p_dots"][0:64, 64 * q:64 * q + 64],
                               T["kFS"][:, lo:lo + 64], rhs,
                               start=True, stop=True, tile_position=(0, 0))
                    last = eng.matmul(
                        T["p_dots"][64:128, 64 * q:64 * q + 64],
                        T["kFS"][:, 64 * c:64 * c + 64], rhs,
                        start=True, stop=True, tile_position=(0, 64))
                else:
                    last = eng.matmul(
                        T["p_dots"][:, 64 * q:64 * q + 64],
                        T["kFS"][:, 64 * c - 64:64 * c + 64],
                        rhs, start=True, stop=True)
            last.then_inc(sld, 1)
            if g >= 1:
                k = g - 1
                if base_v + k - 1 > 0:
                    eng.wait_ge(slv, base_v + k - 1)
                emit_pv(k)
        eng.wait_ge(sle, base_e + 16)
        if base_v + 14 > 0:
            eng.wait_ge(slv, base_v + 14)
        emit_pv(15)
    t_gpe = P.step("tensor", g_pe, after=[t_sorted], tag=f"H{h}GPE")
    L["cd"][0] += 16
    L["cp"][0] += 16

    def g_act(eng, h=h, base_d=base_d, base_p=base_p):
        for g in range(16):
            eng.wait_ge(sld, base_d + g + 1)
            if g >= 1:
                eng.wait_ge(slp, base_p + g)
            eng.activation(out=T["probs"][:], in_=T["p_dots"][:],
                           func=AF.Exp, bias=T["negC"][:, h:h + 1],
                           scale=1.0).then_inc(sle, 1)
    t_gact = P.step("scalar", g_act, after=[t_sorted], tag=f"H{h}GAX")
    L["ce"][0] += 16

    def g_dve(eng, base_p=base_p):
        for g in range(16):
            eng.wait_ge(slp, base_p + g + 1)
            last = None
            for i in range(4):
                blk = 4 * g + i
                s = blk % 8
                last = eng.tensor_copy(T["sortout"][:, blk, 0:65],
                                       pvslot(s))
            last.then_inc(slv, 1)
    t_pv_done = P.step("vector", g_dve, after=[t_sorted], tag=f"H{h}GDV")
    L["cv"][0] += 16
    t_gfin = (t_gpe, t_gact)

    def wout(eng, inc):
        dst = T["lshout"].rearrange("(bb p) w -> p bb w", p=128)
        inc(eng.dma_start(out=dst[:, 0:32, :], in_=T["sortout"][:, 0:32, :]))
        inc(eng.dma_start(out=dst[:, 32:64, :], in_=T["sortout"][:, 32:64, :]))
    t_wo = P.step("sync", wout, after=[t_pv_done], dma=2, tag=f"H{h}wo")

    def unsort(eng, inc):
        for ti in range(NT):
            for n in range(NR):
                inc(eng.indirect_dma_start(
                    T["uns"][:, n * NT + ti, :], None,
                    T["lshout"],
                    bass.IndirectOffsetOnAxis(
                        T["pos4i"][:, ti * NR + n:ti * NR + n + 1], 0)))
    t_un = P.step("gpsimd", unsort, after=[t_wo], dma=64, tag=f"H{h}un")

    if HSTG == "H":
        return t_un

    def comb(eng):
        u = T["uns"]
        eng.tensor_tensor(T["rsum"][:], u[:, 0:16, :], u[:, 16:32, :], op=OP.add)
        eng.drain()
        eng.tensor_tensor(T["rsum"][:], T["rsum"][:], u[:, 32:48, :], op=OP.add)
        eng.drain()
        eng.tensor_tensor(T["rsum"][:], T["rsum"][:], u[:, 48:64, :], op=OP.add)
        eng.drain()
        eng.reciprocal(T["rec"][:], T["rsum"][:, :, 64:65])
        eng.drain()
        eng.tensor_tensor(T["attnh"][:], T["rsum"][:, :, 0:64],
                          T["rec"][:].to_broadcast([128, NT, 64]), op=OP.mult)
    t_cb = P.step("vector", comb, after=[t_un], tag=f"H{h}cb")
    t_cb = tap(f"attnh_{b}_{h}", T["attnh"][:], t_cb, [128, NT, 64], F32)

    tok = t_cb
    for g in range(4):
        def tpa(eng, g=g):
            for q in range(4):
                ti = g * 4 + q
                eng.transpose(T["p_misc"][0:64, 128 * q:128 * (q + 1)],
                              T["attnh"][:, ti, :], T["identf"][:])
        t_t = P.step("tensor", tpa, after=[tok], tag=f"H{h}at{g}")

        def tae(eng, g=g):
            eng.copy(
                out=T["attnFM"][(h % 2) * 64:(h % 2) * 64 + 64, h // 2,
                                g * 512:(g + 1) * 512],
                in_=T["p_misc"][0:64, 0:512])
        tok = P.step("scalar", tae, after=[t_t], tag=f"H{h}ae{g}")
    return tok


def emit_batch_tail(nc, P, b, ins, x1_tm, attnFM, woutt, xres, sqt, trash,
                    lnm, p_b, prev, tap):
    def ldw(eng, inc):
        inc(eng.dma_start(out=woutt[:],
                          in_=ins["w_out_t"].rearrange("(c p) j -> p c j", p=128)))
    t_w = P.step("gpsimd", ldw, after=[prev], dma=1, tag=f"b{b}tw")

    toks = []
    ln_toks = []
    t_x = None
    for ti in range(NT):
        if ti % 4 == 0:
            def ldr(eng, inc, ti=ti):
                inc(eng.dma_start(
                    out=xres[:],
                    in_=ins["x_tm"].rearrange("(bb p) f -> p bb f", p=128)
                    [:, b * NT + ti:b * NT + ti + 4, :]))
            t_x = P.step("gpsimd", ldr,
                         after=[prev] + (ln_toks[-1:] if ln_toks else []),
                         dma=1, tag=f"b{b}tx{ti}")
        for jh in range(2):
            pp = p_b[(ti * 2 + jh) % 2]

            def mm(eng, ti=ti, jh=jh, pp=pp):
                for c in range(8):
                    eng.matmul(pp[:], attnFM[:, c, ti * 128:(ti + 1) * 128],
                               woutt[:, c, jh * 512:(jh + 1) * 512],
                               start=(c == 0), stop=(c == 7))
            t_mm = P.step("tensor", mm,
                          after=[prev, t_w] + (toks[-2:-1] if len(toks) >= 2 else []),
                          tag=f"b{b}tm{ti}.{jh}")

            def ev(eng, ti=ti, jh=jh, pp=pp):
                eng.tensor_tensor(
                    sqt[:, jh * 512:(jh + 1) * 512], pp[:],
                    xres[:, ti % 4, jh * 512:(jh + 1) * 512], op=OP.add)
            toks.append(P.step("vector", ev,
                               after=[t_mm, t_x] + (ln_toks[-1:] if ln_toks else []),
                               tag=f"b{b}tv{ti}.{jh}"))

        def lnmean(eng, ti=ti):
            eng.tensor_reduce(lnm[:, ti, 0:1], sqt[:], axis=AX.X, op=OP.add)
        t_m = P.step("vector", lnmean, after=toks[-2:], tag=f"b{b}lm{ti}")

        def lnsq(eng, ti=ti):
            eng.activation(out=trash[:, 0:1024], in_=sqt[:], func=AF.Square,
                           accum_out=lnm[:, ti, 1:2])
        t_s = P.step("scalar", lnsq, after=toks[-2:], tag=f"b{b}ls{ti}")

        def lnst(eng, ti=ti):
            L = lnm[:, ti, :]
            eng.tensor_scalar(L[:, 2:3], L[:, 0:1], 1.0 / D, None, op0=OP.mult)
            eng.drain()
            eng.tensor_tensor(L[:, 3:4], L[:, 2:3], L[:, 2:3], op=OP.mult)
            eng.tensor_scalar(L[:, 5:6], L[:, 1:2], 1.0 / D, None, op0=OP.mult)
            eng.drain()
            eng.tensor_tensor(L[:, 3:4], L[:, 5:6], L[:, 3:4], op=OP.subtract)
            eng.drain()
            eng.tensor_scalar(L[:, 3:4], L[:, 3:4], 1e-5, None, op0=OP.add)
            eng.drain()
            eng.reciprocal(L[:, 3:4], L[:, 3:4])
        t_st = P.step("vector", lnst, after=[t_m, t_s], tag=f"b{b}lt{ti}")

        def lnrs(eng, ti=ti):
            eng.activation(out=lnm[:, ti, 3:4], in_=lnm[:, ti, 3:4], func=AF.Sqrt)
        t_rs = P.step("scalar", lnrs, after=[t_st], tag=f"b{b}lr{ti}")

        def lnap(eng, ti=ti):
            eng.tensor_scalar(
                x1_tm[:, b * NT + ti, :], sqt[:],
                lnm[:, ti, 2:3], lnm[:, ti, 3:4],
                op0=OP.subtract, op1=OP.mult)
        ln_toks.append(P.step("vector", lnap, after=[t_rs], tag=f"b{b}la{ti}"))
    return P.step("vector", lambda e: None, after=ln_toks[-1:], tag=f"b{b}tl")



F32 = mybir.dt.float32
BF16 = mybir.dt.bfloat16
I32 = mybir.dt.int32
AX = mybir.AxisListType
OP = mybir.AluOpType
AF = mybir.ActivationFunctionType

S = 2048
D = 1024
NH = 16
DH = 64
NKV = 4


def bfv(region, off_f32, n_bf16):
    return region[:, off_f32:off_f32 + n_bf16 // 2].bitcast(BF16)


def emit_gqa(nc, es, P, ins, L, tap, x1_tm, prev, n_cores=8):
    R1, R2, R3, R4, R5 = L["R1"], L["R2"], L["R3"], L["R4"], L["R5"]
    PS = L["PS"]
    ones1, ident, identf = L["ones1"], L["ident"], L["identf"]
    ones_col = L["ones_col"]
    iota256, big128 = L["iota256"], L["big128"]

    qfm = nc.dram_tensor("g_qfm", [1024, S], BF16).ap()
    kfm = nc.dram_tensor("g_kfm", [256, S], BF16).ap()
    vtm = nc.dram_tensor("g_vtm", [S, 256], BF16).ap()
    ofm = nc.dram_tensor("g_ofm", [1024, S], BF16).ap()
    x2loc = nc.dram_tensor("g_x2loc", [2, S, D], BF16)
    rt_in = nc.dram_tensor("g_rtin", [2 * S, 4], F32)
    byex = nc.dram_tensor("g_byex", [2, n_cores * S, D], BF16,
                          addr_space="Shared")
    rt_all = nc.dram_tensor("g_rtall", [n_cores * 2 * S, 4], F32,
                            addr_space="Shared")
    groups = [list(range(n_cores))]
    sem_coll = es.enter_context(nc.semaphore("sem_coll"))
    coll_n = [0]
    fs = dict(st=es.enter_context(nc.semaphore("gs_fst")),
              ex=es.enter_context(nc.semaphore("gs_fex")),
              pv=es.enter_context(nc.semaphore("gs_fpv")),
              qn=es.enter_context(nc.semaphore("gs_fqn")),
              cst=[0], cex=[0], cpv=[0], cqn=[0])

    x1FMw = bfv(R2, 0, 4096).rearrange("p (c t) -> p c t", c=8)
    kwS = bfv(R2, 2048, 2048).rearrange("p (c j) -> p c j", c=8)
    vwS = bfv(R2, 3072, 2048).rearrange("p (c j) -> p c j", c=8)
    qwS = bfv(R3, 0, 8192).rearrange("p (c j) -> p c j", c=8)
    owS = bfv(R3, 0, 8192).rearrange("p (c j) -> p c j", c=8)
    qv = [bfv(R1, 0, 2048)[0:65], bfv(R1, 1024, 2048)[0:65]]
    kv = [bfv(R1, 2048, 2048)[0:65], bfv(R1, 3072, 2048)[0:65]]
    vT = [bfv(R1, 4096, 1040).rearrange("p (b c) -> p b c", c=65),
          bfv(R1, 4616, 1040).rearrange("p (b c) -> p b c", c=65)]
    OFMp = [bfv(R1, 5136, 2048), bfv(R1, 6160, 2048)]
    sqb = L["attnFM"][:].rearrange("p a t -> p (a t)")[0:64, 0:2048]
    scr_rc = R2[0:1, 0:512]
    scrq2 = R2[0:1, 512:1024]
    scrq_a = R2[0:1, 1024:1536]
    maxs = R2[0:1, 1536:1544]
    rbc = R1[0:64, 7184:7696]
    OallS = bfv(R1, 0, 16384).rearrange("p (c t) -> p c t", c=8)

    a_f32 = R4[:, 0:1024]
    x2fin = R4[:, 1024:2048]
    xfmT = [R4[:, 2048:2176], R4[:, 2176:2304]]
    s2a = R4[:, 2304:2305]
    s2b = R4[:, 2305:2306]
    s2 = R4[:, 2306:2307]
    rstd = R4[:, 2307:2308]
    epscol = R4[:, 2308:2309]
    obS = R1[0:1, 0:1024]
    n1wS = R1[0:1, 1024:2048]
    vbS = R1[0:1, 2048:2304]

    obrep = bfv(R5, 0, 1024)
    n1wrep = bfv(R5, 512, 1024)
    vbrep = bfv(R5, 1024, 256)
    gwS = R5[:, 1152:1216].rearrange("p (c e) -> p c e", e=8)
    zTM = R5[:, 1216:1344].rearrange("p (b e) -> p b e", e=8)
    bounce = [bfv(R5, 1344, 512), bfv(R5, 1600, 512)]
    m1all = R5[:, 1856:1872]
    m2all = R5[:, 1872:1888]
    e1all = R5[:, 1888:1904]
    e2all = R5[:, 1904:1920]
    w1all = R5[:, 1920:1936]
    w2all = R5[:, 1936:1952]
    wsum = R5[:, 1952:1968]
    neg30 = R5[:, 1968:1976]
    msk8i = R5[:, 1976:1984].bitcast(I32)
    cnd8 = R5[:, 1984:1992]
    rtblk = R5[:, 1992:2056].rearrange("p (b c) -> p b c", c=4)
    probsT = [bfv(R5, 1344, 512), bfv(R5, 1600, 512)]

    qb_s = L["qb_s"]; kb_s = L["kb_s"]

    p_n = PS[5][0:1, 0:512]
    tpbf = PS[1][:].bitcast(BF16)

    def ld_small(eng, inc):
        inc(eng.dma_start(out=qb_s[:], in_=ins["qb_fm"]))
        inc(eng.dma_start(out=kb_s[:], in_=ins["kb_fm"]))
        inc(eng.dma_start(out=obS, in_=ins["ob"]))
        inc(eng.dma_start(out=n1wS, in_=ins["norm1_w"]))
        inc(eng.dma_start(out=vbS, in_=ins["vb"]))
        inc(eng.dma_start(out=gwS[:], in_=ins["gate_w_fm"]))
    t_ls = P.step("sync", ld_small, after=[prev], dma=6, tag="gq_ls")

    def reps_mm(eng):
        eng.matmul(PS[0][:, 0:512], ones1[0:1, 0:128], obS[0:1, 0:512],
                   start=True, stop=True)
        eng.matmul(PS[2][:, 0:512], ones1[0:1, 0:128], obS[0:1, 512:1024],
                   start=True, stop=True)
        eng.matmul(PS[3][:, 0:512], ones1[0:1, 0:128], n1wS[0:1, 0:512],
                   start=True, stop=True)
        eng.matmul(PS[4][:, 0:512], ones1[0:1, 0:128], n1wS[0:1, 512:1024],
                   start=True, stop=True)
        eng.matmul(PS[6][:, 0:256], ones1[0:1, 0:128], vbS[0:1, 0:256],
                   start=True, stop=True)
    t_rm = P.step("tensor", reps_mm, after=[t_ls, prev], tag="gq_rm")

    def reps_ev(eng):
        eng.tensor_copy(obrep[:, 0:512], PS[0][:, 0:512])
        eng.tensor_copy(obrep[:, 512:1024], PS[2][:, 0:512])
        eng.tensor_copy(n1wrep[:, 0:512], PS[3][:, 0:512])
        eng.tensor_copy(n1wrep[:, 512:1024], PS[4][:, 0:512])
        eng.tensor_copy(vbrep[:], PS[6][:, 0:256])
        eng.memset(epscol, 1e-6)
        eng.memset(neg30, -1e30)
    t_rep = P.step("vector", reps_ev, after=[t_rm], tag="gq_rep")

    prev = P.step("tensor", lambda e: None, after=[t_rep, prev], tag="gq_st")

    xdma_toks = []
    rt_toks = []
    for ex in range(2):
        prev = emit_gp(nc, P, ins, ex, x1_tm, x1FMw, qwS, kwS, vwS, qb_s,
                       kb_s, vbrep, bounce, qfm, kfm, vtm, PS, tpbf, ident,
                       prev)
        prev = emit_gs(nc, P, ex, qv, kv, vT, sqb, OFMp, probsT, scr_rc,
                       scrq2, scrq_a, maxs, rbc, qfm, kfm, vtm, ofm, PS, p_n,
                       ones1, ones_col, fs, prev)
        prev, t_xd = emit_go(nc, P, ins, ex, x1_tm, OallS, owS, ofm, obrep,
                             n1wrep, gwS, a_f32, x2fin, xfmT, s2a, s2b, s2,
                             rstd, epscol, zTM, x2loc, PS, identf, prev)
        xdma_toks.append(t_xd)

        def ag(eng, ex=ex):
            eng.collective_compute(
                "AllGather", OP.bypass, replica_groups=groups,
                ins=[x2loc.ap()[ex]], outs=[byex.ap()[ex]],
            ).then_inc(sem_coll, 1)
        coll_n[0] += 1
        P.step("gpsimd", ag, after=[t_xd], tag=f"gq_ag{ex}")

        prev, t_rt = emit_gating(nc, P, ex, zTM, m1all, m2all, e1all, e2all,
                                 w1all, w2all, wsum, neg30, msk8i, cnd8,
                                 rtblk, iota256, big128, rt_in, tap, prev)
        rt_toks.append(t_rt)

    def agr(eng):
        eng.collective_compute(
            "AllGather", OP.bypass, replica_groups=groups,
            ins=[rt_in.ap()], outs=[rt_all.ap()]).then_inc(sem_coll, 1)
    coll_n[0] += 1
    P.step("gpsimd", agr, after=rt_toks, tag="gq_agr")

    emit_gqa.handles = dict(byex=byex, rt_all=rt_all, x2loc=x2loc,
                            sem_coll=sem_coll, n_coll=coll_n[0])
    return prev


def emit_gp(nc, P, ins, ex, x1_tm, x1FMw, qwS, kwS, vwS, qb_s, kb_s, vbrep,
            bounce, qfm, kfm, vtm, PS, tpbf, ident, prev):
    B0 = 16 * ex

    def ldw(eng, inc):
        inc(eng.dma_start(out=qwS[:],
                          in_=ins["qw"].rearrange("(c p) j -> p c j", p=128)))
        inc(eng.dma_start(out=kwS[:],
                          in_=ins["kw"].rearrange("(c p) j -> p c j", p=128)))
        inc(eng.dma_start(out=vwS[:],
                          in_=ins["vw"].rearrange("(c p) j -> p c j", p=128)))
    t_w = P.step("gpsimd", ldw, after=[prev], dma=3, tag=f"gp{ex}w")
    t_wq = t_w

    qdst = qfm.rearrange("(pr p) t -> p pr t", p=128)
    kdst = kfm.rearrange("(pr p) t -> p pr t", p=128)
    vdst = vtm.rearrange("(blk p) v -> p blk v", p=128)
    t_ev_prev = [t_w, t_w]
    t_dma_prev = t_w
    for w in range(4):
        t_tp_ev = []
        for b4 in range(4):
            def tpp(eng, w=w, b4=b4):
                for c in range(8):
                    eng.transpose(tpbf[:, 128 * c:128 * (c + 1)],
                                  x1_tm[:, B0 + w * 4 + b4,
                                        128 * c:128 * (c + 1)], ident[:])
            t_t = P.step("tensor", tpp,
                         after=[prev] + t_tp_ev[-1:], tag=f"gp{ex}t{w}.{b4}")

            def tpe(eng, w=w, b4=b4):
                for c in range(8):
                    eng.tensor_copy(x1FMw[:, c, 128 * b4:128 * (b4 + 1)],
                                    tpbf[:, 128 * c:128 * (c + 1)])
            t_tp_ev.append(P.step("vector", tpe, after=[t_t],
                                  tag=f"gp{ex}e{w}.{b4}"))
        t_x1w = t_tp_ev[-1]

        jobs = []
        for pr in range(8):
            jobs.append(("q", pr))
        jobs += [("k", 0), ("k", 1)]
        jobs += [("v", 0), ("v", 1), ("v", 2), ("v", 3)]
        for ji, (kind, idx) in enumerate(jobs):
            bank = PS[[0, 2][ji % 2]]
            bb = bounce[ji % 2]

            def mm(eng, kind=kind, idx=idx, bank=bank, w=w):
                for c in range(8):
                    if kind == "q":
                        eng.matmul(bank[:, 0:512],
                                   qwS[:, c, 128 * idx:128 * (idx + 1)],
                                   x1FMw[:, c, :], start=(c == 0),
                                   stop=(c == 7))
                    elif kind == "k":
                        eng.matmul(bank[:, 0:512],
                                   kwS[:, c, 128 * idx:128 * (idx + 1)],
                                   x1FMw[:, c, :], start=(c == 0),
                                   stop=(c == 7))
                    else:
                        eng.matmul(bank[:, 0:256],
                                   x1FMw[:, c, 128 * idx:128 * (idx + 1)],
                                   vwS[:, c, :], start=(c == 0),
                                   stop=(c == 7))
            t_mm = P.step("tensor", mm,
                          after=[t_x1w, t_wq, t_w] + t_ev_prev[-2:-1],
                          tag=f"gp{ex}m{w}.{ji}")

            if kind == "v":
                def ev(eng, idx=idx, bank=bank, bb=bb):
                    eng.tensor_tensor(bb[:, 0:256], bank[:, 0:256],
                                      vbrep[:], op=OP.add)
            elif kind == "q":
                def ev(eng, idx=idx, bank=bank, bb=bb):
                    eng.tensor_scalar(bb[:, 0:512], bank[:, 0:512],
                                      qb_s[:, idx:idx + 1], None, op0=OP.add)
            else:
                def ev(eng, idx=idx, bank=bank, bb=bb):
                    eng.tensor_scalar(bb[:, 0:512], bank[:, 0:512],
                                      kb_s[:, idx:idx + 1], None, op0=OP.add)
            eng_name = "vector"
            t_ev = P.step(eng_name, ev, after=[t_mm, t_dma_prev],
                          tag=f"gp{ex}v{w}.{ji}")
            t_ev_prev.append(t_ev)

            def st(eng, inc, kind=kind, idx=idx, bb=bb, w=w):
                if kind == "q":
                    inc(eng.dma_start(out=qdst[:, idx, 512 * w:512 * (w + 1)],
                                      in_=bb[:, 0:512]))
                elif kind == "k":
                    inc(eng.dma_start(out=kdst[:, idx, 512 * w:512 * (w + 1)],
                                      in_=bb[:, 0:512]))
                else:
                    inc(eng.dma_start(out=vdst[:, w * 4 + idx, :],
                                      in_=bb[:, 0:256]))
            t_dma_prev = P.step("sync", st, after=[t_ev], dma=1,
                                tag=f"gp{ex}s{w}.{ji}")
        prev = t_dma_prev
    return P.step("tensor", lambda e: None, after=[prev], tag=f"gp{ex}fin")


def emit_gs(nc, P, ex, qv, kv, vT, sqb, OFMp, probsT, scr_rc, scrq2, scrq_a,
            maxs, rbc, qfm, kfm, vtm, ofm, PS, p_n, ones1, ones_col, fs,
            prev):
    odst = ofm.rearrange("(pr p) t -> p pr t", p=128)
    vsrc = vtm.rearrange("(blk p) v -> p blk v", p=128)
    sst, sex, spv = fs["st"], fs["ex"], fs["pv"]
    cst, cex, cpv = fs["cst"], fs["cex"], fs["cpv"]
    last_use_q = {0: prev, 1: prev}
    last_use_k = {0: prev, 1: prev}
    t_onorm = [prev, prev]
    t_ofm_dma = prev
    t_qsq_free = prev
    t_pe_last = prev
    pending = None

    for g in range(4):
        gs = g % 2

        def ldk(eng, inc, g=g, gs=gs):
            inc(eng.dma_start(out=kv[gs][0:64, :],
                              in_=kfm[64 * g:64 * (g + 1), :]))
            inc(eng.dma_start(
                out=vT[gs][:, :, 0:64],
                in_=vsrc[:, :, 64 * g:64 * (g + 1)]))
        t_lk = P.step("sync", ldk, after=[last_use_k[gs], prev], dma=2,
                      tag=f"gs{ex}lk{g}")

        def kones(eng, gs=gs):
            eng.memset(kv[gs][64:65, :], 1.0)
            eng.memset(vT[gs][:, :, 64:65], 1.0)
        t_ko = P.step("vector", kones, after=[t_lk], tag=f"gs{ex}ko{g}")

        def ksq(eng, gs=gs):
            eng.activation(out=sqb[:], in_=kv[gs][0:64, :], func=AF.Square)
        t_kq = P.step("scalar", ksq, after=[t_ko, t_qsq_free],
                      tag=f"gs{ex}kq{g}")

        st0, pv0 = cst[0], cpv[0]

        def nmm(eng, st0=st0, pv0=pv0):
            for w4 in range(4):
                if w4 >= 1:
                    eng.wait_ge(spv, pv0 + w4)
                eng.matmul(p_n, ones_col[0:64, :],
                           sqb[:, 512 * w4:512 * (w4 + 1)],
                           start=True, stop=True).then_inc(sst, 1)
        t_nm = P.step("tensor", nmm, after=[t_kq, t_pe_last],
                      tag=f"gs{ex}km{g}")
        cst[0] += 4
        t_pe_last = t_nm

        def nmx(eng, st0=st0):
            for w4 in range(4):
                eng.wait_ge(sst, st0 + w4 + 1)
                eng.tensor_reduce(maxs[0:1, w4:w4 + 1], p_n, axis=AX.X,
                                  op=OP.max).then_inc(spv, 1)
        t_mx = P.step("vector", nmx, after=[t_kq], tag=f"gs{ex}kx{g}")
        cpv[0] += 4

        def kmax(eng):
            eng.tensor_reduce(maxs[0:1, 4:5], maxs[0:1, 0:4], axis=AX.X,
                              op=OP.max)
        t_km = P.step("vector", kmax, after=[t_mx], tag=f"gs{ex}kf{g}")

        def ksqrt(eng):
            eng.activation(out=maxs[0:1, 5:6], in_=maxs[0:1, 4:5],
                           func=AF.Sqrt)
        t_maxk = P.step("scalar", ksqrt, after=[t_km], tag=f"gs{ex}ks{g}")

        for hh in range(4):
            h = 4 * g + hh
            hs = h % 2

            def ldq(eng, inc, h=h, hs=hs):
                inc(eng.dma_start(out=qv[hs][0:64, :],
                                  in_=qfm[64 * h:64 * (h + 1), :]))
            t_lq = P.step("sync", ldq, after=[last_use_q[hs], prev], dma=1,
                          tag=f"gs{ex}lq{h}")

            def qsq(eng, hs=hs):
                eng.activation(out=sqb[:], in_=qv[hs][0:64, :],
                               func=AF.Square)
            t_qq = P.step("scalar", qsq, after=[t_lq, t_maxk, t_qsq_free],
                          tag=f"gs{ex}qq{h}")

            st0, ex0, pv0 = cst[0], cex[0], cpv[0]

            def qmm(eng, st0=st0, ex0=ex0):
                for w4 in range(4):
                    if w4 >= 1:
                        eng.wait_ge(sex, ex0 + w4)
                    eng.matmul(p_n, ones_col[0:64, :],
                               sqb[:, 512 * w4:512 * (w4 + 1)],
                               start=True, stop=True).then_inc(sst, 1)
            t_qm = P.step("tensor", qmm, after=[t_qq, t_pe_last],
                          tag=f"gs{ex}qm{h}")
            cst[0] += 4
            t_pe_last = t_qm
            t_qsq_free = t_qm

            def qsr(eng, st0=st0, pv0=pv0):
                for w4 in range(4):
                    eng.wait_ge(sst, st0 + w4 + 1)
                    if w4 >= 1:
                        eng.wait_ge(spv, pv0 + w4)
                    eng.activation(out=scrq_a, in_=p_n,
                                   func=AF.Sqrt).then_inc(sex, 1)
            t_qs = P.step("scalar", qsr, after=[t_qq], tag=f"gs{ex}qs{h}")
            cex[0] += 4

            def qev(eng, ex0=ex0, hs=hs):
                for w4 in range(4):
                    eng.wait_ge(sex, ex0 + w4 + 1)
                    eng.tensor_scalar(
                        qv[hs][64:65, 512 * w4:512 * (w4 + 1)], scrq_a,
                        maxs[0:1, 5:6], -1.0, op0=OP.mult,
                        op1=OP.mult).then_inc(spv, 1)
            t_qn = P.step("vector", qev, after=[t_qq], tag=f"gs{ex}qe{h}")
            cpv[0] += 4

            for w in range(4):
                pvb = PS[2 + w % 2]
                st0, ex0, pv0 = cst[0], cex[0], cpv[0]

                def pemm(eng, w=w, hs=hs, gs=gs, ex0=ex0, pvb=pvb):
                    for k2 in range(8):
                        if ex0 + 2 * k2 > 0:
                            eng.wait_ge(sex, ex0 + 2 * k2)
                        for j in range(2):
                            blk = 2 * k2 + j
                            eng.matmul(PS[j][:, 0:512],
                                       kv[gs][:, 128 * blk:128 * (blk + 1)],
                                       qv[hs][:, 512 * w:512 * (w + 1)],
                                       start=True, stop=True
                                       ).then_inc(sst, 1)
                        if k2 >= 1:
                            for j in range(2):
                                blk = 2 * (k2 - 1) + j
                                eng.matmul(pvb[0:65, 0:512],
                                           vT[gs][:, blk, :], probsT[j][:],
                                           start=(blk == 0), stop=False
                                           ).then_inc(spv, 1)
                    eng.wait_ge(sex, ex0 + 16)
                    eng.matmul(pvb[0:65, 0:512], vT[gs][:, 14, :],
                               probsT[0][:], start=False, stop=False
                               ).then_inc(spv, 1)
                    eng.matmul(pvb[0:65, 0:512], vT[gs][:, 15, :],
                               probsT[1][:], start=False, stop=True
                               ).then_inc(spv, 1)
                t_pe = P.step("tensor", pemm,
                              after=[t_qn, t_ko, t_onorm[w % 2], t_pe_last],
                              tag=f"gs{ex}PE{h}.{w}")
                cst[0] += 16
                cpv[0] += 16
                t_pe_last = t_pe

                def actx(eng, st0=st0, pv0=pv0):
                    for blk in range(16):
                        eng.wait_ge(sst, st0 + blk + 1)
                        eng.wait_ge(spv, pv0 + blk - 1 if blk >= 2 else pv0)
                        eng.activation(out=probsT[blk % 2][:],
                                       in_=PS[blk % 2][:], func=AF.Exp,
                                       scale=0.125).then_inc(sex, 1)
                t_ax = P.step("scalar", actx, after=[t_qs],
                              tag=f"gs{ex}AX{h}.{w}")
                cex[0] += 16

                def emit_epi(pw, t_pe_pw):
                    nonlocal t_pe_last
                    pvbp = PS[2 + pw % 2]

                    def rcp(eng, pvbp=pvbp):
                        eng.reciprocal(scr_rc, pvbp[64:65, 0:512])
                    t_rc = P.step("vector", rcp, after=[t_pe_pw],
                                  tag=f"gs{ex}r{h}.{pw}")

                    def bcast(eng):
                        eng.matmul(PS[6][0:64, 0:512], ones1[0:1, 0:64],
                                   scr_rc, start=True, stop=True)
                    t_bc = P.step("tensor", bcast,
                                  after=[t_rc, t_onorm[(pw + 1) % 2],
                                         t_pe_last],
                                  tag=f"gs{ex}b{h}.{pw}")
                    t_pe_last = t_bc

                    def onorm(eng, h=h, pw=pw, pvbp=pvbp):
                        eng.tensor_copy(rbc[:], PS[6][0:64, 0:512])
                        eng.drain()
                        eng.tensor_tensor(
                            OFMp[(h // 2) % 2][64 * (h % 2):
                                               64 * (h % 2) + 64,
                                               512 * pw:512 * (pw + 1)],
                            pvbp[0:64, 0:512], rbc[:], op=OP.mult)
                    t_onorm[pw % 2] = P.step("vector", onorm,
                                             after=[t_bc, t_ofm_dma],
                                             tag=f"gs{ex}o{h}.{pw}")

                if pending is not None:
                    emit_epi(pending[0], pending[1])
                pending = (w, t_pe)
            emit_epi(pending[0], pending[1])
            pending = None
            last_use_q[hs] = t_pe
            last_use_k[gs] = t_pe
            if h % 2 == 1:
                def ofdma(eng, inc, h=h):
                    inc(eng.dma_start(out=odst[:, h // 2, :],
                                      in_=OFMp[(h // 2) % 2][:]))
                t_ofm_dma = P.step("sync", ofdma,
                                   after=[t_onorm[0], t_onorm[1]], dma=1,
                                   tag=f"gs{ex}od{h}")
    return P.step("tensor", lambda e: None,
                  after=[t_ofm_dma, t_onorm[0], t_onorm[1]],
                  tag=f"gs{ex}fin")


def emit_go(nc, P, ins, ex, x1_tm, OallS, owS, ofm, obrep, n1wrep, gwS,
            a_f32, x2fin, xfmT, s2a, s2b, s2, rstd, epscol, zTM, x2loc, PS,
            identf, prev):
    B0 = 16 * ex

    def ldo(eng, inc):
        inc(eng.dma_start(out=OallS[:],
                          in_=ofm.rearrange("(c p) t -> p c t", p=128)))
    t_lo1 = P.step("sync", ldo, after=[prev], dma=1, tag=f"go{ex}w")

    def ldow(eng, inc):
        inc(eng.dma_start(out=owS[:],
                          in_=ins["ow"].rearrange("(c p) j -> p c j",
                                                  p=128)))
    t_lo = P.step("gpsimd", ldow, after=[prev], dma=1, tag=f"go{ex}ow")

    t_ev = {0: t_lo, 1: t_lo}
    t_x2b = t_lo
    t_xd = t_lo
    t_pz = t_lo
    for blk in range(16):
        for half in range(2):
            def mm(eng, blk=blk, half=half):
                for c in range(8):
                    eng.matmul(PS[half][:, 0:512],
                               OallS[:, c, 128 * blk:128 * (blk + 1)],
                               owS[:, c, 512 * half:512 * (half + 1)],
                               start=(c == 0), stop=(c == 7))
            t_mm = P.step("tensor", mm, after=[t_lo1, t_lo, t_ev[half]],
                          tag=f"go{ex}m{blk}.{half}")

            def ev(eng, blk=blk, half=half):
                eng.tensor_tensor(a_f32[:, 512 * half:512 * (half + 1)],
                                  PS[half][:, 0:512],
                                  obrep[:, 512 * half:512 * (half + 1)],
                                  op=OP.add)
                eng.drain()
                eng.tensor_tensor(x2fin[:, 512 * half:512 * (half + 1)],
                                  a_f32[:, 512 * half:512 * (half + 1)],
                                  x1_tm[:, B0 + blk,
                                        512 * half:512 * (half + 1)],
                                  op=OP.add)
            t_ev[half] = P.step("vector", ev, after=[t_mm, t_x2b],
                                tag=f"go{ex}e{blk}.{half}")

        def sq(eng):
            eng.activation(out=a_f32[:, 0:512], in_=x2fin[:, 0:512],
                           func=AF.Square, accum_out=s2a)
            eng.drain()
            eng.activation(out=a_f32[:, 512:1024], in_=x2fin[:, 512:1024],
                           func=AF.Square, accum_out=s2b)
        t_sq = P.step("scalar", sq, after=[t_ev[0], t_ev[1]],
                      tag=f"go{ex}q{blk}")

        def st(eng):
            eng.tensor_tensor(s2, s2a, s2b, op=OP.add)
        t_s = P.step("vector", st, after=[t_sq], tag=f"go{ex}s{blk}")

        def rs1(eng):
            eng.tensor_scalar(rstd, s2, 1.0 / D, 1e-6, op0=OP.mult,
                              op1=OP.add)
            eng.drain()
            eng.reciprocal(rstd, rstd)
        t_r0 = P.step("vector", rs1, after=[t_s], tag=f"go{ex}r{blk}a")

        def rs2(eng):
            eng.activation(out=rstd, in_=rstd, func=AF.Sqrt)
        t_r = P.step("scalar", rs2, after=[t_r0], tag=f"go{ex}r{blk}")

        def x2b(eng):
            eng.tensor_scalar(a_f32[:], x2fin[:], rstd, None, op0=OP.mult)
            eng.drain()
            eng.tensor_tensor(x2fin[:], a_f32[:], n1wrep[:], op=OP.mult)
        t_x2b = P.step("vector", x2b, after=[t_r, t_xd], tag=f"go{ex}x{blk}")

        def xd(eng, inc, blk=blk, ex=ex):
            inc(eng.dma_start(
                out=x2loc.ap()[ex].rearrange("(b p) f -> p b f", p=128)
                [:, blk, :], in_=x2fin[:]))
        t_xd = P.step("gpsimd", xd, after=[t_x2b], dma=1, tag=f"go{ex}d{blk}")

        t_tp = t_x2b
        for cb in range(4):
            def tpp(eng, cb=cb):
                for j in range(2):
                    c = 2 * cb + j
                    eng.transpose(PS[4][:, 128 * j:128 * (j + 1)],
                                  x2fin[:, 128 * c:128 * (c + 1)], identf[:])
            t_t = P.step("tensor", tpp, after=[t_x2b, t_tp],
                         tag=f"go{ex}t{blk}.{cb}")

            def tpe(eng, cb=cb):
                for j in range(2):
                    eng.tensor_copy(xfmT[j][:], PS[4][:, 128 * j:128 * (j + 1)])
            t_e = P.step("vector", tpe, after=[t_t], tag=f"go{ex}f{blk}.{cb}")

            def gz(eng, cb=cb, blk=blk):
                for j in range(2):
                    c = 2 * cb + j
                    eng.matmul(PS[5][:, 0:8], xfmT[j][:], gwS[:, c, :],
                               start=(c == 0), stop=(c == 7))
            t_tp = P.step("tensor", gz, after=[t_e, t_pz],
                          tag=f"go{ex}g{blk}.{cb}")

        def pzev(eng, blk=blk):
            eng.tensor_copy(zTM[:, blk, :], PS[5][:, 0:8])
        t_pz = P.step("vector", pzev, after=[t_tp], tag=f"go{ex}z{blk}")
    t_fin = P.step("vector", lambda e: None, after=[t_pz, t_xd, t_x2b],
                   tag=f"go{ex}fin")
    return t_fin, t_xd


def emit_gating(nc, P, ex, zTM, m1all, m2all, e1all, e2all, w1all, w2all,
                wsum, neg30, msk8i, cnd8, rtblk, iota256, big128, rt_in,
                tap, prev):
    iota8 = iota256[:, 0:8]
    big8 = big128[:, 0:8]

    def m1r(eng):
        eng.tensor_reduce(m1all[:], zTM[:], axis=AX.X, op=OP.max)
    t = P.step("vector", m1r, after=[prev], tag=f"gt{ex}m1")

    msk8 = msk8i.bitcast(F32)
    for blk in range(16):
        def sel(eng, blk=blk):
            z = zTM[:, blk, :]
            eng.tensor_tensor(msk8[:], z, m1all[:, blk:blk + 1]
                              .to_broadcast([128, 8]), op=OP.is_ge)
            eng.drain()
            eng.tensor_scalar(msk8[:], msk8[:], -1e9, 1e9, op0=OP.mult,
                              op1=OP.add)
            eng.drain()
            eng.tensor_tensor(cnd8[:], msk8[:], iota8, op=OP.add)
            eng.drain()
            eng.tensor_reduce(e1all[:, blk:blk + 1], cnd8[:],
                              axis=AX.X, op=OP.min)
            eng.drain()
            eng.tensor_tensor(msk8[:], iota8, e1all[:, blk:blk + 1]
                              .to_broadcast([128, 8]), op=OP.is_equal)
            eng.drain()
            eng.tensor_scalar(msk8[:], msk8[:], -1e30, None, op0=OP.mult)
            eng.drain()
            eng.tensor_tensor(cnd8[:], z, msk8[:], op=OP.add)
            eng.drain()
            eng.tensor_reduce(m2all[:, blk:blk + 1], cnd8[:],
                              axis=AX.X, op=OP.max)
            eng.drain()
            eng.tensor_tensor(msk8[:], cnd8[:], m2all[:, blk:blk + 1]
                              .to_broadcast([128, 8]), op=OP.is_ge)
            eng.drain()
            eng.tensor_scalar(msk8[:], msk8[:], -1e9, 1e9, op0=OP.mult,
                              op1=OP.add)
            eng.drain()
            eng.tensor_tensor(cnd8[:], msk8[:], iota8, op=OP.add)
            eng.drain()
            eng.tensor_reduce(e2all[:, blk:blk + 1], cnd8[:],
                              axis=AX.X, op=OP.min)
            eng.drain()
        t = P.step("vector", sel, after=[t], tag=f"gt{ex}s{blk}")

    def sig(eng):
        eng.activation(out=m1all[:], in_=m1all[:], func=AF.Sigmoid)
        eng.drain()
        eng.activation(out=m2all[:], in_=m2all[:], func=AF.Sigmoid)
    t = P.step("scalar", sig, after=[t], tag=f"gt{ex}sg")

    def wts(eng):
        eng.tensor_tensor(wsum[:], m1all[:], m2all[:], op=OP.add)
        eng.drain()
        eng.tensor_scalar(wsum[:], wsum[:], 1e-6, None, op0=OP.add)
        eng.drain()
        eng.reciprocal(wsum[:], wsum[:])
        eng.drain()
        eng.tensor_tensor(w1all[:], m1all[:], wsum[:], op=OP.mult)
        eng.tensor_tensor(w2all[:], m2all[:], wsum[:], op=OP.mult)
    t = P.step("vector", wts, after=[t], tag=f"gt{ex}w")

    def asm(eng):
        eng.tensor_copy(rtblk[:, :, 0], e1all[:])
        eng.tensor_copy(rtblk[:, :, 1], e2all[:])
        eng.tensor_copy(rtblk[:, :, 2], w1all[:])
        eng.tensor_copy(rtblk[:, :, 3], w2all[:])
    t = P.step("vector", asm, after=[t], tag=f"gt{ex}a")
    t = tap(f"rt_{ex}", rtblk[:], t, [128, 16, 4], F32)

    def rtd(eng, inc, ex=ex):
        inc(eng.dma_start(
            out=rt_in.ap().rearrange("(b p) c -> p b c", p=128)
            [:, 16 * ex:16 * (ex + 1), :],
            in_=rtblk[:]))
    t_rt = P.step("sync", rtd, after=[t], dma=1, tag=f"gt{ex}d")
    return t, t_rt



F32 = mybir.dt.float32
BF16 = mybir.dt.bfloat16
I32 = mybir.dt.int32
AX = mybir.AxisListType
OP = mybir.AluOpType
AF = mybir.ActivationFunctionType

D = 1024
DFF = 2048
CAP = 10240


def bfv(region, off_f32, n_bf16):
    return region[:, off_f32:off_f32 + n_bf16 // 2].bitcast(BF16)


def emit_moe_pre(nc, P, L, t_start, n_cores=8):
    R5 = L["R5"]
    NTG = n_cores * 4096
    Z = nc.dram_tensor("m_Z", [NTG + 128, D], BF16)
    slotmap = nc.dram_tensor("m_slot", [CAP + 128, 4], F32)
    zeros = bfv(R5, 2056, 512)
    smfill = R5[:, 2320:2324]

    def zfill(eng):
        eng.memset(zeros[:], 0.0)
        eng.memset(smfill[:], 0.0)
        eng.drain()
        eng.memset(smfill[:, 1:2], float(NTG))
    t_zf = P.step("vector", zfill, after=[t_start], tag="mz0")

    zview = Z.ap().rearrange("(n p) d -> p n d", p=128)
    t_zz = t_zf
    for i in range(NTG // 128 + 1):
        def zz(eng, inc, i=i):
            inc(eng.dma_start(out=zview[:, i, 0:512], in_=zeros[:]))
            inc(eng.dma_start(out=zview[:, i, 512:1024], in_=zeros[:]))
        t_zz = P.step("scalar", zz, after=[t_zf, t_zz], dma=2, tag=f"mzz{i}")
    smview = slotmap.ap().rearrange("(n p) c -> p n c", p=128)
    t_sf = t_zf
    for i in range(CAP // 128 + 1):
        def sf(eng, inc, i=i):
            inc(eng.dma_start(out=smview[:, i, :], in_=smfill[:]))
        t_sf = P.step("scalar", sf, after=[t_zf, t_sf], dma=1, tag=f"msf{i}")
    return dict(Z=Z, slotmap=slotmap, t_zz=t_zz, t_sf=t_sf)


def emit_moe(nc, es, P, ins, L, H, pre, tap, x1_tm, out, prev, n_cores=8):
    R1, R2, R3, R4, R5 = L["R1"], L["R2"], L["R3"], L["R4"], L["R5"]
    PS = L["PS"]
    ones1, ident, identf = L["ones1"], L["ident"], L["identf"]
    ones_col, tri = L["ones_col"], L["tri"]
    rowb = L["rowb"]
    sem_coll = H["sem_coll"]
    byex, rt_all, x2loc = H["byex"], H["rt_all"], H["x2loc"]
    Z, slotmap = pre["Z"], pre["slotmap"]
    NTG = n_cores * 4096
    NBG = NTG // 128
    NC2 = n_cores * 2048
    nch = (NBG + 127) // 128
    groups = [list(range(n_cores))]

    moe_loc = nc.dram_tensor("m_out", [4096, D], BF16)
    x1_hbm = nc.dram_tensor("m_x1", [4096, D], BF16).ap()

    h1FM = bfv(R1, 0, 8192).rearrange("p (f t) -> p f t", f=16)
    xgFM = bfv(R1, 4096, 4096).rearrange("p (c t) -> p c t", c=8)
    xgTM = bfv(R1, 6144, 4096).rearrange("p (b f) -> p b f", b=4)
    rtS = R2[:, 0:1024].rearrange("p (b c) -> p b c", c=4)
    expid_rep = R2[:, 1024:1025]
    expid_st = R2[0:1, 1025:1026]
    colsS = R2[0:1, 1282:1538]
    chunkS = R2[0:1, 1538:1546]
    n2wS = R2[0:1, 2048:3072]
    coloffs = R2[0:1, 3072:3328]
    zstage = bfv(R3, 0, 4096).rearrange("p (b f) -> p b f", b=4)
    keepf = R3[:, 0:256]
    nkeep = R3[:, 256:512]
    sc1 = R3[:, 512:768]
    sc2 = R3[:, 768:1024]
    outstage = R3[:, 2048:3072]
    i1 = R4[:, 0:256]
    i2 = R4[:, 256:512]
    ind = R4[:, 512:768]
    indb = bfv(R4, 768, 256)
    colsb = bfv(R4, 896, 2)
    wsel = R4[:, 900:1156]
    rank = R4[:, 1156:1412]
    pay = R4[:, 1412:2436].rearrange("p (g c) -> p g c", c=4)
    dsti = R3[:, 1280:1536].bitcast(I32)
    iotok = R3[:, 1536:1792]
    colchf = R4[:, 2436:2444]
    excl2 = R4[:, 2444:2446]
    exadd = R4[:, 2446:2447]
    tA = R4[:, 0:1024]
    tB = R4[:, 1024:2048]
    slotS = R5[:, 0:16].rearrange("p (b c) -> p b c", c=4)
    slotI = R5[:, 16:20].bitcast(I32)
    scrA = R5[:, 20:24]
    scrB = R5[:, 24:28]
    scrC = R5[:, 28:32]
    gidxI = R5[:, 32:36].bitcast(I32)
    byexS = R3[:, 1024:1280]
    n2wrep = bfv(R5, 64, 1024)
    lnm = R5[:, 580:596].rearrange("p (b c) -> p b c", c=8)
    xflat = x1_tm[:].rearrange("p b f -> p (b f)")
    w1S = xflat[:, 0:16384].rearrange("p (c j) -> p c j", c=8)
    w2S = xflat[:, 16384:32768].rearrange("p (f j) -> p f j", f=16)
    xbm = bfv(R1, 0, 1024)
    xbx = bfv(R1, 512, 1024)
    xb1 = bfv(R1, 1024, 1024)

    def x1spill(eng, inc):
        inc(eng.dma_start(out=x1_hbm.rearrange("(b p) f -> p b f", p=128),
                          in_=x1_tm[:]))
    t_sp = P.step("sync", x1spill, after=[prev], dma=1, tag="mx1s")

    def wload(eng, inc):
        inc(eng.dma_start(out=w1S[:],
                          in_=ins["w1e"].rearrange("(c p) j -> p c j", p=128)))
        inc(eng.dma_start(out=w2S[:],
                          in_=ins["w2e"].rearrange("(f p) j -> p f j", p=128)))
    t_wl = P.step("gpsimd", wload, after=[t_sp], dma=2, tag="mwl")
    t_wl2 = t_wl

    def rtload(eng, inc):
        eng.wait_ge(sem_coll, 3)
        inc(eng.dma_start(out=rtS[:, 0:NBG, :],
                          in_=rt_all.ap().rearrange("(b p) c -> p b c",
                                                    p=128)))
        inc(eng.dma_start(out=n2wS, in_=ins["norm2_w"]))
        inc(eng.dma_start(out=expid_st, in_=ins["expid"]))
        inc(eng.dma_start(out=iotok[:, 0:NBG],
                          in_=ins["iota_tok"][:, 0:NBG]))
        inc(eng.dma_start(out=byexS[:, 0:NBG],
                          in_=ins["iota_byex"][:, 0:NBG]))
    t_rl = P.step("sync", rtload, after=[prev], dma=5, tag="mrl")

    def repid(eng):
        eng.matmul(PS[0][:, 0:1], ones1[0:1, 0:128], expid_st,
                   start=True, stop=True)
    t_ri = P.step("tensor", repid, after=[t_rl, prev], tag="mri")

    def repev(eng):
        eng.tensor_copy(expid_rep[:], PS[0][:, 0:1])
    t_re = P.step("vector", repev, after=[t_ri], tag="mre")

    def indf(eng):
        eng.tensor_scalar(i1[:, 0:NBG], rtS[:, 0:NBG, 0], expid_rep[:], None,
                          op0=OP.is_equal)
        eng.tensor_scalar(i2[:, 0:NBG], rtS[:, 0:NBG, 1], expid_rep[:], None,
                          op0=OP.is_equal)
        eng.drain()
        eng.tensor_tensor(ind[:, 0:NBG], i1[:, 0:NBG], i2[:, 0:NBG],
                          op=OP.max)
        eng.drain()
        eng.tensor_copy(indb[:, 0:NBG], ind[:, 0:NBG])
        eng.tensor_tensor(i1[:, 0:NBG], i1[:, 0:NBG], rtS[:, 0:NBG, 2],
                          op=OP.mult)
        eng.tensor_tensor(i2[:, 0:NBG], i2[:, 0:NBG], rtS[:, 0:NBG, 3],
                          op=OP.mult)
        eng.drain()
        eng.tensor_tensor(wsel[:, 0:NBG], i1[:, 0:NBG], i2[:, 0:NBG],
                          op=OP.add)
    t_if = P.step("vector", indf, after=[t_re], tag="mif")

    def pref(eng):
        eng.matmul(PS[0][:, 0:NBG], tri[:], indb[:, 0:NBG],
                   start=True, stop=True)
        eng.matmul(PS[1][0:1, 0:NBG], ones_col[:], indb[:, 0:NBG],
                   start=True, stop=True)
    t_pf = P.step("tensor", pref, after=[t_if], tag="mpf")

    def colev(eng):
        eng.memset(colsS[0:1, 0:256], 0.0)
        eng.drain()
        eng.tensor_copy(colsS[0:1, 0:NBG], PS[1][0:1, 0:NBG])
    t_ce = P.step("vector", colev, after=[t_pf], tag="mce")

    def cb1(eng, inc):
        inc(eng.dma_start(out=rowb[0:256], in_=colsS[0:1, 0:256]))
    t_b1 = P.step("sync", cb1, after=[t_ce], dma=1, tag="mb1")

    def cb2(eng, inc):
        inc(eng.dma_start(
            out=colchf[:, 0:2],
            in_=rowb[0:256].rearrange("(a p) -> p a", p=128),
            allow_slow_non_contiguous=True))
    t_b2 = P.step("gpsimd", cb2, after=[t_b1], dma=1, tag="mb2")

    def cbc(eng):
        eng.tensor_copy(colsb[:, 0:2], colchf[:, 0:2])
    t_bc = P.step("vector", cbc, after=[t_b2], tag="mbc")

    def excl(eng):
        eng.matmul(PS[2][:, 0:2], tri[:], colsb[:, 0:2],
                   start=True, stop=True)
        eng.matmul(PS[3][0:1, 0:2], ones_col[:], colsb[:, 0:2],
                   start=True, stop=True)
    t_ex = P.step("tensor", excl, after=[t_bc], tag="mex")

    def exev(eng):
        eng.tensor_tensor(excl2[:, 0:2], PS[2][:, 0:2],
                          colchf[:, 0:2], op=OP.subtract)
    t_exv = P.step("vector", exev, after=[t_ex], tag="mexv")

    def totev(eng):
        eng.tensor_copy(chunkS[0:1, 0:2], PS[3][0:1, 0:2])
    t_tv = P.step("vector", totev, after=[t_ex], tag="mtv")

    def totmm(eng):
        eng.matmul(PS[4][:, 0:1], ones1[0:1, 0:128], chunkS[0:1, 0:1],
                   start=True, stop=True)
    t_tm = P.step("tensor", totmm, after=[t_tv], tag="mtm")

    def totadd(eng):
        eng.tensor_copy(exadd[:], PS[4][:, 0:1])
        eng.drain()
        eng.tensor_tensor(excl2[:, 1:2], excl2[:, 1:2], exadd[:],
                          op=OP.add)
    t_exv = P.step("vector", totadd, after=[t_tm, t_exv], tag="mta")

    def cb3(eng, inc):
        inc(eng.dma_start(out=rowb[0:256].rearrange("(a p) -> p a", p=128),
                          in_=excl2[:, 0:2],
                          allow_slow_non_contiguous=True))
    t_b3 = P.step("sync", cb3, after=[t_exv], dma=1, tag="mb3")

    def cb4(eng, inc):
        inc(eng.dma_start(out=coloffs[0:1, 0:256], in_=rowb[0:256]))
    t_b4 = P.step("gpsimd", cb4, after=[t_b3], dma=1, tag="mb4")

    def offmm(eng):
        eng.matmul(PS[1][:, 0:NBG], ones1[0:1, 0:128], coloffs[0:1, 0:NBG],
                   start=True, stop=True)
    t_om = P.step("tensor", offmm, after=[t_b4, t_ce], tag="mom")

    def rankf(eng):
        eng.tensor_tensor(rank[:, 0:NBG], PS[0][:, 0:NBG], ind[:, 0:NBG],
                          op=OP.subtract)
        eng.drain()
        eng.tensor_tensor(rank[:, 0:NBG], rank[:, 0:NBG], PS[1][:, 0:NBG],
                          op=OP.add)
    t_rk = P.step("vector", rankf, after=[t_om], tag="mrk")

    iota_p = None

    def payf(eng):
        N = NBG
        eng.tensor_scalar(keepf[:, 0:N], rank[:, 0:N], float(CAP) - 0.5,
                          None, op0=OP.is_le)
        eng.drain()
        eng.tensor_tensor(keepf[:, 0:N], keepf[:, 0:N], ind[:, 0:N],
                          op=OP.mult)
        eng.drain()
        eng.tensor_scalar(nkeep[:, 0:N], keepf[:, 0:N], -1.0, 1.0,
                          op0=OP.mult, op1=OP.add)
        eng.drain()
        eng.tensor_tensor(sc1[:, 0:N], rank[:, 0:N], keepf[:, 0:N],
                          op=OP.mult)
        eng.tensor_scalar(sc2[:, 0:N], nkeep[:, 0:N], iotok[:, 0:1],
                          None, op0=OP.mult)
        eng.drain()
        eng.tensor_tensor(sc1[:, 0:N], sc1[:, 0:N], sc2[:, 0:N], op=OP.add)
        eng.tensor_scalar(sc2[:, 0:N], nkeep[:, 0:N], float(CAP), None,
                          op0=OP.mult)
        eng.drain()
        eng.tensor_tensor(sc1[:, 0:N], sc1[:, 0:N], sc2[:, 0:N], op=OP.add)
        eng.drain()
        eng.tensor_copy(dsti[:, 0:N], sc1[:, 0:N])
        eng.tensor_copy(pay[:, 0:N, 0], byexS[:, 0:N])
        eng.tensor_copy(pay[:, 0:N, 1], iotok[:, 0:N])
        eng.tensor_copy(pay[:, 0:N, 2], wsel[:, 0:N])
        eng.memset(pay[:, 0:N, 3], 0.0)
    t_py = P.step("vector", payf, after=[t_rk], tag="mpy")

    def smscat(eng, inc):
        for g in range(NBG):
            inc(eng.indirect_dma_start(
                slotmap.ap(),
                bass.IndirectOffsetOnAxis(dsti[:, g:g + 1], 0),
                pay[:, g, :], None))
    t_sc = P.step("gpsimd", smscat, after=[t_py, pre["t_sf"]], dma=NBG,
                  tag="msc")

    def n2mm(eng):
        eng.matmul(PS[5][:, 0:512], ones1[0:1, 0:128], n2wS[0:1, 0:512],
                   start=True, stop=True)
        eng.matmul(PS[6][:, 0:512], ones1[0:1, 0:128], n2wS[0:1, 512:1024],
                   start=True, stop=True)
    t_n2 = P.step("tensor", n2mm, after=[t_rl, t_om], tag="mn2")

    def n2ev(eng):
        eng.tensor_copy(n2wrep[:, 0:512], PS[5][:, 0:512])
        eng.tensor_copy(n2wrep[:, 512:1024], PS[6][:, 0:512])
    t_n2e = P.step("vector", n2ev, after=[t_n2], tag="mn2e")

    tpbf = PS[0][:].bitcast(BF16)
    byex2 = byex.ap().rearrange("e r d -> (e r) d")
    NWND = CAP // 512
    t_wend = P.step("tensor", lambda e: None, after=[t_n2e, t_sc], tag="mws")
    t_zs = pre["t_zz"]
    t_gel = {0: t_wend, 1: t_wend}

    for wnd in range(NWND):
        def sl(eng, inc, wnd=wnd):
            if wnd == 0:
                eng.wait_ge(sem_coll, 2)
            inc(eng.dma_start(
                out=slotS[:],
                in_=slotmap.ap().rearrange("(b p) c -> p b c", p=128)
                [:, 4 * wnd:4 * wnd + 4, :]))
        t_sl = P.step("sync", sl, after=[t_sc, t_wend], dma=1,
                      tag=f"mw{wnd}s")

        def idx(eng, wnd=wnd):
            tok = slotS[:, :, 1]
            eng.tensor_copy(gidxI[:], slotS[:, :, 0])
            eng.tensor_scalar(scrB[:], tok, float(NTG) - 0.5, None,
                              op0=OP.is_ge)
            eng.drain()
            eng.tensor_scalar(scrB[:], scrB[:], iotok[:, 0:1], None,
                              op0=OP.mult)
            eng.drain()
            eng.tensor_tensor(scrA[:], tok, scrB[:], op=OP.add)
            eng.drain()
            eng.tensor_copy(slotI[:], scrA[:])
        t_idx = P.step("vector", idx, after=[t_sl, t_zs], tag=f"mw{wnd}i")

        def gath(eng, inc, wnd=wnd):
            for b in range(4):
                inc(eng.indirect_dma_start(
                    xgTM[:, b, :], None, byex2,
                    bass.IndirectOffsetOnAxis(gidxI[:, b:b + 1], 0)))
        t_g = P.step("gpsimd", gath, after=[t_idx, t_wend], dma=4,
                     tag=f"mw{wnd}g")

        t_tp_prev = t_g
        for b in range(4):
            def tpp(eng, b=b):
                for c in range(8):
                    eng.transpose(tpbf[:, 128 * c:128 * (c + 1)],
                                  xgTM[:, b, 128 * c:128 * (c + 1)],
                                  ident[:])
            t_tp = P.step("tensor", tpp, after=[t_g, t_wend, t_tp_prev],
                          tag=f"mw{wnd}t{b}")

            def tpe(eng, b=b):
                for c in range(8):
                    eng.tensor_copy(xgFM[:, c, 128 * b:128 * (b + 1)],
                                    tpbf[:, 128 * c:128 * (c + 1)])
            t_tp_prev = P.step("vector", tpe, after=[t_tp],
                               tag=f"mw{wnd}e{b}")
        t_tpe = t_tp_prev

        for f in range(16):
            bank = PS[f % 2]

            def h1m(eng, f=f, bank=bank):
                for c in range(8):
                    eng.matmul(bank[:, 0:512],
                               w1S[:, c, 128 * f:128 * (f + 1)],
                               xgFM[:, c, :], start=(c == 0), stop=(c == 7))
            t_m = P.step("tensor", h1m,
                         after=[t_tpe, t_wl, t_gel[f % 2], t_wend],
                         tag=f"mw{wnd}h1{f}")

            def gel(eng, f=f, bank=bank):
                eng.activation(out=h1FM[:, f, :], in_=bank[:, 0:512],
                               func=AF.Gelu, scale=1.0)
            t_gel[f % 2] = P.step("scalar", gel, after=[t_m],
                                  tag=f"mw{wnd}g{f}")
        t_h1 = P.step("tensor", lambda e: None,
                      after=[t_gel[0], t_gel[1], t_wl2], tag=f"mw{wnd}h1f")

        t_ev_all = []
        for ts4 in range(4):
            for half in range(2):
                bi = ts4 * 2 + half
                bank = PS[bi]

                def h2m(eng, ts4=ts4, half=half, bank=bank):
                    for f in range(16):
                        eng.matmul(
                            bank[:, 0:512],
                            h1FM[:, f, 128 * ts4:128 * (ts4 + 1)],
                            w2S[:, f, 512 * half:512 * (half + 1)],
                            start=(f == 0), stop=(f == 15))
                t_m2 = P.step("tensor", h2m, after=[t_h1],
                              tag=f"mw{wnd}h2{bi}")

                def h2e(eng, ts4=ts4, half=half, bank=bank):
                    eng.tensor_scalar(
                        zstage[:, ts4, 512 * half:512 * (half + 1)],
                        bank[:, 0:512], slotS[:, ts4, 2:3], None,
                        op0=OP.mult)
                tok_ev = P.step("vector", h2e, after=[t_m2, t_zs],
                                tag=f"mw{wnd}z{bi}")
                t_ev_all.append(tok_ev)

        def zscat(eng, inc, wnd=wnd):
            for b in range(4):
                inc(eng.indirect_dma_start(
                    Z.ap(), bass.IndirectOffsetOnAxis(slotI[:, b:b + 1], 0),
                    zstage[:, b, :], None))
        t_zs = P.step("gpsimd", zscat, after=t_ev_all + [pre["t_zz"]],
                      dma=4, tag=f"mw{wnd}zs")
        t_wend = P.step("tensor", lambda e: None, after=t_ev_all,
                        tag=f"mw{wnd}end")

    def rsc(eng):
        eng.collective_compute(
            "ReduceScatter", OP.add, replica_groups=groups,
            ins=[Z.ap()[0:NTG]], outs=[moe_loc.ap()]).then_inc(sem_coll, 1)
    t_rs = P.step("gpsimd", rsc, after=[t_zs], tag="mrs")
    n_coll = H["n_coll"] + 1

    t_ld = prev
    t_st = t_wend
    t_ev_prev = t_wend
    for blk in range(32):
        e = blk // 16
        i = blk % 16

        def ld(eng, inc, blk=blk, e=e, i=i):
            if blk == 0:
                eng.wait_ge(sem_coll, n_coll)
            inc(eng.dma_start(
                out=xbm[:], in_=moe_loc.ap()
                .rearrange("(b p) f -> p b f", p=128)[:, blk, :]))
            inc(eng.dma_start(
                out=xbx[:], in_=x2loc.ap()[e]
                .rearrange("(b p) f -> p b f", p=128)[:, i, :]))
            inc(eng.dma_start(
                out=xb1[:], in_=x1_hbm
                .rearrange("(b p) f -> p b f", p=128)[:, blk, :]))
        t_ld = P.step("sync", ld, after=[t_rs, t_sp, t_ev_prev, t_ld],
                      dma=3, tag=f"mt{blk}l")

        def a2f(eng, blk=blk):
            eng.tensor_tensor(tA[:], xbm[:], xbx[:], op=OP.add)
        t_a = P.step("vector", a2f, after=[t_ld, t_st], tag=f"mt{blk}a")

        def sq(eng):
            eng.activation(out=tB[:, 0:512], in_=tA[:, 0:512],
                           func=AF.Square, accum_out=lnm[:, 0, 0:1])
            eng.drain()
            eng.activation(out=tB[:, 512:1024], in_=tA[:, 512:1024],
                           func=AF.Square, accum_out=lnm[:, 0, 1:2])
        t_q = P.step("scalar", sq, after=[t_a], tag=f"mt{blk}q")

        def rst(eng):
            eng.tensor_tensor(lnm[:, 0, 2:3], lnm[:, 0, 0:1],
                              lnm[:, 0, 1:2], op=OP.add)
            eng.drain()
            eng.tensor_scalar(lnm[:, 0, 2:3], lnm[:, 0, 2:3], 1.0 / D, 1e-6,
                              op0=OP.mult, op1=OP.add)
            eng.drain()
            eng.reciprocal(lnm[:, 0, 2:3], lnm[:, 0, 2:3])
        t_r1 = P.step("vector", rst, after=[t_q], tag=f"mt{blk}r")

        def rs2f(eng):
            eng.activation(out=lnm[:, 0, 3:4], in_=lnm[:, 0, 2:3],
                           func=AF.Sqrt)
        t_r2 = P.step("scalar", rs2f, after=[t_r1], tag=f"mt{blk}r2")

        def ff(eng):
            eng.tensor_scalar(tB[:], tA[:], lnm[:, 0, 3:4], None,
                              op0=OP.mult)
            eng.drain()
            eng.tensor_tensor(tA[:], tB[:], n2wrep[:], op=OP.mult)
            eng.drain()
            eng.tensor_tensor(tB[:], tA[:], xb1[:], op=OP.add)
        t_f = P.step("vector", ff, after=[t_r2], tag=f"mt{blk}f")

        def lmean(eng):
            eng.tensor_reduce(lnm[:, 1, 0:1], tB[:], axis=AX.X, op=OP.add)
        t_m1 = P.step("vector", lmean, after=[t_f], tag=f"mt{blk}m")

        def lsq(eng):
            eng.activation(out=tA[:, 0:512], in_=tB[:, 0:512],
                           func=AF.Square, accum_out=lnm[:, 1, 1:2])
            eng.drain()
            eng.activation(out=tA[:, 512:1024], in_=tB[:, 512:1024],
                           func=AF.Square, accum_out=lnm[:, 1, 2:3])
        t_q2 = P.step("scalar", lsq, after=[t_f], tag=f"mt{blk}q2")

        def lst(eng):
            Lr = lnm[:, 1, :]
            eng.tensor_tensor(Lr[:, 1:2], Lr[:, 1:2], Lr[:, 2:3], op=OP.add)
            eng.tensor_scalar(Lr[:, 3:4], Lr[:, 0:1], 1.0 / D, None,
                              op0=OP.mult)
            eng.drain()
            eng.tensor_tensor(Lr[:, 4:5], Lr[:, 3:4], Lr[:, 3:4], op=OP.mult)
            eng.tensor_scalar(Lr[:, 5:6], Lr[:, 1:2], 1.0 / D, None,
                              op0=OP.mult)
            eng.drain()
            eng.tensor_tensor(Lr[:, 4:5], Lr[:, 5:6], Lr[:, 4:5],
                              op=OP.subtract)
            eng.drain()
            eng.tensor_scalar(Lr[:, 4:5], Lr[:, 4:5], 1e-5, None, op0=OP.add)
            eng.drain()
            eng.reciprocal(Lr[:, 4:5], Lr[:, 4:5])
        t_s2 = P.step("vector", lst, after=[t_m1, t_q2], tag=f"mt{blk}s")

        def lsr(eng):
            eng.activation(out=lnm[:, 1, 4:5], in_=lnm[:, 1, 4:5],
                           func=AF.Sqrt)
        t_s3 = P.step("scalar", lsr, after=[t_s2], tag=f"mt{blk}s3")

        def lap(eng):
            eng.tensor_scalar(outstage[:], tB[:], lnm[:, 1, 3:4],
                              lnm[:, 1, 4:5], op0=OP.subtract, op1=OP.mult)
        t_ev_prev = P.step("vector", lap, after=[t_s3], tag=f"mt{blk}o")

        def sto(eng, inc, blk=blk):
            inc(eng.dma_start(
                out=out.rearrange("(b p) f -> p b f", p=128)[:, blk, :],
                in_=outstage[:]))
        t_st = P.step("sync", sto, after=[t_ev_prev], dma=1,
                      tag=f"mt{blk}st")
    return P.step("sync", lambda e: None, after=[t_st], tag="mfin")



# version 07
# speedup vs baseline: 1.0216x; 1.0158x over previous
import numpy as np
import ml_dtypes
from contextlib import ExitStack

import concourse.bass as bass
import concourse.mybir as mybir
import concourse.library_config as libcfg
from concourse.library_overlay import lower_extended_insts
from concourse.bass_utils import run_bass_kernel_spmd

F32 = mybir.dt.float32
BF16 = mybir.dt.bfloat16
I32 = mybir.dt.int32
I16 = mybir.dt.int16
U32 = mybir.dt.uint32
AX = mybir.AxisListType
OP = mybir.AluOpType
AF = mybir.ActivationFunctionType
BF = ml_dtypes.bfloat16

NC = 8
S = 2048
D = 1024
TLOC = 4096
ENGS = ("sync", "scalar", "tensor", "vector", "gpsimd")


class Prog:

    def __init__(self, nc):
        self.nc = nc
        self.steps = {e: [] for e in ENGS}
        self.count = {e: 0 for e in ENGS}
        self.sems = {}

    def alloc_sems(self, es):
        for e in ENGS:
            self.sems[e] = es.enter_context(self.nc.semaphore(f"sem_{e}"))
            self.sems["d_" + e] = es.enter_context(self.nc.semaphore(f"dsem_{e}"))
            self.count["d_" + e] = 0

    def step(self, eng, fn, after=(), dma=0, tag=""):
        waits = [(self.sems[s], v) for (s, v) in after]
        key = ("d_" + eng) if dma else eng
        amount = 16 * dma if dma else 1
        prev_total = self.count[key]
        self.count[key] += amount
        tok = (key, self.count[key])
        self.steps[eng].append((fn, waits, self.sems[key], amount, dma, tag,
                                prev_total))
        return tok

    def barrier(self):
        toks = [(k, self.count[k]) for k in self.count]
        outs = []
        for e in ENGS:
            outs.append(self.step(e, lambda eng: None,
                                  after=[t for t in toks if t[0] != e],
                                  tag="barrier"))
        return outs

    def emit(self, block):
        def mk(eng_name):
            def run(eng):
                for (fn, waits, sem, amount, dma, tag, prev_total) in self.steps[eng_name]:
                    for (s, v) in waits:
                        eng.wait_ge(s, v)
                    if dma:
                        if prev_total:
                            eng.wait_ge(sem, prev_total)
                        state = {"left": amount}

                        def inc(instr, n=16):
                            instr.then_inc(sem, n)
                            state["left"] -= n
                            return instr
                        fn(eng, inc)
                        assert state["left"] == 0, (eng_name, tag, state["left"])
                    else:
                        fn(eng)
                        if eng_name == "sync":
                            eng.nop().then_inc(sem, amount)
                        else:
                            eng.drain().then_inc(sem, amount)
            return run
        block.sync(mk("sync"))
        block.scalar(mk("scalar"))
        block.tensor(mk("tensor"))
        block.vector(mk("vector"))
        block.gpsimd(mk("gpsimd"))

    def final_wait(self, eng):
        toks = [(k, self.count[k]) for k in self.count]
        self.step(eng, lambda E: None,
                  after=[t for t in toks if t[0] != eng], tag="final")


def host_constants(n_cores=NC):
    import jax
    cpu = jax.devices("cpu")[0]
    with jax.default_device(cpu):
        rot = np.asarray(jax.random.normal(jax.random.key(42),
                                           (16, 64, 4, 32), dtype=np.float32))
    c = {}
    c["rot_all"] = np.ascontiguousarray(rot.reshape(16, 64, 128))
    c["ident"] = np.eye(128, dtype=np.float32).astype(BF)
    c["identf"] = np.eye(128, dtype=np.float32)
    c["tri_incl"] = (np.arange(128)[:, None] <= np.arange(128)[None, :]).astype(BF)
    c["tri_excl64"] = (np.arange(64)[:, None] < np.arange(64)[None, :]).astype(np.float32)
    c["ones_1x128f"] = np.ones((1, 128), np.float32)
    c["ones_col_bf"] = np.ones((128, 1), np.float32).astype(BF)
    c["iota32_row"] = np.tile(np.arange(32, dtype=np.float32), (128, 1))
    c["iota32x4_row"] = np.tile(np.arange(32, dtype=np.float32), (128, 4))
    c["big128"] = np.full((128, 128), 1e9, np.float32)
    c["iota256_row"] = np.tile(np.arange(256, dtype=np.float32), (128, 1))
    c["big"] = np.full((128, 32), 1e9, np.float32)
    c["iota_tok"] = np.ascontiguousarray(
        (np.arange(128, dtype=np.float32)[:, None]
         + 128.0 * np.arange(256, dtype=np.float32)[None, :]))
    t = c["iota_tok"].astype(np.int64)
    cc = t >> 12
    ee = (t >> 11) & 1
    c["iota_byex"] = np.ascontiguousarray(
        (t - 2048 * cc + (n_cores * 2048 - 2048) * ee).astype(np.float32))
    no = np.zeros((1, 256), np.float32)
    for n in range(4):
        no[0, n * 64:(n + 1) * 64] = n * 2048.0 - 1.0
    c["nround_off"] = no
    return c


def prep_inputs(inputs, n_cores=NC):
    x = np.asarray(inputs["L0_attn_x"])
    consts = host_constants(n_cores)
    w = {k: np.asarray(v) for k, v in inputs.items() if k != "L0_attn_x"}
    per_core = []
    for c in range(NC):
        xb = x[2 * c:2 * c + 2].reshape(TLOC, D)
        m = dict(
            x_tm=np.ascontiguousarray(xb),
            x_fm=np.ascontiguousarray(xb.T),
            w_qk_t=np.ascontiguousarray(w["L0_attn_w_qk"].T),
            w_v_t=np.ascontiguousarray(w["L0_attn_w_v"].T),
            w_out_t=np.ascontiguousarray(w["L0_attn_w_out"].T),
            qw=w["L0_ffn_qw"], kw=w["L0_ffn_kw"], vw=w["L0_ffn_vw"],
            ow=w["L0_ffn_ow"],
            qb=w["L0_ffn_qb"].reshape(1, -1), kb=w["L0_ffn_kb"].reshape(1, -1),
            vb=w["L0_ffn_vb"].reshape(1, -1), ob=w["L0_ffn_ob"].reshape(1, -1),
            gate_w=w["L0_ffn_gate_w"],
            w1e=np.ascontiguousarray(w["L0_ffn_w1"][c]),
            w2e=np.ascontiguousarray(w["L0_ffn_w2"][c]),
            norm1_w=w["L0_ffn_norm1_w"].reshape(1, -1),
            norm2_w=w["L0_ffn_norm2_w"].reshape(1, -1),
            gtok_base=np.full((1, 1), 4096.0 * c, np.float32),
            expid=np.full((1, 1), float(c), np.float32),
            qb_fm=np.ascontiguousarray(
                w["L0_ffn_qb"].reshape(8, 128).T),
            kb_fm=np.ascontiguousarray(
                w["L0_ffn_kb"].reshape(2, 128).T),
            gate_w_fm=np.ascontiguousarray(
                w["L0_ffn_gate_w"].reshape(8, 128, 8).transpose(1, 0, 2)),
        )
        m.update(consts)
        per_core.append(m)
    return per_core


DT_MAP = {np.dtype(np.float32): F32, np.dtype(BF): BF16,
          np.dtype(np.int32): I32, np.dtype(np.int16): I16}


def build(sample_map, tap_names=(), phase=3, n_cores=NC):
    nc = bass.Bass("TRN2", target_bir_lowering=False, num_devices=NC)
    ins = {k: nc.dram_tensor(k, list(v.shape), DT_MAP[v.dtype],
                             kind="ExternalInput").ap()
           for k, v in sample_map.items()}
    out = nc.dram_tensor("out", [TLOC, D], F32, kind="ExternalOutput").ap()

    es = ExitStack()
    with es:
        P = Prog(nc)
        P.alloc_sems(es)

        def sbf(name, shape, dt=F32):
            return es.enter_context(nc.sbuf_tensor(name, shape, dt))

        def psf(name, shape, dt=F32):
            return es.enter_context(nc.psum_tensor(name, shape, dt))

        tap_outs = []

        def tap(name, ap, tok, shape, dt):
            if name not in tap_names:
                return tok
            t = nc.dram_tensor(f"tap_{name}", list(shape), dt,
                               kind="ExternalOutput").ap()
            tap_outs.append(name)

            def f(eng, inc):
                inc(eng.dma_start(out=t, in_=ap))
            return P.step("sync", f, after=[tok], dma=1, tag=f"tap_{name}")

        build_body(nc, es, P, ins, out, sbf, psf, tap, phase,
                   n_cores=n_cores)
        block = es.enter_context(nc.Block())
        P.emit(block)
    return nc, tap_outs


def build_body(nc, es, P, ins, out, sbf, psf, tap, phase, n_cores=NC):
    x1_tm = sbf("g_x1_tm", [128, 32, D], BF16)
    qb_s = sbf("g_qb", [128, 8])
    kb_s = sbf("g_kb", [128, 2])
    PS = [psf(f"PSb{i}", [128, 512]) for i in range(8)]
    t0 = P.step("gpsimd", lambda e: None, tag="t0")

    t_lsh = emit_lsh(nc, es, P, ins, sbf, PS, tap, x1_tm, t0)

    if phase == 1:
        ob = emit_lsh.outstage[:, 0:2, :]
        tok = t_lsh
        stok = t_lsh
        for i in range(16):
            def cvt(eng, i=i):
                eng.tensor_copy(ob[:], x1_tm[:, 2 * i:2 * i + 2, :])
            tok = P.step("vector", cvt, after=[tok, stok], tag=f"cvt{i}")

            def st(eng, inc, i=i):
                inc(eng.dma_start(
                    out=out.rearrange("(b p) f -> p b f", p=128)[:, 2 * i:2 * i + 2],
                    in_=ob[:]))
            stok = P.step("sync", st, after=[tok], dma=1, tag=f"st{i}")
        P.final_wait("sync")
        return

    L = dict(emit_lsh.ctx)
    L["qb_s"] = qb_s
    L["kb_s"] = kb_s
    if phase >= 3:
        pre = emit_moe_pre(nc, P, L, t_lsh, n_cores=n_cores)
    t_gqa = emit_gqa(nc, es, P, ins, L, tap, x1_tm, t_lsh,
                              n_cores=n_cores)
    H = emit_gqa.handles
    if phase >= 3:
        t_moe = emit_moe(nc, es, P, ins, L, H, pre, tap, x1_tm,
                                  out, t_gqa, n_cores=n_cores)
        P.final_wait("sync")
        return

    if phase == 2:
        R1 = L["R1"]
        xb = R1[:, 4096:4608].bitcast(BF16)
        xf = R1[:, 0:1024]
        tok = t_gqa
        stok = t_gqa
        ltok = t_gqa
        for i in range(32):
            def ld(eng, inc, i=i):
                inc(eng.dma_start(
                    out=xb[:],
                    in_=H["x2loc"].ap()[i // 16].rearrange(
                        "(b p) f -> p b f", p=128)[:, i % 16, :]))
            ltok = P.step("sync", ld, after=[tok, t_gqa], dma=1, tag=f"x2l{i}")

            def cvt(eng, i=i):
                eng.tensor_copy(xf[:], xb[:])
            tok = P.step("vector", cvt, after=[ltok, stok], tag=f"x2c{i}")

            def st(eng, inc, i=i):
                inc(eng.dma_start(
                    out=out.rearrange("(b p) f -> p b f", p=128)[:, i, :],
                    in_=xf[:]))
            stok = P.step("sync", st, after=[tok], dma=1, tag=f"x2s{i}")

        def wcoll(eng):
            eng.wait_ge(H["sem_coll"], H["n_coll"])
        P.step("gpsimd", wcoll, after=[stok], tag="wcoll")
        P.final_wait("sync")
        return


def kernel(**inputs):
    per_core = prep_inputs(inputs)
    nc, _ = build(per_core[0])
    lower_extended_insts(nc)
    res = run_bass_kernel_spmd(nc, per_core, core_ids=list(range(NC)))
    outs = [np.asarray(res.results[c]["out"]) for c in range(NC)]
    return np.concatenate(outs, axis=0).reshape(16, 2048, 1024)


def kernel_taps(inputs, tap_names=(), on_hw=True, n_cores=NC):
    per_core = prep_inputs(inputs, n_cores=n_cores)
    nc, tnames = build(per_core[0], tap_names=tap_names, n_cores=n_cores)
    lower_extended_insts(nc)
    if on_hw:
        res = run_bass_kernel_spmd(nc, per_core[:n_cores],
                                   core_ids=list(range(n_cores)))
        results = [{k: np.asarray(v) for k, v in r.items()} for r in res.results]
    else:
        from simrun import simrun
        results = simrun(nc, per_core[:n_cores],
                         ["out"] + [f"tap_{n}" for n in tnames], num_cores=n_cores)
    return results, tnames




F32 = mybir.dt.float32
BF16 = mybir.dt.bfloat16
I32 = mybir.dt.int32
AX = mybir.AxisListType
OP = mybir.AluOpType
AF = mybir.ActivationFunctionType

S = 2048
D = 1024
NH = 16
DH = 64
NR = 4
NT = 16
NSlot = NR * S
QKV_W = 256
OUT_W = 80


def bfv(region, off_f32, n_bf16):
    return region[:, off_f32:off_f32 + n_bf16 // 2].bitcast(BF16)


def emit_lsh(nc, es, P, ins, sb, ps, tap, x1_tm, t_start):
    rot = sb("lsh_rot", [64, NH, 128], BF16)
    tri = sb("c_tri", [128, 128], BF16)
    tri64 = sb("c_tri64", [64, 64], F32)
    ones1 = sb("c_ones1", [1, 128], F32)
    iota256 = sb("c_iota256", [128, 256], F32)
    iota32x4 = sb("c_iota32x4", [128, 128], F32)
    big128 = sb("c_big128", [128, 128], F32)
    nround_off = sb("c_nroff", [1, 256], F32)
    ident = sb("c_ident", [128, 128], BF16)
    identf = sb("c_identf", [128, 128], F32)
    ones_col = sb("c_onescol", [128, 1], BF16)

    def ld_consts(eng, inc):
        inc(eng.dma_start(out=rot[:], in_=ins["rot_all"].rearrange("h d r -> d h r")))
        inc(eng.dma_start(out=tri[:], in_=ins["tri_incl"]))
        inc(eng.dma_start(out=tri64[:], in_=ins["tri_excl64"]))
        inc(eng.dma_start(out=ones1[:], in_=ins["ones_1x128f"]))
        inc(eng.dma_start(out=iota256[:], in_=ins["iota256_row"]))
        inc(eng.dma_start(out=iota32x4[:], in_=ins["iota32x4_row"]))
        inc(eng.dma_start(out=big128[:], in_=ins["big128"]))
        inc(eng.dma_start(out=nround_off[:], in_=ins["nround_off"]))
        inc(eng.dma_start(out=ident[:], in_=ins["ident"]))
        inc(eng.dma_start(out=identf[:], in_=ins["identf"]))
        inc(eng.dma_start(out=ones_col[:], in_=ins["ones_col_bf"]))
    t_lc = P.step("gpsimd", ld_consts, dma=11, tag="lsh_consts")

    lshsort = nc.dram_tensor("lshsort", [NSlot, QKV_W], BF16).ap()
    lshout = nc.dram_tensor("lshout", [NSlot, OUT_W], BF16).ap()
    rowb = nc.dram_tensor("rowbounce", [512], F32).ap()
    qk_hbm = nc.dram_tensor("qk_hbm", [S, D], BF16).ap()
    v_hbm = nc.dram_tensor("v_hbm", [S, D], BF16).ap()

    R1 = sb("lsh_R1", [128, 8192], F32)
    R2 = sb("lsh_R2", [128, 4096], F32)
    R3 = sb("lsh_R3", [128, 4224], F32)
    R4 = sb("lsh_R4", [128, 2560], F32)
    R5 = sb("lsh_R5", [128, 2560], F32)

    xfm = bfv(R1, 0, 16384).rearrange("p (c t) -> p c t", c=8)
    qFS = bfv(R1, 0, NSlot)[0:64]
    kFS = bfv(R1, 4096, NSlot)[0:64]
    xres = bfv(R1, 0, 4096).rearrange("p (a f) -> p a f", f=D)
    outstage = R1[:].rearrange("p (a f) -> p a f", f=D)

    wqkt = bfv(R2, 0, 8192).rearrange("p (c j) -> p c j", c=8)
    qkS = bfv(R2, 0, 8192).rearrange("p (b w) -> p b w", w=128)
    uns = bfv(R2, 0, 64 * OUT_W).rearrange("p (b w) -> p b w", w=OUT_W)

    wvt = bfv(R3, 0, 8192).rearrange("p (c j) -> p c j", c=8)
    vvb = bfv(R3, 0, 64 * 65).rearrange("p (b c) -> p b c", c=65)
    vvb2 = bfv(R3, 2080, 64 * 65).rearrange("p (b c) -> p b c", c=65)
    prefbuf = R3[:, 0:4096].rearrange("p (t k) -> p t k", k=256)
    scrA = R3[:, 0:1152].rearrange("p (a k) -> p a k", a=4)
    woutt = bfv(R3, 0, 8192).rearrange("p (c j) -> p c j", c=8)

    sortout = bfv(R4, 0, 64 * OUT_W).rearrange("p (b w) -> p b w", w=OUT_W)
    scrAi = R3[:, 0:1152].bitcast(I32).rearrange("p (a k) -> p a k", a=4)
    trash = R4[:, 0:1024]
    O4 = bfv(R4, 0, NT * 256).rearrange("p (t k) -> p t k", k=256)
    exclrep = R4[:, 2048:2304]

    stg = bfv(R5, 0, NT * QKV_W).rearrange("p (t w) -> p t w", w=QKV_W)
    qkFMh = bfv(R5, 0, NT * 128).rearrange("p (t w) -> p t w", w=128)[0:64]
    rsum = R5[:, 0:NT * OUT_W].rearrange("p (t w) -> p t w", w=OUT_W)
    attnh = R5[:, NT * OUT_W:NT * OUT_W + NT * 64].rearrange("p (t w) -> p t w", w=64)
    rec = R5[:, NT * OUT_W + NT * 64:NT * OUT_W + NT * 64 + NT].rearrange(
        "p (t o) -> p t o", o=1)

    attnFM = sb("lsh_attnFM", [128, 8, S], BF16)
    sqt = sb("lsh_sqt", [128, D], F32)
    S2 = sb("lsh_S2", [128, NT, NH], F32)
    rsb = sb("lsh_rs", [128, NT, NH], F32)
    negC = sb("lsh_negC", [128, NH], F32)
    scr16 = sb("lsh_scr16", [16, 128], F32)
    scr1 = sb("lsh_scr1", [1, 256], F32)
    lnm = sb("lsh_lnm", [128, NT, 8], F32)
    bounce = sb("lsh_bounce", [128, 512], BF16)
    bounce2 = sb("lsh_bounce2", [128, 512], BF16)
    qkh = sb("lsh_qkh", [128, NT, 64], BF16)
    kh = sb("lsh_kh", [128, NT, 64], BF16)
    vh = sb("lsh_vh", [128, NT, 64], BF16)
    buckets = sb("lsh_bkt", [128, NT, NR], F32)
    amaxf = sb("lsh_amaxf", [128, 32], F32)
    amaxi = sb("lsh_amaxi", [128, 16], I32)
    runtot = sb("lsh_runtot", [1, 256], F32)
    cnt64 = sb("lsh_cnt64", [64, 4], F32)
    pos4 = sb("lsh_pos4", [128, NT, NR], F32)
    pos4i = sb("lsh_pos4i", [128, NT * NR], I32)
    probs = bfv(R2, 3072, 512)

    PS = ps
    p_dots = PS[0][:]
    p_cnt = PS[5][0:1, 0:256]
    p_pv = PS[4][:, 0:160]
    p_misc = PS[1][:]
    p_b = [PS[2][:], PS[3][:]]

    p_misc_bf = p_misc.bitcast(BF16)
    T = dict(p_misc_bf=p_misc_bf, rot=rot, tri=tri, tri64=tri64, ones1=ones1, iota256=iota256,
             iota32x4=iota32x4, big128=big128, nround_off=nround_off, ident=ident,
             negC=negC, lshsort=lshsort, lshout=lshout, rowb=rowb, identf=identf,
             attnFM=attnFM, qkh=qkh, kh=kh, vh=vh, qkFMh=qkFMh,
             buckets=buckets, scrA=scrA, scrAi=scrAi, amaxf=amaxf, amaxi=amaxi,
             O4=O4, prefbuf=prefbuf,
             runtot=runtot, exclrep=exclrep, cnt64=cnt64, pos4=pos4,
             pos4i=pos4i, stg=stg, qkS=qkS, qFS=qFS, kFS=kFS, vvb=vvb, vvb2=vvb2, probs=probs,
             sortout=sortout, uns=uns, rsum=rsum, rec=rec, attnh=attnh,
             p_dots=p_dots, p_pv=p_pv, p_misc=p_misc, p_cnt=p_cnt,
             ones_col=ones_col, rsb=rsb,
             qk_hbm=qk_hbm, v_hbm=v_hbm)

    lsem = dict(d=es.enter_context(nc.semaphore("lsh_fd")),
                e=es.enter_context(nc.semaphore("lsh_fe")),
                p=es.enter_context(nc.semaphore("lsh_fp")),
                v=es.enter_context(nc.semaphore("lsh_fv")),
                cd=[0], ce=[0], cp=[0], cv=[0])
    T["lsem"] = lsem
    T["PS"] = PS

    def initR5(eng):
        eng.memset(R5[:], 0.0)
    t_i5 = P.step("vector", initR5, tag="lshi5")
    prev = P.step("gpsimd", lambda e: None, after=[t_lc, t_start, t_i5], tag="lsh0")
    emit_lsh.outstage = outstage
    emit_lsh.ctx = dict(R1=R1, R2=R2, R3=R3, R4=R4, R5=R5, PS=PS,
                        ones1=ones1, ident=ident, identf=identf,
                        ones_col=ones_col, attnFM=attnFM, tri=tri, rowb=rowb,
                        iota256=iota256, big128=big128)
    NHEADS = 16
    NBATCH = 2
    for b in range(NBATCH):
        prev = emit_batch_prep(nc, P, ins, b, xfm, wqkt, wvt, qk_hbm, v_hbm,
                               bounce, bounce2, trash, sqt, S2, rsb, negC,
                               scr16, scr1, ones1, identf, p_b, prev, tap)
        for h in range(NHEADS):
            prev = emit_head(nc, P, b, h, prev, T, tap)
        if True:
            prev = emit_batch_tail(nc, P, b, ins, x1_tm, attnFM, woutt, xres,
                                   sqt, trash, lnm, p_b, prev, tap)
    return prev


def emit_batch_prep(nc, P, ins, b, xfm, wqkt, wvt, qk_hbm, v_hbm, bounce,
                    bounce2, trash, sqt, S2, rsb, negC, scr16, scr1, ones1,
                    identf, p_b, prev, tap):
    S_ = S

    def ld(eng, inc):
        inc(eng.dma_start(
            out=xfm[:],
            in_=ins["x_fm"].rearrange("(c p) t -> p c t", p=128)[:, :, b * S_:(b + 1) * S_]))
        inc(eng.dma_start(out=wqkt[:],
                          in_=ins["w_qk_t"].rearrange("(c p) j -> p c j", p=128)))
        inc(eng.dma_start(out=wvt[:],
                          in_=ins["w_v_t"].rearrange("(c p) j -> p c j", p=128)))
    t_ld1 = P.step("gpsimd", ld, after=[prev], dma=3, tag=f"b{b}ldx")
    t_ld2 = t_ld1

    toks = []
    sq_toks = []
    st_toks = []
    for wi, (wt, hbm) in enumerate(((wqkt, qk_hbm), (wvt, v_hbm))):
        for ti in range(NT):
            for jh in range(2):
                pp = p_b[(ti * 2 + jh) % 2]
                bb = bounce if jh == 0 else bounce2

                def mm(eng, ti=ti, jh=jh, wt=wt, pp=pp):
                    for c in range(8):
                        eng.matmul(pp[:], xfm[:, c, ti * 128:(ti + 1) * 128],
                                   wt[:, c, jh * 512:(jh + 1) * 512],
                                   start=(c == 0), stop=(c == 7))
                t_mm = P.step("tensor", mm,
                              after=[t_ld1, t_ld2] + (toks[-2:-1] if len(toks) >= 2 else []),
                              tag=f"b{b}pj{wi}.{ti}.{jh}")

                def ev(eng, ti=ti, jh=jh, pp=pp, bb=bb):
                    eng.copy(out=bb[:], in_=pp[:])
                aft = [t_mm] + (st_toks[-2:-1] if len(st_toks) >= 2 else [])
                t_ev = P.step("scalar", ev, after=aft, tag=f"b{b}pe{wi}.{ti}.{jh}")
                toks.append(t_ev)

                if wi == 0:
                    def sq(eng, ti=ti, jh=jh, bb=bb):
                        eng.activation(out=trash[:, 0:512], in_=bb[:], func=AF.Square)
                    t_sq = P.step("scalar", sq,
                                  after=[t_ev] + (sq_toks[-1:] if sq_toks else []),
                                  tag=f"b{b}sq{ti}.{jh}")

                    def red(eng, ti=ti, jh=jh):
                        eng.tensor_reduce(
                            S2[:, ti, jh * 8:(jh + 1) * 8],
                            trash[:, 0:512].rearrange("p (h d) -> p h d", d=DH),
                            axis=AX.X, op=OP.add)
                    sq_toks.append(P.step("vector", red, after=[t_sq],
                                          tag=f"b{b}rd{ti}.{jh}"))

                def st(eng, inc, ti=ti, jh=jh, hbm=hbm, bb=bb):
                    inc(eng.dma_start(
                        out=hbm.rearrange("(bb p) j -> p bb j", p=128)
                        [:, ti, jh * 512:(jh + 1) * 512],
                        in_=bb[:]))
                extra = [sq_toks[-1]] if wi == 0 else []
                st_toks.append(P.step("sync", st, after=[t_ev] + extra, dma=1,
                                      tag=f"b{b}ps{wi}.{ti}.{jh}"))
    t_proj = P.step("sync", lambda e: None, after=st_toks[-2:] + sq_toks[-1:],
                    tag=f"b{b}pjd")

    def rs_fn(eng):
        eng.tensor_scalar(rsb[:], S2[:], 1.0 / DH, 1e-6, op0=OP.mult, op1=OP.add)
        eng.drain()
        eng.reciprocal(rsb[:], rsb[:])
    t_rs0 = P.step("vector", rs_fn, after=[t_proj], tag=f"b{b}rs0")

    def rs_sqrt(eng):
        eng.activation(out=rsb[:], in_=rsb[:], func=AF.Sqrt, scale=1.0 / DH)
    t_rs = P.step("scalar", rs_sqrt, after=[t_rs0], tag=f"b{b}rs1")

    def cmax(eng):
        eng.tensor_reduce(negC[:, 0:NH], S2[:].rearrange("p t h -> p h t"),
                          axis=AX.X, op=OP.max)
    t_cm = P.step("vector", cmax, after=[t_proj], tag=f"b{b}cm")

    def ctp(eng):
        eng.transpose(p_b[0][0:NH, 0:128], negC[:, 0:NH], identf[:])
    t_ct = P.step("tensor", ctp, after=[t_cm], tag=f"b{b}ct")

    def cred(eng):
        eng.tensor_reduce(scr16[0:NH, 0:1], p_b[0][0:NH, 0:128], axis=AX.X, op=OP.max)
    t_cr = P.step("vector", cred, after=[t_ct], tag=f"b{b}cr")

    def csqrt(eng):
        eng.activation(out=scr16[0:NH, 1:2], in_=scr16[0:NH, 0:1], func=AF.Sqrt)
    t_cs = P.step("scalar", csqrt, after=[t_cr], tag=f"b{b}cs")

    def ctp2(eng):
        eng.transpose(p_b[0][0:1, 0:NH], scr16[0:NH, 1:2], identf[0:NH, 0:NH])
    t_ct2 = P.step("tensor", ctp2, after=[t_cs, t_cr], tag=f"b{b}ct2")

    def cneg(eng):
        eng.tensor_scalar(scr1[0:1, 0:NH], p_b[0][0:1, 0:NH], -1.0, None, op0=OP.mult)
    t_cn = P.step("vector", cneg, after=[t_ct2], tag=f"b{b}cn")

    def crep(eng):
        eng.matmul(p_b[1][:, 0:NH], ones1[:], scr1[0:1, 0:NH], start=True, stop=True)
    t_crp = P.step("tensor", crep, after=[t_cn], tag=f"b{b}crp")

    def cev(eng):
        eng.tensor_copy(negC[:], p_b[1][:, 0:NH])
    t_c = P.step("vector", cev, after=[t_crp], tag=f"b{b}cev")

    return P.step("vector", lambda e: None, after=[t_rs, t_c], tag=f"b{b}prep")


def emit_head(nc, P, b, h, t_chain, T, tap):
    HSTG = "Z"
    def ldh(eng, inc):
        inc(eng.dma_start(
            out=T["qkh"][:],
            in_=T["qk_hbm"].rearrange("(bb p) j -> p bb j", p=128)
            [:, :, h * 64:(h + 1) * 64]))
        inc(eng.dma_start(
            out=T["vh"][:],
            in_=T["v_hbm"].rearrange("(bb p) j -> p bb j", p=128)
            [:, :, h * 64:(h + 1) * 64]))
    tok = P.step("sync", ldh, after=[t_chain], dma=2, tag=f"H{b}.{h}ld")

    def khf(eng):
        eng.tensor_tensor(
            T["kh"][:], T["qkh"][:],
            T["rsb"][:, :, h:h + 1].to_broadcast([128, NT, DH]), op=OP.mult)
    t_kh = P.step("vector", khf, after=[tok], tag=f"H{h}kh")
    if HSTG == "A":
        return t_kh

    for g in range(4):
        def tp(eng, g=g):
            for q in range(4):
                ti = g * 4 + q
                eng.transpose(T["p_misc_bf"][0:64, 128 * q:128 * (q + 1)],
                              T["qkh"][:, ti, :], T["ident"][:])
        t_tp = P.step("tensor", tp, after=[tok], tag=f"H{h}tp{g}")

        def ev(eng, g=g):
            eng.copy(
                out=T["qkFMh"][0:64, g * 4:(g + 1) * 4, :].rearrange("p a t -> p (a t)"),
                in_=T["p_misc_bf"][0:64, 0:512])
        tok = P.step("scalar", ev, after=[t_tp], tag=f"H{h}te{g}")

    for g in range(4):
        def rv(eng, g=g):
            for q in range(4):
                ti = g * 4 + q
                eng.matmul(T["p_misc"][:, 128 * q:128 * (q + 1)],
                           T["qkFMh"][0:64, ti, :], T["rot"][:, h, :],
                           start=True, stop=True)
        t_rv = P.step("tensor", rv, after=[tok], tag=f"H{h}rv{g}")

        def amax(eng, g=g):
            v = T["p_misc"][:].rearrange("p (a n r) -> p a n r", n=NR, r=32)
            A = T["scrA"]
            Ai = T["scrAi"]
            mp = A[:, :, 0:4]
            mn = A[:, :, 4:8]
            idxp = T["amaxf"][:, 0:16].rearrange("p (a k) -> p a k", a=4)
            idxn = T["amaxf"][:, 16:32].rearrange("p (a k) -> p a k", a=4)
            usep = T["amaxi"][:].rearrange("p (a k) -> p a k", a=4)
            cand4 = A[:, :, 32:160].rearrange("p a (n r) -> p a n r", r=32)
            cand = A[:, :, 32:160]
            maski4 = Ai[:, :, 160:288].rearrange("p a (n r) -> p a n r", r=32)
            maski = Ai[:, :, 160:288]
            eng.tensor_reduce(mp, v, axis=AX.X, op=OP.max)
            eng.tensor_reduce(mn, v, axis=AX.X, op=OP.min)
            eng.drain()
            bc = [128, 4, NR, 32]
            bcf = [128, 4, 128]
            iota = T["iota32x4"][:].rearrange("p (a r) -> p a r", a=1).to_broadcast(bcf)
            bigc = T["big128"][:].rearrange("p (a r) -> p a r", a=1).to_broadcast(bcf)
            eng.tensor_tensor(maski4, v, mp.rearrange("p a (n o) -> p a n o", o=1)
                              .to_broadcast(bc), op=OP.is_ge)
            eng.drain()
            eng.select(cand, maski, iota, bigc, add_drain=True)
            eng.drain()
            eng.tensor_reduce(idxp, cand4, axis=AX.X, op=OP.min)
            eng.drain()
            eng.tensor_tensor(maski4, v, mn.rearrange("p a (n o) -> p a n o", o=1)
                              .to_broadcast(bc), op=OP.is_le)
            eng.drain()
            eng.select(cand, maski, iota, bigc, add_drain=True)
            eng.drain()
            eng.tensor_reduce(idxn, cand4, axis=AX.X, op=OP.min)
            eng.tensor_tensor(mp, mp, mn, op=OP.add)
            eng.drain()
            eng.tensor_tensor(usep, mp, mn, op=OP.is_ge) if False else None
            eng.tensor_scalar(usep, mp, 0.0, None, op0=OP.is_ge)
            eng.tensor_scalar(idxn, idxn, 32.0, None, op0=OP.add)
            eng.drain()
            eng.select(T["buckets"][:, g * 4:(g + 1) * 4, :], usep, idxp, idxn, add_drain=True)
        tok = P.step("vector", amax, after=[t_rv], tag=f"H{h}am{g}")

    if HSTG == "B":
        return tok

    def boff(eng):
        eng.tensor_scalar(T["scrA"][:, 0:1, 0:NR], T["iota256"][0:128, 0:NR],
                          64.0, None, op0=OP.mult)
        eng.drain()
        eng.tensor_tensor(T["buckets"][:], T["buckets"][:],
                          T["scrA"][:, 0:1, 0:NR].to_broadcast([128, NT, NR]),
                          op=OP.add)
    tok = P.step("vector", boff, after=[tok], tag=f"H{h}bo")

    def onehot(eng):
        eng.tensor_tensor(
            T["O4"][:].rearrange("p t (n k) -> p t n k", k=64),
            T["iota256"][:].rearrange("p (t nk) -> p t nk", t=1)
            .rearrange("p t (n k) -> p t n k", k=64)
            .to_broadcast([128, NT, NR, 64]),
            T["buckets"][:].rearrange("p t (n o) -> p t n o", o=1)
            .to_broadcast([128, NT, NR, 64]),
            op=OP.is_equal)
    tok = P.step("vector", onehot, after=[tok], tag=f"H{h}oh")
    tok = tap(f"buckets_{b}_{h}", T["buckets"][:], tok, [128, NT, NR], F32)

    def rt0(eng):
        eng.memset(T["runtot"][:], 0.0)
    tok = P.step("vector", rt0, after=[tok], tag=f"H{h}rt0")

    t_ev_prev = tok
    for ti in range(NT):
        def mmp(eng, ti=ti):
            eng.matmul(T["p_dots"][:, 0:256], T["tri"][:], T["O4"][:, ti, :],
                       start=True, stop=False)
            eng.matmul(T["p_dots"][:, 0:256], T["ones1"][:], T["runtot"][:],
                       start=False, stop=True)
            eng.matmul(T["p_cnt"][:], T["ones_col"][:], T["O4"][:, ti, :],
                       start=True, stop=True)
        t_mm = P.step("tensor", mmp, after=[t_ev_prev], tag=f"H{h}pf{ti}")

        def evp(eng, ti=ti):
            eng.tensor_copy(T["prefbuf"][:, ti, :], T["p_dots"][:, 0:256])
            eng.drain()
            eng.tensor_tensor(T["runtot"][:], T["runtot"][:], T["p_cnt"][:],
                              op=OP.add)
        t_ev_prev = P.step("vector", evp, after=[t_mm], tag=f"H{h}pe{ti}")
    tok = t_ev_prev

    if HSTG == "C":
        return tok

    def rl1(eng, inc):
        inc(eng.dma_start(out=T["rowb"][0:256], in_=T["runtot"][0:1, :]))
    tok = P.step("gpsimd", rl1, after=[tok], dma=1, tag=f"H{h}r1")

    def rl2(eng, inc):
        inc(eng.dma_start(out=T["cnt64"][:],
                          in_=T["rowb"][0:256].rearrange("(n k) -> k n", k=64),
                      allow_slow_non_contiguous=True))
    tok = P.step("gpsimd", rl2, after=[tok], dma=1, tag=f"H{h}r2")

    def mme(eng):
        eng.matmul(T["p_misc"][0:64, 0:4], T["tri64"][:], T["cnt64"][:],
                   start=True, stop=True)
    tok = P.step("tensor", mme, after=[tok], tag=f"H{h}me")

    def eve(eng):
        eng.tensor_copy(T["cnt64"][:], T["p_misc"][0:64, 0:4])
    tok = P.step("vector", eve, after=[tok], tag=f"H{h}ee")

    def rl3(eng, inc):
        inc(eng.dma_start(out=T["rowb"][0:256].rearrange("(n k) -> k n", k=64),
                          in_=T["cnt64"][:], allow_slow_non_contiguous=True))
    tok = P.step("gpsimd", rl3, after=[tok], dma=1, tag=f"H{h}r3")

    def rl4(eng, inc):
        inc(eng.dma_start(out=T["runtot"][0:1, :], in_=T["rowb"][0:256]))
    tok = P.step("gpsimd", rl4, after=[tok], dma=1, tag=f"H{h}r4")

    def addoff(eng):
        eng.tensor_tensor(T["runtot"][:], T["runtot"][:], T["nround_off"][:],
                          op=OP.add)
    tok = P.step("vector", addoff, after=[tok], tag=f"H{h}ao")

    def repl(eng):
        eng.matmul(T["p_misc"][:, 0:256], T["ones1"][:], T["runtot"][:],
                   start=True, stop=True)
    tok = P.step("tensor", repl, after=[tok], tag=f"H{h}rp")

    def evr(eng):
        eng.tensor_copy(T["exclrep"][:], T["p_misc"][:, 0:256])
    tok = P.step("vector", evr, after=[tok], tag=f"H{h}er")

    def posf(eng):
        eng.tensor_tensor(T["prefbuf"][:], T["prefbuf"][:],
                          T["exclrep"][:].rearrange("p (t k) -> p t k", t=1)
                          .to_broadcast([128, NT, 256]), op=OP.add)
        eng.drain()
        eng.tensor_tensor(T["prefbuf"][:], T["prefbuf"][:], T["O4"][:], op=OP.mult)
        eng.drain()
        eng.tensor_reduce(T["pos4"][:],
                          T["prefbuf"][:].rearrange("p t (n k) -> p t n k", k=64),
                          axis=AX.X, op=OP.add)
        eng.drain()
        eng.tensor_copy(T["pos4i"][:], T["pos4"][:].rearrange("p t n -> p (t n)"))
    t_pos = P.step("vector", posf, after=[tok], tag=f"H{h}po")
    t_pos = tap(f"pos4_{b}_{h}", T["pos4"][:], t_pos, [128, NT, NR], F32)

    if HSTG == "D":
        return t_pos

    def stage(eng):
        st = T["stg"][:].rearrange("p t (q d) -> p t q d", d=64)
        eng.tensor_copy(st[:, :, 0, :], T["qkh"][:])
        eng.tensor_copy(st[:, :, 1, :], T["kh"][:])
        eng.tensor_copy(st[:, :, 2, :], T["vh"][:])
        eng.memset(st[:, :, 3, :], 0.0)
    t_stage = P.step("vector", stage, after=[t_pos, t_kh], tag=f"H{h}sg")

    def scat(eng, inc):
        for ti in range(NT):
            for n in range(NR):
                inc(eng.indirect_dma_start(
                    T["lshsort"],
                    bass.IndirectOffsetOnAxis(
                        T["pos4i"][:, ti * NR + n:ti * NR + n + 1], 0),
                    T["stg"][:, ti, :], None))
    t_scat = P.step("gpsimd", scat, after=[t_stage], dma=64, tag=f"H{h}sc")

    if HSTG == "E":
        return t_scat

    def rb(eng, inc):
        src = T["lshsort"].rearrange("(bb p) w -> p bb w", p=128)
        inc(eng.dma_start(out=T["qkS"][:, 0:32, :], in_=src[:, 0:32, 0:128]))
        inc(eng.dma_start(out=T["qkS"][:, 32:64, :], in_=src[:, 32:64, 0:128]))
        inc(eng.dma_start(out=T["vvb"][:, :, 0:64], in_=src[:, :, 128:192]))
        inc(eng.dma_start(out=T["vvb2"][64:128, :, 0:64], in_=src[0:64, :, 128:192]))
        inc(eng.dma_start(out=T["vvb2"][0:64, 1:64, 0:64],
                          in_=src[64:128, 0:63, 128:192]))
        inc(eng.dma_start(out=T["vvb2"][0:64, 0:1, 0:64],
                          in_=src[64:128, 63:64, 128:192]))
    t_rb = P.step("sync", rb, after=[t_scat], dma=6, tag=f"H{h}rb")

    def vones(eng):
        eng.memset(T["vvb"][:, :, 64:65], 1.0)
        eng.memset(T["vvb2"][:, :, 64:65], 1.0)
    t_rb = P.step("vector", vones, after=[t_rb], tag=f"H{h}vo")

    tok = t_rb
    for g in range(16):
        def tps(eng, g=g):
            for q in range(4):
                ti = g * 4 + q
                eng.transpose(T["p_misc_bf"][:, 128 * q:128 * (q + 1)],
                              T["qkS"][:, ti, :], T["ident"][:])
        t_t = P.step("tensor", tps, after=[tok], tag=f"H{h}st{g}")

        def tse(eng, g=g):
            eng.copy(out=T["qFS"][:, g * 512:(g + 1) * 512],
                     in_=T["p_misc_bf"][0:64, 0:512])
            eng.copy(out=T["kFS"][:, g * 512:(g + 1) * 512],
                     in_=T["p_misc_bf"][64:128, 0:512])
        tok = P.step("scalar", tse, after=[t_t], tag=f"H{h}se{g}")
    t_sorted = tok

    if HSTG == "F":
        return t_sorted
    L = T["lsem"]
    sld, sle, slp, slv = L["d"], L["e"], L["p"], L["v"]
    base_d, base_e = L["cd"][0], L["ce"][0]
    base_p, base_v = L["cp"][0], L["cv"][0]
    PSp = T["PS"]

    def pvslot(s):
        if s < 4:
            return PSp[4][:, 65 * s:65 * s + 65]
        return PSp[5][:, 65 * (s - 4):65 * (s - 4) + 65]

    def g_pe(eng, h=h, base_d=base_d, base_e=base_e, base_p=base_p,
             base_v=base_v):
        def emit_pv(g):
            last = None
            for q in range(8):
                c = g * 8 + q
                po = 64 * (c % 2)
                s = (c // 2) % 8
                if c % 2 == 1:
                    vsrc = T["vvb"][:, (c - 1) // 2, 0:65]
                else:
                    vsrc = T["vvb2"][:, c // 2, 0:65]
                sl = pvslot(s)
                last = eng.matmul(sl[po:po + 64, :],
                                  T["probs"][:, 64 * q:64 * q + 64],
                                  vsrc, start=True, stop=True,
                                  tile_position=(0, po))
            last.then_inc(slp, 1)
        for g in range(16):
            if base_e + g > 0:
                eng.wait_ge(sle, base_e + g)
            last = None
            for q in range(8):
                c = g * 8 + q
                rhs = T["qFS"][:, 64 * c:64 * c + 64]
                if c % 2 == 0:
                    lo = (64 * c - 64) % NSlot
                    eng.matmul(T["p_dots"][0:64, 64 * q:64 * q + 64],
                               T["kFS"][:, lo:lo + 64], rhs,
                               start=True, stop=True, tile_position=(0, 0))
                    last = eng.matmul(
                        T["p_dots"][64:128, 64 * q:64 * q + 64],
                        T["kFS"][:, 64 * c:64 * c + 64], rhs,
                        start=True, stop=True, tile_position=(0, 64))
                else:
                    last = eng.matmul(
                        T["p_dots"][:, 64 * q:64 * q + 64],
                        T["kFS"][:, 64 * c - 64:64 * c + 64],
                        rhs, start=True, stop=True)
            last.then_inc(sld, 1)
            if g >= 1:
                k = g - 1
                if base_v + k - 1 > 0:
                    eng.wait_ge(slv, base_v + k - 1)
                emit_pv(k)
        eng.wait_ge(sle, base_e + 16)
        if base_v + 14 > 0:
            eng.wait_ge(slv, base_v + 14)
        emit_pv(15)
    t_gpe = P.step("tensor", g_pe, after=[t_sorted], tag=f"H{h}GPE")
    L["cd"][0] += 16
    L["cp"][0] += 16

    def g_act(eng, h=h, base_d=base_d, base_p=base_p):
        for g in range(16):
            eng.wait_ge(sld, base_d + g + 1)
            if g >= 1:
                eng.wait_ge(slp, base_p + g)
            eng.activation(out=T["probs"][:], in_=T["p_dots"][:],
                           func=AF.Exp, bias=T["negC"][:, h:h + 1],
                           scale=1.0).then_inc(sle, 1)
    t_gact = P.step("scalar", g_act, after=[t_sorted], tag=f"H{h}GAX")
    L["ce"][0] += 16

    def g_dve(eng, base_p=base_p):
        for g in range(16):
            eng.wait_ge(slp, base_p + g + 1)
            last = None
            for i in range(4):
                blk = 4 * g + i
                s = blk % 8
                last = eng.tensor_copy(T["sortout"][:, blk, 0:65],
                                       pvslot(s))
            last.then_inc(slv, 1)
    t_pv_done = P.step("vector", g_dve, after=[t_sorted], tag=f"H{h}GDV")
    L["cv"][0] += 16
    t_gfin = (t_gpe, t_gact)

    def wout(eng, inc):
        dst = T["lshout"].rearrange("(bb p) w -> p bb w", p=128)
        inc(eng.dma_start(out=dst[:, 0:32, :], in_=T["sortout"][:, 0:32, :]))
        inc(eng.dma_start(out=dst[:, 32:64, :], in_=T["sortout"][:, 32:64, :]))
    t_wo = P.step("sync", wout, after=[t_pv_done], dma=2, tag=f"H{h}wo")

    def unsort(eng, inc):
        for ti in range(NT):
            for n in range(NR):
                inc(eng.indirect_dma_start(
                    T["uns"][:, n * NT + ti, :], None,
                    T["lshout"],
                    bass.IndirectOffsetOnAxis(
                        T["pos4i"][:, ti * NR + n:ti * NR + n + 1], 0)))
    t_un = P.step("gpsimd", unsort, after=[t_wo], dma=64, tag=f"H{h}un")

    if HSTG == "H":
        return t_un

    def comb(eng):
        u = T["uns"]
        eng.tensor_tensor(T["rsum"][:], u[:, 0:16, :], u[:, 16:32, :], op=OP.add)
        eng.drain()
        eng.tensor_tensor(T["rsum"][:], T["rsum"][:], u[:, 32:48, :], op=OP.add)
        eng.drain()
        eng.tensor_tensor(T["rsum"][:], T["rsum"][:], u[:, 48:64, :], op=OP.add)
        eng.drain()
        eng.reciprocal(T["rec"][:], T["rsum"][:, :, 64:65])
        eng.drain()
        eng.tensor_tensor(T["attnh"][:], T["rsum"][:, :, 0:64],
                          T["rec"][:].to_broadcast([128, NT, 64]), op=OP.mult)
    t_cb = P.step("vector", comb, after=[t_un], tag=f"H{h}cb")
    t_cb = tap(f"attnh_{b}_{h}", T["attnh"][:], t_cb, [128, NT, 64], F32)

    tok = t_cb
    for g in range(4):
        def tpa(eng, g=g):
            for q in range(4):
                ti = g * 4 + q
                eng.transpose(T["p_misc"][0:64, 128 * q:128 * (q + 1)],
                              T["attnh"][:, ti, :], T["identf"][:])
        t_t = P.step("tensor", tpa, after=[tok], tag=f"H{h}at{g}")

        def tae(eng, g=g):
            eng.copy(
                out=T["attnFM"][(h % 2) * 64:(h % 2) * 64 + 64, h // 2,
                                g * 512:(g + 1) * 512],
                in_=T["p_misc"][0:64, 0:512])
        tok = P.step("scalar", tae, after=[t_t], tag=f"H{h}ae{g}")
    return tok


def emit_batch_tail(nc, P, b, ins, x1_tm, attnFM, woutt, xres, sqt, trash,
                    lnm, p_b, prev, tap):
    def ldw(eng, inc):
        inc(eng.dma_start(out=woutt[:],
                          in_=ins["w_out_t"].rearrange("(c p) j -> p c j", p=128)))
    t_w = P.step("gpsimd", ldw, after=[prev], dma=1, tag=f"b{b}tw")

    toks = []
    ln_toks = []
    t_x = None
    for ti in range(NT):
        if ti % 4 == 0:
            def ldr(eng, inc, ti=ti):
                inc(eng.dma_start(
                    out=xres[:],
                    in_=ins["x_tm"].rearrange("(bb p) f -> p bb f", p=128)
                    [:, b * NT + ti:b * NT + ti + 4, :]))
            t_x = P.step("gpsimd", ldr,
                         after=[prev] + (ln_toks[-1:] if ln_toks else []),
                         dma=1, tag=f"b{b}tx{ti}")
        for jh in range(2):
            pp = p_b[(ti * 2 + jh) % 2]

            def mm(eng, ti=ti, jh=jh, pp=pp):
                for c in range(8):
                    eng.matmul(pp[:], attnFM[:, c, ti * 128:(ti + 1) * 128],
                               woutt[:, c, jh * 512:(jh + 1) * 512],
                               start=(c == 0), stop=(c == 7))
            t_mm = P.step("tensor", mm,
                          after=[prev, t_w] + (toks[-2:-1] if len(toks) >= 2 else []),
                          tag=f"b{b}tm{ti}.{jh}")

            def ev(eng, ti=ti, jh=jh, pp=pp):
                eng.tensor_tensor(
                    sqt[:, jh * 512:(jh + 1) * 512], pp[:],
                    xres[:, ti % 4, jh * 512:(jh + 1) * 512], op=OP.add)
            toks.append(P.step("vector", ev,
                               after=[t_mm, t_x] + (ln_toks[-1:] if ln_toks else []),
                               tag=f"b{b}tv{ti}.{jh}"))

        def lnmean(eng, ti=ti):
            eng.tensor_reduce(lnm[:, ti, 0:1], sqt[:], axis=AX.X, op=OP.add)
        t_m = P.step("vector", lnmean, after=toks[-2:], tag=f"b{b}lm{ti}")

        def lnsq(eng, ti=ti):
            eng.activation(out=trash[:, 0:1024], in_=sqt[:], func=AF.Square,
                           accum_out=lnm[:, ti, 1:2])
        t_s = P.step("scalar", lnsq, after=toks[-2:], tag=f"b{b}ls{ti}")

        def lnst(eng, ti=ti):
            L = lnm[:, ti, :]
            eng.tensor_scalar(L[:, 2:3], L[:, 0:1], 1.0 / D, None, op0=OP.mult)
            eng.drain()
            eng.tensor_tensor(L[:, 3:4], L[:, 2:3], L[:, 2:3], op=OP.mult)
            eng.tensor_scalar(L[:, 5:6], L[:, 1:2], 1.0 / D, None, op0=OP.mult)
            eng.drain()
            eng.tensor_tensor(L[:, 3:4], L[:, 5:6], L[:, 3:4], op=OP.subtract)
            eng.drain()
            eng.tensor_scalar(L[:, 3:4], L[:, 3:4], 1e-5, None, op0=OP.add)
            eng.drain()
            eng.reciprocal(L[:, 3:4], L[:, 3:4])
        t_st = P.step("vector", lnst, after=[t_m, t_s], tag=f"b{b}lt{ti}")

        def lnrs(eng, ti=ti):
            eng.activation(out=lnm[:, ti, 3:4], in_=lnm[:, ti, 3:4], func=AF.Sqrt)
        t_rs = P.step("scalar", lnrs, after=[t_st], tag=f"b{b}lr{ti}")

        def lnap(eng, ti=ti):
            eng.tensor_scalar(
                x1_tm[:, b * NT + ti, :], sqt[:],
                lnm[:, ti, 2:3], lnm[:, ti, 3:4],
                op0=OP.subtract, op1=OP.mult)
        ln_toks.append(P.step("vector", lnap, after=[t_rs], tag=f"b{b}la{ti}"))
    return P.step("vector", lambda e: None, after=ln_toks[-1:], tag=f"b{b}tl")



F32 = mybir.dt.float32
BF16 = mybir.dt.bfloat16
I32 = mybir.dt.int32
AX = mybir.AxisListType
OP = mybir.AluOpType
AF = mybir.ActivationFunctionType

S = 2048
D = 1024
NH = 16
DH = 64
NKV = 4


def bfv(region, off_f32, n_bf16):
    return region[:, off_f32:off_f32 + n_bf16 // 2].bitcast(BF16)


def emit_gqa(nc, es, P, ins, L, tap, x1_tm, prev, n_cores=8):
    R1, R2, R3, R4, R5 = L["R1"], L["R2"], L["R3"], L["R4"], L["R5"]
    PS = L["PS"]
    ones1, ident, identf = L["ones1"], L["ident"], L["identf"]
    ones_col = L["ones_col"]
    iota256, big128 = L["iota256"], L["big128"]

    qfm = nc.dram_tensor("g_qfm", [1024, S], BF16).ap()
    kfm = nc.dram_tensor("g_kfm", [256, S], BF16).ap()
    vtm = nc.dram_tensor("g_vtm", [S, 256], BF16).ap()
    ofm = nc.dram_tensor("g_ofm", [1024, S], BF16).ap()
    x2loc = nc.dram_tensor("g_x2loc", [2, S, D], BF16)
    rt_in = nc.dram_tensor("g_rtin", [2 * S, 4], F32)
    byex = nc.dram_tensor("g_byex", [2, n_cores * S, D], BF16,
                          addr_space="Shared")
    rt_all = nc.dram_tensor("g_rtall", [n_cores * 2 * S, 4], F32,
                            addr_space="Shared")
    groups = [list(range(n_cores))]
    sem_coll = es.enter_context(nc.semaphore("sem_coll"))
    coll_n = [0]
    fs = dict(st=es.enter_context(nc.semaphore("gs_fst")),
              ex=es.enter_context(nc.semaphore("gs_fex")),
              pv=es.enter_context(nc.semaphore("gs_fpv")),
              qn=es.enter_context(nc.semaphore("gs_fqn")),
              cst=[0], cex=[0], cpv=[0], cqn=[0])

    x1FMw = bfv(R2, 0, 4096).rearrange("p (c t) -> p c t", c=8)
    kwS = bfv(R2, 2048, 2048).rearrange("p (c j) -> p c j", c=8)
    vwS = bfv(R2, 3072, 2048).rearrange("p (c j) -> p c j", c=8)
    qwS = bfv(R3, 0, 8192).rearrange("p (c j) -> p c j", c=8)
    owS = bfv(R3, 0, 8192).rearrange("p (c j) -> p c j", c=8)
    qv = [bfv(R1, 0, 2048)[0:65], bfv(R1, 1024, 2048)[0:65]]
    kv = [bfv(R1, 2048, 2048)[0:65], bfv(R1, 3072, 2048)[0:65]]
    vT = [bfv(R1, 4096, 1040).rearrange("p (b c) -> p b c", c=65),
          bfv(R1, 4616, 1040).rearrange("p (b c) -> p b c", c=65)]
    OFMp = [bfv(R1, 5136, 2048), bfv(R1, 6160, 2048)]
    sqb = L["attnFM"][:].rearrange("p a t -> p (a t)")[0:64, 0:2048]
    scr_rc = R2[0:1, 0:512]
    scrq2 = R2[0:1, 512:1024]
    scrq_a = R2[0:1, 1024:1536]
    maxs = R2[0:1, 1536:1544]
    rbc = R1[0:64, 7184:7696]
    OallS = bfv(R1, 0, 16384).rearrange("p (c t) -> p c t", c=8)

    a_f32 = R4[:, 0:1024]
    x2fin = R4[:, 1024:2048]
    xfmT = [R4[:, 2048:2176], R4[:, 2176:2304]]
    s2a = R4[:, 2304:2305]
    s2b = R4[:, 2305:2306]
    s2 = R4[:, 2306:2307]
    rstd = R4[:, 2307:2308]
    epscol = R4[:, 2308:2309]
    obS = R1[0:1, 0:1024]
    n1wS = R1[0:1, 1024:2048]
    vbS = R1[0:1, 2048:2304]

    obrep = bfv(R5, 0, 1024)
    n1wrep = bfv(R5, 512, 1024)
    vbrep = bfv(R5, 1024, 256)
    gwS = R5[:, 1152:1216].rearrange("p (c e) -> p c e", e=8)
    zTM = R5[:, 1216:1344].rearrange("p (b e) -> p b e", e=8)
    bounce = [bfv(R5, 1344, 512), bfv(R5, 1600, 512)]
    m1all = R5[:, 1856:1872]
    m2all = R5[:, 1872:1888]
    e1all = R5[:, 1888:1904]
    e2all = R5[:, 1904:1920]
    w1all = R5[:, 1920:1936]
    w2all = R5[:, 1936:1952]
    wsum = R5[:, 1952:1968]
    neg30 = R5[:, 1968:1976]
    msk8i = R5[:, 1976:1984].bitcast(I32)
    cnd8 = R5[:, 1984:1992]
    rtblk = R5[:, 1992:2056].rearrange("p (b c) -> p b c", c=4)
    probsT = [bfv(R5, 1344, 512), bfv(R5, 1600, 512), bfv(R3, 0, 512)]

    qb_s = L["qb_s"]; kb_s = L["kb_s"]

    p_n = PS[5][0:1, 0:512]
    tpbf = PS[1][:].bitcast(BF16)

    def ld_small(eng, inc):
        inc(eng.dma_start(out=qb_s[:], in_=ins["qb_fm"]))
        inc(eng.dma_start(out=kb_s[:], in_=ins["kb_fm"]))
        inc(eng.dma_start(out=obS, in_=ins["ob"]))
        inc(eng.dma_start(out=n1wS, in_=ins["norm1_w"]))
        inc(eng.dma_start(out=vbS, in_=ins["vb"]))
        inc(eng.dma_start(out=gwS[:], in_=ins["gate_w_fm"]))
    t_ls = P.step("sync", ld_small, after=[prev], dma=6, tag="gq_ls")

    def reps_mm(eng):
        eng.matmul(PS[0][:, 0:512], ones1[0:1, 0:128], obS[0:1, 0:512],
                   start=True, stop=True)
        eng.matmul(PS[2][:, 0:512], ones1[0:1, 0:128], obS[0:1, 512:1024],
                   start=True, stop=True)
        eng.matmul(PS[3][:, 0:512], ones1[0:1, 0:128], n1wS[0:1, 0:512],
                   start=True, stop=True)
        eng.matmul(PS[4][:, 0:512], ones1[0:1, 0:128], n1wS[0:1, 512:1024],
                   start=True, stop=True)
        eng.matmul(PS[6][:, 0:256], ones1[0:1, 0:128], vbS[0:1, 0:256],
                   start=True, stop=True)
    t_rm = P.step("tensor", reps_mm, after=[t_ls, prev], tag="gq_rm")

    def reps_ev(eng):
        eng.tensor_copy(obrep[:, 0:512], PS[0][:, 0:512])
        eng.tensor_copy(obrep[:, 512:1024], PS[2][:, 0:512])
        eng.tensor_copy(n1wrep[:, 0:512], PS[3][:, 0:512])
        eng.tensor_copy(n1wrep[:, 512:1024], PS[4][:, 0:512])
        eng.tensor_copy(vbrep[:], PS[6][:, 0:256])
        eng.memset(epscol, 1e-6)
        eng.memset(neg30, -1e30)
    t_rep = P.step("vector", reps_ev, after=[t_rm], tag="gq_rep")

    prev = P.step("tensor", lambda e: None, after=[t_rep, prev], tag="gq_st")

    xdma_toks = []
    rt_toks = []
    for ex in range(2):
        prev = emit_gp(nc, P, ins, ex, x1_tm, x1FMw, qwS, kwS, vwS, qb_s,
                       kb_s, vbrep, bounce, qfm, kfm, vtm, PS, tpbf, ident,
                       prev)
        prev = emit_gs(nc, P, ex, qv, kv, vT, sqb, OFMp, probsT, scr_rc,
                       scrq2, scrq_a, maxs, rbc, qfm, kfm, vtm, ofm, PS, p_n,
                       ones1, ones_col, fs, prev)
        prev, t_xd = emit_go(nc, P, ins, ex, x1_tm, OallS, owS, ofm, obrep,
                             n1wrep, gwS, a_f32, x2fin, xfmT, s2a, s2b, s2,
                             rstd, epscol, zTM, x2loc, PS, identf, prev)
        xdma_toks.append(t_xd)

        def ag(eng, ex=ex):
            eng.collective_compute(
                "AllGather", OP.bypass, replica_groups=groups,
                ins=[x2loc.ap()[ex]], outs=[byex.ap()[ex]],
            ).then_inc(sem_coll, 1)
        coll_n[0] += 1
        P.step("gpsimd", ag, after=[t_xd], tag=f"gq_ag{ex}")

        prev, t_rt = emit_gating(nc, P, ex, zTM, m1all, m2all, e1all, e2all,
                                 w1all, w2all, wsum, neg30, msk8i, cnd8,
                                 rtblk, iota256, big128, rt_in, tap, prev)
        rt_toks.append(t_rt)

    def agr(eng):
        eng.collective_compute(
            "AllGather", OP.bypass, replica_groups=groups,
            ins=[rt_in.ap()], outs=[rt_all.ap()]).then_inc(sem_coll, 1)
    coll_n[0] += 1
    P.step("gpsimd", agr, after=rt_toks, tag="gq_agr")

    emit_gqa.handles = dict(byex=byex, rt_all=rt_all, x2loc=x2loc,
                            sem_coll=sem_coll, n_coll=coll_n[0])
    return prev


def emit_gp(nc, P, ins, ex, x1_tm, x1FMw, qwS, kwS, vwS, qb_s, kb_s, vbrep,
            bounce, qfm, kfm, vtm, PS, tpbf, ident, prev):
    B0 = 16 * ex

    def ldw(eng, inc):
        inc(eng.dma_start(out=qwS[:],
                          in_=ins["qw"].rearrange("(c p) j -> p c j", p=128)))
        inc(eng.dma_start(out=kwS[:],
                          in_=ins["kw"].rearrange("(c p) j -> p c j", p=128)))
        inc(eng.dma_start(out=vwS[:],
                          in_=ins["vw"].rearrange("(c p) j -> p c j", p=128)))
    t_w = P.step("gpsimd", ldw, after=[prev], dma=3, tag=f"gp{ex}w")
    t_wq = t_w

    qdst = qfm.rearrange("(pr p) t -> p pr t", p=128)
    kdst = kfm.rearrange("(pr p) t -> p pr t", p=128)
    vdst = vtm.rearrange("(blk p) v -> p blk v", p=128)
    t_ev_prev = [t_w, t_w]
    t_dma_prev = t_w
    for w in range(4):
        t_tp_ev = []
        for b4 in range(4):
            def tpp(eng, w=w, b4=b4):
                for c in range(8):
                    eng.transpose(tpbf[:, 128 * c:128 * (c + 1)],
                                  x1_tm[:, B0 + w * 4 + b4,
                                        128 * c:128 * (c + 1)], ident[:])
            t_t = P.step("tensor", tpp,
                         after=[prev] + t_tp_ev[-1:], tag=f"gp{ex}t{w}.{b4}")

            def tpe(eng, w=w, b4=b4):
                for c in range(8):
                    eng.tensor_copy(x1FMw[:, c, 128 * b4:128 * (b4 + 1)],
                                    tpbf[:, 128 * c:128 * (c + 1)])
            t_tp_ev.append(P.step("vector", tpe, after=[t_t],
                                  tag=f"gp{ex}e{w}.{b4}"))
        t_x1w = t_tp_ev[-1]

        jobs = []
        for pr in range(8):
            jobs.append(("q", pr))
        jobs += [("k", 0), ("k", 1)]
        jobs += [("v", 0), ("v", 1), ("v", 2), ("v", 3)]
        for ji, (kind, idx) in enumerate(jobs):
            bank = PS[[0, 2][ji % 2]]
            bb = bounce[ji % 2]

            def mm(eng, kind=kind, idx=idx, bank=bank, w=w):
                for c in range(8):
                    if kind == "q":
                        eng.matmul(bank[:, 0:512],
                                   qwS[:, c, 128 * idx:128 * (idx + 1)],
                                   x1FMw[:, c, :], start=(c == 0),
                                   stop=(c == 7))
                    elif kind == "k":
                        eng.matmul(bank[:, 0:512],
                                   kwS[:, c, 128 * idx:128 * (idx + 1)],
                                   x1FMw[:, c, :], start=(c == 0),
                                   stop=(c == 7))
                    else:
                        eng.matmul(bank[:, 0:256],
                                   x1FMw[:, c, 128 * idx:128 * (idx + 1)],
                                   vwS[:, c, :], start=(c == 0),
                                   stop=(c == 7))
            t_mm = P.step("tensor", mm,
                          after=[t_x1w, t_wq, t_w] + t_ev_prev[-2:-1],
                          tag=f"gp{ex}m{w}.{ji}")

            if kind == "v":
                def ev(eng, idx=idx, bank=bank, bb=bb):
                    eng.tensor_tensor(bb[:, 0:256], bank[:, 0:256],
                                      vbrep[:], op=OP.add)
            elif kind == "q":
                def ev(eng, idx=idx, bank=bank, bb=bb):
                    eng.tensor_scalar(bb[:, 0:512], bank[:, 0:512],
                                      qb_s[:, idx:idx + 1], None, op0=OP.add)
            else:
                def ev(eng, idx=idx, bank=bank, bb=bb):
                    eng.tensor_scalar(bb[:, 0:512], bank[:, 0:512],
                                      kb_s[:, idx:idx + 1], None, op0=OP.add)
            eng_name = "vector"
            t_ev = P.step(eng_name, ev, after=[t_mm, t_dma_prev],
                          tag=f"gp{ex}v{w}.{ji}")
            t_ev_prev.append(t_ev)

            def st(eng, inc, kind=kind, idx=idx, bb=bb, w=w):
                if kind == "q":
                    inc(eng.dma_start(out=qdst[:, idx, 512 * w:512 * (w + 1)],
                                      in_=bb[:, 0:512]))
                elif kind == "k":
                    inc(eng.dma_start(out=kdst[:, idx, 512 * w:512 * (w + 1)],
                                      in_=bb[:, 0:512]))
                else:
                    inc(eng.dma_start(out=vdst[:, w * 4 + idx, :],
                                      in_=bb[:, 0:256]))
            t_dma_prev = P.step("sync", st, after=[t_ev], dma=1,
                                tag=f"gp{ex}s{w}.{ji}")
        prev = t_dma_prev
    return P.step("tensor", lambda e: None, after=[prev], tag=f"gp{ex}fin")


def emit_gs(nc, P, ex, qv, kv, vT, sqb, OFMp, probsT, scr_rc, scrq2, scrq_a,
            maxs, rbc, qfm, kfm, vtm, ofm, PS, p_n, ones1, ones_col, fs,
            prev):
    odst = ofm.rearrange("(pr p) t -> p pr t", p=128)
    vsrc = vtm.rearrange("(blk p) v -> p blk v", p=128)
    sst, sex, spv = fs["st"], fs["ex"], fs["pv"]
    cst, cex, cpv = fs["cst"], fs["cex"], fs["cpv"]
    last_use_q = {0: prev, 1: prev}
    last_use_k = {0: prev, 1: prev}
    t_onorm = [prev, prev]
    t_ofm_dma = prev
    t_qsq_free = prev
    t_pe_last = prev
    pending = None

    for g in range(4):
        gs = g % 2

        def ldk(eng, inc, g=g, gs=gs):
            inc(eng.dma_start(out=kv[gs][0:64, :],
                              in_=kfm[64 * g:64 * (g + 1), :]))
            inc(eng.dma_start(
                out=vT[gs][:, :, 0:64],
                in_=vsrc[:, :, 64 * g:64 * (g + 1)]))
        t_lk = P.step("sync", ldk, after=[last_use_k[gs], prev], dma=2,
                      tag=f"gs{ex}lk{g}")

        def kones(eng, gs=gs):
            eng.memset(kv[gs][64:65, :], 1.0)
            eng.memset(vT[gs][:, :, 64:65], 1.0)
        t_ko = P.step("vector", kones, after=[t_lk], tag=f"gs{ex}ko{g}")

        def ksq(eng, gs=gs):
            eng.activation(out=sqb[:], in_=kv[gs][0:64, :], func=AF.Square)
        t_kq = P.step("scalar", ksq, after=[t_ko, t_qsq_free],
                      tag=f"gs{ex}kq{g}")

        st0, pv0 = cst[0], cpv[0]

        def nmm(eng, st0=st0, pv0=pv0):
            for w4 in range(4):
                if w4 >= 1:
                    eng.wait_ge(spv, pv0 + w4)
                eng.matmul(p_n, ones_col[0:64, :],
                           sqb[:, 512 * w4:512 * (w4 + 1)],
                           start=True, stop=True).then_inc(sst, 1)
        t_nm = P.step("tensor", nmm, after=[t_kq, t_pe_last],
                      tag=f"gs{ex}km{g}")
        cst[0] += 4
        t_pe_last = t_nm

        def nmx(eng, st0=st0):
            for w4 in range(4):
                eng.wait_ge(sst, st0 + w4 + 1)
                eng.tensor_reduce(maxs[0:1, w4:w4 + 1], p_n, axis=AX.X,
                                  op=OP.max).then_inc(spv, 1)
        t_mx = P.step("vector", nmx, after=[t_kq], tag=f"gs{ex}kx{g}")
        cpv[0] += 4

        def kmax(eng):
            eng.tensor_reduce(maxs[0:1, 4:5], maxs[0:1, 0:4], axis=AX.X,
                              op=OP.max)
        t_km = P.step("vector", kmax, after=[t_mx], tag=f"gs{ex}kf{g}")

        def ksqrt(eng):
            eng.activation(out=maxs[0:1, 5:6], in_=maxs[0:1, 4:5],
                           func=AF.Sqrt)
        t_maxk = P.step("scalar", ksqrt, after=[t_km], tag=f"gs{ex}ks{g}")

        for hh in range(4):
            h = 4 * g + hh
            hs = h % 2

            def ldq(eng, inc, h=h, hs=hs):
                inc(eng.dma_start(out=qv[hs][0:64, :],
                                  in_=qfm[64 * h:64 * (h + 1), :]))
            t_lq = P.step("sync", ldq, after=[last_use_q[hs], prev], dma=1,
                          tag=f"gs{ex}lq{h}")

            def qsq(eng, hs=hs):
                eng.activation(out=sqb[:], in_=qv[hs][0:64, :],
                               func=AF.Square)
            t_qq = P.step("scalar", qsq, after=[t_lq, t_maxk, t_qsq_free],
                          tag=f"gs{ex}qq{h}")

            st0, ex0, pv0 = cst[0], cex[0], cpv[0]

            def qmm(eng, st0=st0, ex0=ex0):
                for w4 in range(4):
                    if w4 >= 1:
                        eng.wait_ge(sex, ex0 + w4)
                    eng.matmul(p_n, ones_col[0:64, :],
                               sqb[:, 512 * w4:512 * (w4 + 1)],
                               start=True, stop=True).then_inc(sst, 1)
            t_qm = P.step("tensor", qmm, after=[t_qq, t_pe_last],
                          tag=f"gs{ex}qm{h}")
            cst[0] += 4
            t_pe_last = t_qm
            t_qsq_free = t_qm

            def qsr(eng, st0=st0, pv0=pv0):
                for w4 in range(4):
                    eng.wait_ge(sst, st0 + w4 + 1)
                    if w4 >= 1:
                        eng.wait_ge(spv, pv0 + w4)
                    eng.activation(out=scrq_a, in_=p_n,
                                   func=AF.Sqrt).then_inc(sex, 1)
            t_qs = P.step("scalar", qsr, after=[t_qq], tag=f"gs{ex}qs{h}")
            cex[0] += 4

            def qev(eng, ex0=ex0, hs=hs):
                for w4 in range(4):
                    eng.wait_ge(sex, ex0 + w4 + 1)
                    eng.tensor_scalar(
                        qv[hs][64:65, 512 * w4:512 * (w4 + 1)], scrq_a,
                        maxs[0:1, 5:6], -1.0, op0=OP.mult,
                        op1=OP.mult).then_inc(spv, 1)
            t_qn = P.step("vector", qev, after=[t_qq], tag=f"gs{ex}qe{h}")
            cpv[0] += 4

            for w in range(4):
                pvb = PS[2 + w % 2]
                st0, ex0, pv0 = cst[0], cex[0], cpv[0]

                def pemm(eng, w=w, hs=hs, gs=gs, ex0=ex0, pvb=pvb):
                    for k2 in range(8):
                        if ex0 + 2 * k2 > 0:
                            eng.wait_ge(sex, ex0 + 2 * k2)
                        for j in range(2):
                            blk = 2 * k2 + j
                            eng.matmul(PS[j][:, 0:512],
                                       kv[gs][:, 128 * blk:128 * (blk + 1)],
                                       qv[hs][:, 512 * w:512 * (w + 1)],
                                       start=True, stop=True
                                       ).then_inc(sst, 1)
                        if k2 >= 1:
                            for j in range(2):
                                blk = 2 * (k2 - 1) + j
                                eng.matmul(pvb[0:65, 0:512],
                                           vT[gs][:, blk, :],
                                           probsT[blk % 3][:],
                                           start=(blk == 0), stop=False
                                           ).then_inc(spv, 1)
                    eng.wait_ge(sex, ex0 + 16)
                    eng.matmul(pvb[0:65, 0:512], vT[gs][:, 14, :],
                               probsT[14 % 3][:], start=False, stop=False
                               ).then_inc(spv, 1)
                    eng.matmul(pvb[0:65, 0:512], vT[gs][:, 15, :],
                               probsT[15 % 3][:], start=False, stop=True
                               ).then_inc(spv, 1)
                t_pe = P.step("tensor", pemm,
                              after=[t_qn, t_ko, t_onorm[w % 2], t_pe_last],
                              tag=f"gs{ex}PE{h}.{w}")
                cst[0] += 16
                cpv[0] += 16
                t_pe_last = t_pe

                def actx(eng, st0=st0, pv0=pv0):
                    for blk in range(16):
                        eng.wait_ge(sst, st0 + blk + 1)
                        eng.wait_ge(spv, pv0 + blk - 2 if blk >= 3 else pv0)
                        eng.activation(out=probsT[blk % 3][:],
                                       in_=PS[blk % 2][:], func=AF.Exp,
                                       scale=0.125).then_inc(sex, 1)
                t_ax = P.step("scalar", actx, after=[t_qs],
                              tag=f"gs{ex}AX{h}.{w}")
                cex[0] += 16

                def emit_epi(pw, t_pe_pw):
                    nonlocal t_pe_last
                    pvbp = PS[2 + pw % 2]

                    def rcp(eng, pvbp=pvbp):
                        eng.reciprocal(scr_rc, pvbp[64:65, 0:512])
                    t_rc = P.step("vector", rcp, after=[t_pe_pw],
                                  tag=f"gs{ex}r{h}.{pw}")

                    def bcast(eng):
                        eng.matmul(PS[6][0:64, 0:512], ones1[0:1, 0:64],
                                   scr_rc, start=True, stop=True)
                    t_bc = P.step("tensor", bcast,
                                  after=[t_rc, t_onorm[(pw + 1) % 2],
                                         t_pe_last],
                                  tag=f"gs{ex}b{h}.{pw}")
                    t_pe_last = t_bc

                    def onorm(eng, h=h, pw=pw, pvbp=pvbp):
                        eng.tensor_copy(rbc[:], PS[6][0:64, 0:512])
                        eng.drain()
                        eng.tensor_tensor(
                            OFMp[(h // 2) % 2][64 * (h % 2):
                                               64 * (h % 2) + 64,
                                               512 * pw:512 * (pw + 1)],
                            pvbp[0:64, 0:512], rbc[:], op=OP.mult)
                    t_onorm[pw % 2] = P.step("vector", onorm,
                                             after=[t_bc, t_ofm_dma],
                                             tag=f"gs{ex}o{h}.{pw}")

                if pending is not None:
                    emit_epi(pending[0], pending[1])
                pending = (w, t_pe)
            emit_epi(pending[0], pending[1])
            pending = None
            last_use_q[hs] = t_pe
            last_use_k[gs] = t_pe
            if h % 2 == 1:
                def ofdma(eng, inc, h=h):
                    inc(eng.dma_start(out=odst[:, h // 2, :],
                                      in_=OFMp[(h // 2) % 2][:]))
                t_ofm_dma = P.step("sync", ofdma,
                                   after=[t_onorm[0], t_onorm[1]], dma=1,
                                   tag=f"gs{ex}od{h}")
    return P.step("tensor", lambda e: None,
                  after=[t_ofm_dma, t_onorm[0], t_onorm[1]],
                  tag=f"gs{ex}fin")


def emit_go(nc, P, ins, ex, x1_tm, OallS, owS, ofm, obrep, n1wrep, gwS,
            a_f32, x2fin, xfmT, s2a, s2b, s2, rstd, epscol, zTM, x2loc, PS,
            identf, prev):
    B0 = 16 * ex

    def ldo(eng, inc):
        inc(eng.dma_start(out=OallS[:],
                          in_=ofm.rearrange("(c p) t -> p c t", p=128)))
    t_lo1 = P.step("sync", ldo, after=[prev], dma=1, tag=f"go{ex}w")

    def ldow(eng, inc):
        inc(eng.dma_start(out=owS[:],
                          in_=ins["ow"].rearrange("(c p) j -> p c j",
                                                  p=128)))
    t_lo = P.step("gpsimd", ldow, after=[prev], dma=1, tag=f"go{ex}ow")

    t_ev = {0: t_lo, 1: t_lo}
    t_x2b = t_lo
    t_xd = t_lo
    t_pz = t_lo
    for blk in range(16):
        for half in range(2):
            def mm(eng, blk=blk, half=half):
                for c in range(8):
                    eng.matmul(PS[half][:, 0:512],
                               OallS[:, c, 128 * blk:128 * (blk + 1)],
                               owS[:, c, 512 * half:512 * (half + 1)],
                               start=(c == 0), stop=(c == 7))
            t_mm = P.step("tensor", mm, after=[t_lo1, t_lo, t_ev[half]],
                          tag=f"go{ex}m{blk}.{half}")

            def ev(eng, blk=blk, half=half):
                eng.tensor_tensor(a_f32[:, 512 * half:512 * (half + 1)],
                                  PS[half][:, 0:512],
                                  obrep[:, 512 * half:512 * (half + 1)],
                                  op=OP.add)
                eng.drain()
                eng.tensor_tensor(x2fin[:, 512 * half:512 * (half + 1)],
                                  a_f32[:, 512 * half:512 * (half + 1)],
                                  x1_tm[:, B0 + blk,
                                        512 * half:512 * (half + 1)],
                                  op=OP.add)
            t_ev[half] = P.step("vector", ev, after=[t_mm, t_x2b],
                                tag=f"go{ex}e{blk}.{half}")

        def sq(eng):
            eng.activation(out=a_f32[:, 0:512], in_=x2fin[:, 0:512],
                           func=AF.Square, accum_out=s2a)
            eng.drain()
            eng.activation(out=a_f32[:, 512:1024], in_=x2fin[:, 512:1024],
                           func=AF.Square, accum_out=s2b)
        t_sq = P.step("scalar", sq, after=[t_ev[0], t_ev[1]],
                      tag=f"go{ex}q{blk}")

        def st(eng):
            eng.tensor_tensor(s2, s2a, s2b, op=OP.add)
        t_s = P.step("vector", st, after=[t_sq], tag=f"go{ex}s{blk}")

        def rs1(eng):
            eng.tensor_scalar(rstd, s2, 1.0 / D, 1e-6, op0=OP.mult,
                              op1=OP.add)
            eng.drain()
            eng.reciprocal(rstd, rstd)
        t_r0 = P.step("vector", rs1, after=[t_s], tag=f"go{ex}r{blk}a")

        def rs2(eng):
            eng.activation(out=rstd, in_=rstd, func=AF.Sqrt)
        t_r = P.step("scalar", rs2, after=[t_r0], tag=f"go{ex}r{blk}")

        def x2b(eng):
            eng.tensor_scalar(a_f32[:], x2fin[:], rstd, None, op0=OP.mult)
            eng.drain()
            eng.tensor_tensor(x2fin[:], a_f32[:], n1wrep[:], op=OP.mult)
        t_x2b = P.step("vector", x2b, after=[t_r, t_xd], tag=f"go{ex}x{blk}")

        def xd(eng, inc, blk=blk, ex=ex):
            inc(eng.dma_start(
                out=x2loc.ap()[ex].rearrange("(b p) f -> p b f", p=128)
                [:, blk, :], in_=x2fin[:]))
        t_xd = P.step("gpsimd", xd, after=[t_x2b], dma=1, tag=f"go{ex}d{blk}")

        t_tp = t_x2b
        for cb in range(4):
            def tpp(eng, cb=cb):
                for j in range(2):
                    c = 2 * cb + j
                    eng.transpose(PS[4][:, 128 * j:128 * (j + 1)],
                                  x2fin[:, 128 * c:128 * (c + 1)], identf[:])
            t_t = P.step("tensor", tpp, after=[t_x2b, t_tp],
                         tag=f"go{ex}t{blk}.{cb}")

            def tpe(eng, cb=cb):
                for j in range(2):
                    eng.tensor_copy(xfmT[j][:], PS[4][:, 128 * j:128 * (j + 1)])
            t_e = P.step("vector", tpe, after=[t_t], tag=f"go{ex}f{blk}.{cb}")

            def gz(eng, cb=cb, blk=blk):
                for j in range(2):
                    c = 2 * cb + j
                    eng.matmul(PS[5][:, 0:8], xfmT[j][:], gwS[:, c, :],
                               start=(c == 0), stop=(c == 7))
            t_tp = P.step("tensor", gz, after=[t_e, t_pz],
                          tag=f"go{ex}g{blk}.{cb}")

        def pzev(eng, blk=blk):
            eng.tensor_copy(zTM[:, blk, :], PS[5][:, 0:8])
        t_pz = P.step("vector", pzev, after=[t_tp], tag=f"go{ex}z{blk}")
    t_fin = P.step("vector", lambda e: None, after=[t_pz, t_xd, t_x2b],
                   tag=f"go{ex}fin")
    return t_fin, t_xd


def emit_gating(nc, P, ex, zTM, m1all, m2all, e1all, e2all, w1all, w2all,
                wsum, neg30, msk8i, cnd8, rtblk, iota256, big128, rt_in,
                tap, prev):
    iota8 = iota256[:, 0:8]
    big8 = big128[:, 0:8]

    def m1r(eng):
        eng.tensor_reduce(m1all[:], zTM[:], axis=AX.X, op=OP.max)
    t = P.step("vector", m1r, after=[prev], tag=f"gt{ex}m1")

    msk8 = msk8i.bitcast(F32)
    for blk in range(16):
        def sel(eng, blk=blk):
            z = zTM[:, blk, :]
            eng.tensor_tensor(msk8[:], z, m1all[:, blk:blk + 1]
                              .to_broadcast([128, 8]), op=OP.is_ge)
            eng.drain()
            eng.tensor_scalar(msk8[:], msk8[:], -1e9, 1e9, op0=OP.mult,
                              op1=OP.add)
            eng.drain()
            eng.tensor_tensor(cnd8[:], msk8[:], iota8, op=OP.add)
            eng.drain()
            eng.tensor_reduce(e1all[:, blk:blk + 1], cnd8[:],
                              axis=AX.X, op=OP.min)
            eng.drain()
            eng.tensor_tensor(msk8[:], iota8, e1all[:, blk:blk + 1]
                              .to_broadcast([128, 8]), op=OP.is_equal)
            eng.drain()
            eng.tensor_scalar(msk8[:], msk8[:], -1e30, None, op0=OP.mult)
            eng.drain()
            eng.tensor_tensor(cnd8[:], z, msk8[:], op=OP.add)
            eng.drain()
            eng.tensor_reduce(m2all[:, blk:blk + 1], cnd8[:],
                              axis=AX.X, op=OP.max)
            eng.drain()
            eng.tensor_tensor(msk8[:], cnd8[:], m2all[:, blk:blk + 1]
                              .to_broadcast([128, 8]), op=OP.is_ge)
            eng.drain()
            eng.tensor_scalar(msk8[:], msk8[:], -1e9, 1e9, op0=OP.mult,
                              op1=OP.add)
            eng.drain()
            eng.tensor_tensor(cnd8[:], msk8[:], iota8, op=OP.add)
            eng.drain()
            eng.tensor_reduce(e2all[:, blk:blk + 1], cnd8[:],
                              axis=AX.X, op=OP.min)
            eng.drain()
        t = P.step("vector", sel, after=[t], tag=f"gt{ex}s{blk}")

    def sig(eng):
        eng.activation(out=m1all[:], in_=m1all[:], func=AF.Sigmoid)
        eng.drain()
        eng.activation(out=m2all[:], in_=m2all[:], func=AF.Sigmoid)
    t = P.step("scalar", sig, after=[t], tag=f"gt{ex}sg")

    def wts(eng):
        eng.tensor_tensor(wsum[:], m1all[:], m2all[:], op=OP.add)
        eng.drain()
        eng.tensor_scalar(wsum[:], wsum[:], 1e-6, None, op0=OP.add)
        eng.drain()
        eng.reciprocal(wsum[:], wsum[:])
        eng.drain()
        eng.tensor_tensor(w1all[:], m1all[:], wsum[:], op=OP.mult)
        eng.tensor_tensor(w2all[:], m2all[:], wsum[:], op=OP.mult)
    t = P.step("vector", wts, after=[t], tag=f"gt{ex}w")

    def asm(eng):
        eng.tensor_copy(rtblk[:, :, 0], e1all[:])
        eng.tensor_copy(rtblk[:, :, 1], e2all[:])
        eng.tensor_copy(rtblk[:, :, 2], w1all[:])
        eng.tensor_copy(rtblk[:, :, 3], w2all[:])
    t = P.step("vector", asm, after=[t], tag=f"gt{ex}a")
    t = tap(f"rt_{ex}", rtblk[:], t, [128, 16, 4], F32)

    def rtd(eng, inc, ex=ex):
        inc(eng.dma_start(
            out=rt_in.ap().rearrange("(b p) c -> p b c", p=128)
            [:, 16 * ex:16 * (ex + 1), :],
            in_=rtblk[:]))
    t_rt = P.step("sync", rtd, after=[t], dma=1, tag=f"gt{ex}d")
    return t, t_rt



F32 = mybir.dt.float32
BF16 = mybir.dt.bfloat16
I32 = mybir.dt.int32
AX = mybir.AxisListType
OP = mybir.AluOpType
AF = mybir.ActivationFunctionType

D = 1024
DFF = 2048
CAP = 10240


def bfv(region, off_f32, n_bf16):
    return region[:, off_f32:off_f32 + n_bf16 // 2].bitcast(BF16)


def emit_moe_pre(nc, P, L, t_start, n_cores=8):
    R5 = L["R5"]
    NTG = n_cores * 4096
    Z = nc.dram_tensor("m_Z", [NTG + 128, D], BF16)
    slotmap = nc.dram_tensor("m_slot", [CAP + 128, 4], F32)
    zeros = bfv(R5, 2056, 512)
    smfill = R5[:, 2320:2324]

    def zfill(eng):
        eng.memset(zeros[:], 0.0)
        eng.memset(smfill[:], 0.0)
        eng.drain()
        eng.memset(smfill[:, 1:2], float(NTG))
    t_zf = P.step("vector", zfill, after=[t_start], tag="mz0")

    zview = Z.ap().rearrange("(n p) d -> p n d", p=128)
    t_zz = t_zf
    for i in range(NTG // 128 + 1):
        def zz(eng, inc, i=i):
            inc(eng.dma_start(out=zview[:, i, 0:512], in_=zeros[:]))
            inc(eng.dma_start(out=zview[:, i, 512:1024], in_=zeros[:]))
        t_zz = P.step("scalar", zz, after=[t_zf, t_zz], dma=2, tag=f"mzz{i}")
    smview = slotmap.ap().rearrange("(n p) c -> p n c", p=128)
    t_sf = t_zf
    for i in range(CAP // 128 + 1):
        def sf(eng, inc, i=i):
            inc(eng.dma_start(out=smview[:, i, :], in_=smfill[:]))
        t_sf = P.step("scalar", sf, after=[t_zf, t_sf], dma=1, tag=f"msf{i}")
    return dict(Z=Z, slotmap=slotmap, t_zz=t_zz, t_sf=t_sf)


def emit_moe(nc, es, P, ins, L, H, pre, tap, x1_tm, out, prev, n_cores=8):
    R1, R2, R3, R4, R5 = L["R1"], L["R2"], L["R3"], L["R4"], L["R5"]
    PS = L["PS"]
    ones1, ident, identf = L["ones1"], L["ident"], L["identf"]
    ones_col, tri = L["ones_col"], L["tri"]
    rowb = L["rowb"]
    sem_coll = H["sem_coll"]
    byex, rt_all, x2loc = H["byex"], H["rt_all"], H["x2loc"]
    Z, slotmap = pre["Z"], pre["slotmap"]
    NTG = n_cores * 4096
    NBG = NTG // 128
    NC2 = n_cores * 2048
    nch = (NBG + 127) // 128
    groups = [list(range(n_cores))]

    moe_loc = nc.dram_tensor("m_out", [4096, D], BF16)
    x1_hbm = nc.dram_tensor("m_x1", [4096, D], BF16).ap()

    h1FM = bfv(R1, 0, 8192).rearrange("p (f t) -> p f t", f=16)
    xgFM = bfv(R1, 4096, 4096).rearrange("p (c t) -> p c t", c=8)
    xgTM = bfv(R1, 6144, 4096).rearrange("p (b f) -> p b f", b=4)
    rtS = R2[:, 0:1024].rearrange("p (b c) -> p b c", c=4)
    expid_rep = R2[:, 1024:1025]
    expid_st = R2[0:1, 1025:1026]
    colsS = R2[0:1, 1282:1538]
    chunkS = R2[0:1, 1538:1546]
    n2wS = R2[0:1, 2048:3072]
    coloffs = R2[0:1, 3072:3328]
    zstage = bfv(R3, 0, 4096).rearrange("p (b f) -> p b f", b=4)
    keepf = R3[:, 0:256]
    nkeep = R3[:, 256:512]
    sc1 = R3[:, 512:768]
    sc2 = R3[:, 768:1024]
    outstage = R3[:, 2048:3072]
    i1 = R4[:, 0:256]
    i2 = R4[:, 256:512]
    ind = R4[:, 512:768]
    indb = bfv(R4, 768, 256)
    colsb = bfv(R4, 896, 2)
    wsel = R4[:, 900:1156]
    rank = R4[:, 1156:1412]
    pay = R4[:, 1412:2436].rearrange("p (g c) -> p g c", c=4)
    dsti = R3[:, 1280:1536].bitcast(I32)
    iotok = R3[:, 1536:1792]
    colchf = R4[:, 2436:2444]
    excl2 = R4[:, 2444:2446]
    exadd = R4[:, 2446:2447]
    tA = R4[:, 0:1024]
    tB = R4[:, 1024:2048]
    slotS = R5[:, 0:16].rearrange("p (b c) -> p b c", c=4)
    slotI = R5[:, 16:20].bitcast(I32)
    scrA = R5[:, 20:24]
    scrB = R5[:, 24:28]
    scrC = R5[:, 28:32]
    gidxI = R5[:, 32:36].bitcast(I32)
    byexS = R3[:, 1024:1280]
    n2wrep = bfv(R5, 64, 1024)
    lnm = R5[:, 580:596].rearrange("p (b c) -> p b c", c=8)
    xflat = x1_tm[:].rearrange("p b f -> p (b f)")
    w1S = xflat[:, 0:16384].rearrange("p (c j) -> p c j", c=8)
    w2S = xflat[:, 16384:32768].rearrange("p (f j) -> p f j", f=16)
    xbm = bfv(R1, 0, 1024)
    xbx = bfv(R1, 512, 1024)
    xb1 = bfv(R1, 1024, 1024)

    def x1spill(eng, inc):
        inc(eng.dma_start(out=x1_hbm.rearrange("(b p) f -> p b f", p=128),
                          in_=x1_tm[:]))
    t_sp = P.step("sync", x1spill, after=[prev], dma=1, tag="mx1s")

    def wload(eng, inc):
        inc(eng.dma_start(out=w1S[:],
                          in_=ins["w1e"].rearrange("(c p) j -> p c j", p=128)))
        inc(eng.dma_start(out=w2S[:],
                          in_=ins["w2e"].rearrange("(f p) j -> p f j", p=128)))
    t_wl = P.step("gpsimd", wload, after=[t_sp], dma=2, tag="mwl")
    t_wl2 = t_wl

    def rtload(eng, inc):
        eng.wait_ge(sem_coll, 3)
        inc(eng.dma_start(out=rtS[:, 0:NBG, :],
                          in_=rt_all.ap().rearrange("(b p) c -> p b c",
                                                    p=128)))
        inc(eng.dma_start(out=n2wS, in_=ins["norm2_w"]))
        inc(eng.dma_start(out=expid_st, in_=ins["expid"]))
        inc(eng.dma_start(out=iotok[:, 0:NBG],
                          in_=ins["iota_tok"][:, 0:NBG]))
        inc(eng.dma_start(out=byexS[:, 0:NBG],
                          in_=ins["iota_byex"][:, 0:NBG]))
    t_rl = P.step("sync", rtload, after=[prev], dma=5, tag="mrl")

    def repid(eng):
        eng.matmul(PS[0][:, 0:1], ones1[0:1, 0:128], expid_st,
                   start=True, stop=True)
    t_ri = P.step("tensor", repid, after=[t_rl, prev], tag="mri")

    def repev(eng):
        eng.tensor_copy(expid_rep[:], PS[0][:, 0:1])
    t_re = P.step("vector", repev, after=[t_ri], tag="mre")

    def indf(eng):
        eng.tensor_scalar(i1[:, 0:NBG], rtS[:, 0:NBG, 0], expid_rep[:], None,
                          op0=OP.is_equal)
        eng.tensor_scalar(i2[:, 0:NBG], rtS[:, 0:NBG, 1], expid_rep[:], None,
                          op0=OP.is_equal)
        eng.drain()
        eng.tensor_tensor(ind[:, 0:NBG], i1[:, 0:NBG], i2[:, 0:NBG],
                          op=OP.max)
        eng.drain()
        eng.tensor_copy(indb[:, 0:NBG], ind[:, 0:NBG])
        eng.tensor_tensor(i1[:, 0:NBG], i1[:, 0:NBG], rtS[:, 0:NBG, 2],
                          op=OP.mult)
        eng.tensor_tensor(i2[:, 0:NBG], i2[:, 0:NBG], rtS[:, 0:NBG, 3],
                          op=OP.mult)
        eng.drain()
        eng.tensor_tensor(wsel[:, 0:NBG], i1[:, 0:NBG], i2[:, 0:NBG],
                          op=OP.add)
    t_if = P.step("vector", indf, after=[t_re], tag="mif")

    def pref(eng):
        eng.matmul(PS[0][:, 0:NBG], tri[:], indb[:, 0:NBG],
                   start=True, stop=True)
        eng.matmul(PS[1][0:1, 0:NBG], ones_col[:], indb[:, 0:NBG],
                   start=True, stop=True)
    t_pf = P.step("tensor", pref, after=[t_if], tag="mpf")

    def colev(eng):
        eng.memset(colsS[0:1, 0:256], 0.0)
        eng.drain()
        eng.tensor_copy(colsS[0:1, 0:NBG], PS[1][0:1, 0:NBG])
    t_ce = P.step("vector", colev, after=[t_pf], tag="mce")

    def cb1(eng, inc):
        inc(eng.dma_start(out=rowb[0:256], in_=colsS[0:1, 0:256]))
    t_b1 = P.step("sync", cb1, after=[t_ce], dma=1, tag="mb1")

    def cb2(eng, inc):
        inc(eng.dma_start(
            out=colchf[:, 0:2],
            in_=rowb[0:256].rearrange("(a p) -> p a", p=128),
            allow_slow_non_contiguous=True))
    t_b2 = P.step("gpsimd", cb2, after=[t_b1], dma=1, tag="mb2")

    def cbc(eng):
        eng.tensor_copy(colsb[:, 0:2], colchf[:, 0:2])
    t_bc = P.step("vector", cbc, after=[t_b2], tag="mbc")

    def excl(eng):
        eng.matmul(PS[2][:, 0:2], tri[:], colsb[:, 0:2],
                   start=True, stop=True)
        eng.matmul(PS[3][0:1, 0:2], ones_col[:], colsb[:, 0:2],
                   start=True, stop=True)
    t_ex = P.step("tensor", excl, after=[t_bc], tag="mex")

    def exev(eng):
        eng.tensor_tensor(excl2[:, 0:2], PS[2][:, 0:2],
                          colchf[:, 0:2], op=OP.subtract)
    t_exv = P.step("vector", exev, after=[t_ex], tag="mexv")

    def totev(eng):
        eng.tensor_copy(chunkS[0:1, 0:2], PS[3][0:1, 0:2])
    t_tv = P.step("vector", totev, after=[t_ex], tag="mtv")

    def totmm(eng):
        eng.matmul(PS[4][:, 0:1], ones1[0:1, 0:128], chunkS[0:1, 0:1],
                   start=True, stop=True)
    t_tm = P.step("tensor", totmm, after=[t_tv], tag="mtm")

    def totadd(eng):
        eng.tensor_copy(exadd[:], PS[4][:, 0:1])
        eng.drain()
        eng.tensor_tensor(excl2[:, 1:2], excl2[:, 1:2], exadd[:],
                          op=OP.add)
    t_exv = P.step("vector", totadd, after=[t_tm, t_exv], tag="mta")

    def cb3(eng, inc):
        inc(eng.dma_start(out=rowb[0:256].rearrange("(a p) -> p a", p=128),
                          in_=excl2[:, 0:2],
                          allow_slow_non_contiguous=True))
    t_b3 = P.step("sync", cb3, after=[t_exv], dma=1, tag="mb3")

    def cb4(eng, inc):
        inc(eng.dma_start(out=coloffs[0:1, 0:256], in_=rowb[0:256]))
    t_b4 = P.step("gpsimd", cb4, after=[t_b3], dma=1, tag="mb4")

    def offmm(eng):
        eng.matmul(PS[1][:, 0:NBG], ones1[0:1, 0:128], coloffs[0:1, 0:NBG],
                   start=True, stop=True)
    t_om = P.step("tensor", offmm, after=[t_b4, t_ce], tag="mom")

    def rankf(eng):
        eng.tensor_tensor(rank[:, 0:NBG], PS[0][:, 0:NBG], ind[:, 0:NBG],
                          op=OP.subtract)
        eng.drain()
        eng.tensor_tensor(rank[:, 0:NBG], rank[:, 0:NBG], PS[1][:, 0:NBG],
                          op=OP.add)
    t_rk = P.step("vector", rankf, after=[t_om], tag="mrk")

    iota_p = None

    def payf(eng):
        N = NBG
        eng.tensor_scalar(keepf[:, 0:N], rank[:, 0:N], float(CAP) - 0.5,
                          None, op0=OP.is_le)
        eng.drain()
        eng.tensor_tensor(keepf[:, 0:N], keepf[:, 0:N], ind[:, 0:N],
                          op=OP.mult)
        eng.drain()
        eng.tensor_scalar(nkeep[:, 0:N], keepf[:, 0:N], -1.0, 1.0,
                          op0=OP.mult, op1=OP.add)
        eng.drain()
        eng.tensor_tensor(sc1[:, 0:N], rank[:, 0:N], keepf[:, 0:N],
                          op=OP.mult)
        eng.tensor_scalar(sc2[:, 0:N], nkeep[:, 0:N], iotok[:, 0:1],
                          None, op0=OP.mult)
        eng.drain()
        eng.tensor_tensor(sc1[:, 0:N], sc1[:, 0:N], sc2[:, 0:N], op=OP.add)
        eng.tensor_scalar(sc2[:, 0:N], nkeep[:, 0:N], float(CAP), None,
                          op0=OP.mult)
        eng.drain()
        eng.tensor_tensor(sc1[:, 0:N], sc1[:, 0:N], sc2[:, 0:N], op=OP.add)
        eng.drain()
        eng.tensor_copy(dsti[:, 0:N], sc1[:, 0:N])
        eng.tensor_copy(pay[:, 0:N, 0], byexS[:, 0:N])
        eng.tensor_copy(pay[:, 0:N, 1], iotok[:, 0:N])
        eng.tensor_copy(pay[:, 0:N, 2], wsel[:, 0:N])
        eng.memset(pay[:, 0:N, 3], 0.0)
    t_py = P.step("vector", payf, after=[t_rk], tag="mpy")

    def smscat(eng, inc):
        for g in range(NBG):
            inc(eng.indirect_dma_start(
                slotmap.ap(),
                bass.IndirectOffsetOnAxis(dsti[:, g:g + 1], 0),
                pay[:, g, :], None))
    t_sc = P.step("gpsimd", smscat, after=[t_py, pre["t_sf"]], dma=NBG,
                  tag="msc")

    def n2mm(eng):
        eng.matmul(PS[5][:, 0:512], ones1[0:1, 0:128], n2wS[0:1, 0:512],
                   start=True, stop=True)
        eng.matmul(PS[6][:, 0:512], ones1[0:1, 0:128], n2wS[0:1, 512:1024],
                   start=True, stop=True)
    t_n2 = P.step("tensor", n2mm, after=[t_rl, t_om], tag="mn2")

    def n2ev(eng):
        eng.tensor_copy(n2wrep[:, 0:512], PS[5][:, 0:512])
        eng.tensor_copy(n2wrep[:, 512:1024], PS[6][:, 0:512])
    t_n2e = P.step("vector", n2ev, after=[t_n2], tag="mn2e")

    tpbf = PS[0][:].bitcast(BF16)
    byex2 = byex.ap().rearrange("e r d -> (e r) d")
    NWND = CAP // 512
    t_wend = P.step("tensor", lambda e: None, after=[t_n2e, t_sc], tag="mws")
    t_zs = pre["t_zz"]
    t_gel = {0: t_wend, 1: t_wend}

    for wnd in range(NWND):
        def sl(eng, inc, wnd=wnd):
            if wnd == 0:
                eng.wait_ge(sem_coll, 2)
            inc(eng.dma_start(
                out=slotS[:],
                in_=slotmap.ap().rearrange("(b p) c -> p b c", p=128)
                [:, 4 * wnd:4 * wnd + 4, :]))
        t_sl = P.step("sync", sl, after=[t_sc, t_wend], dma=1,
                      tag=f"mw{wnd}s")

        def idx(eng, wnd=wnd):
            tok = slotS[:, :, 1]
            eng.tensor_copy(gidxI[:], slotS[:, :, 0])
            eng.tensor_scalar(scrB[:], tok, float(NTG) - 0.5, None,
                              op0=OP.is_ge)
            eng.drain()
            eng.tensor_scalar(scrB[:], scrB[:], iotok[:, 0:1], None,
                              op0=OP.mult)
            eng.drain()
            eng.tensor_tensor(scrA[:], tok, scrB[:], op=OP.add)
            eng.drain()
            eng.tensor_copy(slotI[:], scrA[:])
        t_idx = P.step("vector", idx, after=[t_sl, t_zs], tag=f"mw{wnd}i")

        def gath(eng, inc, wnd=wnd):
            for b in range(4):
                inc(eng.indirect_dma_start(
                    xgTM[:, b, :], None, byex2,
                    bass.IndirectOffsetOnAxis(gidxI[:, b:b + 1], 0)))
        t_g = P.step("gpsimd", gath, after=[t_idx, t_wend], dma=4,
                     tag=f"mw{wnd}g")

        t_tp_prev = t_g
        for b in range(4):
            def tpp(eng, b=b):
                for c in range(8):
                    eng.transpose(tpbf[:, 128 * c:128 * (c + 1)],
                                  xgTM[:, b, 128 * c:128 * (c + 1)],
                                  ident[:])
            t_tp = P.step("tensor", tpp, after=[t_g, t_wend, t_tp_prev],
                          tag=f"mw{wnd}t{b}")

            def tpe(eng, b=b):
                for c in range(8):
                    eng.tensor_copy(xgFM[:, c, 128 * b:128 * (b + 1)],
                                    tpbf[:, 128 * c:128 * (c + 1)])
            t_tp_prev = P.step("vector", tpe, after=[t_tp],
                               tag=f"mw{wnd}e{b}")
        t_tpe = t_tp_prev

        for f in range(16):
            bank = PS[f % 2]

            def h1m(eng, f=f, bank=bank):
                for c in range(8):
                    eng.matmul(bank[:, 0:512],
                               w1S[:, c, 128 * f:128 * (f + 1)],
                               xgFM[:, c, :], start=(c == 0), stop=(c == 7))
            t_m = P.step("tensor", h1m,
                         after=[t_tpe, t_wl, t_gel[f % 2], t_wend],
                         tag=f"mw{wnd}h1{f}")

            def gel(eng, f=f, bank=bank):
                eng.activation(out=h1FM[:, f, :], in_=bank[:, 0:512],
                               func=AF.Gelu, scale=1.0)
            t_gel[f % 2] = P.step("scalar", gel, after=[t_m],
                                  tag=f"mw{wnd}g{f}")
        t_h1 = P.step("tensor", lambda e: None,
                      after=[t_gel[0], t_gel[1], t_wl2], tag=f"mw{wnd}h1f")

        t_ev_all = []
        for ts4 in range(4):
            for half in range(2):
                bi = ts4 * 2 + half
                bank = PS[bi]

                def h2m(eng, ts4=ts4, half=half, bank=bank):
                    for f in range(16):
                        eng.matmul(
                            bank[:, 0:512],
                            h1FM[:, f, 128 * ts4:128 * (ts4 + 1)],
                            w2S[:, f, 512 * half:512 * (half + 1)],
                            start=(f == 0), stop=(f == 15))
                t_m2 = P.step("tensor", h2m, after=[t_h1],
                              tag=f"mw{wnd}h2{bi}")

                def h2e(eng, ts4=ts4, half=half, bank=bank):
                    eng.tensor_scalar(
                        zstage[:, ts4, 512 * half:512 * (half + 1)],
                        bank[:, 0:512], slotS[:, ts4, 2:3], None,
                        op0=OP.mult)
                tok_ev = P.step("vector", h2e, after=[t_m2, t_zs],
                                tag=f"mw{wnd}z{bi}")
                t_ev_all.append(tok_ev)

        def zscat(eng, inc, wnd=wnd):
            for b in range(4):
                inc(eng.indirect_dma_start(
                    Z.ap(), bass.IndirectOffsetOnAxis(slotI[:, b:b + 1], 0),
                    zstage[:, b, :], None))
        t_zs = P.step("gpsimd", zscat, after=t_ev_all + [pre["t_zz"]],
                      dma=4, tag=f"mw{wnd}zs")
        t_wend = P.step("tensor", lambda e: None, after=t_ev_all,
                        tag=f"mw{wnd}end")

    def rsc(eng):
        eng.collective_compute(
            "ReduceScatter", OP.add, replica_groups=groups,
            ins=[Z.ap()[0:NTG]], outs=[moe_loc.ap()]).then_inc(sem_coll, 1)
    t_rs = P.step("gpsimd", rsc, after=[t_zs], tag="mrs")
    n_coll = H["n_coll"] + 1

    t_ld = prev
    t_st = t_wend
    t_ev_prev = t_wend
    for blk in range(32):
        e = blk // 16
        i = blk % 16

        def ld(eng, inc, blk=blk, e=e, i=i):
            if blk == 0:
                eng.wait_ge(sem_coll, n_coll)
            inc(eng.dma_start(
                out=xbm[:], in_=moe_loc.ap()
                .rearrange("(b p) f -> p b f", p=128)[:, blk, :]))
            inc(eng.dma_start(
                out=xbx[:], in_=x2loc.ap()[e]
                .rearrange("(b p) f -> p b f", p=128)[:, i, :]))
            inc(eng.dma_start(
                out=xb1[:], in_=x1_hbm
                .rearrange("(b p) f -> p b f", p=128)[:, blk, :]))
        t_ld = P.step("sync", ld, after=[t_rs, t_sp, t_ev_prev, t_ld],
                      dma=3, tag=f"mt{blk}l")

        def a2f(eng, blk=blk):
            eng.tensor_tensor(tA[:], xbm[:], xbx[:], op=OP.add)
        t_a = P.step("vector", a2f, after=[t_ld, t_st], tag=f"mt{blk}a")

        def sq(eng):
            eng.activation(out=tB[:, 0:512], in_=tA[:, 0:512],
                           func=AF.Square, accum_out=lnm[:, 0, 0:1])
            eng.drain()
            eng.activation(out=tB[:, 512:1024], in_=tA[:, 512:1024],
                           func=AF.Square, accum_out=lnm[:, 0, 1:2])
        t_q = P.step("scalar", sq, after=[t_a], tag=f"mt{blk}q")

        def rst(eng):
            eng.tensor_tensor(lnm[:, 0, 2:3], lnm[:, 0, 0:1],
                              lnm[:, 0, 1:2], op=OP.add)
            eng.drain()
            eng.tensor_scalar(lnm[:, 0, 2:3], lnm[:, 0, 2:3], 1.0 / D, 1e-6,
                              op0=OP.mult, op1=OP.add)
            eng.drain()
            eng.reciprocal(lnm[:, 0, 2:3], lnm[:, 0, 2:3])
        t_r1 = P.step("vector", rst, after=[t_q], tag=f"mt{blk}r")

        def rs2f(eng):
            eng.activation(out=lnm[:, 0, 3:4], in_=lnm[:, 0, 2:3],
                           func=AF.Sqrt)
        t_r2 = P.step("scalar", rs2f, after=[t_r1], tag=f"mt{blk}r2")

        def ff(eng):
            eng.tensor_scalar(tB[:], tA[:], lnm[:, 0, 3:4], None,
                              op0=OP.mult)
            eng.drain()
            eng.tensor_tensor(tA[:], tB[:], n2wrep[:], op=OP.mult)
            eng.drain()
            eng.tensor_tensor(tB[:], tA[:], xb1[:], op=OP.add)
        t_f = P.step("vector", ff, after=[t_r2], tag=f"mt{blk}f")

        def lmean(eng):
            eng.tensor_reduce(lnm[:, 1, 0:1], tB[:], axis=AX.X, op=OP.add)
        t_m1 = P.step("vector", lmean, after=[t_f], tag=f"mt{blk}m")

        def lsq(eng):
            eng.activation(out=tA[:, 0:512], in_=tB[:, 0:512],
                           func=AF.Square, accum_out=lnm[:, 1, 1:2])
            eng.drain()
            eng.activation(out=tA[:, 512:1024], in_=tB[:, 512:1024],
                           func=AF.Square, accum_out=lnm[:, 1, 2:3])
        t_q2 = P.step("scalar", lsq, after=[t_f], tag=f"mt{blk}q2")

        def lst(eng):
            Lr = lnm[:, 1, :]
            eng.tensor_tensor(Lr[:, 1:2], Lr[:, 1:2], Lr[:, 2:3], op=OP.add)
            eng.tensor_scalar(Lr[:, 3:4], Lr[:, 0:1], 1.0 / D, None,
                              op0=OP.mult)
            eng.drain()
            eng.tensor_tensor(Lr[:, 4:5], Lr[:, 3:4], Lr[:, 3:4], op=OP.mult)
            eng.tensor_scalar(Lr[:, 5:6], Lr[:, 1:2], 1.0 / D, None,
                              op0=OP.mult)
            eng.drain()
            eng.tensor_tensor(Lr[:, 4:5], Lr[:, 5:6], Lr[:, 4:5],
                              op=OP.subtract)
            eng.drain()
            eng.tensor_scalar(Lr[:, 4:5], Lr[:, 4:5], 1e-5, None, op0=OP.add)
            eng.drain()
            eng.reciprocal(Lr[:, 4:5], Lr[:, 4:5])
        t_s2 = P.step("vector", lst, after=[t_m1, t_q2], tag=f"mt{blk}s")

        def lsr(eng):
            eng.activation(out=lnm[:, 1, 4:5], in_=lnm[:, 1, 4:5],
                           func=AF.Sqrt)
        t_s3 = P.step("scalar", lsr, after=[t_s2], tag=f"mt{blk}s3")

        def lap(eng):
            eng.tensor_scalar(outstage[:], tB[:], lnm[:, 1, 3:4],
                              lnm[:, 1, 4:5], op0=OP.subtract, op1=OP.mult)
        t_ev_prev = P.step("vector", lap, after=[t_s3], tag=f"mt{blk}o")

        def sto(eng, inc, blk=blk):
            inc(eng.dma_start(
                out=out.rearrange("(b p) f -> p b f", p=128)[:, blk, :],
                in_=outstage[:]))
        t_st = P.step("sync", sto, after=[t_ev_prev], dma=1,
                      tag=f"mt{blk}st")
    return P.step("sync", lambda e: None, after=[t_st], tag="mfin")



# version 08 (final)
# speedup vs baseline: 1.0604x; 1.0380x over previous
import numpy as np
import ml_dtypes
from contextlib import ExitStack

import concourse.bass as bass
import concourse.mybir as mybir
import concourse.library_config as libcfg
from concourse.library_overlay import lower_extended_insts
from concourse.bass_utils import run_bass_kernel_spmd

F32 = mybir.dt.float32
BF16 = mybir.dt.bfloat16
I32 = mybir.dt.int32
I16 = mybir.dt.int16
U32 = mybir.dt.uint32
AX = mybir.AxisListType
OP = mybir.AluOpType
AF = mybir.ActivationFunctionType
BF = ml_dtypes.bfloat16

NC = 8
S = 2048
D = 1024
TLOC = 4096
ENGS = ("sync", "scalar", "tensor", "vector", "gpsimd")


class Prog:

    def __init__(self, nc):
        self.nc = nc
        self.steps = {e: [] for e in ENGS}
        self.count = {e: 0 for e in ENGS}
        self.sems = {}

    def alloc_sems(self, es):
        for e in ENGS:
            self.sems[e] = es.enter_context(self.nc.semaphore(f"sem_{e}"))
            self.sems["d_" + e] = es.enter_context(self.nc.semaphore(f"dsem_{e}"))
            self.count["d_" + e] = 0

    def step(self, eng, fn, after=(), dma=0, tag=""):
        waits = [(self.sems[s], v) for (s, v) in after]
        key = ("d_" + eng) if dma else eng
        amount = 16 * dma if dma else 1
        prev_total = self.count[key]
        self.count[key] += amount
        tok = (key, self.count[key])
        self.steps[eng].append((fn, waits, self.sems[key], amount, dma, tag,
                                prev_total))
        return tok

    def barrier(self):
        toks = [(k, self.count[k]) for k in self.count]
        outs = []
        for e in ENGS:
            outs.append(self.step(e, lambda eng: None,
                                  after=[t for t in toks if t[0] != e],
                                  tag="barrier"))
        return outs

    def emit(self, block):
        def mk(eng_name):
            def run(eng):
                for (fn, waits, sem, amount, dma, tag, prev_total) in self.steps[eng_name]:
                    for (s, v) in waits:
                        eng.wait_ge(s, v)
                    if dma:
                        if prev_total:
                            eng.wait_ge(sem, prev_total)
                        state = {"left": amount}

                        def inc(instr, n=16):
                            instr.then_inc(sem, n)
                            state["left"] -= n
                            return instr
                        fn(eng, inc)
                        assert state["left"] == 0, (eng_name, tag, state["left"])
                    else:
                        fn(eng)
                        if eng_name == "sync":
                            eng.nop().then_inc(sem, amount)
                        else:
                            eng.drain().then_inc(sem, amount)
            return run
        block.sync(mk("sync"))
        block.scalar(mk("scalar"))
        block.tensor(mk("tensor"))
        block.vector(mk("vector"))
        block.gpsimd(mk("gpsimd"))

    def final_wait(self, eng):
        toks = [(k, self.count[k]) for k in self.count]
        self.step(eng, lambda E: None,
                  after=[t for t in toks if t[0] != eng], tag="final")


def host_constants(n_cores=NC):
    import jax
    cpu = jax.devices("cpu")[0]
    with jax.default_device(cpu):
        rot = np.asarray(jax.random.normal(jax.random.key(42),
                                           (16, 64, 4, 32), dtype=np.float32))
    c = {}
    c["rot_all"] = np.ascontiguousarray(rot.reshape(16, 64, 128))
    c["ident"] = np.eye(128, dtype=np.float32).astype(BF)
    c["identf"] = np.eye(128, dtype=np.float32)
    c["tri_incl"] = (np.arange(128)[:, None] <= np.arange(128)[None, :]).astype(BF)
    c["tri_excl64"] = (np.arange(64)[:, None] < np.arange(64)[None, :]).astype(np.float32)
    c["ones_1x128f"] = np.ones((1, 128), np.float32)
    c["ones_col_bf"] = np.ones((128, 1), np.float32).astype(BF)
    c["iota32_row"] = np.tile(np.arange(32, dtype=np.float32), (128, 1))
    c["iota32x4_row"] = np.tile(np.arange(32, dtype=np.float32), (128, 4))
    c["big128"] = np.full((128, 128), 1e9, np.float32)
    c["iota256_row"] = np.tile(np.arange(256, dtype=np.float32), (128, 1))
    c["big"] = np.full((128, 32), 1e9, np.float32)
    c["iota_tok"] = np.ascontiguousarray(
        (np.arange(128, dtype=np.float32)[:, None]
         + 128.0 * np.arange(256, dtype=np.float32)[None, :]))
    t = c["iota_tok"].astype(np.int64)
    cc = t >> 12
    ee = (t >> 11) & 1
    c["iota_byex"] = np.ascontiguousarray(
        (t - 2048 * cc + (n_cores * 2048 - 2048) * ee).astype(np.float32))
    no = np.zeros((1, 256), np.float32)
    for n in range(4):
        no[0, n * 64:(n + 1) * 64] = n * 2048.0 - 1.0
    c["nround_off"] = no
    return c


def prep_inputs(inputs, n_cores=NC):
    x = np.asarray(inputs["L0_attn_x"])
    consts = host_constants(n_cores)
    w = {k: np.asarray(v) for k, v in inputs.items() if k != "L0_attn_x"}
    per_core = []
    for c in range(NC):
        xb = x[2 * c:2 * c + 2].reshape(TLOC, D)
        m = dict(
            x_tm=np.ascontiguousarray(xb),
            x_fm=np.ascontiguousarray(xb.T),
            w_qk_t=np.ascontiguousarray(w["L0_attn_w_qk"].T),
            w_v_t=np.ascontiguousarray(w["L0_attn_w_v"].T),
            w_out_t=np.ascontiguousarray(w["L0_attn_w_out"].T),
            qw=w["L0_ffn_qw"], kw=w["L0_ffn_kw"], vw=w["L0_ffn_vw"],
            ow=w["L0_ffn_ow"],
            qb=w["L0_ffn_qb"].reshape(1, -1), kb=w["L0_ffn_kb"].reshape(1, -1),
            vb=w["L0_ffn_vb"].reshape(1, -1), ob=w["L0_ffn_ob"].reshape(1, -1),
            gate_w=w["L0_ffn_gate_w"],
            w1e=np.ascontiguousarray(w["L0_ffn_w1"][c]),
            w2e=np.ascontiguousarray(w["L0_ffn_w2"][c]),
            norm1_w=w["L0_ffn_norm1_w"].reshape(1, -1),
            norm2_w=w["L0_ffn_norm2_w"].reshape(1, -1),
            gtok_base=np.full((1, 1), 4096.0 * c, np.float32),
            expid=np.full((1, 1), float(c), np.float32),
            qb_fm=np.ascontiguousarray(
                w["L0_ffn_qb"].reshape(8, 128).T),
            kb_fm=np.ascontiguousarray(
                w["L0_ffn_kb"].reshape(2, 128).T),
            gate_w_fm=np.ascontiguousarray(
                w["L0_ffn_gate_w"].reshape(8, 128, 8).transpose(1, 0, 2)),
        )
        m.update(consts)
        per_core.append(m)
    return per_core


DT_MAP = {np.dtype(np.float32): F32, np.dtype(BF): BF16,
          np.dtype(np.int32): I32, np.dtype(np.int16): I16}


def build(sample_map, tap_names=(), phase=3, n_cores=NC):
    nc = bass.Bass("TRN2", target_bir_lowering=False, num_devices=NC)
    ins = {k: nc.dram_tensor(k, list(v.shape), DT_MAP[v.dtype],
                             kind="ExternalInput").ap()
           for k, v in sample_map.items()}
    out = nc.dram_tensor("out", [TLOC, D], F32, kind="ExternalOutput").ap()

    es = ExitStack()
    with es:
        P = Prog(nc)
        P.alloc_sems(es)

        def sbf(name, shape, dt=F32):
            return es.enter_context(nc.sbuf_tensor(name, shape, dt))

        def psf(name, shape, dt=F32):
            return es.enter_context(nc.psum_tensor(name, shape, dt))

        tap_outs = []

        def tap(name, ap, tok, shape, dt):
            if name not in tap_names:
                return tok
            t = nc.dram_tensor(f"tap_{name}", list(shape), dt,
                               kind="ExternalOutput").ap()
            tap_outs.append(name)

            def f(eng, inc):
                inc(eng.dma_start(out=t, in_=ap))
            return P.step("sync", f, after=[tok], dma=1, tag=f"tap_{name}")

        build_body(nc, es, P, ins, out, sbf, psf, tap, phase,
                   n_cores=n_cores)
        block = es.enter_context(nc.Block())
        P.emit(block)
    return nc, tap_outs


def build_body(nc, es, P, ins, out, sbf, psf, tap, phase, n_cores=NC):
    x1_tm = sbf("g_x1_tm", [128, 32, D], BF16)
    qb_s = sbf("g_qb", [128, 8])
    kb_s = sbf("g_kb", [128, 2])
    PS = [psf(f"PSb{i}", [128, 512]) for i in range(8)]
    t0 = P.step("gpsimd", lambda e: None, tag="t0")

    t_lsh = emit_lsh(nc, es, P, ins, sbf, PS, tap, x1_tm, t0)

    if phase == 1:
        ob = emit_lsh.outstage[:, 0:2, :]
        tok = t_lsh
        stok = t_lsh
        for i in range(16):
            def cvt(eng, i=i):
                eng.tensor_copy(ob[:], x1_tm[:, 2 * i:2 * i + 2, :])
            tok = P.step("vector", cvt, after=[tok, stok], tag=f"cvt{i}")

            def st(eng, inc, i=i):
                inc(eng.dma_start(
                    out=out.rearrange("(b p) f -> p b f", p=128)[:, 2 * i:2 * i + 2],
                    in_=ob[:]))
            stok = P.step("sync", st, after=[tok], dma=1, tag=f"st{i}")
        P.final_wait("sync")
        return

    L = dict(emit_lsh.ctx)
    L["qb_s"] = qb_s
    L["kb_s"] = kb_s
    if phase >= 3:
        pre = emit_moe_pre(nc, P, L, t_lsh, n_cores=n_cores)
    t_gqa = emit_gqa(nc, es, P, ins, L, tap, x1_tm, t_lsh,
                              n_cores=n_cores)
    H = emit_gqa.handles
    if phase >= 3:
        t_moe = emit_moe(nc, es, P, ins, L, H, pre, tap, x1_tm,
                                  out, t_gqa, n_cores=n_cores)
        P.final_wait("sync")
        return

    if phase == 2:
        R1 = L["R1"]
        xb = R1[:, 4096:4608].bitcast(BF16)
        xf = R1[:, 0:1024]
        tok = t_gqa
        stok = t_gqa
        ltok = t_gqa
        for i in range(32):
            def ld(eng, inc, i=i):
                inc(eng.dma_start(
                    out=xb[:],
                    in_=H["x2loc"].ap()[i // 16].rearrange(
                        "(b p) f -> p b f", p=128)[:, i % 16, :]))
            ltok = P.step("sync", ld, after=[tok, t_gqa], dma=1, tag=f"x2l{i}")

            def cvt(eng, i=i):
                eng.tensor_copy(xf[:], xb[:])
            tok = P.step("vector", cvt, after=[ltok, stok], tag=f"x2c{i}")

            def st(eng, inc, i=i):
                inc(eng.dma_start(
                    out=out.rearrange("(b p) f -> p b f", p=128)[:, i, :],
                    in_=xf[:]))
            stok = P.step("sync", st, after=[tok], dma=1, tag=f"x2s{i}")

        def wcoll(eng):
            eng.wait_ge(H["sem_coll"], H["n_coll"])
        P.step("gpsimd", wcoll, after=[stok], tag="wcoll")
        P.final_wait("sync")
        return


def kernel(**inputs):
    per_core = prep_inputs(inputs)
    nc, _ = build(per_core[0])
    lower_extended_insts(nc)
    res = run_bass_kernel_spmd(nc, per_core, core_ids=list(range(NC)))
    outs = [np.asarray(res.results[c]["out"]) for c in range(NC)]
    return np.concatenate(outs, axis=0).reshape(16, 2048, 1024)


def kernel_taps(inputs, tap_names=(), on_hw=True, n_cores=NC):
    per_core = prep_inputs(inputs, n_cores=n_cores)
    nc, tnames = build(per_core[0], tap_names=tap_names, n_cores=n_cores)
    lower_extended_insts(nc)
    if on_hw:
        res = run_bass_kernel_spmd(nc, per_core[:n_cores],
                                   core_ids=list(range(n_cores)))
        results = [{k: np.asarray(v) for k, v in r.items()} for r in res.results]
    else:
        from simrun import simrun
        results = simrun(nc, per_core[:n_cores],
                         ["out"] + [f"tap_{n}" for n in tnames], num_cores=n_cores)
    return results, tnames




F32 = mybir.dt.float32
BF16 = mybir.dt.bfloat16
I32 = mybir.dt.int32
AX = mybir.AxisListType
OP = mybir.AluOpType
AF = mybir.ActivationFunctionType

S = 2048
D = 1024
NH = 16
DH = 64
NR = 4
NT = 16
NSlot = NR * S
QKV_W = 256
OUT_W = 80


def bfv(region, off_f32, n_bf16):
    return region[:, off_f32:off_f32 + n_bf16 // 2].bitcast(BF16)


def emit_lsh(nc, es, P, ins, sb, ps, tap, x1_tm, t_start):
    rot = sb("lsh_rot", [64, NH, 128], BF16)
    tri = sb("c_tri", [128, 128], BF16)
    tri64 = sb("c_tri64", [64, 64], F32)
    ones1 = sb("c_ones1", [1, 128], F32)
    iota256 = sb("c_iota256", [128, 256], F32)
    iota32x4 = sb("c_iota32x4", [128, 128], F32)
    big128 = sb("c_big128", [128, 128], F32)
    nround_off = sb("c_nroff", [1, 256], F32)
    ident = sb("c_ident", [128, 128], BF16)
    identf = sb("c_identf", [128, 128], F32)
    ones_col = sb("c_onescol", [128, 1], BF16)

    def ld_consts(eng, inc):
        inc(eng.dma_start(out=rot[:], in_=ins["rot_all"].rearrange("h d r -> d h r")))
        inc(eng.dma_start(out=tri[:], in_=ins["tri_incl"]))
        inc(eng.dma_start(out=tri64[:], in_=ins["tri_excl64"]))
        inc(eng.dma_start(out=ones1[:], in_=ins["ones_1x128f"]))
        inc(eng.dma_start(out=iota256[:], in_=ins["iota256_row"]))
        inc(eng.dma_start(out=iota32x4[:], in_=ins["iota32x4_row"]))
        inc(eng.dma_start(out=big128[:], in_=ins["big128"]))
        inc(eng.dma_start(out=nround_off[:], in_=ins["nround_off"]))
        inc(eng.dma_start(out=ident[:], in_=ins["ident"]))
        inc(eng.dma_start(out=identf[:], in_=ins["identf"]))
        inc(eng.dma_start(out=ones_col[:], in_=ins["ones_col_bf"]))
    t_lc = P.step("gpsimd", ld_consts, dma=11, tag="lsh_consts")

    lshsort = nc.dram_tensor("lshsort", [NSlot, QKV_W], BF16).ap()
    lshout = nc.dram_tensor("lshout", [NSlot, OUT_W], BF16).ap()
    rowb = nc.dram_tensor("rowbounce", [512], F32).ap()
    qk_hbm = nc.dram_tensor("qk_hbm", [S, D], BF16).ap()
    v_hbm = nc.dram_tensor("v_hbm", [S, D], BF16).ap()

    R1 = sb("lsh_R1", [128, 8192], F32)
    R2 = sb("lsh_R2", [128, 4096], F32)
    R3 = sb("lsh_R3", [128, 4224], F32)
    R4 = sb("lsh_R4", [128, 2560], F32)
    R5 = sb("lsh_R5", [128, 2560], F32)

    xfm = bfv(R1, 0, 16384).rearrange("p (c t) -> p c t", c=8)
    qFS = bfv(R1, 0, NSlot)[0:64]
    kFS = bfv(R1, 4096, NSlot)[0:64]
    xres = bfv(R1, 0, 4096).rearrange("p (a f) -> p a f", f=D)
    outstage = R1[:].rearrange("p (a f) -> p a f", f=D)

    wqkt = bfv(R2, 0, 8192).rearrange("p (c j) -> p c j", c=8)
    qkS = bfv(R2, 0, 8192).rearrange("p (b w) -> p b w", w=128)
    uns = bfv(R2, 0, 64 * OUT_W).rearrange("p (b w) -> p b w", w=OUT_W)

    wvt = bfv(R3, 0, 8192).rearrange("p (c j) -> p c j", c=8)
    vvb = bfv(R3, 0, 64 * 65).rearrange("p (b c) -> p b c", c=65)
    vvb2 = bfv(R3, 2080, 64 * 65).rearrange("p (b c) -> p b c", c=65)
    prefbuf = R3[:, 0:4096].rearrange("p (t k) -> p t k", k=256)
    scrA = R3[:, 0:1152].rearrange("p (a k) -> p a k", a=4)
    woutt = bfv(R3, 0, 8192).rearrange("p (c j) -> p c j", c=8)

    sortout = bfv(R4, 0, 64 * OUT_W).rearrange("p (b w) -> p b w", w=OUT_W)
    scrAi = R3[:, 0:1152].bitcast(I32).rearrange("p (a k) -> p a k", a=4)
    trash = R4[:, 0:1024]
    O4 = bfv(R4, 0, NT * 256).rearrange("p (t k) -> p t k", k=256)
    exclrep = R4[:, 2048:2304]

    stg = bfv(R5, 0, NT * QKV_W).rearrange("p (t w) -> p t w", w=QKV_W)
    qkFMh = bfv(R5, 0, NT * 128).rearrange("p (t w) -> p t w", w=128)[0:64]
    rsum = R5[:, 0:NT * OUT_W].rearrange("p (t w) -> p t w", w=OUT_W)
    attnh = R5[:, NT * OUT_W:NT * OUT_W + NT * 64].rearrange("p (t w) -> p t w", w=64)
    rec = R5[:, NT * OUT_W + NT * 64:NT * OUT_W + NT * 64 + NT].rearrange(
        "p (t o) -> p t o", o=1)

    attnFM = sb("lsh_attnFM", [128, 8, S], BF16)
    sqt = sb("lsh_sqt", [128, D], F32)
    S2 = sb("lsh_S2", [128, NT, NH], F32)
    rsb = sb("lsh_rs", [128, NT, NH], F32)
    negC = sb("lsh_negC", [128, NH], F32)
    scr16 = sb("lsh_scr16", [16, 128], F32)
    scr1 = sb("lsh_scr1", [1, 256], F32)
    lnm = sb("lsh_lnm", [128, NT, 8], F32)
    bounce = sb("lsh_bounce", [128, 512], BF16)
    bounce2 = sb("lsh_bounce2", [128, 512], BF16)
    qkh = sb("lsh_qkh", [128, NT, 64], BF16)
    kh = sb("lsh_kh", [128, NT, 64], BF16)
    vh = sb("lsh_vh", [128, NT, 64], BF16)
    buckets = sb("lsh_bkt", [128, NT, NR], F32)
    amaxf = sb("lsh_amaxf", [128, 32], F32)
    amaxi = sb("lsh_amaxi", [128, 16], I32)
    runtot = sb("lsh_runtot", [1, 256], F32)
    cnt64 = sb("lsh_cnt64", [64, 4], F32)
    pos4 = sb("lsh_pos4", [128, NT, NR], F32)
    pos4i = sb("lsh_pos4i", [128, NT * NR], I32)
    probs = bfv(R2, 3072, 512)

    PS = ps
    p_dots = PS[0][:]
    p_cnt = PS[5][0:1, 0:256]
    p_pv = PS[4][:, 0:160]
    p_misc = PS[1][:]
    p_b = [PS[2][:], PS[3][:]]

    p_misc_bf = p_misc.bitcast(BF16)
    T = dict(p_misc_bf=p_misc_bf, rot=rot, tri=tri, tri64=tri64, ones1=ones1, iota256=iota256,
             iota32x4=iota32x4, big128=big128, nround_off=nround_off, ident=ident,
             negC=negC, lshsort=lshsort, lshout=lshout, rowb=rowb, identf=identf,
             attnFM=attnFM, qkh=qkh, kh=kh, vh=vh, qkFMh=qkFMh,
             buckets=buckets, scrA=scrA, scrAi=scrAi, amaxf=amaxf, amaxi=amaxi,
             O4=O4, prefbuf=prefbuf,
             runtot=runtot, exclrep=exclrep, cnt64=cnt64, pos4=pos4,
             pos4i=pos4i, stg=stg, qkS=qkS, qFS=qFS, kFS=kFS, vvb=vvb, vvb2=vvb2, probs=probs,
             sortout=sortout, uns=uns, rsum=rsum, rec=rec, attnh=attnh,
             p_dots=p_dots, p_pv=p_pv, p_misc=p_misc, p_cnt=p_cnt,
             ones_col=ones_col, rsb=rsb,
             qk_hbm=qk_hbm, v_hbm=v_hbm)

    lsem = dict(d=es.enter_context(nc.semaphore("lsh_fd")),
                e=es.enter_context(nc.semaphore("lsh_fe")),
                p=es.enter_context(nc.semaphore("lsh_fp")),
                v=es.enter_context(nc.semaphore("lsh_fv")),
                cd=[0], ce=[0], cp=[0], cv=[0])
    T["lsem"] = lsem
    T["PS"] = PS

    def initR5(eng):
        eng.memset(R5[:], 0.0)
    t_i5 = P.step("vector", initR5, tag="lshi5")
    prev = P.step("gpsimd", lambda e: None, after=[t_lc, t_start, t_i5], tag="lsh0")
    emit_lsh.outstage = outstage
    emit_lsh.ctx = dict(R1=R1, R2=R2, R3=R3, R4=R4, R5=R5, PS=PS,
                        ones1=ones1, ident=ident, identf=identf,
                        ones_col=ones_col, attnFM=attnFM, tri=tri, rowb=rowb,
                        iota256=iota256, big128=big128)
    NHEADS = 16
    NBATCH = 2
    for b in range(NBATCH):
        prev = emit_batch_prep(nc, P, ins, b, xfm, wqkt, wvt, qk_hbm, v_hbm,
                               bounce, bounce2, trash, sqt, S2, rsb, negC,
                               scr16, scr1, ones1, identf, p_b, prev, tap)
        for h in range(NHEADS):
            prev = emit_head(nc, P, b, h, prev, T, tap)
        if True:
            prev = emit_batch_tail(nc, P, b, ins, x1_tm, attnFM, woutt, xres,
                                   sqt, trash, lnm, p_b, prev, tap)
    return prev


def emit_batch_prep(nc, P, ins, b, xfm, wqkt, wvt, qk_hbm, v_hbm, bounce,
                    bounce2, trash, sqt, S2, rsb, negC, scr16, scr1, ones1,
                    identf, p_b, prev, tap):
    S_ = S

    def ld(eng, inc):
        inc(eng.dma_start(
            out=xfm[:],
            in_=ins["x_fm"].rearrange("(c p) t -> p c t", p=128)[:, :, b * S_:(b + 1) * S_]))
        inc(eng.dma_start(out=wqkt[:],
                          in_=ins["w_qk_t"].rearrange("(c p) j -> p c j", p=128)))
        inc(eng.dma_start(out=wvt[:],
                          in_=ins["w_v_t"].rearrange("(c p) j -> p c j", p=128)))
    t_ld1 = P.step("gpsimd", ld, after=[prev], dma=3, tag=f"b{b}ldx")
    t_ld2 = t_ld1

    toks = []
    sq_toks = []
    st_toks = []
    for wi, (wt, hbm) in enumerate(((wqkt, qk_hbm), (wvt, v_hbm))):
        for ti in range(NT):
            for jh in range(2):
                pp = p_b[(ti * 2 + jh) % 2]
                bb = bounce if jh == 0 else bounce2

                def mm(eng, ti=ti, jh=jh, wt=wt, pp=pp):
                    for c in range(8):
                        eng.matmul(pp[:], xfm[:, c, ti * 128:(ti + 1) * 128],
                                   wt[:, c, jh * 512:(jh + 1) * 512],
                                   start=(c == 0), stop=(c == 7))
                t_mm = P.step("tensor", mm,
                              after=[t_ld1, t_ld2] + (toks[-2:-1] if len(toks) >= 2 else []),
                              tag=f"b{b}pj{wi}.{ti}.{jh}")

                def ev(eng, ti=ti, jh=jh, pp=pp, bb=bb):
                    eng.copy(out=bb[:], in_=pp[:])
                aft = [t_mm] + (st_toks[-2:-1] if len(st_toks) >= 2 else [])
                t_ev = P.step("scalar", ev, after=aft, tag=f"b{b}pe{wi}.{ti}.{jh}")
                toks.append(t_ev)

                if wi == 0:
                    def sq(eng, ti=ti, jh=jh, bb=bb):
                        eng.activation(out=trash[:, 0:512], in_=bb[:], func=AF.Square)
                    t_sq = P.step("scalar", sq,
                                  after=[t_ev] + (sq_toks[-1:] if sq_toks else []),
                                  tag=f"b{b}sq{ti}.{jh}")

                    def red(eng, ti=ti, jh=jh):
                        eng.tensor_reduce(
                            S2[:, ti, jh * 8:(jh + 1) * 8],
                            trash[:, 0:512].rearrange("p (h d) -> p h d", d=DH),
                            axis=AX.X, op=OP.add)
                    sq_toks.append(P.step("vector", red, after=[t_sq],
                                          tag=f"b{b}rd{ti}.{jh}"))

                def st(eng, inc, ti=ti, jh=jh, hbm=hbm, bb=bb):
                    inc(eng.dma_start(
                        out=hbm.rearrange("(bb p) j -> p bb j", p=128)
                        [:, ti, jh * 512:(jh + 1) * 512],
                        in_=bb[:]))
                extra = [sq_toks[-1]] if wi == 0 else []
                st_toks.append(P.step("sync", st, after=[t_ev] + extra, dma=1,
                                      tag=f"b{b}ps{wi}.{ti}.{jh}"))
    t_proj = P.step("sync", lambda e: None, after=st_toks[-2:] + sq_toks[-1:],
                    tag=f"b{b}pjd")

    def rs_fn(eng):
        eng.tensor_scalar(rsb[:], S2[:], 1.0 / DH, 1e-6, op0=OP.mult, op1=OP.add)
        eng.drain()
        eng.reciprocal(rsb[:], rsb[:])
    t_rs0 = P.step("vector", rs_fn, after=[t_proj], tag=f"b{b}rs0")

    def rs_sqrt(eng):
        eng.activation(out=rsb[:], in_=rsb[:], func=AF.Sqrt, scale=1.0 / DH)
    t_rs = P.step("scalar", rs_sqrt, after=[t_rs0], tag=f"b{b}rs1")

    def cmax(eng):
        eng.tensor_reduce(negC[:, 0:NH], S2[:].rearrange("p t h -> p h t"),
                          axis=AX.X, op=OP.max)
    t_cm = P.step("vector", cmax, after=[t_proj], tag=f"b{b}cm")

    def ctp(eng):
        eng.transpose(p_b[0][0:NH, 0:128], negC[:, 0:NH], identf[:])
    t_ct = P.step("tensor", ctp, after=[t_cm], tag=f"b{b}ct")

    def cred(eng):
        eng.tensor_reduce(scr16[0:NH, 0:1], p_b[0][0:NH, 0:128], axis=AX.X, op=OP.max)
    t_cr = P.step("vector", cred, after=[t_ct], tag=f"b{b}cr")

    def csqrt(eng):
        eng.activation(out=scr16[0:NH, 1:2], in_=scr16[0:NH, 0:1], func=AF.Sqrt)
    t_cs = P.step("scalar", csqrt, after=[t_cr], tag=f"b{b}cs")

    def ctp2(eng):
        eng.transpose(p_b[0][0:1, 0:NH], scr16[0:NH, 1:2], identf[0:NH, 0:NH])
    t_ct2 = P.step("tensor", ctp2, after=[t_cs, t_cr], tag=f"b{b}ct2")

    def cneg(eng):
        eng.tensor_scalar(scr1[0:1, 0:NH], p_b[0][0:1, 0:NH], -1.0, None, op0=OP.mult)
    t_cn = P.step("vector", cneg, after=[t_ct2], tag=f"b{b}cn")

    def crep(eng):
        eng.matmul(p_b[1][:, 0:NH], ones1[:], scr1[0:1, 0:NH], start=True, stop=True)
    t_crp = P.step("tensor", crep, after=[t_cn], tag=f"b{b}crp")

    def cev(eng):
        eng.tensor_copy(negC[:], p_b[1][:, 0:NH])
    t_c = P.step("vector", cev, after=[t_crp], tag=f"b{b}cev")

    return P.step("vector", lambda e: None, after=[t_rs, t_c], tag=f"b{b}prep")


def emit_head(nc, P, b, h, t_chain, T, tap):
    HSTG = "Z"
    def ldh(eng, inc):
        inc(eng.dma_start(
            out=T["qkh"][:],
            in_=T["qk_hbm"].rearrange("(bb p) j -> p bb j", p=128)
            [:, :, h * 64:(h + 1) * 64]))
        inc(eng.dma_start(
            out=T["vh"][:],
            in_=T["v_hbm"].rearrange("(bb p) j -> p bb j", p=128)
            [:, :, h * 64:(h + 1) * 64]))
    tok = P.step("sync", ldh, after=[t_chain], dma=2, tag=f"H{b}.{h}ld")

    def khf(eng):
        eng.tensor_tensor(
            T["kh"][:], T["qkh"][:],
            T["rsb"][:, :, h:h + 1].to_broadcast([128, NT, DH]), op=OP.mult)
    t_kh = P.step("vector", khf, after=[tok], tag=f"H{h}kh")
    if HSTG == "A":
        return t_kh

    for g in range(4):
        def tp(eng, g=g):
            for q in range(4):
                ti = g * 4 + q
                eng.transpose(T["p_misc_bf"][0:64, 128 * q:128 * (q + 1)],
                              T["qkh"][:, ti, :], T["ident"][:])
        t_tp = P.step("tensor", tp, after=[tok], tag=f"H{h}tp{g}")

        def ev(eng, g=g):
            eng.copy(
                out=T["qkFMh"][0:64, g * 4:(g + 1) * 4, :].rearrange("p a t -> p (a t)"),
                in_=T["p_misc_bf"][0:64, 0:512])
        tok = P.step("scalar", ev, after=[t_tp], tag=f"H{h}te{g}")

    for g in range(4):
        def rv(eng, g=g):
            for q in range(4):
                ti = g * 4 + q
                eng.matmul(T["p_misc"][:, 128 * q:128 * (q + 1)],
                           T["qkFMh"][0:64, ti, :], T["rot"][:, h, :],
                           start=True, stop=True)
        t_rv = P.step("tensor", rv, after=[tok], tag=f"H{h}rv{g}")

        def amax(eng, g=g):
            v = T["p_misc"][:].rearrange("p (a n r) -> p a n r", n=NR, r=32)
            A = T["scrA"]
            Ai = T["scrAi"]
            mp = A[:, :, 0:4]
            mn = A[:, :, 4:8]
            idxp = T["amaxf"][:, 0:16].rearrange("p (a k) -> p a k", a=4)
            idxn = T["amaxf"][:, 16:32].rearrange("p (a k) -> p a k", a=4)
            usep = T["amaxi"][:].rearrange("p (a k) -> p a k", a=4)
            cand4 = A[:, :, 32:160].rearrange("p a (n r) -> p a n r", r=32)
            cand = A[:, :, 32:160]
            maski4 = Ai[:, :, 160:288].rearrange("p a (n r) -> p a n r", r=32)
            maski = Ai[:, :, 160:288]
            eng.tensor_reduce(mp, v, axis=AX.X, op=OP.max)
            eng.tensor_reduce(mn, v, axis=AX.X, op=OP.min)
            eng.drain()
            bc = [128, 4, NR, 32]
            bcf = [128, 4, 128]
            iota = T["iota32x4"][:].rearrange("p (a r) -> p a r", a=1).to_broadcast(bcf)
            bigc = T["big128"][:].rearrange("p (a r) -> p a r", a=1).to_broadcast(bcf)
            eng.tensor_tensor(maski4, v, mp.rearrange("p a (n o) -> p a n o", o=1)
                              .to_broadcast(bc), op=OP.is_ge)
            eng.drain()
            eng.select(cand, maski, iota, bigc, add_drain=True)
            eng.drain()
            eng.tensor_reduce(idxp, cand4, axis=AX.X, op=OP.min)
            eng.drain()
            eng.tensor_tensor(maski4, v, mn.rearrange("p a (n o) -> p a n o", o=1)
                              .to_broadcast(bc), op=OP.is_le)
            eng.drain()
            eng.select(cand, maski, iota, bigc, add_drain=True)
            eng.drain()
            eng.tensor_reduce(idxn, cand4, axis=AX.X, op=OP.min)
            eng.tensor_tensor(mp, mp, mn, op=OP.add)
            eng.drain()
            eng.tensor_tensor(usep, mp, mn, op=OP.is_ge) if False else None
            eng.tensor_scalar(usep, mp, 0.0, None, op0=OP.is_ge)
            eng.tensor_scalar(idxn, idxn, 32.0, None, op0=OP.add)
            eng.drain()
            eng.select(T["buckets"][:, g * 4:(g + 1) * 4, :], usep, idxp, idxn, add_drain=True)
        tok = P.step("vector", amax, after=[t_rv], tag=f"H{h}am{g}")

    if HSTG == "B":
        return tok

    def boff(eng):
        eng.tensor_scalar(T["scrA"][:, 0:1, 0:NR], T["iota256"][0:128, 0:NR],
                          64.0, None, op0=OP.mult)
        eng.drain()
        eng.tensor_tensor(T["buckets"][:], T["buckets"][:],
                          T["scrA"][:, 0:1, 0:NR].to_broadcast([128, NT, NR]),
                          op=OP.add)
    tok = P.step("vector", boff, after=[tok], tag=f"H{h}bo")

    def onehot(eng):
        eng.tensor_tensor(
            T["O4"][:].rearrange("p t (n k) -> p t n k", k=64),
            T["iota256"][:].rearrange("p (t nk) -> p t nk", t=1)
            .rearrange("p t (n k) -> p t n k", k=64)
            .to_broadcast([128, NT, NR, 64]),
            T["buckets"][:].rearrange("p t (n o) -> p t n o", o=1)
            .to_broadcast([128, NT, NR, 64]),
            op=OP.is_equal)
    tok = P.step("vector", onehot, after=[tok], tag=f"H{h}oh")
    tok = tap(f"buckets_{b}_{h}", T["buckets"][:], tok, [128, NT, NR], F32)

    def rt0(eng):
        eng.memset(T["runtot"][:], 0.0)
    tok = P.step("vector", rt0, after=[tok], tag=f"H{h}rt0")

    t_ev_prev = tok
    for ti in range(NT):
        def mmp(eng, ti=ti):
            eng.matmul(T["p_dots"][:, 0:256], T["tri"][:], T["O4"][:, ti, :],
                       start=True, stop=False)
            eng.matmul(T["p_dots"][:, 0:256], T["ones1"][:], T["runtot"][:],
                       start=False, stop=True)
            eng.matmul(T["p_cnt"][:], T["ones_col"][:], T["O4"][:, ti, :],
                       start=True, stop=True)
        t_mm = P.step("tensor", mmp, after=[t_ev_prev], tag=f"H{h}pf{ti}")

        def evp(eng, ti=ti):
            eng.tensor_copy(T["prefbuf"][:, ti, :], T["p_dots"][:, 0:256])
            eng.drain()
            eng.tensor_tensor(T["runtot"][:], T["runtot"][:], T["p_cnt"][:],
                              op=OP.add)
        t_ev_prev = P.step("vector", evp, after=[t_mm], tag=f"H{h}pe{ti}")
    tok = t_ev_prev

    if HSTG == "C":
        return tok

    def rl1(eng, inc):
        inc(eng.dma_start(out=T["rowb"][0:256], in_=T["runtot"][0:1, :]))
    tok = P.step("gpsimd", rl1, after=[tok], dma=1, tag=f"H{h}r1")

    def rl2(eng, inc):
        inc(eng.dma_start(out=T["cnt64"][:],
                          in_=T["rowb"][0:256].rearrange("(n k) -> k n", k=64),
                      allow_slow_non_contiguous=True))
    tok = P.step("gpsimd", rl2, after=[tok], dma=1, tag=f"H{h}r2")

    def mme(eng):
        eng.matmul(T["p_misc"][0:64, 0:4], T["tri64"][:], T["cnt64"][:],
                   start=True, stop=True)
    tok = P.step("tensor", mme, after=[tok], tag=f"H{h}me")

    def eve(eng):
        eng.tensor_copy(T["cnt64"][:], T["p_misc"][0:64, 0:4])
    tok = P.step("vector", eve, after=[tok], tag=f"H{h}ee")

    def rl3(eng, inc):
        inc(eng.dma_start(out=T["rowb"][0:256].rearrange("(n k) -> k n", k=64),
                          in_=T["cnt64"][:], allow_slow_non_contiguous=True))
    tok = P.step("gpsimd", rl3, after=[tok], dma=1, tag=f"H{h}r3")

    def rl4(eng, inc):
        inc(eng.dma_start(out=T["runtot"][0:1, :], in_=T["rowb"][0:256]))
    tok = P.step("gpsimd", rl4, after=[tok], dma=1, tag=f"H{h}r4")

    def addoff(eng):
        eng.tensor_tensor(T["runtot"][:], T["runtot"][:], T["nround_off"][:],
                          op=OP.add)
    tok = P.step("vector", addoff, after=[tok], tag=f"H{h}ao")

    def repl(eng):
        eng.matmul(T["p_misc"][:, 0:256], T["ones1"][:], T["runtot"][:],
                   start=True, stop=True)
    tok = P.step("tensor", repl, after=[tok], tag=f"H{h}rp")

    def evr(eng):
        eng.tensor_copy(T["exclrep"][:], T["p_misc"][:, 0:256])
    tok = P.step("vector", evr, after=[tok], tag=f"H{h}er")

    def posf(eng):
        eng.tensor_tensor(T["prefbuf"][:], T["prefbuf"][:],
                          T["exclrep"][:].rearrange("p (t k) -> p t k", t=1)
                          .to_broadcast([128, NT, 256]), op=OP.add)
        eng.drain()
        eng.tensor_tensor(T["prefbuf"][:], T["prefbuf"][:], T["O4"][:], op=OP.mult)
        eng.drain()
        eng.tensor_reduce(T["pos4"][:],
                          T["prefbuf"][:].rearrange("p t (n k) -> p t n k", k=64),
                          axis=AX.X, op=OP.add)
        eng.drain()
        eng.tensor_copy(T["pos4i"][:], T["pos4"][:].rearrange("p t n -> p (t n)"))
    t_pos = P.step("vector", posf, after=[tok], tag=f"H{h}po")
    t_pos = tap(f"pos4_{b}_{h}", T["pos4"][:], t_pos, [128, NT, NR], F32)

    if HSTG == "D":
        return t_pos

    def stage(eng):
        st = T["stg"][:].rearrange("p t (q d) -> p t q d", d=64)
        eng.tensor_copy(st[:, :, 0, :], T["qkh"][:])
        eng.tensor_copy(st[:, :, 1, :], T["kh"][:])
        eng.tensor_copy(st[:, :, 2, :], T["vh"][:])
        eng.memset(st[:, :, 3, :], 0.0)
    t_stage = P.step("vector", stage, after=[t_pos, t_kh], tag=f"H{h}sg")

    def scat(eng, inc):
        for ti in range(NT):
            for n in range(NR):
                inc(eng.indirect_dma_start(
                    T["lshsort"],
                    bass.IndirectOffsetOnAxis(
                        T["pos4i"][:, ti * NR + n:ti * NR + n + 1], 0),
                    T["stg"][:, ti, :], None))
    t_scat = P.step("gpsimd", scat, after=[t_stage], dma=64, tag=f"H{h}sc")

    if HSTG == "E":
        return t_scat

    def rb(eng, inc):
        src = T["lshsort"].rearrange("(bb p) w -> p bb w", p=128)
        inc(eng.dma_start(out=T["qkS"][:, 0:32, :], in_=src[:, 0:32, 0:128]))
        inc(eng.dma_start(out=T["qkS"][:, 32:64, :], in_=src[:, 32:64, 0:128]))
        inc(eng.dma_start(out=T["vvb"][:, :, 0:64], in_=src[:, :, 128:192]))
        inc(eng.dma_start(out=T["vvb2"][64:128, :, 0:64], in_=src[0:64, :, 128:192]))
        inc(eng.dma_start(out=T["vvb2"][0:64, 1:64, 0:64],
                          in_=src[64:128, 0:63, 128:192]))
        inc(eng.dma_start(out=T["vvb2"][0:64, 0:1, 0:64],
                          in_=src[64:128, 63:64, 128:192]))
    t_rb = P.step("sync", rb, after=[t_scat], dma=6, tag=f"H{h}rb")

    def vones(eng):
        eng.memset(T["vvb"][:, :, 64:65], 1.0)
        eng.memset(T["vvb2"][:, :, 64:65], 1.0)
    t_rb = P.step("vector", vones, after=[t_rb], tag=f"H{h}vo")

    tok = t_rb
    for g in range(16):
        def tps(eng, g=g):
            for q in range(4):
                ti = g * 4 + q
                eng.transpose(T["p_misc_bf"][:, 128 * q:128 * (q + 1)],
                              T["qkS"][:, ti, :], T["ident"][:])
        t_t = P.step("tensor", tps, after=[tok], tag=f"H{h}st{g}")

        def tse(eng, g=g):
            eng.copy(out=T["qFS"][:, g * 512:(g + 1) * 512],
                     in_=T["p_misc_bf"][0:64, 0:512])
            eng.copy(out=T["kFS"][:, g * 512:(g + 1) * 512],
                     in_=T["p_misc_bf"][64:128, 0:512])
        tok = P.step("scalar", tse, after=[t_t], tag=f"H{h}se{g}")
    t_sorted = tok

    if HSTG == "F":
        return t_sorted
    L = T["lsem"]
    sld, sle, slp, slv = L["d"], L["e"], L["p"], L["v"]
    base_d, base_e = L["cd"][0], L["ce"][0]
    base_p, base_v = L["cp"][0], L["cv"][0]
    PSp = T["PS"]

    def pvslot(s):
        if s < 4:
            return PSp[4][:, 65 * s:65 * s + 65]
        return PSp[5][:, 65 * (s - 4):65 * (s - 4) + 65]

    def g_pe(eng, h=h, base_d=base_d, base_e=base_e, base_p=base_p,
             base_v=base_v):
        def emit_pv(g):
            last = None
            for q in range(8):
                c = g * 8 + q
                po = 64 * (c % 2)
                s = (c // 2) % 8
                if c % 2 == 1:
                    vsrc = T["vvb"][:, (c - 1) // 2, 0:65]
                else:
                    vsrc = T["vvb2"][:, c // 2, 0:65]
                sl = pvslot(s)
                last = eng.matmul(sl[po:po + 64, :],
                                  T["probs"][:, 64 * q:64 * q + 64],
                                  vsrc, start=True, stop=True,
                                  tile_position=(0, po))
            last.then_inc(slp, 1)
        for g in range(16):
            if base_e + g > 0:
                eng.wait_ge(sle, base_e + g)
            last = None
            for q in range(8):
                c = g * 8 + q
                rhs = T["qFS"][:, 64 * c:64 * c + 64]
                if c % 2 == 0:
                    lo = (64 * c - 64) % NSlot
                    eng.matmul(T["p_dots"][0:64, 64 * q:64 * q + 64],
                               T["kFS"][:, lo:lo + 64], rhs,
                               start=True, stop=True, tile_position=(0, 0))
                    last = eng.matmul(
                        T["p_dots"][64:128, 64 * q:64 * q + 64],
                        T["kFS"][:, 64 * c:64 * c + 64], rhs,
                        start=True, stop=True, tile_position=(0, 64))
                else:
                    last = eng.matmul(
                        T["p_dots"][:, 64 * q:64 * q + 64],
                        T["kFS"][:, 64 * c - 64:64 * c + 64],
                        rhs, start=True, stop=True)
            last.then_inc(sld, 1)
            if g >= 1:
                k = g - 1
                if base_v + k - 1 > 0:
                    eng.wait_ge(slv, base_v + k - 1)
                emit_pv(k)
        eng.wait_ge(sle, base_e + 16)
        if base_v + 14 > 0:
            eng.wait_ge(slv, base_v + 14)
        emit_pv(15)
    t_gpe = P.step("tensor", g_pe, after=[t_sorted], tag=f"H{h}GPE")
    L["cd"][0] += 16
    L["cp"][0] += 16

    def g_act(eng, h=h, base_d=base_d, base_p=base_p):
        for g in range(16):
            eng.wait_ge(sld, base_d + g + 1)
            if g >= 1:
                eng.wait_ge(slp, base_p + g)
            eng.activation(out=T["probs"][:], in_=T["p_dots"][:],
                           func=AF.Exp, bias=T["negC"][:, h:h + 1],
                           scale=1.0).then_inc(sle, 1)
    t_gact = P.step("scalar", g_act, after=[t_sorted], tag=f"H{h}GAX")
    L["ce"][0] += 16

    def g_dve(eng, base_p=base_p):
        for g in range(16):
            eng.wait_ge(slp, base_p + g + 1)
            last = None
            for i in range(4):
                blk = 4 * g + i
                s = blk % 8
                last = eng.tensor_copy(T["sortout"][:, blk, 0:65],
                                       pvslot(s))
            last.then_inc(slv, 1)
    t_pv_done = P.step("vector", g_dve, after=[t_sorted], tag=f"H{h}GDV")
    L["cv"][0] += 16
    t_gfin = (t_gpe, t_gact)

    def wout(eng, inc):
        dst = T["lshout"].rearrange("(bb p) w -> p bb w", p=128)
        inc(eng.dma_start(out=dst[:, 0:32, :], in_=T["sortout"][:, 0:32, :]))
        inc(eng.dma_start(out=dst[:, 32:64, :], in_=T["sortout"][:, 32:64, :]))
    t_wo = P.step("sync", wout, after=[t_pv_done], dma=2, tag=f"H{h}wo")

    def unsort(eng, inc):
        for ti in range(NT):
            for n in range(NR):
                inc(eng.indirect_dma_start(
                    T["uns"][:, n * NT + ti, :], None,
                    T["lshout"],
                    bass.IndirectOffsetOnAxis(
                        T["pos4i"][:, ti * NR + n:ti * NR + n + 1], 0)))
    t_un = P.step("gpsimd", unsort, after=[t_wo], dma=64, tag=f"H{h}un")

    if HSTG == "H":
        return t_un

    def comb(eng):
        u = T["uns"]
        eng.tensor_tensor(T["rsum"][:], u[:, 0:16, :], u[:, 16:32, :], op=OP.add)
        eng.drain()
        eng.tensor_tensor(T["rsum"][:], T["rsum"][:], u[:, 32:48, :], op=OP.add)
        eng.drain()
        eng.tensor_tensor(T["rsum"][:], T["rsum"][:], u[:, 48:64, :], op=OP.add)
        eng.drain()
        eng.reciprocal(T["rec"][:], T["rsum"][:, :, 64:65])
        eng.drain()
        eng.tensor_tensor(T["attnh"][:], T["rsum"][:, :, 0:64],
                          T["rec"][:].to_broadcast([128, NT, 64]), op=OP.mult)
    t_cb = P.step("vector", comb, after=[t_un], tag=f"H{h}cb")
    t_cb = tap(f"attnh_{b}_{h}", T["attnh"][:], t_cb, [128, NT, 64], F32)

    tok = t_cb
    for g in range(4):
        def tpa(eng, g=g):
            for q in range(4):
                ti = g * 4 + q
                eng.transpose(T["p_misc"][0:64, 128 * q:128 * (q + 1)],
                              T["attnh"][:, ti, :], T["identf"][:])
        t_t = P.step("tensor", tpa, after=[tok], tag=f"H{h}at{g}")

        def tae(eng, g=g):
            eng.copy(
                out=T["attnFM"][(h % 2) * 64:(h % 2) * 64 + 64, h // 2,
                                g * 512:(g + 1) * 512],
                in_=T["p_misc"][0:64, 0:512])
        tok = P.step("scalar", tae, after=[t_t], tag=f"H{h}ae{g}")
    return tok


def emit_batch_tail(nc, P, b, ins, x1_tm, attnFM, woutt, xres, sqt, trash,
                    lnm, p_b, prev, tap):
    def ldw(eng, inc):
        inc(eng.dma_start(out=woutt[:],
                          in_=ins["w_out_t"].rearrange("(c p) j -> p c j", p=128)))
    t_w = P.step("gpsimd", ldw, after=[prev], dma=1, tag=f"b{b}tw")

    toks = []
    ln_toks = []
    t_x = None
    for ti in range(NT):
        if ti % 4 == 0:
            def ldr(eng, inc, ti=ti):
                inc(eng.dma_start(
                    out=xres[:],
                    in_=ins["x_tm"].rearrange("(bb p) f -> p bb f", p=128)
                    [:, b * NT + ti:b * NT + ti + 4, :]))
            t_x = P.step("gpsimd", ldr,
                         after=[prev] + (ln_toks[-1:] if ln_toks else []),
                         dma=1, tag=f"b{b}tx{ti}")
        for jh in range(2):
            pp = p_b[(ti * 2 + jh) % 2]

            def mm(eng, ti=ti, jh=jh, pp=pp):
                for c in range(8):
                    eng.matmul(pp[:], attnFM[:, c, ti * 128:(ti + 1) * 128],
                               woutt[:, c, jh * 512:(jh + 1) * 512],
                               start=(c == 0), stop=(c == 7))
            t_mm = P.step("tensor", mm,
                          after=[prev, t_w] + (toks[-2:-1] if len(toks) >= 2 else []),
                          tag=f"b{b}tm{ti}.{jh}")

            def ev(eng, ti=ti, jh=jh, pp=pp):
                eng.tensor_tensor(
                    sqt[:, jh * 512:(jh + 1) * 512], pp[:],
                    xres[:, ti % 4, jh * 512:(jh + 1) * 512], op=OP.add)
            toks.append(P.step("vector", ev,
                               after=[t_mm, t_x] + (ln_toks[-1:] if ln_toks else []),
                               tag=f"b{b}tv{ti}.{jh}"))

        def lnmean(eng, ti=ti):
            eng.tensor_reduce(lnm[:, ti, 0:1], sqt[:], axis=AX.X, op=OP.add)
        t_m = P.step("vector", lnmean, after=toks[-2:], tag=f"b{b}lm{ti}")

        def lnsq(eng, ti=ti):
            eng.activation(out=trash[:, 0:1024], in_=sqt[:], func=AF.Square,
                           accum_out=lnm[:, ti, 1:2])
        t_s = P.step("scalar", lnsq, after=toks[-2:], tag=f"b{b}ls{ti}")

        def lnst(eng, ti=ti):
            L = lnm[:, ti, :]
            eng.tensor_scalar(L[:, 2:3], L[:, 0:1], 1.0 / D, None, op0=OP.mult)
            eng.drain()
            eng.tensor_tensor(L[:, 3:4], L[:, 2:3], L[:, 2:3], op=OP.mult)
            eng.tensor_scalar(L[:, 5:6], L[:, 1:2], 1.0 / D, None, op0=OP.mult)
            eng.drain()
            eng.tensor_tensor(L[:, 3:4], L[:, 5:6], L[:, 3:4], op=OP.subtract)
            eng.drain()
            eng.tensor_scalar(L[:, 3:4], L[:, 3:4], 1e-5, None, op0=OP.add)
            eng.drain()
            eng.reciprocal(L[:, 3:4], L[:, 3:4])
        t_st = P.step("vector", lnst, after=[t_m, t_s], tag=f"b{b}lt{ti}")

        def lnrs(eng, ti=ti):
            eng.activation(out=lnm[:, ti, 3:4], in_=lnm[:, ti, 3:4], func=AF.Sqrt)
        t_rs = P.step("scalar", lnrs, after=[t_st], tag=f"b{b}lr{ti}")

        def lnap(eng, ti=ti):
            eng.tensor_scalar(
                x1_tm[:, b * NT + ti, :], sqt[:],
                lnm[:, ti, 2:3], lnm[:, ti, 3:4],
                op0=OP.subtract, op1=OP.mult)
        ln_toks.append(P.step("vector", lnap, after=[t_rs], tag=f"b{b}la{ti}"))
    return P.step("vector", lambda e: None, after=ln_toks[-1:], tag=f"b{b}tl")



F32 = mybir.dt.float32
BF16 = mybir.dt.bfloat16
I32 = mybir.dt.int32
AX = mybir.AxisListType
OP = mybir.AluOpType
AF = mybir.ActivationFunctionType

S = 2048
D = 1024
NH = 16
DH = 64
NKV = 4


def bfv(region, off_f32, n_bf16):
    return region[:, off_f32:off_f32 + n_bf16 // 2].bitcast(BF16)


def emit_gqa(nc, es, P, ins, L, tap, x1_tm, prev, n_cores=8):
    R1, R2, R3, R4, R5 = L["R1"], L["R2"], L["R3"], L["R4"], L["R5"]
    PS = L["PS"]
    ones1, ident, identf = L["ones1"], L["ident"], L["identf"]
    ones_col = L["ones_col"]
    iota256, big128 = L["iota256"], L["big128"]

    qfm = nc.dram_tensor("g_qfm", [1024, S], BF16).ap()
    kfm = nc.dram_tensor("g_kfm", [256, S], BF16).ap()
    vtm = nc.dram_tensor("g_vtm", [S, 256], BF16).ap()
    ofm = nc.dram_tensor("g_ofm", [1024, S], BF16).ap()
    x2loc = nc.dram_tensor("g_x2loc", [2, S, D], BF16)
    rt_in = nc.dram_tensor("g_rtin", [2 * S, 4], F32)
    byex = nc.dram_tensor("g_byex", [2, n_cores * S, D], BF16,
                          addr_space="Shared")
    rt_all = nc.dram_tensor("g_rtall", [n_cores * 2 * S, 4], F32,
                            addr_space="Shared")
    groups = [list(range(n_cores))]
    sem_coll = es.enter_context(nc.semaphore("sem_coll"))
    coll_n = [0]
    fs = dict(st=es.enter_context(nc.semaphore("gs_fst")),
              ex=es.enter_context(nc.semaphore("gs_fex")),
              pv=es.enter_context(nc.semaphore("gs_fpv")),
              qn=es.enter_context(nc.semaphore("gs_fqn")),
              cst=[0], cex=[0], cpv=[0], cqn=[0])

    x1FMw = bfv(R2, 0, 4096).rearrange("p (c t) -> p c t", c=8)
    kwS = bfv(R2, 2048, 2048).rearrange("p (c j) -> p c j", c=8)
    vwS = bfv(R2, 3072, 2048).rearrange("p (c j) -> p c j", c=8)
    qwS = bfv(R3, 0, 8192).rearrange("p (c j) -> p c j", c=8)
    owS = bfv(R3, 0, 8192).rearrange("p (c j) -> p c j", c=8)
    qv = [bfv(R1, 0, 2048)[0:65], bfv(R1, 1024, 2048)[0:65]]
    kv = [bfv(R1, 2048, 2048)[0:65], bfv(R1, 3072, 2048)[0:65]]
    vT = [bfv(R1, 4096, 1040).rearrange("p (b c) -> p b c", c=65),
          bfv(R1, 4616, 1040).rearrange("p (b c) -> p b c", c=65)]
    OFMp = [bfv(R1, 5136, 2048), bfv(R1, 6160, 2048)]
    sqb = L["attnFM"][:].rearrange("p a t -> p (a t)")[0:64, 0:2048]
    scr_rc = R2[0:1, 0:512]
    scrq2 = R2[0:1, 512:1024]
    scrq_a = R2[0:1, 1024:1536]
    maxs = R2[0:1, 1536:1544]
    rbc = R1[0:64, 7184:7696]
    OallS = bfv(R1, 0, 16384).rearrange("p (c t) -> p c t", c=8)

    a_f32 = R4[:, 0:1024]
    x2fin = R4[:, 1024:2048]
    xfmT = [R4[:, 2048:2176], R4[:, 2176:2304]]
    s2a = R4[:, 2304:2305]
    s2b = R4[:, 2305:2306]
    s2 = R4[:, 2306:2307]
    rstd = R4[:, 2307:2308]
    epscol = R4[:, 2308:2309]
    obS = R1[0:1, 0:1024]
    n1wS = R1[0:1, 1024:2048]
    vbS = R1[0:1, 2048:2304]

    obrep = bfv(R5, 0, 1024)
    n1wrep = bfv(R5, 512, 1024)
    vbrep = bfv(R5, 1024, 256)
    gwS = R5[:, 1152:1216].rearrange("p (c e) -> p c e", e=8)
    zTM = R5[:, 1216:1344].rearrange("p (b e) -> p b e", e=8)
    bounce = [bfv(R5, 1344, 512), bfv(R5, 1600, 512)]
    m1all = R5[:, 1856:1872]
    m2all = R5[:, 1872:1888]
    e1all = R5[:, 1888:1904]
    e2all = R5[:, 1904:1920]
    w1all = R5[:, 1920:1936]
    w2all = R5[:, 1936:1952]
    wsum = R5[:, 1952:1968]
    neg30 = R5[:, 1968:1976]
    msk8i = R5[:, 1976:1984].bitcast(I32)
    cnd8 = R5[:, 1984:1992]
    rtblk = R5[:, 1992:2056].rearrange("p (b c) -> p b c", c=4)
    probsT = [bfv(R5, 1344, 512), bfv(R5, 1600, 512), bfv(R3, 0, 512)]

    qb_s = L["qb_s"]; kb_s = L["kb_s"]

    p_n = PS[5][0:1, 0:512]
    tpbf = PS[1][:].bitcast(BF16)

    def ld_small(eng, inc):
        inc(eng.dma_start(out=qb_s[:], in_=ins["qb_fm"]))
        inc(eng.dma_start(out=kb_s[:], in_=ins["kb_fm"]))
        inc(eng.dma_start(out=obS, in_=ins["ob"]))
        inc(eng.dma_start(out=n1wS, in_=ins["norm1_w"]))
        inc(eng.dma_start(out=vbS, in_=ins["vb"]))
        inc(eng.dma_start(out=gwS[:], in_=ins["gate_w_fm"]))
    t_ls = P.step("sync", ld_small, after=[prev], dma=6, tag="gq_ls")

    def reps_mm(eng):
        eng.matmul(PS[0][:, 0:512], ones1[0:1, 0:128], obS[0:1, 0:512],
                   start=True, stop=True)
        eng.matmul(PS[2][:, 0:512], ones1[0:1, 0:128], obS[0:1, 512:1024],
                   start=True, stop=True)
        eng.matmul(PS[3][:, 0:512], ones1[0:1, 0:128], n1wS[0:1, 0:512],
                   start=True, stop=True)
        eng.matmul(PS[4][:, 0:512], ones1[0:1, 0:128], n1wS[0:1, 512:1024],
                   start=True, stop=True)
        eng.matmul(PS[6][:, 0:256], ones1[0:1, 0:128], vbS[0:1, 0:256],
                   start=True, stop=True)
    t_rm = P.step("tensor", reps_mm, after=[t_ls, prev], tag="gq_rm")

    def reps_ev(eng):
        eng.tensor_copy(obrep[:, 0:512], PS[0][:, 0:512])
        eng.tensor_copy(obrep[:, 512:1024], PS[2][:, 0:512])
        eng.tensor_copy(n1wrep[:, 0:512], PS[3][:, 0:512])
        eng.tensor_copy(n1wrep[:, 512:1024], PS[4][:, 0:512])
        eng.tensor_copy(vbrep[:], PS[6][:, 0:256])
        eng.memset(epscol, 1e-6)
        eng.memset(neg30, -1e30)
    t_rep = P.step("vector", reps_ev, after=[t_rm], tag="gq_rep")

    prev = P.step("tensor", lambda e: None, after=[t_rep, prev], tag="gq_st")

    xdma_toks = []
    rt_toks = []
    for ex in range(2):
        prev = emit_gp(nc, P, ins, ex, x1_tm, x1FMw, qwS, kwS, vwS, qb_s,
                       kb_s, vbrep, bounce, qfm, kfm, vtm, PS, tpbf, ident,
                       prev)
        prev = emit_gs(nc, P, ex, qv, kv, vT, sqb, OFMp, probsT, scr_rc,
                       scrq2, scrq_a, maxs, rbc, qfm, kfm, vtm, ofm, PS, p_n,
                       ones1, ones_col, fs, prev)
        prev, t_xd = emit_go(nc, P, ins, ex, x1_tm, OallS, owS, ofm, obrep,
                             n1wrep, gwS, a_f32, x2fin, xfmT, s2a, s2b, s2,
                             rstd, epscol, zTM, x2loc, PS, identf, prev)
        xdma_toks.append(t_xd)

        def ag(eng, ex=ex):
            eng.collective_compute(
                "AllGather", OP.bypass, replica_groups=groups,
                ins=[x2loc.ap()[ex]], outs=[byex.ap()[ex]],
            ).then_inc(sem_coll, 1)
        coll_n[0] += 1
        P.step("gpsimd", ag, after=[t_xd], tag=f"gq_ag{ex}")

        prev, t_rt = emit_gating(nc, P, ex, zTM, m1all, m2all, e1all, e2all,
                                 w1all, w2all, wsum, neg30, msk8i, cnd8,
                                 rtblk, iota256, big128, rt_in, tap, prev)
        rt_toks.append(t_rt)

    def agr(eng):
        eng.collective_compute(
            "AllGather", OP.bypass, replica_groups=groups,
            ins=[rt_in.ap()], outs=[rt_all.ap()]).then_inc(sem_coll, 1)
    coll_n[0] += 1
    P.step("gpsimd", agr, after=rt_toks, tag="gq_agr")

    emit_gqa.handles = dict(byex=byex, rt_all=rt_all, x2loc=x2loc,
                            sem_coll=sem_coll, n_coll=coll_n[0])
    return prev


def emit_gp(nc, P, ins, ex, x1_tm, x1FMw, qwS, kwS, vwS, qb_s, kb_s, vbrep,
            bounce, qfm, kfm, vtm, PS, tpbf, ident, prev):
    B0 = 16 * ex

    def ldw(eng, inc):
        inc(eng.dma_start(out=qwS[:],
                          in_=ins["qw"].rearrange("(c p) j -> p c j", p=128)))
        inc(eng.dma_start(out=kwS[:],
                          in_=ins["kw"].rearrange("(c p) j -> p c j", p=128)))
        inc(eng.dma_start(out=vwS[:],
                          in_=ins["vw"].rearrange("(c p) j -> p c j", p=128)))
    t_w = P.step("gpsimd", ldw, after=[prev], dma=3, tag=f"gp{ex}w")
    t_wq = t_w

    qdst = qfm.rearrange("(pr p) t -> p pr t", p=128)
    kdst = kfm.rearrange("(pr p) t -> p pr t", p=128)
    vdst = vtm.rearrange("(blk p) v -> p blk v", p=128)
    t_ev_prev = [t_w, t_w]
    t_dma_prev = t_w
    for w in range(4):
        t_tp_ev = []
        for b4 in range(4):
            def tpp(eng, w=w, b4=b4):
                for c in range(8):
                    eng.transpose(tpbf[:, 128 * c:128 * (c + 1)],
                                  x1_tm[:, B0 + w * 4 + b4,
                                        128 * c:128 * (c + 1)], ident[:])
            t_t = P.step("tensor", tpp,
                         after=[prev] + t_tp_ev[-1:], tag=f"gp{ex}t{w}.{b4}")

            def tpe(eng, w=w, b4=b4):
                for c in range(8):
                    eng.tensor_copy(x1FMw[:, c, 128 * b4:128 * (b4 + 1)],
                                    tpbf[:, 128 * c:128 * (c + 1)])
            t_tp_ev.append(P.step("vector", tpe, after=[t_t],
                                  tag=f"gp{ex}e{w}.{b4}"))
        t_x1w = t_tp_ev[-1]

        jobs = []
        for pr in range(8):
            jobs.append(("q", pr))
        jobs += [("k", 0), ("k", 1)]
        jobs += [("v", 0), ("v", 1), ("v", 2), ("v", 3)]
        for ji, (kind, idx) in enumerate(jobs):
            bank = PS[[0, 2][ji % 2]]
            bb = bounce[ji % 2]

            def mm(eng, kind=kind, idx=idx, bank=bank, w=w):
                for c in range(8):
                    if kind == "q":
                        eng.matmul(bank[:, 0:512],
                                   qwS[:, c, 128 * idx:128 * (idx + 1)],
                                   x1FMw[:, c, :], start=(c == 0),
                                   stop=(c == 7))
                    elif kind == "k":
                        eng.matmul(bank[:, 0:512],
                                   kwS[:, c, 128 * idx:128 * (idx + 1)],
                                   x1FMw[:, c, :], start=(c == 0),
                                   stop=(c == 7))
                    else:
                        eng.matmul(bank[:, 0:256],
                                   x1FMw[:, c, 128 * idx:128 * (idx + 1)],
                                   vwS[:, c, :], start=(c == 0),
                                   stop=(c == 7))
            t_mm = P.step("tensor", mm,
                          after=[t_x1w, t_wq, t_w] + t_ev_prev[-2:-1],
                          tag=f"gp{ex}m{w}.{ji}")

            if kind == "v":
                def ev(eng, idx=idx, bank=bank, bb=bb):
                    eng.tensor_tensor(bb[:, 0:256], bank[:, 0:256],
                                      vbrep[:], op=OP.add)
            elif kind == "q":
                def ev(eng, idx=idx, bank=bank, bb=bb):
                    eng.tensor_scalar(bb[:, 0:512], bank[:, 0:512],
                                      qb_s[:, idx:idx + 1], None, op0=OP.add)
            else:
                def ev(eng, idx=idx, bank=bank, bb=bb):
                    eng.tensor_scalar(bb[:, 0:512], bank[:, 0:512],
                                      kb_s[:, idx:idx + 1], None, op0=OP.add)
            eng_name = "vector"
            t_ev = P.step(eng_name, ev, after=[t_mm, t_dma_prev],
                          tag=f"gp{ex}v{w}.{ji}")
            t_ev_prev.append(t_ev)

            def st(eng, inc, kind=kind, idx=idx, bb=bb, w=w):
                if kind == "q":
                    inc(eng.dma_start(out=qdst[:, idx, 512 * w:512 * (w + 1)],
                                      in_=bb[:, 0:512]))
                elif kind == "k":
                    inc(eng.dma_start(out=kdst[:, idx, 512 * w:512 * (w + 1)],
                                      in_=bb[:, 0:512]))
                else:
                    inc(eng.dma_start(out=vdst[:, w * 4 + idx, :],
                                      in_=bb[:, 0:256]))
            t_dma_prev = P.step("sync", st, after=[t_ev], dma=1,
                                tag=f"gp{ex}s{w}.{ji}")
        prev = t_dma_prev
    return P.step("tensor", lambda e: None, after=[prev], tag=f"gp{ex}fin")


def emit_gs(nc, P, ex, qv, kv, vT, sqb, OFMp, probsT, scr_rc, scrq2, scrq_a,
            maxs, rbc, qfm, kfm, vtm, ofm, PS, p_n, ones1, ones_col, fs,
            prev):
    odst = ofm.rearrange("(pr p) t -> p pr t", p=128)
    vsrc = vtm.rearrange("(blk p) v -> p blk v", p=128)
    sst, sex, spv = fs["st"], fs["ex"], fs["pv"]
    cst, cex, cpv = fs["cst"], fs["cex"], fs["cpv"]
    last_use_q = {0: prev, 1: prev}
    last_use_k = {0: prev, 1: prev}
    t_onorm = [prev, prev]
    t_ofm_dma = prev
    t_qsq_free = prev
    t_pe_last = prev
    pending = None

    for g in range(4):
        gs = g % 2

        def ldk(eng, inc, g=g, gs=gs):
            inc(eng.dma_start(out=kv[gs][0:64, :],
                              in_=kfm[64 * g:64 * (g + 1), :]))
            inc(eng.dma_start(
                out=vT[gs][:, :, 0:64],
                in_=vsrc[:, :, 64 * g:64 * (g + 1)]))
        t_lk = P.step("sync", ldk, after=[last_use_k[gs], prev], dma=2,
                      tag=f"gs{ex}lk{g}")

        def kones(eng, gs=gs):
            eng.memset(kv[gs][64:65, :], 1.0)
            eng.memset(vT[gs][:, :, 64:65], 1.0)
        t_ko = P.step("vector", kones, after=[t_lk], tag=f"gs{ex}ko{g}")

        def ksq(eng, gs=gs):
            eng.activation(out=sqb[:], in_=kv[gs][0:64, :], func=AF.Square)
        t_kq = P.step("scalar", ksq, after=[t_ko, t_qsq_free],
                      tag=f"gs{ex}kq{g}")

        st0, pv0 = cst[0], cpv[0]

        def nmm(eng, st0=st0, pv0=pv0):
            for w4 in range(4):
                if w4 >= 1:
                    eng.wait_ge(spv, pv0 + w4)
                eng.matmul(p_n, ones_col[0:64, :],
                           sqb[:, 512 * w4:512 * (w4 + 1)],
                           start=True, stop=True).then_inc(sst, 1)
        t_nm = P.step("tensor", nmm, after=[t_kq, t_pe_last],
                      tag=f"gs{ex}km{g}")
        cst[0] += 4
        t_pe_last = t_nm

        def nmx(eng, st0=st0):
            for w4 in range(4):
                eng.wait_ge(sst, st0 + w4 + 1)
                eng.tensor_reduce(maxs[0:1, w4:w4 + 1], p_n, axis=AX.X,
                                  op=OP.max).then_inc(spv, 1)
        t_mx = P.step("vector", nmx, after=[t_kq], tag=f"gs{ex}kx{g}")
        cpv[0] += 4

        def kmax(eng):
            eng.tensor_reduce(maxs[0:1, 4:5], maxs[0:1, 0:4], axis=AX.X,
                              op=OP.max)
        t_km = P.step("vector", kmax, after=[t_mx], tag=f"gs{ex}kf{g}")

        def ksqrt(eng):
            eng.activation(out=maxs[0:1, 5:6], in_=maxs[0:1, 4:5],
                           func=AF.Sqrt)
        t_maxk = P.step("scalar", ksqrt, after=[t_km], tag=f"gs{ex}ks{g}")

        for hh in range(4):
            h = 4 * g + hh
            hs = h % 2

            def ldq(eng, inc, h=h, hs=hs):
                inc(eng.dma_start(out=qv[hs][0:64, :],
                                  in_=qfm[64 * h:64 * (h + 1), :]))
            t_lq = P.step("sync", ldq, after=[last_use_q[hs], prev], dma=1,
                          tag=f"gs{ex}lq{h}")

            def qsq(eng, hs=hs):
                eng.activation(out=sqb[:], in_=qv[hs][0:64, :],
                               func=AF.Square)
            t_qq = P.step("scalar", qsq, after=[t_lq, t_maxk, t_qsq_free],
                          tag=f"gs{ex}qq{h}")

            st0, ex0, pv0 = cst[0], cex[0], cpv[0]

            def qmm(eng, st0=st0, ex0=ex0):
                for w4 in range(4):
                    if w4 >= 1:
                        eng.wait_ge(sex, ex0 + w4)
                    eng.matmul(p_n, ones_col[0:64, :],
                               sqb[:, 512 * w4:512 * (w4 + 1)],
                               start=True, stop=True).then_inc(sst, 1)
            t_qm = P.step("tensor", qmm, after=[t_qq, t_pe_last],
                          tag=f"gs{ex}qm{h}")
            cst[0] += 4
            t_pe_last = t_qm
            t_qsq_free = t_qm

            def qsr(eng, st0=st0, pv0=pv0):
                for w4 in range(4):
                    eng.wait_ge(sst, st0 + w4 + 1)
                    if w4 >= 1:
                        eng.wait_ge(spv, pv0 + w4)
                    eng.activation(out=scrq_a, in_=p_n,
                                   func=AF.Sqrt).then_inc(sex, 1)
            t_qs = P.step("scalar", qsr, after=[t_qq], tag=f"gs{ex}qs{h}")
            cex[0] += 4

            def qev(eng, ex0=ex0, hs=hs):
                for w4 in range(4):
                    eng.wait_ge(sex, ex0 + w4 + 1)
                    eng.tensor_scalar(
                        qv[hs][64:65, 512 * w4:512 * (w4 + 1)], scrq_a,
                        maxs[0:1, 5:6], -1.0, op0=OP.mult,
                        op1=OP.mult).then_inc(spv, 1)
            t_qn = P.step("vector", qev, after=[t_qq], tag=f"gs{ex}qe{h}")
            cpv[0] += 4

            for w in range(4):
                pvb = PS[2 + w % 2]
                st0, ex0, pv0 = cst[0], cex[0], cpv[0]

                def pemm(eng, w=w, hs=hs, gs=gs, ex0=ex0, pvb=pvb):
                    stb = [PS[0], PS[1], PS[6]]
                    for k2 in range(8):
                        wv = ex0 + (2 * k2 - 1 if k2 >= 1 else 0)
                        if wv > 0:
                            eng.wait_ge(sex, wv)
                        for j in range(2):
                            blk = 2 * k2 + j
                            eng.matmul(stb[blk % 3][:, 0:512],
                                       kv[gs][:, 128 * blk:128 * (blk + 1)],
                                       qv[hs][:, 512 * w:512 * (w + 1)],
                                       start=True, stop=True
                                       ).then_inc(sst, 1)
                        if k2 >= 1:
                            eng.wait_ge(sex, ex0 + 2 * k2)
                            for j in range(2):
                                blk = 2 * (k2 - 1) + j
                                eng.matmul(pvb[0:65, 0:512],
                                           vT[gs][:, blk, :],
                                           probsT[blk % 3][:],
                                           start=(blk == 0), stop=False
                                           ).then_inc(spv, 1)
                    eng.wait_ge(sex, ex0 + 16)
                    eng.matmul(pvb[0:65, 0:512], vT[gs][:, 14, :],
                               probsT[14 % 3][:], start=False, stop=False
                               ).then_inc(spv, 1)
                    eng.matmul(pvb[0:65, 0:512], vT[gs][:, 15, :],
                               probsT[15 % 3][:], start=False, stop=True
                               ).then_inc(spv, 1)
                t_pe = P.step("tensor", pemm,
                              after=[t_qn, t_ko, t_onorm[w % 2], t_pe_last],
                              tag=f"gs{ex}PE{h}.{w}")
                cst[0] += 16
                cpv[0] += 16
                t_pe_last = t_pe

                def actx(eng, st0=st0, pv0=pv0):
                    stb = [PS[0], PS[1], PS[6]]
                    for blk in range(16):
                        eng.wait_ge(sst, st0 + blk + 1)
                        eng.wait_ge(spv, pv0 + blk - 2 if blk >= 3 else pv0)
                        eng.activation(out=probsT[blk % 3][:],
                                       in_=stb[blk % 3][:], func=AF.Exp,
                                       scale=0.125).then_inc(sex, 1)
                t_ax = P.step("scalar", actx, after=[t_qs],
                              tag=f"gs{ex}AX{h}.{w}")
                cex[0] += 16

                def emit_epi(pw, t_pe_pw):
                    nonlocal t_pe_last
                    pvbp = PS[2 + pw % 2]

                    def rcp(eng, pvbp=pvbp):
                        eng.reciprocal(scr_rc, pvbp[64:65, 0:512])
                    t_rc = P.step("vector", rcp, after=[t_pe_pw],
                                  tag=f"gs{ex}r{h}.{pw}")

                    def bcast(eng):
                        eng.matmul(PS[7][0:64, 0:512], ones1[0:1, 0:64],
                                   scr_rc, start=True, stop=True)
                    t_bc = P.step("tensor", bcast,
                                  after=[t_rc, t_onorm[(pw + 1) % 2],
                                         t_pe_last],
                                  tag=f"gs{ex}b{h}.{pw}")
                    t_pe_last = t_bc

                    def onorm(eng, h=h, pw=pw, pvbp=pvbp):
                        eng.tensor_copy(rbc[:], PS[7][0:64, 0:512])
                        eng.drain()
                        eng.tensor_tensor(
                            OFMp[(h // 2) % 2][64 * (h % 2):
                                               64 * (h % 2) + 64,
                                               512 * pw:512 * (pw + 1)],
                            pvbp[0:64, 0:512], rbc[:], op=OP.mult)
                    t_onorm[pw % 2] = P.step("vector", onorm,
                                             after=[t_bc, t_ofm_dma],
                                             tag=f"gs{ex}o{h}.{pw}")

                if pending is not None:
                    emit_epi(pending[0], pending[1])
                pending = (w, t_pe)
            emit_epi(pending[0], pending[1])
            pending = None
            last_use_q[hs] = t_pe
            last_use_k[gs] = t_pe
            if h % 2 == 1:
                def ofdma(eng, inc, h=h):
                    inc(eng.dma_start(out=odst[:, h // 2, :],
                                      in_=OFMp[(h // 2) % 2][:]))
                t_ofm_dma = P.step("sync", ofdma,
                                   after=[t_onorm[0], t_onorm[1]], dma=1,
                                   tag=f"gs{ex}od{h}")
    return P.step("tensor", lambda e: None,
                  after=[t_ofm_dma, t_onorm[0], t_onorm[1]],
                  tag=f"gs{ex}fin")


def emit_go(nc, P, ins, ex, x1_tm, OallS, owS, ofm, obrep, n1wrep, gwS,
            a_f32, x2fin, xfmT, s2a, s2b, s2, rstd, epscol, zTM, x2loc, PS,
            identf, prev):
    B0 = 16 * ex

    def ldo(eng, inc):
        inc(eng.dma_start(out=OallS[:],
                          in_=ofm.rearrange("(c p) t -> p c t", p=128)))
    t_lo1 = P.step("sync", ldo, after=[prev], dma=1, tag=f"go{ex}w")

    def ldow(eng, inc):
        inc(eng.dma_start(out=owS[:],
                          in_=ins["ow"].rearrange("(c p) j -> p c j",
                                                  p=128)))
    t_lo = P.step("gpsimd", ldow, after=[prev], dma=1, tag=f"go{ex}ow")

    t_ev = {0: t_lo, 1: t_lo}
    t_x2b = t_lo
    t_xd = t_lo
    t_pz = t_lo
    for blk in range(16):
        for half in range(2):
            def mm(eng, blk=blk, half=half):
                for c in range(8):
                    eng.matmul(PS[half][:, 0:512],
                               OallS[:, c, 128 * blk:128 * (blk + 1)],
                               owS[:, c, 512 * half:512 * (half + 1)],
                               start=(c == 0), stop=(c == 7))
            t_mm = P.step("tensor", mm, after=[t_lo1, t_lo, t_ev[half]],
                          tag=f"go{ex}m{blk}.{half}")

            def ev(eng, blk=blk, half=half):
                eng.tensor_tensor(a_f32[:, 512 * half:512 * (half + 1)],
                                  PS[half][:, 0:512],
                                  obrep[:, 512 * half:512 * (half + 1)],
                                  op=OP.add)
                eng.drain()
                eng.tensor_tensor(x2fin[:, 512 * half:512 * (half + 1)],
                                  a_f32[:, 512 * half:512 * (half + 1)],
                                  x1_tm[:, B0 + blk,
                                        512 * half:512 * (half + 1)],
                                  op=OP.add)
            t_ev[half] = P.step("vector", ev, after=[t_mm, t_x2b],
                                tag=f"go{ex}e{blk}.{half}")

        def sq(eng):
            eng.activation(out=a_f32[:, 0:512], in_=x2fin[:, 0:512],
                           func=AF.Square, accum_out=s2a)
            eng.drain()
            eng.activation(out=a_f32[:, 512:1024], in_=x2fin[:, 512:1024],
                           func=AF.Square, accum_out=s2b)
        t_sq = P.step("scalar", sq, after=[t_ev[0], t_ev[1]],
                      tag=f"go{ex}q{blk}")

        def st(eng):
            eng.tensor_tensor(s2, s2a, s2b, op=OP.add)
        t_s = P.step("vector", st, after=[t_sq], tag=f"go{ex}s{blk}")

        def rs1(eng):
            eng.tensor_scalar(rstd, s2, 1.0 / D, 1e-6, op0=OP.mult,
                              op1=OP.add)
            eng.drain()
            eng.reciprocal(rstd, rstd)
        t_r0 = P.step("vector", rs1, after=[t_s], tag=f"go{ex}r{blk}a")

        def rs2(eng):
            eng.activation(out=rstd, in_=rstd, func=AF.Sqrt)
        t_r = P.step("scalar", rs2, after=[t_r0], tag=f"go{ex}r{blk}")

        def x2b(eng):
            eng.tensor_scalar(a_f32[:], x2fin[:], rstd, None, op0=OP.mult)
            eng.drain()
            eng.tensor_tensor(x2fin[:], a_f32[:], n1wrep[:], op=OP.mult)
        t_x2b = P.step("vector", x2b, after=[t_r, t_xd], tag=f"go{ex}x{blk}")

        def xd(eng, inc, blk=blk, ex=ex):
            inc(eng.dma_start(
                out=x2loc.ap()[ex].rearrange("(b p) f -> p b f", p=128)
                [:, blk, :], in_=x2fin[:]))
        t_xd = P.step("gpsimd", xd, after=[t_x2b], dma=1, tag=f"go{ex}d{blk}")

        t_tp = t_x2b
        for cb in range(4):
            def tpp(eng, cb=cb):
                for j in range(2):
                    c = 2 * cb + j
                    eng.transpose(PS[4][:, 128 * j:128 * (j + 1)],
                                  x2fin[:, 128 * c:128 * (c + 1)], identf[:])
            t_t = P.step("tensor", tpp, after=[t_x2b, t_tp],
                         tag=f"go{ex}t{blk}.{cb}")

            def tpe(eng, cb=cb):
                for j in range(2):
                    eng.tensor_copy(xfmT[j][:], PS[4][:, 128 * j:128 * (j + 1)])
            t_e = P.step("vector", tpe, after=[t_t], tag=f"go{ex}f{blk}.{cb}")

            def gz(eng, cb=cb, blk=blk):
                for j in range(2):
                    c = 2 * cb + j
                    eng.matmul(PS[5][:, 0:8], xfmT[j][:], gwS[:, c, :],
                               start=(c == 0), stop=(c == 7))
            t_tp = P.step("tensor", gz, after=[t_e, t_pz],
                          tag=f"go{ex}g{blk}.{cb}")

        def pzev(eng, blk=blk):
            eng.tensor_copy(zTM[:, blk, :], PS[5][:, 0:8])
        t_pz = P.step("vector", pzev, after=[t_tp], tag=f"go{ex}z{blk}")
    t_fin = P.step("vector", lambda e: None, after=[t_pz, t_xd, t_x2b],
                   tag=f"go{ex}fin")
    return t_fin, t_xd


def emit_gating(nc, P, ex, zTM, m1all, m2all, e1all, e2all, w1all, w2all,
                wsum, neg30, msk8i, cnd8, rtblk, iota256, big128, rt_in,
                tap, prev):
    iota8 = iota256[:, 0:8]
    big8 = big128[:, 0:8]

    def m1r(eng):
        eng.tensor_reduce(m1all[:], zTM[:], axis=AX.X, op=OP.max)
    t = P.step("vector", m1r, after=[prev], tag=f"gt{ex}m1")

    msk8 = msk8i.bitcast(F32)
    for blk in range(16):
        def sel(eng, blk=blk):
            z = zTM[:, blk, :]
            eng.tensor_tensor(msk8[:], z, m1all[:, blk:blk + 1]
                              .to_broadcast([128, 8]), op=OP.is_ge)
            eng.drain()
            eng.tensor_scalar(msk8[:], msk8[:], -1e9, 1e9, op0=OP.mult,
                              op1=OP.add)
            eng.drain()
            eng.tensor_tensor(cnd8[:], msk8[:], iota8, op=OP.add)
            eng.drain()
            eng.tensor_reduce(e1all[:, blk:blk + 1], cnd8[:],
                              axis=AX.X, op=OP.min)
            eng.drain()
            eng.tensor_tensor(msk8[:], iota8, e1all[:, blk:blk + 1]
                              .to_broadcast([128, 8]), op=OP.is_equal)
            eng.drain()
            eng.tensor_scalar(msk8[:], msk8[:], -1e30, None, op0=OP.mult)
            eng.drain()
            eng.tensor_tensor(cnd8[:], z, msk8[:], op=OP.add)
            eng.drain()
            eng.tensor_reduce(m2all[:, blk:blk + 1], cnd8[:],
                              axis=AX.X, op=OP.max)
            eng.drain()
            eng.tensor_tensor(msk8[:], cnd8[:], m2all[:, blk:blk + 1]
                              .to_broadcast([128, 8]), op=OP.is_ge)
            eng.drain()
            eng.tensor_scalar(msk8[:], msk8[:], -1e9, 1e9, op0=OP.mult,
                              op1=OP.add)
            eng.drain()
            eng.tensor_tensor(cnd8[:], msk8[:], iota8, op=OP.add)
            eng.drain()
            eng.tensor_reduce(e2all[:, blk:blk + 1], cnd8[:],
                              axis=AX.X, op=OP.min)
            eng.drain()
        t = P.step("vector", sel, after=[t], tag=f"gt{ex}s{blk}")

    def sig(eng):
        eng.activation(out=m1all[:], in_=m1all[:], func=AF.Sigmoid)
        eng.drain()
        eng.activation(out=m2all[:], in_=m2all[:], func=AF.Sigmoid)
    t = P.step("scalar", sig, after=[t], tag=f"gt{ex}sg")

    def wts(eng):
        eng.tensor_tensor(wsum[:], m1all[:], m2all[:], op=OP.add)
        eng.drain()
        eng.tensor_scalar(wsum[:], wsum[:], 1e-6, None, op0=OP.add)
        eng.drain()
        eng.reciprocal(wsum[:], wsum[:])
        eng.drain()
        eng.tensor_tensor(w1all[:], m1all[:], wsum[:], op=OP.mult)
        eng.tensor_tensor(w2all[:], m2all[:], wsum[:], op=OP.mult)
    t = P.step("vector", wts, after=[t], tag=f"gt{ex}w")

    def asm(eng):
        eng.tensor_copy(rtblk[:, :, 0], e1all[:])
        eng.tensor_copy(rtblk[:, :, 1], e2all[:])
        eng.tensor_copy(rtblk[:, :, 2], w1all[:])
        eng.tensor_copy(rtblk[:, :, 3], w2all[:])
    t = P.step("vector", asm, after=[t], tag=f"gt{ex}a")
    t = tap(f"rt_{ex}", rtblk[:], t, [128, 16, 4], F32)

    def rtd(eng, inc, ex=ex):
        inc(eng.dma_start(
            out=rt_in.ap().rearrange("(b p) c -> p b c", p=128)
            [:, 16 * ex:16 * (ex + 1), :],
            in_=rtblk[:]))
    t_rt = P.step("sync", rtd, after=[t], dma=1, tag=f"gt{ex}d")
    return t, t_rt



F32 = mybir.dt.float32
BF16 = mybir.dt.bfloat16
I32 = mybir.dt.int32
AX = mybir.AxisListType
OP = mybir.AluOpType
AF = mybir.ActivationFunctionType

D = 1024
DFF = 2048
CAP = 10240


def bfv(region, off_f32, n_bf16):
    return region[:, off_f32:off_f32 + n_bf16 // 2].bitcast(BF16)


def emit_moe_pre(nc, P, L, t_start, n_cores=8):
    R5 = L["R5"]
    NTG = n_cores * 4096
    Z = nc.dram_tensor("m_Z", [NTG + 128, D], BF16)
    slotmap = nc.dram_tensor("m_slot", [CAP + 128, 4], F32)
    zeros = bfv(R5, 2056, 512)
    smfill = R5[:, 2320:2324]

    def zfill(eng):
        eng.memset(zeros[:], 0.0)
        eng.memset(smfill[:], 0.0)
        eng.drain()
        eng.memset(smfill[:, 1:2], float(NTG))
    t_zf = P.step("vector", zfill, after=[t_start], tag="mz0")

    zview = Z.ap().rearrange("(n p) d -> p n d", p=128)
    t_zz = t_zf
    for i in range(NTG // 128 + 1):
        def zz(eng, inc, i=i):
            inc(eng.dma_start(out=zview[:, i, 0:512], in_=zeros[:]))
            inc(eng.dma_start(out=zview[:, i, 512:1024], in_=zeros[:]))
        t_zz = P.step("scalar", zz, after=[t_zf, t_zz], dma=2, tag=f"mzz{i}")
    smview = slotmap.ap().rearrange("(n p) c -> p n c", p=128)
    t_sf = t_zf
    for i in range(CAP // 128 + 1):
        def sf(eng, inc, i=i):
            inc(eng.dma_start(out=smview[:, i, :], in_=smfill[:]))
        t_sf = P.step("scalar", sf, after=[t_zf, t_sf], dma=1, tag=f"msf{i}")
    return dict(Z=Z, slotmap=slotmap, t_zz=t_zz, t_sf=t_sf)


def emit_moe(nc, es, P, ins, L, H, pre, tap, x1_tm, out, prev, n_cores=8):
    R1, R2, R3, R4, R5 = L["R1"], L["R2"], L["R3"], L["R4"], L["R5"]
    PS = L["PS"]
    ones1, ident, identf = L["ones1"], L["ident"], L["identf"]
    ones_col, tri = L["ones_col"], L["tri"]
    rowb = L["rowb"]
    sem_coll = H["sem_coll"]
    byex, rt_all, x2loc = H["byex"], H["rt_all"], H["x2loc"]
    Z, slotmap = pre["Z"], pre["slotmap"]
    NTG = n_cores * 4096
    NBG = NTG // 128
    NC2 = n_cores * 2048
    nch = (NBG + 127) // 128
    groups = [list(range(n_cores))]

    moe_loc = nc.dram_tensor("m_out", [4096, D], BF16)
    x1_hbm = nc.dram_tensor("m_x1", [4096, D], BF16).ap()

    h1FM = bfv(R1, 0, 8192).rearrange("p (f t) -> p f t", f=16)
    xgFM = bfv(R1, 4096, 4096).rearrange("p (c t) -> p c t", c=8)
    xgTM = bfv(R1, 6144, 4096).rearrange("p (b f) -> p b f", b=4)
    rtS = R2[:, 0:1024].rearrange("p (b c) -> p b c", c=4)
    expid_rep = R2[:, 1024:1025]
    expid_st = R2[0:1, 1025:1026]
    colsS = R2[0:1, 1282:1538]
    chunkS = R2[0:1, 1538:1546]
    n2wS = R2[0:1, 2048:3072]
    coloffs = R2[0:1, 3072:3328]
    zstage = bfv(R3, 0, 4096).rearrange("p (b f) -> p b f", b=4)
    keepf = R3[:, 0:256]
    nkeep = R3[:, 256:512]
    sc1 = R3[:, 512:768]
    sc2 = R3[:, 768:1024]
    outstage = R3[:, 2048:3072]
    i1 = R4[:, 0:256]
    i2 = R4[:, 256:512]
    ind = R4[:, 512:768]
    indb = bfv(R4, 768, 256)
    colsb = bfv(R4, 896, 2)
    wsel = R4[:, 900:1156]
    rank = R4[:, 1156:1412]
    pay = R4[:, 1412:2436].rearrange("p (g c) -> p g c", c=4)
    dsti = R3[:, 1280:1536].bitcast(I32)
    iotok = R3[:, 1536:1792]
    colchf = R4[:, 2436:2444]
    excl2 = R4[:, 2444:2446]
    exadd = R4[:, 2446:2447]
    tA = R4[:, 0:1024]
    tB = R4[:, 1024:2048]
    slotS = R5[:, 0:16].rearrange("p (b c) -> p b c", c=4)
    slotI = R5[:, 16:20].bitcast(I32)
    scrA = R5[:, 20:24]
    scrB = R5[:, 24:28]
    scrC = R5[:, 28:32]
    gidxI = R5[:, 32:36].bitcast(I32)
    byexS = R3[:, 1024:1280]
    n2wrep = bfv(R5, 64, 1024)
    lnm = R5[:, 580:596].rearrange("p (b c) -> p b c", c=8)
    xflat = x1_tm[:].rearrange("p b f -> p (b f)")
    w1S = xflat[:, 0:16384].rearrange("p (c j) -> p c j", c=8)
    w2S = xflat[:, 16384:32768].rearrange("p (f j) -> p f j", f=16)
    xbm = bfv(R1, 0, 1024)
    xbx = bfv(R1, 512, 1024)
    xb1 = bfv(R1, 1024, 1024)

    def x1spill(eng, inc):
        inc(eng.dma_start(out=x1_hbm.rearrange("(b p) f -> p b f", p=128),
                          in_=x1_tm[:]))
    t_sp = P.step("sync", x1spill, after=[prev], dma=1, tag="mx1s")

    def wload(eng, inc):
        inc(eng.dma_start(out=w1S[:],
                          in_=ins["w1e"].rearrange("(c p) j -> p c j", p=128)))
        inc(eng.dma_start(out=w2S[:],
                          in_=ins["w2e"].rearrange("(f p) j -> p f j", p=128)))
    t_wl = P.step("gpsimd", wload, after=[t_sp], dma=2, tag="mwl")
    t_wl2 = t_wl

    def rtload(eng, inc):
        eng.wait_ge(sem_coll, 3)
        inc(eng.dma_start(out=rtS[:, 0:NBG, :],
                          in_=rt_all.ap().rearrange("(b p) c -> p b c",
                                                    p=128)))
        inc(eng.dma_start(out=n2wS, in_=ins["norm2_w"]))
        inc(eng.dma_start(out=expid_st, in_=ins["expid"]))
        inc(eng.dma_start(out=iotok[:, 0:NBG],
                          in_=ins["iota_tok"][:, 0:NBG]))
        inc(eng.dma_start(out=byexS[:, 0:NBG],
                          in_=ins["iota_byex"][:, 0:NBG]))
    t_rl = P.step("sync", rtload, after=[prev], dma=5, tag="mrl")

    def repid(eng):
        eng.matmul(PS[0][:, 0:1], ones1[0:1, 0:128], expid_st,
                   start=True, stop=True)
    t_ri = P.step("tensor", repid, after=[t_rl, prev], tag="mri")

    def repev(eng):
        eng.tensor_copy(expid_rep[:], PS[0][:, 0:1])
    t_re = P.step("vector", repev, after=[t_ri], tag="mre")

    def indf(eng):
        eng.tensor_scalar(i1[:, 0:NBG], rtS[:, 0:NBG, 0], expid_rep[:], None,
                          op0=OP.is_equal)
        eng.tensor_scalar(i2[:, 0:NBG], rtS[:, 0:NBG, 1], expid_rep[:], None,
                          op0=OP.is_equal)
        eng.drain()
        eng.tensor_tensor(ind[:, 0:NBG], i1[:, 0:NBG], i2[:, 0:NBG],
                          op=OP.max)
        eng.drain()
        eng.tensor_copy(indb[:, 0:NBG], ind[:, 0:NBG])
        eng.tensor_tensor(i1[:, 0:NBG], i1[:, 0:NBG], rtS[:, 0:NBG, 2],
                          op=OP.mult)
        eng.tensor_tensor(i2[:, 0:NBG], i2[:, 0:NBG], rtS[:, 0:NBG, 3],
                          op=OP.mult)
        eng.drain()
        eng.tensor_tensor(wsel[:, 0:NBG], i1[:, 0:NBG], i2[:, 0:NBG],
                          op=OP.add)
    t_if = P.step("vector", indf, after=[t_re], tag="mif")

    def pref(eng):
        eng.matmul(PS[0][:, 0:NBG], tri[:], indb[:, 0:NBG],
                   start=True, stop=True)
        eng.matmul(PS[1][0:1, 0:NBG], ones_col[:], indb[:, 0:NBG],
                   start=True, stop=True)
    t_pf = P.step("tensor", pref, after=[t_if], tag="mpf")

    def colev(eng):
        eng.memset(colsS[0:1, 0:256], 0.0)
        eng.drain()
        eng.tensor_copy(colsS[0:1, 0:NBG], PS[1][0:1, 0:NBG])
    t_ce = P.step("vector", colev, after=[t_pf], tag="mce")

    def cb1(eng, inc):
        inc(eng.dma_start(out=rowb[0:256], in_=colsS[0:1, 0:256]))
    t_b1 = P.step("sync", cb1, after=[t_ce], dma=1, tag="mb1")

    def cb2(eng, inc):
        inc(eng.dma_start(
            out=colchf[:, 0:2],
            in_=rowb[0:256].rearrange("(a p) -> p a", p=128),
            allow_slow_non_contiguous=True))
    t_b2 = P.step("gpsimd", cb2, after=[t_b1], dma=1, tag="mb2")

    def cbc(eng):
        eng.tensor_copy(colsb[:, 0:2], colchf[:, 0:2])
    t_bc = P.step("vector", cbc, after=[t_b2], tag="mbc")

    def excl(eng):
        eng.matmul(PS[2][:, 0:2], tri[:], colsb[:, 0:2],
                   start=True, stop=True)
        eng.matmul(PS[3][0:1, 0:2], ones_col[:], colsb[:, 0:2],
                   start=True, stop=True)
    t_ex = P.step("tensor", excl, after=[t_bc], tag="mex")

    def exev(eng):
        eng.tensor_tensor(excl2[:, 0:2], PS[2][:, 0:2],
                          colchf[:, 0:2], op=OP.subtract)
    t_exv = P.step("vector", exev, after=[t_ex], tag="mexv")

    def totev(eng):
        eng.tensor_copy(chunkS[0:1, 0:2], PS[3][0:1, 0:2])
    t_tv = P.step("vector", totev, after=[t_ex], tag="mtv")

    def totmm(eng):
        eng.matmul(PS[4][:, 0:1], ones1[0:1, 0:128], chunkS[0:1, 0:1],
                   start=True, stop=True)
    t_tm = P.step("tensor", totmm, after=[t_tv], tag="mtm")

    def totadd(eng):
        eng.tensor_copy(exadd[:], PS[4][:, 0:1])
        eng.drain()
        eng.tensor_tensor(excl2[:, 1:2], excl2[:, 1:2], exadd[:],
                          op=OP.add)
    t_exv = P.step("vector", totadd, after=[t_tm, t_exv], tag="mta")

    def cb3(eng, inc):
        inc(eng.dma_start(out=rowb[0:256].rearrange("(a p) -> p a", p=128),
                          in_=excl2[:, 0:2],
                          allow_slow_non_contiguous=True))
    t_b3 = P.step("sync", cb3, after=[t_exv], dma=1, tag="mb3")

    def cb4(eng, inc):
        inc(eng.dma_start(out=coloffs[0:1, 0:256], in_=rowb[0:256]))
    t_b4 = P.step("gpsimd", cb4, after=[t_b3], dma=1, tag="mb4")

    def offmm(eng):
        eng.matmul(PS[1][:, 0:NBG], ones1[0:1, 0:128], coloffs[0:1, 0:NBG],
                   start=True, stop=True)
    t_om = P.step("tensor", offmm, after=[t_b4, t_ce], tag="mom")

    def rankf(eng):
        eng.tensor_tensor(rank[:, 0:NBG], PS[0][:, 0:NBG], ind[:, 0:NBG],
                          op=OP.subtract)
        eng.drain()
        eng.tensor_tensor(rank[:, 0:NBG], rank[:, 0:NBG], PS[1][:, 0:NBG],
                          op=OP.add)
    t_rk = P.step("vector", rankf, after=[t_om], tag="mrk")

    iota_p = None

    def payf(eng):
        N = NBG
        eng.tensor_scalar(keepf[:, 0:N], rank[:, 0:N], float(CAP) - 0.5,
                          None, op0=OP.is_le)
        eng.drain()
        eng.tensor_tensor(keepf[:, 0:N], keepf[:, 0:N], ind[:, 0:N],
                          op=OP.mult)
        eng.drain()
        eng.tensor_scalar(nkeep[:, 0:N], keepf[:, 0:N], -1.0, 1.0,
                          op0=OP.mult, op1=OP.add)
        eng.drain()
        eng.tensor_tensor(sc1[:, 0:N], rank[:, 0:N], keepf[:, 0:N],
                          op=OP.mult)
        eng.tensor_scalar(sc2[:, 0:N], nkeep[:, 0:N], iotok[:, 0:1],
                          None, op0=OP.mult)
        eng.drain()
        eng.tensor_tensor(sc1[:, 0:N], sc1[:, 0:N], sc2[:, 0:N], op=OP.add)
        eng.tensor_scalar(sc2[:, 0:N], nkeep[:, 0:N], float(CAP), None,
                          op0=OP.mult)
        eng.drain()
        eng.tensor_tensor(sc1[:, 0:N], sc1[:, 0:N], sc2[:, 0:N], op=OP.add)
        eng.drain()
        eng.tensor_copy(dsti[:, 0:N], sc1[:, 0:N])
        eng.tensor_copy(pay[:, 0:N, 0], byexS[:, 0:N])
        eng.tensor_copy(pay[:, 0:N, 1], iotok[:, 0:N])
        eng.tensor_copy(pay[:, 0:N, 2], wsel[:, 0:N])
        eng.memset(pay[:, 0:N, 3], 0.0)
    t_py = P.step("vector", payf, after=[t_rk], tag="mpy")

    def smscat(eng, inc):
        for g in range(NBG):
            inc(eng.indirect_dma_start(
                slotmap.ap(),
                bass.IndirectOffsetOnAxis(dsti[:, g:g + 1], 0),
                pay[:, g, :], None))
    t_sc = P.step("gpsimd", smscat, after=[t_py, pre["t_sf"]], dma=NBG,
                  tag="msc")

    def n2mm(eng):
        eng.matmul(PS[5][:, 0:512], ones1[0:1, 0:128], n2wS[0:1, 0:512],
                   start=True, stop=True)
        eng.matmul(PS[6][:, 0:512], ones1[0:1, 0:128], n2wS[0:1, 512:1024],
                   start=True, stop=True)
    t_n2 = P.step("tensor", n2mm, after=[t_rl, t_om], tag="mn2")

    def n2ev(eng):
        eng.tensor_copy(n2wrep[:, 0:512], PS[5][:, 0:512])
        eng.tensor_copy(n2wrep[:, 512:1024], PS[6][:, 0:512])
    t_n2e = P.step("vector", n2ev, after=[t_n2], tag="mn2e")

    tpbf = PS[0][:].bitcast(BF16)
    byex2 = byex.ap().rearrange("e r d -> (e r) d")
    NWND = CAP // 512
    t_wend = P.step("tensor", lambda e: None, after=[t_n2e, t_sc], tag="mws")
    t_zs = pre["t_zz"]
    t_gel = {0: t_wend, 1: t_wend}

    for wnd in range(NWND):
        def sl(eng, inc, wnd=wnd):
            if wnd == 0:
                eng.wait_ge(sem_coll, 2)
            inc(eng.dma_start(
                out=slotS[:],
                in_=slotmap.ap().rearrange("(b p) c -> p b c", p=128)
                [:, 4 * wnd:4 * wnd + 4, :]))
        t_sl = P.step("sync", sl, after=[t_sc, t_wend], dma=1,
                      tag=f"mw{wnd}s")

        def idx(eng, wnd=wnd):
            tok = slotS[:, :, 1]
            eng.tensor_copy(gidxI[:], slotS[:, :, 0])
            eng.tensor_scalar(scrB[:], tok, float(NTG) - 0.5, None,
                              op0=OP.is_ge)
            eng.drain()
            eng.tensor_scalar(scrB[:], scrB[:], iotok[:, 0:1], None,
                              op0=OP.mult)
            eng.drain()
            eng.tensor_tensor(scrA[:], tok, scrB[:], op=OP.add)
            eng.drain()
            eng.tensor_copy(slotI[:], scrA[:])
        t_idx = P.step("vector", idx, after=[t_sl, t_zs], tag=f"mw{wnd}i")

        def gath(eng, inc, wnd=wnd):
            for b in range(4):
                inc(eng.indirect_dma_start(
                    xgTM[:, b, :], None, byex2,
                    bass.IndirectOffsetOnAxis(gidxI[:, b:b + 1], 0)))
        t_g = P.step("gpsimd", gath, after=[t_idx, t_wend], dma=4,
                     tag=f"mw{wnd}g")

        t_tp_prev = t_g
        for b in range(4):
            def tpp(eng, b=b):
                for c in range(8):
                    eng.transpose(tpbf[:, 128 * c:128 * (c + 1)],
                                  xgTM[:, b, 128 * c:128 * (c + 1)],
                                  ident[:])
            t_tp = P.step("tensor", tpp, after=[t_g, t_wend, t_tp_prev],
                          tag=f"mw{wnd}t{b}")

            def tpe(eng, b=b):
                for c in range(8):
                    eng.tensor_copy(xgFM[:, c, 128 * b:128 * (b + 1)],
                                    tpbf[:, 128 * c:128 * (c + 1)])
            t_tp_prev = P.step("vector", tpe, after=[t_tp],
                               tag=f"mw{wnd}e{b}")
        t_tpe = t_tp_prev

        for f in range(16):
            bank = PS[f % 2]

            def h1m(eng, f=f, bank=bank):
                for c in range(8):
                    eng.matmul(bank[:, 0:512],
                               w1S[:, c, 128 * f:128 * (f + 1)],
                               xgFM[:, c, :], start=(c == 0), stop=(c == 7))
            t_m = P.step("tensor", h1m,
                         after=[t_tpe, t_wl, t_gel[f % 2], t_wend],
                         tag=f"mw{wnd}h1{f}")

            def gel(eng, f=f, bank=bank):
                eng.activation(out=h1FM[:, f, :], in_=bank[:, 0:512],
                               func=AF.Gelu, scale=1.0)
            t_gel[f % 2] = P.step("scalar", gel, after=[t_m],
                                  tag=f"mw{wnd}g{f}")
        t_h1 = P.step("tensor", lambda e: None,
                      after=[t_gel[0], t_gel[1], t_wl2], tag=f"mw{wnd}h1f")

        t_ev_all = []
        for ts4 in range(4):
            for half in range(2):
                bi = ts4 * 2 + half
                bank = PS[bi]

                def h2m(eng, ts4=ts4, half=half, bank=bank):
                    for f in range(16):
                        eng.matmul(
                            bank[:, 0:512],
                            h1FM[:, f, 128 * ts4:128 * (ts4 + 1)],
                            w2S[:, f, 512 * half:512 * (half + 1)],
                            start=(f == 0), stop=(f == 15))
                t_m2 = P.step("tensor", h2m, after=[t_h1],
                              tag=f"mw{wnd}h2{bi}")

                def h2e(eng, ts4=ts4, half=half, bank=bank):
                    eng.tensor_scalar(
                        zstage[:, ts4, 512 * half:512 * (half + 1)],
                        bank[:, 0:512], slotS[:, ts4, 2:3], None,
                        op0=OP.mult)
                tok_ev = P.step("vector", h2e, after=[t_m2, t_zs],
                                tag=f"mw{wnd}z{bi}")
                t_ev_all.append(tok_ev)

        def zscat(eng, inc, wnd=wnd):
            for b in range(4):
                inc(eng.indirect_dma_start(
                    Z.ap(), bass.IndirectOffsetOnAxis(slotI[:, b:b + 1], 0),
                    zstage[:, b, :], None))
        t_zs = P.step("gpsimd", zscat, after=t_ev_all + [pre["t_zz"]],
                      dma=4, tag=f"mw{wnd}zs")
        t_wend = P.step("tensor", lambda e: None, after=t_ev_all,
                        tag=f"mw{wnd}end")

    def rsc(eng):
        eng.collective_compute(
            "ReduceScatter", OP.add, replica_groups=groups,
            ins=[Z.ap()[0:NTG]], outs=[moe_loc.ap()]).then_inc(sem_coll, 1)
    t_rs = P.step("gpsimd", rsc, after=[t_zs], tag="mrs")
    n_coll = H["n_coll"] + 1

    t_ld = prev
    t_st = t_wend
    t_ev_prev = t_wend
    for blk in range(32):
        e = blk // 16
        i = blk % 16

        def ld(eng, inc, blk=blk, e=e, i=i):
            if blk == 0:
                eng.wait_ge(sem_coll, n_coll)
            inc(eng.dma_start(
                out=xbm[:], in_=moe_loc.ap()
                .rearrange("(b p) f -> p b f", p=128)[:, blk, :]))
            inc(eng.dma_start(
                out=xbx[:], in_=x2loc.ap()[e]
                .rearrange("(b p) f -> p b f", p=128)[:, i, :]))
            inc(eng.dma_start(
                out=xb1[:], in_=x1_hbm
                .rearrange("(b p) f -> p b f", p=128)[:, blk, :]))
        t_ld = P.step("sync", ld, after=[t_rs, t_sp, t_ev_prev, t_ld],
                      dma=3, tag=f"mt{blk}l")

        def a2f(eng, blk=blk):
            eng.tensor_tensor(tA[:], xbm[:], xbx[:], op=OP.add)
        t_a = P.step("vector", a2f, after=[t_ld, t_st], tag=f"mt{blk}a")

        def sq(eng):
            eng.activation(out=tB[:, 0:512], in_=tA[:, 0:512],
                           func=AF.Square, accum_out=lnm[:, 0, 0:1])
            eng.drain()
            eng.activation(out=tB[:, 512:1024], in_=tA[:, 512:1024],
                           func=AF.Square, accum_out=lnm[:, 0, 1:2])
        t_q = P.step("scalar", sq, after=[t_a], tag=f"mt{blk}q")

        def rst(eng):
            eng.tensor_tensor(lnm[:, 0, 2:3], lnm[:, 0, 0:1],
                              lnm[:, 0, 1:2], op=OP.add)
            eng.drain()
            eng.tensor_scalar(lnm[:, 0, 2:3], lnm[:, 0, 2:3], 1.0 / D, 1e-6,
                              op0=OP.mult, op1=OP.add)
            eng.drain()
            eng.reciprocal(lnm[:, 0, 2:3], lnm[:, 0, 2:3])
        t_r1 = P.step("vector", rst, after=[t_q], tag=f"mt{blk}r")

        def rs2f(eng):
            eng.activation(out=lnm[:, 0, 3:4], in_=lnm[:, 0, 2:3],
                           func=AF.Sqrt)
        t_r2 = P.step("scalar", rs2f, after=[t_r1], tag=f"mt{blk}r2")

        def ff(eng):
            eng.tensor_scalar(tB[:], tA[:], lnm[:, 0, 3:4], None,
                              op0=OP.mult)
            eng.drain()
            eng.tensor_tensor(tA[:], tB[:], n2wrep[:], op=OP.mult)
            eng.drain()
            eng.tensor_tensor(tB[:], tA[:], xb1[:], op=OP.add)
        t_f = P.step("vector", ff, after=[t_r2], tag=f"mt{blk}f")

        def lmean(eng):
            eng.tensor_reduce(lnm[:, 1, 0:1], tB[:], axis=AX.X, op=OP.add)
        t_m1 = P.step("vector", lmean, after=[t_f], tag=f"mt{blk}m")

        def lsq(eng):
            eng.activation(out=tA[:, 0:512], in_=tB[:, 0:512],
                           func=AF.Square, accum_out=lnm[:, 1, 1:2])
            eng.drain()
            eng.activation(out=tA[:, 512:1024], in_=tB[:, 512:1024],
                           func=AF.Square, accum_out=lnm[:, 1, 2:3])
        t_q2 = P.step("scalar", lsq, after=[t_f], tag=f"mt{blk}q2")

        def lst(eng):
            Lr = lnm[:, 1, :]
            eng.tensor_tensor(Lr[:, 1:2], Lr[:, 1:2], Lr[:, 2:3], op=OP.add)
            eng.tensor_scalar(Lr[:, 3:4], Lr[:, 0:1], 1.0 / D, None,
                              op0=OP.mult)
            eng.drain()
            eng.tensor_tensor(Lr[:, 4:5], Lr[:, 3:4], Lr[:, 3:4], op=OP.mult)
            eng.tensor_scalar(Lr[:, 5:6], Lr[:, 1:2], 1.0 / D, None,
                              op0=OP.mult)
            eng.drain()
            eng.tensor_tensor(Lr[:, 4:5], Lr[:, 5:6], Lr[:, 4:5],
                              op=OP.subtract)
            eng.drain()
            eng.tensor_scalar(Lr[:, 4:5], Lr[:, 4:5], 1e-5, None, op0=OP.add)
            eng.drain()
            eng.reciprocal(Lr[:, 4:5], Lr[:, 4:5])
        t_s2 = P.step("vector", lst, after=[t_m1, t_q2], tag=f"mt{blk}s")

        def lsr(eng):
            eng.activation(out=lnm[:, 1, 4:5], in_=lnm[:, 1, 4:5],
                           func=AF.Sqrt)
        t_s3 = P.step("scalar", lsr, after=[t_s2], tag=f"mt{blk}s3")

        def lap(eng):
            eng.tensor_scalar(outstage[:], tB[:], lnm[:, 1, 3:4],
                              lnm[:, 1, 4:5], op0=OP.subtract, op1=OP.mult)
        t_ev_prev = P.step("vector", lap, after=[t_s3], tag=f"mt{blk}o")

        def sto(eng, inc, blk=blk):
            inc(eng.dma_start(
                out=out.rearrange("(b p) f -> p b f", p=128)[:, blk, :],
                in_=outstage[:]))
        t_st = P.step("sync", sto, after=[t_ev_prev], dma=1,
                      tag=f"mt{blk}st")
    return P.step("sync", lambda e: None, after=[t_st], tag="mfin")

